# version 07
# baseline (speedup 1.0000x reference)
import numpy as np
import ml_dtypes
import concourse.bass as bass
import concourse.mybir as mybir
from concourse.bass_utils import run_bass_kernel_spmd
from contextlib import ExitStack

F32 = mybir.dt.float32
F16 = mybir.dt.float16
BF16 = mybir.dt.bfloat16
I32 = mybir.dt.int32
AF = mybir.ActivationFunctionType
ALU = mybir.AluOpType
AX = mybir.AxisListType.X

NCORES = 8
S, T, C, NCH, NH, HS, NB = 2, 2048, 1024, 8, 16, 64, 127
KTAP, CSTR = 32, 16
I3, NI = 3072, 24
EPS = 1e-5
QMAX = 32767.0
DEN_EPS = 1e-37
ARENA_BYTES = 172800

_DEBUG_TAPS = []
_PHASES = ("dequant", "attn0", "attn1", "ffn")
_ATTN_CUT = None


def _dtsz(dt):
    return {F32: 4, F16: 2, BF16: 2, I32: 4}[dt]


class KB:
    def __init__(self, nc):
        self.nc = nc
        self.ops = {e: [] for e in ("t", "v", "s", "g", "y")}
        self.sem = {}
        self.cnt = {}
        self.waited = {e: {} for e in ("t", "v", "s", "g", "y")}
        self.stack = ExitStack()
        names = ["st", "sv", "ss", "sg", "sy", "cc"]
        for q in ("dy", "dg", "ds"):
            names += [f"{q}{i}" for i in range(4)]
        for name in names:
            self.sem[name] = self.stack.enter_context(nc.semaphore(name))
            self.cnt[name] = 0
        self.qrr = {"y": 0, "g": 0, "s": 0}
        self.bank_tok = {}
        self.guard = {}

    def sb(self, name, shape, dt=F16):
        return self.stack.enter_context(self.nc.sbuf_tensor(name, shape, dt))

    AUTO_SER = {"v": "sv", "s": "ss", "g": "sg"}

    def emit(self, eng, fn, deps=(), inc=None, inc_by=1):
        own = self.AUTO_SER.get(eng)
        is_dma = isinstance(inc, str) and inc.startswith(("d", "c"))
        if own is not None and not is_dma:
            deps = list(deps) + [(own, self.cnt[own])]
            if not isinstance(inc, str):
                inc = own
        waits = []
        for d in deps:
            if d is None:
                continue
            sem, val = d
            if val <= 0 or self.waited[eng].get(sem, 0) >= val:
                continue
            self.waited[eng][sem] = val
            waits.append((self.sem[sem], val))
        tok, sem_h = None, None
        if inc:
            inc = {"t": "st", "v": "sv", "s": "ss", "g": "sg", "y": "sy"}[eng] \
                if inc is True else inc
            self.cnt[inc] += inc_by
            tok = (inc, self.cnt[inc])
            sem_h = self.sem[inc]

        def run(e, waits=waits, fn=fn, sem_h=sem_h, inc_by=inc_by):
            for (h, v) in waits:
                e.wait_ge(h, v)
            ins = fn(e)
            if sem_h is not None:
                ins.then_inc(sem_h, inc_by)
        self.ops[eng].append(run)
        return tok

    def t(self, fn, deps=(), inc=False):
        return self.emit("t", fn, deps, inc)

    def v(self, fn, deps=(), inc=False):
        return self.emit("v", fn, deps, inc)

    def s(self, fn, deps=(), inc=False):
        return self.emit("s", fn, deps, inc)

    def g(self, fn, deps=(), inc=False):
        return self.emit("g", fn, deps, inc)

    def y(self, fn, deps=(), inc=False):
        return self.emit("y", fn, deps, inc)

    def dma(self, queue, out, in_, deps=()):
        base = {"y": "dy", "g": "dg", "s": "ds"}[queue]
        sem = f"{base}{self.qrr[queue]}"
        self.qrr[queue] = (self.qrr[queue] + 1) % 4
        deps = list(deps) + [(sem, self.cnt[sem])]
        return self.emit(queue, lambda e, o=out, i=in_: e.dma_start(out=o, in_=i),
                         deps, sem, 16)

    def bdep(self, *banks):
        return [self.bank_tok.get(b) for b in banks]

    def bset(self, tok, *banks):
        for b in banks:
            self.bank_tok[b] = tok

    def gd(self, name):
        return [self.guard.get(name)]

    def gset(self, name, tok):
        self.guard[name] = tok

    def build(self):
        with self.nc.Block() as block:
            @block.tensor
            def _(e):
                for f in self.ops["t"]:
                    f(e)

            @block.vector
            def _(e):
                for f in self.ops["v"]:
                    f(e)

            @block.scalar
            def _(e):
                for f in self.ops["s"]:
                    f(e)

            @block.gpsimd
            def _(e):
                for f in self.ops["g"]:
                    f(e)

            @block.sync
            def _(e):
                for f in self.ops["y"]:
                    f(e)


class Arena:

    def __init__(self, kb, nbytes):
        self.kb = kb
        self.u8 = kb.sb("sb_ARENA", [128, nbytes], mybir.dt.uint8)
        self.free_list = [(0, nbytes, [])]
        self.live = {}

    def _check(self):
        spans = sorted([(o, o + s, "free") for (o, s, _) in self.free_list] +
                       [(o, o + s, n) for n, (o, s) in self.live.items()])
        for a, b in zip(spans, spans[1:]):
            assert a[1] <= b[0], f"arena overlap: {a} vs {b}"

    def alloc(self, name, nelem, dt=F16):
        self._check()
        size = (nelem * _dtsz(dt) + 31) & ~31
        for i, (off, sz, toks) in enumerate(self.free_list):
            if sz >= size:
                if sz == size:
                    self.free_list.pop(i)
                else:
                    self.free_list[i] = (off + size, sz - size, toks)
                view = self.u8[:, off:off + nelem * _dtsz(dt)].bitcast(dt)
                self.live[name] = (off, size)
                return view, list(toks)
        raise MemoryError(f"arena OOM for {name} ({size}B); free={self.free_list}")

    def free(self, name, tok):
        off, size = self.live.pop(name)
        self.free_list.append((off, size, [tok] if tok else []))
        self.free_list.sort()
        merged = []
        for blk in self.free_list:
            if merged and merged[-1][0] + merged[-1][1] == blk[0]:
                o, s, t1 = merged[-1]
                comb = {}
                for t in list(t1) + list(blk[2]):
                    if t is None:
                        continue
                    if t[0] not in comb or comb[t[0]][1] < t[1]:
                        comb[t[0]] = t
                merged[-1] = (o, s + blk[1], list(comb.values()))
            else:
                merged.append(blk)
        self.free_list = merged


def build_program(taps=()):
    nc = bass.Bass("TRN2", target_bir_lowering=False, debug=False)
    kb = KB(nc)
    sb = kb.sb
    ar = Arena(kb, ARENA_BYTES)

    dx = nc.dram_tensor("x", [S, C, T], F32, kind="ExternalInput").ap()
    dyo = nc.dram_tensor("y", [S, C, T], F32, kind="ExternalOutput").ap()
    d_wattn = nc.dram_tensor("w_attn", [C, 3 * C], F32, kind="ExternalInput").ap()
    d_battn = nc.dram_tensor("b_attn", [128, NI], F32, kind="ExternalInput").ap()
    d_wproj = nc.dram_tensor("w_proj", [C, C], F32, kind="ExternalInput").ap()
    d_bproj = nc.dram_tensor("b_proj", [128, NCH], F32, kind="ExternalInput").ap()
    d_wconv = nc.dram_tensor("w_conv", [128, KTAP * HS], F32, kind="ExternalInput").ap()
    d_bconv = nc.dram_tensor("b_conv", [128, 1], F32, kind="ExternalInput").ap()
    d_compw = nc.dram_tensor("comp_w", [1, HS], F32, kind="ExternalInput").ap()
    d_compb = nc.dram_tensor("comp_b", [1, HS], F32, kind="ExternalInput").ap()
    d_lnlw = nc.dram_tensor("lnl_w", [128, NCH], F32, kind="ExternalInput").ap()
    d_lnlb = nc.dram_tensor("lnl_b", [128, NCH], F32, kind="ExternalInput").ap()
    d_gw1 = nc.dram_tensor("g_w1", [2 * C, C], F32, kind="ExternalInput").ap()
    d_gb1 = nc.dram_tensor("g_b1", [1, C], F32, kind="ExternalInput").ap()
    d_glnw = nc.dram_tensor("g_lnw", [1, C], F32, kind="ExternalInput").ap()
    d_glnb = nc.dram_tensor("g_lnb", [1, C], F32, kind="ExternalInput").ap()
    d_gw2 = nc.dram_tensor("g_w2", [C, 2], F32, kind="ExternalInput").ap()
    d_gb2 = nc.dram_tensor("g_b2", [1, 2], F32, kind="ExternalInput").ap()
    d_wup = nc.dram_tensor("w_up", [I3, C], F32, kind="ExternalInput").ap()
    d_wgate = nc.dram_tensor("w_gate", [I3, C], F32, kind="ExternalInput").ap()
    d_wdown = nc.dram_tensor("w_down", [C, I3], F32, kind="ExternalInput").ap()

    d_x2 = nc.dram_tensor("x2_sp", [S, C, T], F32).ap()
    d_ln2 = nc.dram_tensor("ln2_sp", [1, S * 2 * T], F32).ap()
    d_upT = nc.dram_tensor("upT_sp", [C, I3], F16).ap()
    d_gateT = nc.dram_tensor("gateT_sp", [C, I3], F16).ap()
    d_downT = nc.dram_tensor("downT_sp", [I3, C], F16).ap()
    d_up = nc.dram_tensor("up_sp", [128, NI * S * T], F16).ap()
    d_gate = nc.dram_tensor("gate_sp", [128, NI * S * T], F16).ap()
    d_t2 = nc.dram_tensor("t2_sp", [128, NI * S * T], F16).ap()
    d_oraw = nc.dram_tensor("oraw_sp", [128, NI * S * T], F16).ap()
    d_hrow = nc.dram_tensor("hrow_sp", [1, C], F32).ap()
    d_bmov = nc.dram_tensor("bmov_sp", [64, 1024], F16).ap()
    d_ccin = [nc.dram_tensor(f"ccin{i}", [1, 8], F32).ap() for i in range(4)]
    d_ccout = [nc.dram_tensor(f"ccout{i}", [1, 8], F32).ap() for i in range(4)]
    d_taps = {}

    ones1 = sb("sb_ones1", [128, 1], F16)
    idf = sb("sb_idf", [128, 128], F16)
    bc_w = sb("sb_bc_w", [128, 128], F16)
    bc_w32 = sb("sb_bc_w32", [32, 128], F32)
    bc_stage = sb("sb_bc_stage", [32, 512], F16)
    bc_st32 = sb("sb_bc_st32", [32, 128], F32)
    div_st = sb("sb_div_st", [128, 1024], BF16)
    dvb = sb("sb_dvb", [64, 1024], BF16)
    bstg = sb("sb_bstg", [64, 1024], F16)
    b_attn = sb("sb_b_attn", [128, NI], F32)
    b_proj = sb("sb_b_proj", [128, NCH], F32)
    b_conv = sb("sb_b_conv", [128, 1], F32)
    lnl_w = sb("sb_lnl_w", [128, NCH], F32)
    lnl_b = sb("sb_lnl_b", [128, NCH], F32)
    w_conv = sb("sb_w_conv", [128, KTAP * HS], F16)
    w_g2 = sb("sb_w_g2", [128, NCH * 2], F16)
    crep_w = sb("sb_crep_w", [128, NH * HS], F16)
    crep_b = sb("sb_crep_b", [128, NH * HS], F16)
    mn_bc = sb("sb_mn_bc", [128, 32], F32)
    hcol = sb("sb_hcol", [128, NCH], F16)
    gbc = sb("sb_gbc", [128, 2], F32)
    scbc = sb("sb_scbc", [128, 16], F32)
    maxc = sb("sb_maxc", [128, 256], F32)
    stat_ev = sb("sb_stat_ev", [128, 1024], F32)
    nrow = sb("sb_nrow", [1, 128], F32)
    vcompT = sb("sb_vcompT", [128, NH * 65], F16)
    red16 = sb("sb_red16", [128, 4 * NH], F32)
    pt = sb("sb_pt", [128, 4 * 512], BF16)
    vt = sb("sb_vt", [128, 2 * 16 * 65], F16)
    kcomp = sb("sb_kcomp", [128, NCH * NB], F16)
    epcol = sb("sb_epcol", [128, 2], F32)

    RQ, RK, RKC, RM = 0, NH, 2 * NH, 3 * NH
    RG = 5 * NH
    RX = RG + 16

    PS = kb.stack.enter_context(nc.psum_tensor("PS", [128, 4096], F32))
    PS16 = PS[:, :].bitcast(F16)

    def bank(i, r0=0, r1=128, c0=0, c1=512):
        return PS[r0:r1, 512 * i + c0: 512 * i + c1]

    def tap(name, ap, deps):
        if name not in taps:
            return
        d = nc.dram_tensor("tap_" + name, list(ap.shape), ap.dtype,
                           kind="ExternalOutput").ap()
        d_taps[name] = d
        kb.dma("y", d, ap, deps=deps)

    tok_ones = kb.v(lambda e: e.memset(ones1[:, :], 1.0), inc=True)
    kb.v(lambda e: e.memset(nrow[:, :], 0.0))
    kb.v(lambda e: e.memset(maxc[:, :], 0.0))
    kb.v(lambda e: e.memset(stat_ev[:, :], 0.0))
    kb.v(lambda e: e.memset(bc_w[:, :], 0.0))
    kb.v(lambda e: e.memset(bc_w[0:1, :], 1.0))
    tok_bcw = kb.v(lambda e: e.memset(bc_w[64:65, :], 1.0), inc=True)
    kb.v(lambda e: e.memset(bc_w32[:, :], 0.0))
    tok_bcw32 = kb.v(lambda e: e.memset(bc_w32[0:1, :], 1.0), inc=True)
    tok_stage = kb.v(lambda e: e.memset(bc_stage[:, :], 0.0), inc=True)
    tok_st32 = kb.v(lambda e: e.memset(bc_st32[:, :], 0.0), inc=True)
    tok_dst = kb.v(lambda e: e.memset(div_st[:, :], 0.0), inc=True)
    kb.v(lambda e: e.memset(epcol[:, 0:1], DEN_EPS))
    tok_ep = kb.v(lambda e: e.memset(epcol[:, 1:2], EPS), inc=True)
    kb.v(lambda e: e.memset(vt[:, :], 0.0))
    tok_vt0 = kb.v(lambda e: e.memset(
        vt[:, :].rearrange("p (k c) -> p k c", c=65)[:, :, 64:65], 1.0),
        inc=True)
    kb.v(lambda e: e.memset(vcompT[:, :], 0.0))
    tok_vc0 = kb.v(lambda e: e.memset(
        vcompT[:, :].rearrange("p (h c) -> p h c", c=65)[:, :, 64:65], 1.0),
        inc=True)
    tok_idt = kb.g(lambda e: e.memset(idf[:, :], 1.0), inc=True)
    tok_id = kb.g(lambda e: e.affine_select(
        idf[:, :], idf[:, :], pattern=[[1, 128]], compare_op=ALU.is_equal,
        fill=0.0, base=0, channel_multiplier=-1), deps=[tok_idt], inc=True)

    tok_wconv = kb.dma("g", w_conv[:, :], d_wconv)
    tok_wg2 = kb.dma("g", w_g2[:, :].rearrange("p (ci j) -> p ci j", j=2),
                     d_gw2.rearrange("(ci p) j -> p ci j", p=128))
    tok_battn = kb.dma("y", b_attn[:, :], d_battn)
    tok_bproj = kb.dma("y", b_proj[:, :], d_bproj)
    tok_bconv = kb.dma("y", b_conv[:, :], d_bconv)
    tok_lnlw = kb.dma("y", lnl_w[:, :], d_lnlw)
    tok_lnlb = kb.dma("y", lnl_b[:, :], d_lnlb)
    tok_cw = kb.dma("y", bc_st32[0:1, 0:HS], d_compw, deps=[tok_st32])
    tok_cb = kb.dma("y", bc_st32[0:1, HS:2 * HS], d_compb, deps=[tok_st32])

    tok_m1 = kb.t(lambda e: e.matmul(bank(0, c1=64), bc_w32[0:32, :],
                                     bc_st32[0:32, 0:64], start=True, stop=True),
                  deps=[tok_cw, tok_st32, tok_bcw32], inc=True)
    tok_m2 = kb.t(lambda e: e.matmul(bank(1, c1=64), bc_w32[0:32, :],
                                     bc_st32[0:32, HS:2 * HS],
                                     start=True, stop=True),
                  deps=[tok_cb, tok_st32, tok_bcw32], inc=True)
    crep_toks = []
    for h in range(NH):
        crep_toks.append(kb.v(lambda e, h=h: e.tensor_copy(
            crep_w[:, h * 64:(h + 1) * 64], bank(0, c1=64)),
            deps=[tok_m1], inc=(h == NH - 1)))
    for h in range(NH):
        crep_toks.append(kb.v(lambda e, h=h: e.tensor_copy(
            crep_b[:, h * 64:(h + 1) * 64], bank(1, c1=64)),
            deps=[tok_m2], inc=(h == NH - 1)))
    tok_crep = crep_toks[-1]
    kb.bset(tok_crep, 0, 1)

    def bc_row16(src_row, src_off, n, dst16, dst_off, deps, banks=(0, 1)):
        toks = []
        off, bi = 0, 0
        prev_cp = None
        while off < n:
            w = min(512, n - off)
            bki = banks[bi % len(banks)]
            cp = kb.v(lambda e, o=off, w=w: e.tensor_copy(
                bc_stage[0:1, 0:w], src_row[0:1, src_off + o:src_off + o + w]),
                deps=list(deps) + [tok_stage, tok_bcw, prev_cp], inc=True)
            mm = kb.t(lambda e, bk=bki, w=w: e.matmul(
                bank(bk, c1=w), bc_w[0:32, :], bc_stage[0:32, 0:w],
                start=True, stop=True), deps=[cp] + kb.bdep(bki), inc=True)
            prev_cp = mm
            ev = kb.v(lambda e, bk=bki, o=off, w=w: e.tensor_copy(
                dst16[:, dst_off + o:dst_off + o + w], bank(bk, c1=w)),
                deps=[mm], inc=True)
            kb.bset(ev, bki)
            toks.append(ev)
            off += w
            bi += 1
        return toks

    def bc_row32(src_row, src_off, n, dst32, dst_off, deps, bki=0):
        cp = kb.v(lambda e, n=n: e.tensor_copy(
            bc_st32[0:1, 0:n], src_row[0:1, src_off:src_off + n]),
            deps=list(deps) + [tok_st32, tok_bcw32], inc=True)
        mm = kb.t(lambda e, n=n, bk=bki: e.matmul(
            bank(bk, c1=n), bc_w32[0:32, :], bc_st32[0:32, 0:n],
            start=True, stop=True), deps=[cp] + kb.bdep(bki), inc=True)
        ev = kb.v(lambda e, n=n, bk=bki: e.tensor_copy(
            dst32[:, dst_off:dst_off + n], bank(bk, c1=n)), deps=[mm], inc=True)
        kb.bset(ev, bki)
        return ev

    def ln_stats(src16, nch, lnr, sq2, src_deps, lnr_deps=()):
        sq_free = [None, None]
        tok_mm = None
        for ci in range(nch):
            sl = (ci % 2) * T
            tok_sq = kb.s(lambda e, ci=ci, sl=sl: e.activation(
                sq2[:, sl:sl + T], src16[:, ci * T:(ci + 1) * T], AF.Square),
                deps=list(src_deps) + [sq_free[ci % 2]], inc=True)
            for g in range(4):
                dep0 = (kb.bdep(6, 7) + list(src_deps) + [tok_ones]) \
                    if ci == 0 else []
                kb.t(lambda e, ci=ci, g=g: e.matmul(
                    bank(6, r0=32 * g, r1=32 * g + 1), ones1[:, 0:1],
                    src16[:, ci * T + 512 * g: ci * T + 512 * (g + 1)],
                    start=(ci == 0), stop=(ci == nch - 1),
                    tile_position=(0, 32 * g), skip_group_check=True),
                    deps=dep0)
                tok_mm = kb.t(lambda e, g=g, sl=sl, ci=ci: e.matmul(
                    bank(7, r0=32 * g, r1=32 * g + 1), ones1[:, 0:1],
                    sq2[:, sl + 512 * g: sl + 512 * (g + 1)],
                    start=(ci == 0), stop=(ci == nch - 1),
                    tile_position=(0, 32 * g), skip_group_check=True),
                    deps=[tok_sq] + (dep0 if ci == 0 else []),
                    inc=(g == 3))
            sq_free[ci % 2] = tok_mm
        for g in range(4):
            ev1 = kb.v(lambda e, g=g: e.tensor_copy(
                stat_ev[32 * g:32 * g + 1, 0:512],
                bank(6, r0=32 * g, r1=32 * g + 1)),
                deps=([tok_mm] + kb.gd("stat_ev")) if g == 0 else [], inc=True)
        for g in range(4):
            ev2 = kb.v(lambda e, g=g: e.tensor_copy(
                stat_ev[32 * g:32 * g + 1, 512:1024],
                bank(7, r0=32 * g, r1=32 * g + 1)),
                deps=[tok_mm] if g == 0 else [], inc=True)
        kb.bset(ev2, 6, 7)
        g1 = kb.dma("y", lnr[0:1, 0:T], stat_ev[0:97:32, 0:512],
                    deps=[ev1] + list(lnr_deps))
        g2 = kb.dma("y", lnr[0:1, T:2 * T], stat_ev[0:97:32, 512:1024],
                    deps=[ev2] + list(lnr_deps))
        kb.gset("stat_ev", g2)
        inv = 1.0 / (nch * 128)
        kb.v(lambda e: e.tensor_scalar(lnr[0:1, 2 * T:3 * T], lnr[0:1, 0:T],
                                       inv, None, op0=ALU.mult), deps=[g1])
        kb.v(lambda e: e.tensor_scalar(lnr[0:1, T:2 * T], lnr[0:1, T:2 * T],
                                       inv, None, op0=ALU.mult), deps=[g2])
        kb.v(lambda e: e.tensor_tensor(lnr[0:1, 0:T], lnr[0:1, 2 * T:3 * T],
                                       lnr[0:1, 2 * T:3 * T], op=ALU.mult))
        tvar = kb.v(lambda e: e.tensor_tensor(lnr[0:1, T:2 * T],
                                              lnr[0:1, T:2 * T],
                                              lnr[0:1, 0:T],
                                              op=ALU.subtract), inc=True)
        tln = kb.s(lambda e: e.activation(lnr[0:1, 0:T], lnr[0:1, T:2 * T],
                                          AF.Ln, bias=epcol[0:1, 1:2]),
                   deps=[tvar, tok_ep], inc=True)
        tr = kb.s(lambda e: e.activation(lnr[0:1, 0:T], lnr[0:1, 0:T],
                                         AF.Exp, scale=-0.5),
                  deps=[tln], inc=True)
        tmr = kb.v(lambda e: e.tensor_tensor(lnr[0:1, T:2 * T],
                                             lnr[0:1, 2 * T:3 * T],
                                             lnr[0:1, 0:T], op=ALU.mult),
                   deps=[tr], inc=True)
        return tr, tmr

    def ln_bcast(lnr, brw, tok_r, tok_mr):
        tR = bc_row16(lnr, 0, T, brw, 0, [tok_r], banks=(0, 1))
        tMR = bc_row16(lnr, T, T, brw, T, [tok_mr], banks=(2, 3))
        return tR + tMR

    def ln_apply(dst, dst_off, src, src_off, brw, deps):
        t1 = kb.v(lambda e: e.tensor_tensor(
            dst[:, dst_off:dst_off + T], src[:, src_off:src_off + T],
            brw[:, 0:T], op=ALU.mult), deps=deps, inc=True)
        return kb.v(lambda e: e.tensor_tensor(
            dst[:, dst_off:dst_off + T], dst[:, dst_off:dst_off + T],
            brw[:, T:2 * T], op=ALU.subtract), deps=[t1], inc=True)

    def emit_dequant():
        dqw, dw_dep = ar.alloc("dqw", I3, F32)
        dqs, ds_dep = ar.alloc("dqs", I3 // 32, F32)
        dqr, dr_dep = ar.alloc("dqr", I3 // 32, F32)
        dqi, di_dep = ar.alloc("dqi", I3, I32)
        dqq, dq_dep = ar.alloc("dqq", I3, F16)
        dqt, dt_dep = ar.alloc("dqt", I3, F16)
        base_dep = dw_dep + ds_dep + dr_dep + di_dep + dq_dep + dt_dep
        specs = [(d_wup, d_upT, I3, C), (d_wgate, d_gateT, I3, C),
                 (d_wdown, d_downT, C, I3)]
        prev_q = None
        prev_sp = None
        last_sp = None
        for (dsrc, ddst, O, I) in specs:
            nb = I // 32
            nic = I // 128
            for oc in range(O // 128):
                ld = kb.dma("y", dqw[:, 0:I], dsrc[oc * 128:(oc + 1) * 128, :],
                            deps=[prev_q] + base_dep)
                bm = kb.v(lambda e, I=I, nb=nb: e.tensor_reduce(
                    dqs[:, 0:nb],
                    dqw[:, 0:I].rearrange("p (nb b) -> p nb b", b=32),
                    axis=AX, op=ALU.max, apply_absolute_value=True),
                    deps=[ld], inc=True)
                sc = kb.v(lambda e, nb=nb: e.tensor_scalar(
                    dqs[:, 0:nb], dqs[:, 0:nb], 1.0 / 7.0, 1e-12,
                    op0=ALU.mult, op1=ALU.max), deps=[bm], inc=True)
                rc = kb.v(lambda e, nb=nb: e.reciprocal(dqr[:, 0:nb],
                                                        dqs[:, 0:nb]),
                          deps=[sc], inc=True)
                nn = kb.v(lambda e, I=I, nb=nb: e.tensor_tensor(
                    dqi[:, 0:I], dqw[:, 0:I],
                    dqr[:, 0:nb].unsqueeze(2).broadcast_to((128, nb, 32)), op=ALU.mult),
                    deps=[rc], inc=True)
                qq = kb.v(lambda e, I=I, nb=nb: e.tensor_tensor(
                    dqq[:, 0:I], dqi[:, 0:I],
                    dqs[:, 0:nb].unsqueeze(2).broadcast_to((128, nb, 32)), op=ALU.mult),
                    deps=[nn], inc=True)
                prev_q = qq
                ev_toks = []
                for grp in range((nic + 7) // 8):
                    tb = 4 + grp % 2
                    ntr = min(8, nic - grp * 8)
                    tk = None
                    for k in range(ntr):
                        ci = grp * 8 + k
                        tk = kb.t(lambda e, tb=tb, k=k, ci=ci: e.matmul(
                            PS16[:, tb * 1024 + 128 * k:
                                 tb * 1024 + 128 * (k + 1)],
                            dqq[:, ci * 128:(ci + 1) * 128], idf[:, :],
                            start=True, stop=True, is_transpose=True,
                            skip_group_check=True),
                            deps=([qq, tok_id] + kb.bdep(tb)) if k == 0 else [],
                            inc=(k == ntr - 1))
                    ev = kb.v(lambda e, tb=tb, grp=grp, ntr=ntr: e.tensor_copy(
                        dqt[:, grp * 1024: grp * 1024 + ntr * 128],
                        PS16[:, tb * 1024: tb * 1024 + ntr * 128]),
                        deps=[tk, prev_sp], inc=True)
                    kb.bset(ev, tb)
                    ev_toks.append(ev)
                sp = kb.dma("s", ddst.rearrange("(ci p) j -> p ci j", p=128)
                            [:, :, oc * 128:(oc + 1) * 128],
                            dqt[:, 0:I].rearrange("p (ci j) -> p ci j", j=128),
                            deps=ev_toks)
                prev_sp = sp
                last_sp = sp
        return last_sp, ("dqw", "dqs", "dqr", "dqi", "dqq", "dqt")

    def emit_attention(s, first_deps, free_after_qkv=None):
        xln, d5 = ar.alloc(f"xln{s}", NCH * T, F16)
        lnr, d1 = ar.alloc(f"lnr{s}", 3 * T, F32)
        sq2, d2 = ar.alloc(f"sq2{s}", 2 * T, F16)
        xst, d3 = ar.alloc(f"xst{s}", 2 * T, F16)
        brw, d4 = ar.alloc(f"brw{s}", 2 * T, F16)
        adeps = d1 + d2 + d3 + d4 + d5 + list(first_deps)
        xst_free = [None, None]
        st_deps = []
        for ci in range(NCH):
            sl = (ci % 2) * T
            lx = kb.dma("g", xst[:, sl:sl + T],
                        dx[s, ci * 128:(ci + 1) * 128, :],
                        deps=adeps + [xst_free[ci % 2]])
            tok_sq = kb.s(lambda e, sl=sl: e.activation(
                sq2[:, sl:sl + T], xst[:, sl:sl + T], AF.Square),
                deps=[lx, xst_free[ci % 2]], inc=True)
            for g in range(4):
                dep0 = (kb.bdep(6, 7) + [tok_ones]) if ci == 0 else []
                t1 = kb.t(lambda e, g=g, sl=sl, ci=ci: e.matmul(
                    bank(6, r0=32 * g, r1=32 * g + 1), ones1[:, 0:1],
                    xst[:, sl + 512 * g: sl + 512 * (g + 1)],
                    start=(ci == 0), stop=(ci == NCH - 1),
                    tile_position=(0, 32 * g), skip_group_check=True),
                    deps=dep0 + [lx], inc=(g == 3))
                t2 = kb.t(lambda e, g=g, sl=sl, ci=ci: e.matmul(
                    bank(7, r0=32 * g, r1=32 * g + 1), ones1[:, 0:1],
                    sq2[:, sl + 512 * g: sl + 512 * (g + 1)],
                    start=(ci == 0), stop=(ci == NCH - 1),
                    tile_position=(0, 32 * g), skip_group_check=True),
                    deps=[tok_sq] + dep0, inc=(g == 3))
            xst_free[ci % 2] = t2
            st_deps.append(t2)
        tok_mm = st_deps[-1]
        for g in range(4):
            ev1 = kb.v(lambda e, g=g: e.tensor_copy(
                stat_ev[32 * g:32 * g + 1, 0:512],
                bank(6, r0=32 * g, r1=32 * g + 1)),
                deps=([tok_mm] + kb.gd("stat_ev")) if g == 0 else [], inc=True)
        for g in range(4):
            ev2 = kb.v(lambda e, g=g: e.tensor_copy(
                stat_ev[32 * g:32 * g + 1, 512:1024],
                bank(7, r0=32 * g, r1=32 * g + 1)),
                deps=[tok_mm] if g == 0 else [], inc=True)
        kb.bset(ev2, 6, 7)
        g1 = kb.dma("y", lnr[0:1, 0:T], stat_ev[0:97:32, 0:512], deps=[ev1])
        g2 = kb.dma("y", lnr[0:1, T:2 * T], stat_ev[0:97:32, 512:1024],
                    deps=[ev2])
        kb.gset("stat_ev", g2)
        inv = 1.0 / C
        kb.v(lambda e: e.tensor_scalar(lnr[0:1, 2 * T:3 * T], lnr[0:1, 0:T],
                                       inv, None, op0=ALU.mult), deps=[g1])
        kb.v(lambda e: e.tensor_scalar(lnr[0:1, T:2 * T], lnr[0:1, T:2 * T],
                                       inv, None, op0=ALU.mult), deps=[g2])
        kb.v(lambda e: e.tensor_tensor(lnr[0:1, 0:T], lnr[0:1, 2 * T:3 * T],
                                       lnr[0:1, 2 * T:3 * T], op=ALU.mult))
        tvar = kb.v(lambda e: e.tensor_tensor(lnr[0:1, T:2 * T],
                                              lnr[0:1, T:2 * T],
                                              lnr[0:1, 0:T],
                                              op=ALU.subtract), inc=True)
        tln = kb.s(lambda e: e.activation(lnr[0:1, 0:T], lnr[0:1, T:2 * T],
                                          AF.Ln, bias=epcol[0:1, 1:2]),
                   deps=[tvar, tok_ep], inc=True)
        tr = kb.s(lambda e: e.activation(lnr[0:1, 0:T], lnr[0:1, 0:T],
                                         AF.Exp, scale=-0.5),
                  deps=[tln], inc=True)
        tmr = kb.v(lambda e: e.tensor_tensor(lnr[0:1, T:2 * T],
                                             lnr[0:1, 2 * T:3 * T],
                                             lnr[0:1, 0:T], op=ALU.mult),
                   deps=[tr], inc=True)
        bts = ln_bcast(lnr, brw, tr, tmr)
        xln_toks = []
        xst_free2 = [st_deps[-1], st_deps[-1]]
        for ci in range(NCH):
            sl = (ci % 2) * T
            lx = kb.dma("g", xst[:, sl:sl + T],
                        dx[s, ci * 128:(ci + 1) * 128, :],
                        deps=[xst_free2[ci % 2]] + bts)
            tk = ln_apply(xln, ci * T, xst, sl, brw, [lx] + bts)
            xst_free2[ci % 2] = tk
            xln_toks.append(tk)
        tap(f"xln{s}", xln[:, :], xln_toks)
        ar.free(f"lnr{s}", xln_toks[-1])
        ar.free(f"sq2{s}", xln_toks[-1])
        ar.free(f"xst{s}", xln_toks[-1])
        ar.free(f"brw{s}", xln_toks[-1])
        if _ATTN_CUT == 1:
            return xln_toks

        wst, d7 = ar.alloc(f"wst{s}", 2 * NCH * 128, F16)
        qkvb, d6 = ar.alloc(f"qkv{s}", 3 * NCH * T, F16)
        qb = qkvb[:, 0:NCH * T]
        kbu = qkvb[:, NCH * T:2 * NCH * T]
        vbu = qkvb[:, 2 * NCH * T:3 * NCH * T]
        qkv_ev = []
        wld = {}
        for J in range(NI):
            ws = J % 2
        wfree = {}
        for J in range(NI):
            ws = J % 2
            wld[J] = kb.dma("g",
                            wst[:, ws * 1024:(ws + 1) * 1024]
                            .rearrange("p (ci j) -> p ci j", j=128),
                            d_wattn.rearrange("(ci p) j -> p ci j", p=128)
                            [:, :, J * 128:(J + 1) * 128],
                            deps=d7 + d6 + [wfree.get(J - 2)])
            dci = J % NCH
            dstoff = (J // NCH) * NCH * T + dci * T
            for tg in range(4):
                bk = (J * 4 + tg) % 4
                tkmm = None
                for ci in range(NCH):
                    tkmm = kb.t(lambda e, J=J, ws=ws, tg=tg, ci=ci, bk=bk:
                                e.matmul(
                        bank(bk),
                        wst[:, ws * 1024 + ci * 128: ws * 1024 + (ci + 1) * 128],
                        xln[:, ci * T + tg * 512: ci * T + (tg + 1) * 512],
                        start=(ci == 0), stop=(ci == NCH - 1)),
                        deps=(kb.bdep(bk) + [wld[J], wfree.get(J - 2)]
                              + xln_toks) if ci == 0 else [],
                        inc=(ci == NCH - 1))
                ev = kb.s(lambda e, off=dstoff, tg=tg, bk=bk, J=J: e.activation(
                    qkvb[:, off + tg * 512: off + (tg + 1) * 512],
                    bank(bk), AF.Identity, bias=b_attn[:, J:J + 1]),
                    deps=[tkmm, tok_battn], inc=True)
                kb.bset(ev, bk)
                qkv_ev.append(ev)
            wfree[J] = qkv_ev[-1]
        ar.free(f"wst{s}", qkv_ev[-1])
        ar.free(f"xln{s}", qkv_ev[-1])
        if free_after_qkv is not None:
            for nm in free_after_qkv[0]:
                ar.free(nm, free_after_qkv[1])
        tap(f"qb{s}", qb, [qkv_ev[-1]])
        tap(f"kbu{s}", kbu, [qkv_ev[-1]])
        tap(f"vbu{s}", vbu, [qkv_ev[-1]])
        if _ATTN_CUT == 2:
            return [qkv_ev[-1]]

        ccst, d8 = ar.alloc(f"ccst{s}", NH * HS, F16)
        conv_ev, d9 = ar.alloc(f"cev{s}", 2 * 4 * NB, F16)
        sqn, d10 = ar.alloc(f"sqn{s}", 2 * T, F16)
        cdeps = d8 + d9 + d10
        kc_toks, vc_toks, kcn_tok = emit_compress(
            s, qb, kbu, vbu, ccst, conv_ev, sqn, qkv_ev, cdeps)
        if _ATTN_CUT == 3 or (_ATTN_CUT or 0) > 300:
            return [t for t in kc_toks + vc_toks + [kcn_tok]
                    if t is not None]

        tok_mn = emit_norms(s, qb, kbu, sqn, qkv_ev, kcn_tok)
        ar.free(f"ccst{s}", kc_toks[-1])
        ar.free(f"cev{s}", kc_toks[-1])
        ar.free(f"sqn{s}", tok_mn)
        if _ATTN_CUT == 4:
            return [tok_mn]

        locb, d11 = ar.alloc(f"loc{s}", NCH * T, F16)
        cmpb, d12 = ar.alloc(f"cmp{s}", NCH * T, F16)
        loc_toks, cmp_toks = emit_attn_core(
            s, qb, kbu, vbu, locb, cmpb, kc_toks, vc_toks, tok_mn, qkv_ev,
            d11 + d12)
        ar.free(f"qkv{s}", cmp_toks[-1])
        if _ATTN_CUT == 5:
            return loc_toks + cmp_toks

        lnmb, d13 = ar.alloc(f"lnm{s}", NCH * T, F16)
        lnrb, d14 = ar.alloc(f"lnr2{s}", 3 * T, F32)
        sq2b_, d15 = ar.alloc(f"sq2b{s}", 2 * T, F16)
        brwb, d16 = ar.alloc(f"brw2{s}", 2 * T, F16)
        ldeps = d13 + d14 + d15 + d16 + loc_toks + cmp_toks
        tr, tmr = ln_stats(locb, NCH, lnrb, sq2b_, ldeps)
        bts = ln_bcast(lnrb, brwb, tr, tmr)
        lnm_toks = []
        for ci in range(NCH):
            tt = ln_apply(lnmb, ci * T, locb, ci * T, brwb, ldeps + bts)
            tt2 = kb.v(lambda e, ci=ci: e.tensor_scalar(
                lnmb[:, ci * T:(ci + 1) * T], lnmb[:, ci * T:(ci + 1) * T],
                lnl_w[:, ci:ci + 1], lnl_b[:, ci:ci + 1],
                op0=ALU.mult, op1=ALU.add), deps=[tt, tok_lnlw, tok_lnlb],
                inc=True)
            lnm_toks.append(tt2)
        tap(f"lnm{s}", lnmb[:, :], lnm_toks)
        if _ATTN_CUT == 6:
            return lnm_toks

        wg1, dwg1 = ar.alloc(f"wg1{s}", 8 * C, F16)
        tok_wg1 = kb.dma("g",
                         wg1[:, :].rearrange("p (f j) -> p f j", j=C),
                         d_gw1.rearrange("(f p) j -> p f j", p=128)[:, 0:8, :],
                         deps=dwg1)
        tok_g = emit_gate(s, lnmb, cmpb, lnrb, lnm_toks, cmp_toks, wg1,
                          tok_wg1, d_gw1)
        ar.free(f"wg1{s}", tok_g)
        if _ATTN_CUT == 7:
            return [tok_g]

        comb_toks = []
        for ci in range(NCH):
            t1 = kb.v(lambda e, ci=ci: e.tensor_scalar(
                cmpb[:, ci * T:(ci + 1) * T], cmpb[:, ci * T:(ci + 1) * T],
                gbc[:, 1:2], None, op0=ALU.mult),
                deps=[tok_g], inc=True)
            t2 = kb.v(lambda e, ci=ci: e.scalar_tensor_tensor(
                locb[:, ci * T:(ci + 1) * T], lnmb[:, ci * T:(ci + 1) * T],
                gbc[:, 0:1], cmpb[:, ci * T:(ci + 1) * T],
                op0=ALU.mult, op1=ALU.add), deps=[t1], inc=True)
            comb_toks.append(t2)
        ar.free(f"cmp{s}", comb_toks[-1])
        tap(f"comb{s}", locb[:, :], comb_toks)
        if _ATTN_CUT == 8:
            return comb_toks

        wstp, d17 = ar.alloc(f"wstp{s}", 2 * NCH * 128, F16)
        proj_ev = []
        pwld = {}
        pwfree = {}
        for J in range(NCH):
            ws = J % 2
            pwld[J] = kb.dma("g",
                             wstp[:, ws * 1024:(ws + 1) * 1024]
                             .rearrange("p (ci j) -> p ci j", j=128),
                             d_wproj.rearrange("(ci p) j -> p ci j", p=128)
                             [:, :, J * 128:(J + 1) * 128],
                             deps=d17 + [pwfree.get(J - 2)])
            for tg in range(4):
                bk = (J * 4 + tg) % 4
                tkmm = None
                for ci in range(NCH):
                    tkmm = kb.t(lambda e, ws=ws, J=J, tg=tg, ci=ci, bk=bk:
                                e.matmul(
                        bank(bk),
                        wstp[:, ws * 1024 + ci * 128: ws * 1024 + (ci + 1) * 128],
                        locb[:, ci * T + tg * 512: ci * T + (tg + 1) * 512],
                        start=(ci == 0), stop=(ci == NCH - 1)),
                        deps=(kb.bdep(bk) + [pwld[J]] + comb_toks)
                        if ci == 0 else [], inc=(ci == NCH - 1))
                ev = kb.s(lambda e, J=J, tg=tg, bk=bk: e.activation(
                    lnmb[:, J * T + tg * 512: J * T + (tg + 1) * 512],
                    bank(bk), AF.Identity, bias=b_proj[:, J:J + 1]),
                    deps=[tkmm, tok_bproj] + lnm_toks, inc=True)
                kb.bset(ev, bk)
                proj_ev.append(ev)
            pwfree[J] = proj_ev[-1]
        ar.free(f"wstp{s}", proj_ev[-1])
        tap(f"attn{s}", lnmb[:, :], [proj_ev[-1]])
        if _ATTN_CUT == 9:
            return [proj_ev[-1]]

        x2c, d18 = ar.alloc(f"x2c{s}", 2 * T, F32)
        tr, tmr = ln_stats(lnmb, NCH, lnrb, sq2b_, proj_ev + d18,
                           lnr_deps=[tok_g])
        bts = ln_bcast(lnrb, brwb, tr, tmr)
        x2_toks = []
        ln2_mm = None
        ln2_mm6 = None
        prev_ld = None
        prev_cst = [None]
        for ci in range(NCH):
            u = ln_apply(locb, ci * T, lnmb, ci * T, brwb,
                         proj_ev + bts + comb_toks)
            xr = kb.dma("y", x2c[:, 0:T], dx[s, ci * 128:(ci + 1) * 128, :],
                        deps=[prev_ld])
            ad = kb.v(lambda e, ci=ci: e.tensor_tensor(
                x2c[:, T:2 * T], locb[:, ci * T:(ci + 1) * T],
                x2c[:, 0:T], op=ALU.add),
                deps=[u, xr, prev_cst[0]], inc=True)
            st = kb.dma("y", d_x2[s, ci * 128:(ci + 1) * 128, :],
                        x2c[:, T:2 * T], deps=[ad])
            prev_ld = st
            cst = kb.s(lambda e: e.activation(
                sq2b_[:, 0:T], x2c[:, T:2 * T], AF.Identity),
                deps=[ad, ln2_mm6, ln2_mm], inc=True)
            prev_cst[0] = cst
            sqq = kb.s(lambda e: e.activation(
                sq2b_[:, T:2 * T], sq2b_[:, 0:T], AF.Square),
                deps=[cst, ln2_mm], inc=True)
            for g in range(4):
                dep0 = (kb.bdep(6, 7)) if ci == 0 else []
                ln2_mm6 = kb.t(lambda e, g=g, ci=ci: e.matmul(
                    bank(6, r0=32 * g, r1=32 * g + 1), ones1[:, 0:1],
                    sq2b_[:, 512 * g: 512 * (g + 1)],
                    start=(ci == 0), stop=(ci == NCH - 1),
                    tile_position=(0, 32 * g), skip_group_check=True),
                    deps=dep0 + [cst], inc=(g == 3))
                ln2_mm = kb.t(lambda e, g=g, ci=ci: e.matmul(
                    bank(7, r0=32 * g, r1=32 * g + 1), ones1[:, 0:1],
                    sq2b_[:, T + 512 * g: T + 512 * (g + 1)],
                    start=(ci == 0), stop=(ci == NCH - 1),
                    tile_position=(0, 32 * g), skip_group_check=True),
                    deps=dep0 + [sqq], inc=(g == 3))
            x2_toks.append(st)
        for g in range(4):
            ev1 = kb.v(lambda e, g=g: e.tensor_copy(
                stat_ev[32 * g:32 * g + 1, 0:512],
                bank(6, r0=32 * g, r1=32 * g + 1)),
                deps=([ln2_mm] + kb.gd("stat_ev")) if g == 0 else [], inc=True)
        for g in range(4):
            ev2 = kb.v(lambda e, g=g: e.tensor_copy(
                stat_ev[32 * g:32 * g + 1, 512:1024],
                bank(7, r0=32 * g, r1=32 * g + 1)),
                deps=[ln2_mm] if g == 0 else [], inc=True)
        kb.bset(ev2, 6, 7)
        g1 = kb.dma("y", lnrb[0:1, 0:T], stat_ev[0:97:32, 0:512], deps=[ev1])
        g2 = kb.dma("y", lnrb[0:1, T:2 * T], stat_ev[0:97:32, 512:1024],
                    deps=[ev2])
        kb.gset("stat_ev", g2)
        inv = 1.0 / C
        kb.v(lambda e: e.tensor_scalar(lnrb[0:1, 2 * T:3 * T], lnrb[0:1, 0:T],
                                       inv, None, op0=ALU.mult), deps=[g1])
        kb.v(lambda e: e.tensor_scalar(lnrb[0:1, T:2 * T], lnrb[0:1, T:2 * T],
                                       inv, None, op0=ALU.mult), deps=[g2])
        kb.v(lambda e: e.tensor_tensor(lnrb[0:1, 0:T], lnrb[0:1, 2 * T:3 * T],
                                       lnrb[0:1, 2 * T:3 * T], op=ALU.mult))
        tvar = kb.v(lambda e: e.tensor_tensor(lnrb[0:1, T:2 * T],
                                              lnrb[0:1, T:2 * T],
                                              lnrb[0:1, 0:T],
                                              op=ALU.subtract), inc=True)
        tln = kb.s(lambda e: e.activation(lnrb[0:1, 0:T], lnrb[0:1, T:2 * T],
                                          AF.Ln, bias=epcol[0:1, 1:2]),
                   deps=[tvar, tok_ep], inc=True)
        trr = kb.s(lambda e: e.activation(lnrb[0:1, 0:T], lnrb[0:1, 0:T],
                                          AF.Exp, scale=-0.5),
                   deps=[tln], inc=True)
        sv1 = kb.dma("y", d_ln2[0:1, s * 2 * T: s * 2 * T + T],
                     lnrb[0:1, 2 * T:3 * T], deps=[trr])
        sv2 = kb.dma("y", d_ln2[0:1, s * 2 * T + T: s * 2 * T + 2 * T],
                     lnrb[0:1, 0:T], deps=[trr, sv1])
        end = sv2
        for nm in (f"lnm{s}", f"lnr2{s}", f"sq2b{s}", f"brw2{s}", f"x2c{s}",
                   f"loc{s}"):
            ar.free(nm, end)
        return x2_toks + [sv1, sv2]

    def emit_compress(s, qb, kbu, vbu, ccst, conv_ev, sqn, qkv_ev, cdeps):
        kc_toks, vc_toks, kcn = [], [], [None]
        for (zbuf, is_k) in ((kbu, True), (vbu, False)):
            mm_done = []
            for tap_ in range(KTAP):
                for halfc in range(2):
                    for par in range(2):
                        bk = halfc + 2 * par
                        deps = (qkv_ev + [tok_wconv] + kb.bdep(bk) + cdeps) \
                            if tap_ == 0 else []
                        tk = kb.t(lambda e, tap_=tap_, halfc=halfc, par=par,
                                  bk=bk, z=zbuf: e.matmul(
                            bank(bk, r0=0, r1=64, c1=508),
                            w_conv[par * 64:(par + 1) * 64,
                                   tap_ * 64:(tap_ + 1) * 64],
                            z[par * 64:(par + 1) * 64, :]
                            .rearrange("p (ci t) -> p ci t", t=T)
                            [:, 4 * halfc:4 * halfc + 4,
                             tap_:tap_ + CSTR * 126 + 1:CSTR],
                            start=(tap_ == 0), stop=(tap_ == KTAP - 1),
                            tile_position=(par * 64, 0)),
                            deps=deps, inc=(tap_ == KTAP - 1))
                        if tap_ == KTAP - 1:
                            mm_done.append(tk)
            ev_toks = []
            for halfc in range(2):
                for par in range(2):
                    bk = halfc + 2 * par
                    ev = kb.s(lambda e, halfc=halfc, par=par, bk=bk:
                              e.activation(
                        conv_ev[par * 64:(par + 1) * 64,
                                halfc * 508:(halfc + 1) * 508],
                        bank(bk, r0=0, r1=64, c1=508), AF.Identity,
                        bias=b_conv[par * 64:(par + 1) * 64, :]),
                        deps=mm_done + [tok_bconv], inc=True)
                    kb.bset(ev, bk)
                    ev_toks.append(ev)
            if _ATTN_CUT == 301:
                return ev_toks, [], None
            if _ATTN_CUT == 305 and not is_k:
                return kc_toks, [ev_toks[-1]], kcn[0]
            tk = None
            for par in range(2):
                hs = [h for h in range(NH) if h % 2 == par]
                for j, h in enumerate(hs):
                    halfc, pos = divmod(h // 2, 4)
                    first = (par == 0 and j == 0)
                    tk = kb.t(lambda e, h=h, par=par, halfc=halfc, pos=pos:
                              e.matmul(
                        PS16[0:NB, 4 * 1024 + 64 * h: 4 * 1024 + 64 * (h + 1)],
                        conv_ev[par * 64:(par + 1) * 64,
                                halfc * 508 + pos * 127:
                                halfc * 508 + (pos + 1) * 127],
                        idf[par * 64:(par + 1) * 64, par * 64:par * 64 + 64],
                        start=True, stop=True, is_transpose=True,
                        skip_group_check=True),
                        deps=(ev_toks + [tok_id] + kb.bdep(4)) if first
                        else ([tk] if j == 0 else []),
                        inc=(j == len(hs) - 1))
            if _ATTN_CUT == 302:
                ev302 = kb.v(lambda e: e.tensor_copy(
                    ccst[0:NB, :], PS16[0:NB, 4 * 1024: 4 * 1024 + 1024]),
                    deps=[tk], inc=True)
                kb.bset(ev302, 4)
                return [ev302], [], None
            t_sz = kb.v(lambda e: e.tensor_reduce(
                red16[0:NB, 0:NH],
                PS16[0:NB, 4 * 1024: 4 * 1024 + 1024]
                .rearrange("p (h c) -> p h c", c=64), axis=AX, op=ALU.add),
                deps=[tk], inc=True)
            if _ATTN_CUT == 3031:
                kb.bset(t_sz, 4)
                return [t_sz], [], None
            t_sq = kb.s(lambda e: e.activation(
                ccst[0:NB, :], PS16[0:NB, 4 * 1024: 4 * 1024 + 1024],
                AF.Square), deps=[tk, t_sz], inc=True)
            if _ATTN_CUT == 3032:
                kb.bset(t_sq, 4)
                return [t_sz, t_sq], [], None
            t_s2 = kb.v(lambda e: e.tensor_reduce(
                red16[0:NB, NH:2 * NH],
                ccst[0:NB, :].rearrange("p (h c) -> p h c", c=64),
                axis=AX, op=ALU.add), deps=[t_sq], inc=True)
            kb.v(lambda e: e.tensor_scalar(red16[0:NB, 0:NH],
                                           red16[0:NB, 0:NH], 1.0 / 64, None,
                                           op0=ALU.mult), deps=[t_sz])
            kb.v(lambda e: e.tensor_scalar(red16[0:NB, NH:2 * NH],
                                           red16[0:NB, NH:2 * NH], 1.0 / 64,
                                           None, op0=ALU.mult), deps=[t_s2])
            kb.v(lambda e: e.tensor_tensor(red16[0:NB, 2 * NH:3 * NH],
                                           red16[0:NB, 0:NH],
                                           red16[0:NB, 0:NH], op=ALU.mult))
            kb.v(lambda e: e.tensor_tensor(red16[0:NB, NH:2 * NH],
                                           red16[0:NB, NH:2 * NH],
                                           red16[0:NB, 2 * NH:3 * NH],
                                           op=ALU.subtract))
            kb.v(lambda e: e.tensor_scalar(red16[0:NB, NH:2 * NH],
                                           red16[0:NB, NH:2 * NH], EPS, None,
                                           op0=ALU.add))
            t_iv = kb.v(lambda e: e.reciprocal(red16[0:NB, NH:2 * NH],
                                               red16[0:NB, NH:2 * NH]),
                        inc=True)
            t_r = kb.s(lambda e: e.activation(red16[0:NB, NH:2 * NH],
                                              red16[0:NB, NH:2 * NH],
                                              AF.Sqrt), deps=[t_iv], inc=True)
            t_nmr = kb.v(lambda e: e.tensor_tensor(
                red16[0:NB, 2 * NH:3 * NH], red16[0:NB, 0:NH],
                red16[0:NB, NH:2 * NH], op=ALU.mult), deps=[t_r], inc=True)
            t_nm2 = kb.v(lambda e: e.tensor_scalar(
                red16[0:NB, 2 * NH:3 * NH], red16[0:NB, 2 * NH:3 * NH],
                -1.0, None, op0=ALU.mult), deps=[t_nmr], inc=True)
            if _ATTN_CUT == 3033:
                kb.bset(t_nm2, 4)
                return [t_nm2, t_sz], [], None
            nrm = []
            for h in range(NH):
                nrm.append(kb.v(lambda e, h=h: e.tensor_scalar(
                    ccst[0:NB, h * 64:(h + 1) * 64],
                    PS16[0:NB, 4 * 1024 + 64 * h: 4 * 1024 + 64 * (h + 1)],
                    red16[0:NB, NH + h:NH + h + 1],
                    red16[0:NB, 2 * NH + h:2 * NH + h + 1],
                    op0=ALU.mult, op1=ALU.add), deps=[t_nm2], inc=True))
            kb.bset(nrm[-1], 4)
            if _ATTN_CUT == 3034:
                return [nrm[-1], t_sz], [], None
            t_gm = kb.v(lambda e: e.tensor_tensor(
                ccst[0:NB, :], ccst[0:NB, :], crep_w[0:NB, :], op=ALU.mult),
                deps=nrm + [tok_crep], inc=True)
            t_gb = kb.v(lambda e: e.tensor_tensor(
                ccst[0:NB, :], ccst[0:NB, :], crep_b[0:NB, :], op=ALU.add),
                deps=[t_gm], inc=True)
            if _ATTN_CUT == 303:
                kb.bset(t_gb, 4)
                return [t_gb], [], None
            if _ATTN_CUT == 306 and not is_k:
                kb.bset(t_gb, 4)
                return kc_toks, [t_gb], kcn[0]
            if is_k:
                t_act = kb.s(lambda e: e.activation(
                    ccst[0:NB, :], ccst[0:NB, :], AF.Gelu), deps=[t_gb],
                    inc=True)
                t_k2 = kb.s(lambda e: e.activation(
                    sqn[0:NB, 0:NH * HS], ccst[0:NB, :], AF.Square),
                    deps=[t_act], inc=True)
                kcn[0] = kb.v(lambda e: e.tensor_reduce(
                    red16[0:NB, 3 * NH:4 * NH],
                    sqn[0:NB, 0:NH * HS].rearrange("p (h c) -> p h c", c=64),
                    axis=AX, op=ALU.add), deps=[t_k2], inc=True)
                for h in range(NH):
                    par, ci = h % 2, h // 2
                    tk2 = kb.t(lambda e, h=h: e.matmul(
                        PS16[0:64, 5 * 1024: 5 * 1024 + NB],
                        ccst[0:NB, h * 64:(h + 1) * 64], idf[0:NB, 0:NB],
                        start=True, stop=True, is_transpose=True,
                        skip_group_check=True),
                        deps=[t_act, tok_id] + kb.bdep(5), inc=True)
                    ev3 = kb.v(lambda e, par=par, ci=ci: e.tensor_copy(
                        kcomp[par * 64:(par + 1) * 64,
                              ci * NB:(ci + 1) * NB],
                        PS16[0:64, 5 * 1024: 5 * 1024 + NB]), deps=[tk2],
                        inc=True)
                    kb.bset(ev3, 5)
                    kc_toks.append(ev3)
                tap(f"kcomp{s}", kcomp[:, :], [kc_toks[-1]])
                if _ATTN_CUT == 304:
                    return kc_toks, [], kcn[0]
            else:
                t_act = kb.s(lambda e: e.activation(
                    vcompT[0:NB, :].rearrange("p (h c) -> p h c", c=65)
                    [:, :, 0:64],
                    ccst[0:NB, :].rearrange("p (h c) -> p h c", c=64),
                    AF.Gelu), deps=[t_gb, tok_vc0], inc=True)
                vc_toks = [t_act]
                kb.bset(t_act, 4)
                tap(f"vcompT{s}", vcompT[:, :], vc_toks)
        return kc_toks, vc_toks, kcn[0]

    def emit_norms(s, qb, kbu, sqn, qkv_ev, kcn_tok):
        sq_free = [None, None]
        last_rd = None
        for (zbuf, base_off) in ((qb, RQ), (kbu, RK)):
            for ci in range(NCH):
                sl = (ci % 2) * T
                tsq = kb.s(lambda e, z=zbuf, ci=ci, sl=sl: e.activation(
                    sqn[:, sl:sl + T], z[:, ci * T:(ci + 1) * T], AF.Square),
                    deps=qkv_ev + [sq_free[ci % 2]], inc=True)
                for par in range(2):
                    h = ci * 2 + par
                    mm_last = None
                    for tg in range(4):
                        mm_last = kb.t(lambda e, par=par, sl=sl, tg=tg:
                                       e.matmul(
                            bank(4 + tg, r0=0, r1=1),
                            ones1[par * 64:(par + 1) * 64, 0:1],
                            sqn[par * 64:(par + 1) * 64,
                                sl + tg * 512: sl + (tg + 1) * 512],
                            start=True, stop=True,
                            tile_position=(par * 64, 0)),
                            deps=[tsq] + kb.bdep(4 + tg), inc=(tg == 3))
                    rd = kb.v(lambda e, off=base_off, h=h: e.tensor_reduce(
                        nrow[0:1, off + h: off + h + 1],
                        PS[0:1, 4 * 512: 8 * 512], axis=AX, op=ALU.max),
                        deps=[mm_last], inc=True)
                    kb.bset(rd, 4, 5, 6, 7)
                    last_rd = rd
                sq_free[ci % 2] = last_rd
        kc_rd = None
        for h in range(NH):
            gdt = kb.dma("y", stat_ev[0:1, 0:NB],
                         red16[0:NB, 3 * NH + h:3 * NH + h + 1],
                         deps=[kcn_tok, kc_rd] + kb.gd("stat_ev"))
            kc_rd = kb.v(lambda e, h=h: e.tensor_reduce(
                nrow[0:1, RKC + h:RKC + h + 1], stat_ev[0:1, 0:NB],
                axis=AX, op=ALU.max), deps=[gdt], inc=True)
        kb.gset("stat_ev", kc_rd)
        t_sqr = kb.s(lambda e: e.activation(nrow[0:1, RQ:RQ + 3 * NH],
                                            nrow[0:1, RQ:RQ + 3 * NH],
                                            AF.Sqrt),
                     deps=[kc_rd, last_rd], inc=True)
        kb.v(lambda e: e.tensor_tensor(nrow[0:1, RM:RM + NH],
                                       nrow[0:1, RQ:RQ + NH],
                                       nrow[0:1, RK:RK + NH], op=ALU.mult),
             deps=[t_sqr])
        kb.v(lambda e: e.tensor_tensor(nrow[0:1, RM + NH:RM + 2 * NH],
                                       nrow[0:1, RQ:RQ + NH],
                                       nrow[0:1, RKC:RKC + NH], op=ALU.mult),
             deps=[t_sqr])
        t_M = kb.v(lambda e: e.tensor_scalar(nrow[0:1, RM:RM + 2 * NH],
                                             nrow[0:1, RM:RM + 2 * NH],
                                             -0.125, None, op0=ALU.mult),
                   inc=True)
        tok_mn = bc_row32(nrow, RM, 32, mn_bc, 0, [t_M], bki=0)
        tap(f"mrow{s}", nrow[0:1, 0:5 * NH], [tok_mn])
        return tok_mn

    def emit_attn_core(s, qb, kbu, vbu, locb, cmpb, kc_toks, vc_toks,
                       tok_mn, qkv_ev, adeps):
        loc_toks, cmp_toks = [], []
        for branch in ("local", "comp"):
            dst = locb if branch == "local" else cmpb
            klen = 128 if branch == "local" else NB
            for ch in range(NCH):
                pair_toks = []
                vt_ev = []
                if branch == "local":
                    for par in range(2):
                        tk = None
                        for kt in range(16):
                            tk = kb.t(lambda e, par=par, kt=kt, ch=ch:
                                      e.matmul(
                                PS16[0:128, 4 * 1024 + kt * 64:
                                     4 * 1024 + (kt + 1) * 64],
                                vbu[par * 64:(par + 1) * 64,
                                    ch * T + kt * 128:
                                    ch * T + (kt + 1) * 128],
                                idf[par * 64:(par + 1) * 64,
                                    par * 64:par * 64 + 64],
                                start=True, stop=True, is_transpose=True,
                                skip_group_check=True),
                                deps=(qkv_ev + [tok_id] + kb.bdep(4))
                                if kt == 0 else [], inc=(kt == 15))
                        ev = kb.v(lambda e, par=par: e.tensor_copy(
                            vt[:, par * 1040:(par + 1) * 1040]
                            .rearrange("p (k c) -> p k c", c=65)[:, :, 0:64],
                            PS16[0:128, 4 * 1024: 4 * 1024 + 1024]
                            .rearrange("p (k c) -> p k c", c=64)),
                            deps=[tk, tok_vt0], inc=True)
                        kb.bset(ev, 4)
                        vt_ev.append(ev)
                for half in range(2):
                    q0 = half * 1024
                    ktmax = (8 * half + 8) if branch == "local" else 1
                    units = []
                    for kt in range(ktmax):
                        for qc in range(2):
                            cs = q0 + qc * 512
                            lo = max(kt * 128, cs) if branch == "local" else cs
                            if lo >= cs + 512:
                                continue
                            units.append((kt, qc, lo, cs + 512 - lo))
                    firstu, lastu = {}, {}
                    for i, (kt, qc, lo, w) in enumerate(units):
                        lastu[qc] = i
                        if qc not in firstu:
                            firstu[qc] = i
                    exp_toks, pv_toks = {}, {}
                    for i, (kt, qc, lo, w) in enumerate(units):
                        slot = i % 2
                        for par in range(2):
                            h = ch * 2 + par
                            sbk = slot + 2 * par
                            deps_s = kb.bdep(sbk) + [tok_mn] + qkv_ev + adeps
                            if branch == "comp":
                                deps_s = deps_s + kc_toks
                            if branch == "local":
                                stat = kbu[par * 64:(par + 1) * 64,
                                           ch * T + kt * 128:
                                           ch * T + kt * 128 + 128]
                            else:
                                stat = kcomp[par * 64:(par + 1) * 64,
                                             ch * NB:(ch + 1) * NB]
                            mm = kb.t(lambda e, stat=stat, par=par, ch=ch,
                                      lo=lo, w=w, sbk=sbk, klen=klen:
                                      e.matmul(
                                bank(sbk, r0=0, r1=klen, c1=w), stat,
                                qb[par * 64:(par + 1) * 64,
                                   ch * T + lo: ch * T + lo + w],
                                start=True, stop=True,
                                tile_position=(par * 64, 0)),
                                deps=deps_s + [exp_toks.get((i - 2, par))],
                                inc=True)
                            mcol = h if branch == "local" else NH + h
                            pslot = slot + 2 * par
                            ex = kb.s(lambda e, pslot=pslot, w=w, sbk=sbk,
                                      mcol=mcol, klen=klen: e.activation(
                                pt[0:klen, pslot * 512: pslot * 512 + w],
                                bank(sbk, r0=0, r1=klen, c1=w), AF.Exp,
                                bias=mn_bc[0:klen, mcol:mcol + 1],
                                scale=0.125),
                                deps=[mm, pv_toks.get((i - 2, par))],
                                inc=True)
                            msk = None
                            if branch == "local" and lo == kt * 128:
                                msk = kb.g(lambda e, pslot=pslot:
                                           e.affine_select(
                                    pt[0:128, pslot * 512: pslot * 512 + 128],
                                    pt[0:128, pslot * 512: pslot * 512 + 128],
                                    pattern=[[1, 128]], compare_op=ALU.is_ge,
                                    fill=0.0, base=0, channel_multiplier=-1),
                                    deps=[ex], inc=True)
                            if branch == "comp" and half == 0 and qc == 0:
                                msk = kb.g(lambda e, pslot=pslot:
                                           e.affine_select(
                                    pt[0:NB, pslot * 512: pslot * 512 + NB],
                                    pt[0:NB, pslot * 512: pslot * 512 + NB],
                                    pattern=[[1, NB]], compare_op=ALU.is_ge,
                                    fill=0.0, base=0, channel_multiplier=-1),
                                    deps=[ex], inc=True)
                            exp_toks[(i, par)] = msk or ex
                            ob = 4 + 2 * par
                            col = lo - q0
                            if branch == "local":
                                pvl = vt[:, par * 1040 + kt * 65:
                                         par * 1040 + (kt + 1) * 65]
                            else:
                                pvl = vcompT[0:NB, h * 65:(h + 1) * 65]
                            dep_pv = [exp_toks[(i, par)]]
                            if i == firstu[qc]:
                                dep_pv += kb.bdep(ob + qc)
                                dep_pv += vt_ev if branch == "local" \
                                    else vc_toks
                            pv = kb.t(lambda e, pvl=pvl, pslot=pslot, col=col,
                                      w=w, ob=ob, klen=klen,
                                      first=(i == firstu[qc]),
                                      last=(i == lastu[qc]): e.matmul(
                                PS[0:65, ob * 512 + col: ob * 512 + col + w],
                                pvl,
                                pt[0:klen, pslot * 512: pslot * 512 + w],
                                start=first, stop=last), deps=dep_pv,
                                inc=True)
                            pv_toks[(i, par)] = pv
                            if i == lastu[qc]:
                                pv_toks[("last", par, qc)] = pv
                    nun = len(units)
                    for par in range(2):
                        for back in (1, 2):
                            if nun - back >= 0 and (nun - back, par) in exp_toks:
                                kb.bset(exp_toks[(nun - back, par)],
                                        (nun - back) % 2 + 2 * par)
                    for par in range(2):
                        ob = 4 + 2 * par
                        dps = [pv_toks[("last", par, 0)],
                               pv_toks[("last", par, 1)]]
                        t1 = kb.v(lambda e, ob=ob: e.tensor_scalar(
                            div_st[64:65, 0:1024],
                            PS[64:65, ob * 512: ob * 512 + 1024],
                            DEN_EPS, None, op0=ALU.add),
                            deps=dps + [tok_dst] + kb.gd("div"), inc=True)
                        def _recip(e):
                            with nc.allow_low_precision(
                                    reason="softmax denom recip f16"):
                                return e.reciprocal(div_st[64:65, 0:1024],
                                                    div_st[64:65, 0:1024])
                        t2 = kb.v(_recip, deps=[t1], inc=True)
                        wr_toks = []
                        for qc in range(2):
                            bb = qc
                            mmb = kb.t(lambda e, qc=qc, bb=bb: e.matmul(
                                bank(bb), bc_w[64:96, :],
                                div_st[64:96, qc * 512:(qc + 1) * 512],
                                start=True, stop=True),
                                deps=[t2] + kb.bdep(bb), inc=True)
                            evb = kb.v(lambda e, qc=qc, bb=bb: e.tensor_copy(
                                dvb[0:64, qc * 512:(qc + 1) * 512],
                                bank(bb, r0=0, r1=64)),
                                deps=[mmb] + kb.gd("dvb"), inc=True)
                            kb.bset(evb, bb)
                            wdst = dst[0:64, ch * T + q0 + qc * 512:
                                       ch * T + q0 + (qc + 1) * 512] \
                                if par == 0 else \
                                bstg[0:64, qc * 512:(qc + 1) * 512]
                            wr = kb.v(lambda e, wdst=wdst, qc=qc, ob=ob:
                                      e.tensor_tensor(
                                wdst,
                                PS[0:64, ob * 512 + qc * 512:
                                   ob * 512 + (qc + 1) * 512],
                                dvb[0:64, qc * 512:(qc + 1) * 512],
                                op=ALU.mult),
                                deps=[evb] + (kb.gd("bstg")
                                              if par == 1 else []), inc=True)
                            wr_toks.append(wr)
                        kb.gset("div", wr_toks[-1])
                        kb.gset("dvb", wr_toks[-1])
                        if par == 1:
                            mv1 = kb.dma("g", d_bmov, bstg[0:64, :],
                                         deps=wr_toks)
                            mv2 = kb.dma("g",
                                         dst[64:128,
                                             ch * T + q0: ch * T + q0 + 1024],
                                         d_bmov, deps=[mv1])
                            kb.gset("bstg", mv2)
                            kb.bset(mv2, ob, ob + 1)
                            pair_toks.append(mv2)
                        else:
                            kb.bset(wr_toks[-1], ob, ob + 1)
                            pair_toks.append(wr_toks[-1])
                if branch == "local":
                    loc_toks.extend(pair_toks)
                else:
                    cmp_toks.extend(pair_toks)
        tap(f"loc{s}", locb[:, :], loc_toks)
        tap(f"cmp{s}", cmpb[:, :], cmp_toks)
        return loc_toks, cmp_toks

    def emit_gate(s, lnmb, cmpb, lnr, lnm_toks, cmp_toks, wg1, tok_wg1,
                  d_gw1):
        lb1 = kb.dma("y", lnr[0:1, 2 * T:2 * T + C], d_gb1,
                     deps=[lnm_toks[-1]])
        lw = kb.dma("y", lnr[0:1, T:T + C], d_glnw, deps=[lnm_toks[-1]])
        lb = kb.dma("y", lnr[0:1, T + C:T + 2 * C], d_glnb,
                     deps=[lnm_toks[-1]])
        mm_h = None
        tok_wg1b = None
        for f in range(16):
            srcb = lnmb if f < NCH else cmpb
            ci = f % NCH
            if f == 8:
                tok_wg1b = kb.dma("g",
                                  wg1[:, :].rearrange("p (f j) -> p f j", j=C),
                                  d_gw1.rearrange("(f p) j -> p f j", p=128)
                                  [:, 8:16, :], deps=[mm_h])
            for hf in range(2):
                mm_h = kb.t(lambda e, f=f, ci=ci, srcb=srcb, hf=hf: e.matmul(
                    bank(2 + hf, r0=0, r1=1),
                    srcb[:, ci * T + T - 1: ci * T + T],
                    wg1[:, (f % 8) * C + hf * 512:
                        (f % 8) * C + (hf + 1) * 512],
                    start=(f == 0), stop=(f == 15)),
                    deps=(lnm_toks + cmp_toks + [tok_wg1] + kb.bdep(2, 3))
                    if f == 0 else ([tok_wg1b] if f == 8 else []),
                    inc=((f == 15 or f == 7) and hf == 1))
        t_h = kb.v(lambda e: e.tensor_tensor(
            lnr[0:1, 0:512], bank(2, r0=0, r1=1), lnr[0:1, 2 * T:2 * T + 512],
            op=ALU.add), deps=[mm_h, lb1], inc=True)
        t_h2 = kb.v(lambda e: e.tensor_tensor(
            lnr[0:1, 512:1024], bank(3, r0=0, r1=1),
            lnr[0:1, 2 * T + 512:2 * T + 1024], op=ALU.add),
            deps=[mm_h, lb1], inc=True)
        kb.bset(t_h2, 2, 3)
        t_s = kb.v(lambda e: e.tensor_reduce(
            nrow[0:1, RG:RG + 1], lnr[0:1, 0:C], axis=AX, op=ALU.add),
            deps=[t_h, t_h2], inc=True)
        t_q = kb.s(lambda e: e.activation(
            lnr[0:1, 2 * T:2 * T + C], lnr[0:1, 0:C], AF.Square,
            accum_out=nrow[0:1, RG + 1:RG + 2]), deps=[t_h, t_h2, lb1],
            inc=True)
        kb.v(lambda e: e.tensor_scalar(nrow[0:1, RG:RG + 2],
                                       nrow[0:1, RG:RG + 2], 1.0 / C, None,
                                       op0=ALU.mult), deps=[t_s, t_q])
        kb.v(lambda e: e.tensor_tensor(nrow[0:1, RG + 2:RG + 3],
                                       nrow[0:1, RG:RG + 1],
                                       nrow[0:1, RG:RG + 1], op=ALU.mult))
        kb.v(lambda e: e.tensor_tensor(nrow[0:1, RG + 1:RG + 2],
                                       nrow[0:1, RG + 1:RG + 2],
                                       nrow[0:1, RG + 2:RG + 3],
                                       op=ALU.subtract))
        kb.v(lambda e: e.tensor_scalar(nrow[0:1, RG + 1:RG + 2],
                                       nrow[0:1, RG + 1:RG + 2], EPS, None,
                                       op0=ALU.add))
        ti = kb.v(lambda e: e.reciprocal(nrow[0:1, RG + 1:RG + 2],
                                         nrow[0:1, RG + 1:RG + 2]), inc=True)
        trq = kb.s(lambda e: e.activation(nrow[0:1, RG + 1:RG + 2],
                                          nrow[0:1, RG + 1:RG + 2], AF.Sqrt),
                   deps=[ti], inc=True)
        tmr = kb.v(lambda e: e.tensor_tensor(nrow[0:1, RG + 2:RG + 3],
                                             nrow[0:1, RG:RG + 1],
                                             nrow[0:1, RG + 1:RG + 2],
                                             op=ALU.mult), deps=[trq],
                   inc=True)
        tnm = kb.v(lambda e: e.tensor_scalar(nrow[0:1, RG + 2:RG + 3],
                                             nrow[0:1, RG + 2:RG + 3], -1.0,
                                             None, op0=ALU.mult), deps=[tmr],
                   inc=True)
        t_n = kb.v(lambda e: e.tensor_scalar(
            lnr[0:1, 0:C], lnr[0:1, 0:C],
            nrow[0:1, RG + 1:RG + 2], nrow[0:1, RG + 2:RG + 3],
            op0=ALU.mult, op1=ALU.add), deps=[tnm], inc=True)
        t_g1 = kb.v(lambda e: e.tensor_tensor(
            lnr[0:1, 0:C], lnr[0:1, 0:C], lnr[0:1, T:T + C], op=ALU.mult),
            deps=[t_n, lw], inc=True)
        t_g2 = kb.v(lambda e: e.tensor_tensor(
            lnr[0:1, 0:C], lnr[0:1, 0:C], lnr[0:1, T + C:T + 2 * C],
            op=ALU.add), deps=[t_g1, lb], inc=True)
        t_rl = kb.v(lambda e: e.tensor_scalar(
            lnr[0:1, 0:C], lnr[0:1, 0:C], 0.0, None, op0=ALU.max),
            deps=[t_g2], inc=True)
        dd1 = kb.dma("y", d_hrow, lnr[0:1, 0:C], deps=[t_rl])
        sem_h = f"dg{kb.qrr['g']}"
        kb.qrr['g'] = (kb.qrr['g'] + 1) % 4
        dd2 = kb.emit("g", lambda e: e.dma_start(
            out=hcol[:, :],
            in_=d_hrow.rearrange("o (ci p) -> (o p) ci", p=128),
            allow_slow_non_contiguous=True),
            deps=[dd1, (sem_h, kb.cnt[sem_h])], inc=sem_h, inc_by=16)
        mm2 = None
        for ci in range(NCH):
            mm2 = kb.t(lambda e, ci=ci: e.matmul(
                bank(2, r0=0, r1=1, c1=2), hcol[:, ci:ci + 1],
                w_g2[:, ci * 2:(ci + 1) * 2], start=(ci == 0),
                stop=(ci == NCH - 1)),
                deps=([dd2, tok_wg2] + kb.bdep(2)) if ci == 0 else [],
                inc=(ci == NCH - 1))
        lg2 = kb.dma("y", nrow[0:1, RG + 10:RG + 12], d_gb2, deps=[])
        t_l = kb.v(lambda e: e.tensor_tensor(
            nrow[0:1, RG + 4:RG + 6], bank(2, r0=0, r1=1, c1=2),
            nrow[0:1, RG + 10:RG + 12], op=ALU.add), deps=[mm2, lg2],
            inc=True)
        kb.bset(t_l, 2)
        t_d = kb.v(lambda e: e.tensor_tensor(
            nrow[0:1, RG + 6:RG + 7], nrow[0:1, RG + 4:RG + 5],
            nrow[0:1, RG + 5:RG + 6], op=ALU.subtract), deps=[t_l], inc=True)
        t_sg = kb.s(lambda e: e.activation(nrow[0:1, RG + 8:RG + 9],
                                           nrow[0:1, RG + 6:RG + 7],
                                           AF.Sigmoid), deps=[t_d], inc=True)
        t_g1r = kb.v(lambda e: e.tensor_scalar(nrow[0:1, RG + 9:RG + 10],
                                               nrow[0:1, RG + 8:RG + 9], -1.0,
                                               1.0, op0=ALU.mult, op1=ALU.add),
                     deps=[t_sg], inc=True)
        tok_g = bc_row32(nrow, RG + 8, 2, gbc, 0, [t_g1r], bki=2)
        tap(f"gates{s}", nrow[0:1, RG + 8:RG + 10], [tok_g])
        return tok_g

    def cc_exchange(idx, row, off, deps):
        cp = kb.v(lambda e, o=off: e.tensor_copy(
            stat_ev[0:1, 0:8], row[0:1, o:o + 8]),
            deps=list(deps) + kb.gd("stat_ev"), inc=True)
        d1 = kb.dma("g", d_ccin[idx], stat_ev[0:1, 0:8], deps=[cp])
        t_cc = kb.emit("g", lambda e, i=idx: e.collective_compute(
            "AllReduce", ALU.max, replica_groups=[list(range(NCORES))],
            ins=[d_ccin[i].opt()], outs=[d_ccout[i].opt()]),
            deps=[d1], inc="cc")
        d2 = kb.dma("g", row[0:1, off:off + 8], d_ccout[idx], deps=[t_cc])
        kb.gset("stat_ev", d2)
        return d2

    def colmax(c0, c1, row, dst_off, deps):
        rd = kb.v(lambda e, c0=c0, c1=c1: e.tensor_reduce(
            maxc[:, 255:256], maxc[:, c0:c1], axis=AX, op=ALU.max),
            deps=deps, inc=True)
        gdd = kb.dma("y", stat_ev[0:1, 0:128], maxc[:, 255:256],
                     deps=[rd] + kb.gd("stat_ev"))
        rd2 = kb.v(lambda e, d=dst_off: e.tensor_reduce(
            row[0:1, d:d + 1], stat_ev[0:1, 0:128], axis=AX, op=ALU.max),
            deps=[gdd], inc=True)
        kb.gset("stat_ev", rd2)
        return rd2

    def scale_pair(row, src, dst_s, dst_rs, deps):
        t1 = kb.v(lambda e: e.tensor_scalar(
            row[0:1, dst_s:dst_s + 1], row[0:1, src:src + 1], 1.0 / QMAX,
            1e-12, op0=ALU.mult, op1=ALU.max), deps=deps, inc=True)
        t2 = kb.v(lambda e: e.reciprocal(
            row[0:1, dst_rs:dst_rs + 1], row[0:1, dst_s:dst_s + 1]),
            deps=[t1], inc=True)
        return t2

    def emit_ffn(attn_done):
        xqb, a1 = ar.alloc("xq", 16 * T, F16)
        lnr, a2 = ar.alloc("lnrF", 3 * T, F32)
        brw, a3 = ar.alloc("brwF", 2 * T, F16)
        x2c, a4 = ar.alloc("x2cF", 2 * T, F16)
        fdeps = a1 + a2 + a3 + a4 + list(attn_done)
        xq_toks = []
        red_toks = []
        mx = 0
        prev_lx = [None, None]
        prev_bts = []
        for s in range(S):
            lm = kb.dma("y", lnr[0:1, 2 * T:3 * T],
                        d_ln2[0:1, s * 2 * T: s * 2 * T + T],
                        deps=fdeps + prev_bts)
            lr = kb.dma("y", lnr[0:1, 0:T],
                        d_ln2[0:1, s * 2 * T + T: s * 2 * T + 2 * T],
                        deps=fdeps + prev_bts + [lm])
            tmr = kb.v(lambda e: e.tensor_tensor(
                lnr[0:1, T:2 * T], lnr[0:1, 2 * T:3 * T], lnr[0:1, 0:T],
                op=ALU.mult), deps=[lm, lr], inc=True)
            bts = ln_bcast(lnr, brw, lr, tmr)
            prev_bts = bts + [tmr]
            for ci in range(NCH):
                sl = (ci % 2) * T
                lx = kb.dma("g", x2c[:, sl:sl + T],
                            d_x2[s, ci * 128:(ci + 1) * 128, :],
                            deps=fdeps + [prev_lx[ci % 2]])
                xcol = (ci * 2 + s) * T
                tk = ln_apply(xqb, xcol, x2c, sl, brw, [lx] + bts)
                prev_lx[ci % 2] = tk
                rd = kb.v(lambda e, xcol=xcol, mx=mx: e.tensor_reduce(
                    maxc[:, mx:mx + 1], xqb[:, xcol:xcol + T], axis=AX,
                    op=ALU.max, apply_absolute_value=True), deps=[tk],
                    inc=True)
                red_toks.append(rd)
                xq_toks.append(tk)
                mx += 1
        tok_mx = colmax(0, 16, nrow, RX, red_toks)
        t_x0 = cc_exchange(0, nrow, RX, [tok_mx])
        t_s0 = scale_pair(nrow, RX, RX + 16, RX + 17, [t_x0])
        sb0 = bc_row32(nrow, RX + 16, 2, scbc, 0, [t_s0], bki=0)
        q_toks = []
        chiq, a5 = ar.alloc("chiq", 2 * T, I32)
        for col in range(16):
            xcol = col * T
            q0 = (col % 2) * T
            tin = kb.v(lambda e, xcol=xcol, q0=q0: e.tensor_scalar(
                chiq[:, q0:q0 + T], xqb[:, xcol:xcol + T], scbc[:, 1:2], None,
                op0=ALU.mult), deps=[sb0] + xq_toks + a5 +
                [q_toks[-2] if len(q_toks) > 1 else None], inc=True)
            tq = kb.s(lambda e, xcol=xcol, q0=q0: e.activation(
                xqb[:, xcol:xcol + T], chiq[:, q0:q0 + T], AF.Identity,
                scale=scbc[:, 0:1]), deps=[tin], inc=True)
            q_toks.append(tq)
        tap("xq", xqb[:, :], q_toks)
        for nm in ("lnrF", "brwF", "x2cF"):
            ar.free(nm, q_toks[-1])
        ar.free("chiq", q_toks[-1])

        wst, a6 = ar.alloc("wstF", 2 * NCH * 128, F16)
        sps, a7 = ar.alloc("spsF", 2 * S * T, F16)
        p1deps = a6 + a7 + q_toks
        mxu0, mxg0 = 16, 64
        spill_toks = []
        wfree = {}
        spfree = {}
        for mi, (dW, dSp) in enumerate(((d_upT, d_up), (d_gateT, d_gate))):
            for J in range(NI):
                ws = (mi * NI + J) % 2
                lw = kb.dma("y",
                            wst[:, ws * 1024:(ws + 1) * 1024]
                            .rearrange("p (ci j) -> p ci j", j=128),
                            dW.rearrange("(ci p) j -> p ci j", p=128)
                            [:, :, J * 128:(J + 1) * 128],
                            deps=p1deps + [wfree.get((mi * NI + J) - 2)])
                ev_list = []
                for tg in range(8):
                    bk = tg % 4
                    tkmm = None
                    for ci in range(NCH):
                        tkmm = kb.t(lambda e, ws=ws, ci=ci, tg=tg, bk=bk:
                                    e.matmul(
                            bank(bk),
                            wst[:, ws * 1024 + ci * 128:
                                ws * 1024 + (ci + 1) * 128],
                            xqb[:, ci * 2 * T + tg * 512:
                                ci * 2 * T + (tg + 1) * 512],
                            start=(ci == 0), stop=(ci == NCH - 1)),
                            deps=(kb.bdep(bk) + [lw] + q_toks)
                            if ci == 0 else [], inc=(ci == NCH - 1))
                    ev = kb.s(lambda e, ws=ws, tg=tg, bk=bk: e.activation(
                        sps[:, ws * S * T + tg * 512:
                            ws * S * T + (tg + 1) * 512],
                        bank(bk), AF.Identity),
                        deps=[tkmm] + list(spfree.get((mi * NI + J) - 2, ())),
                        inc=True)
                    kb.bset(ev, bk)
                    ev_list.append(ev)
                rd = kb.v(lambda e, ws=ws, mc=(mxu0 if mi == 0 else mxg0) + J:
                          e.tensor_reduce(
                    maxc[:, mc:mc + 1], sps[:, ws * S * T:(ws + 1) * S * T],
                    axis=AX, op=ALU.max, apply_absolute_value=True),
                    deps=ev_list, inc=True)
                sp = kb.dma("s", dSp[:, J * S * T:(J + 1) * S * T],
                            sps[:, ws * S * T:(ws + 1) * S * T],
                            deps=ev_list)
                wfree[mi * NI + J] = tkmm
                spfree[mi * NI + J] = (sp, rd)
                spill_toks.append(sp)
                spill_toks.append(rd)
        t_mu = colmax(mxu0, mxu0 + NI, nrow, RX + 1, spill_toks)
        t_mg = colmax(mxg0, mxg0 + NI, nrow, RX + 2, [t_mu] + spill_toks)
        t_x1 = cc_exchange(1, nrow, RX + 1, [t_mg])
        t_su = scale_pair(nrow, RX + 1, RX + 18, RX + 19, [t_x1])
        t_sg = scale_pair(nrow, RX + 2, RX + 20, RX + 21, [t_su])
        sb1 = bc_row32(nrow, RX + 18, 4, scbc, 2, [t_sg], bki=1)
        ar.free("xq", spill_toks[-1])
        ar.free("wstF", spill_toks[-1])

        gch, b1 = ar.alloc("gch", 2 * T, F16)
        chi, b2 = ar.alloc("chiF", 2 * T, I32)
        sga, b3 = ar.alloc("sga", 2 * T, F16)
        gq, b4 = ar.alloc("gq", 2 * T, F16)
        t2b, b5 = ar.alloc("t2b", 2 * T, F16)
        p2deps = b1 + b2 + b3 + b4 + b5 + [sb1]
        NCK = NI * S * T // T
        prev = {}
        p2_toks = []
        for ck in range(NCK):
            sl = (ck % 2)
            q0 = sl * T
            lg = kb.dma("y", gch[:, sl * T:(sl + 1) * T],
                        d_gate[:, ck * T:(ck + 1) * T],
                        deps=p2deps + [prev.get(("ng", ck - 2))])
            tng = kb.v(lambda e, sl=sl, q0=q0: e.tensor_scalar(
                chi[:, q0:q0 + T], gch[:, sl * T:(sl + 1) * T], scbc[:, 5:6],
                None, op0=ALU.mult),
                deps=[lg, prev.get(("sig", ck - 2)),
                      prev.get(("gq", ck - 2))], inc=True)
            tsg = kb.s(lambda e, q0=q0: e.activation(
                sga[:, q0:q0 + T], chi[:, q0:q0 + T], AF.Sigmoid,
                scale=scbc[:, 4:5]),
                deps=[tng, prev.get(("ns", ck - 2)),
                      prev.get(("t2", ck - 2))], inc=True)
            tgq = kb.s(lambda e, q0=q0: e.activation(
                gq[:, q0:q0 + T], chi[:, q0:q0 + T], AF.Identity,
                scale=scbc[:, 4:5]),
                deps=[tng, prev.get(("t2", ck - 2))], inc=True)
            tns = kb.v(lambda e, q0=q0: e.tensor_scalar(
                chi[:, q0:q0 + T], sga[:, q0:q0 + T], 65536.0, None,
                op0=ALU.mult), deps=[tsg, tgq], inc=True)
            tsq = kb.v(lambda e, q0=q0: e.tensor_scalar(
                sga[:, q0:q0 + T], chi[:, q0:q0 + T], 65535.0, 1.0 / 65536.0,
                op0=ALU.min, op1=ALU.mult),
                deps=[tns, prev.get(("t2", ck - 2))], inc=True)
            tt2 = kb.g(lambda e, sl=sl, q0=q0: e.tensor_tensor(
                t2b[:, sl * T:(sl + 1) * T], gq[:, q0:q0 + T],
                sga[:, q0:q0 + T], op=ALU.mult),
                deps=[tsq, tgq, prev.get(("t2", ck - 2))],
                inc=True)
            trd = kb.v(lambda e, sl=sl, mc=112 + (ck % 64): e.tensor_reduce(
                maxc[:, mc:mc + 1], t2b[:, sl * T:(sl + 1) * T], axis=AX,
                op=ALU.max, apply_absolute_value=True),
                deps=[tt2, prev.get(("rd", ck - 64))], inc=True)
            sp = kb.dma("g", d_t2[:, ck * T:(ck + 1) * T],
                        t2b[:, sl * T:(sl + 1) * T], deps=[tt2])
            prev[("sig", ck)] = tsg
            prev[("gq", ck)] = tgq
            prev[("ns", ck)] = tns
            prev[("ng", ck)] = tng
            prev[("t2", ck)] = sp
            prev[("rd", ck)] = trd
            p2_toks += [sp, trd]
        t_m2 = colmax(112, 112 + 64 if NCK > 64 else 112 + NCK, nrow, RX + 3,
                      p2_toks)
        t_x2 = cc_exchange(2, nrow, RX + 3, [t_m2])
        t_s2 = scale_pair(nrow, RX + 3, RX + 22, RX + 23, [t_x2])
        sb2 = bc_row32(nrow, RX + 22, 2, scbc, 6, [t_s2], bki=2)

        osp, c1 = ar.alloc("osp", 2 * T, F16)
        p3deps = c1 + [sb2] + p2_toks
        p3_toks = []
        prev3 = {}
        for ck in range(NCK):
            sl = ck % 2
            q0 = sl * T
            lt = kb.dma("y", gch[:, sl * T:(sl + 1) * T],
                        d_t2[:, ck * T:(ck + 1) * T],
                        deps=p3deps + [prev3.get(("a", ck - 2))])
            lu = kb.dma("s", t2b[:, sl * T:(sl + 1) * T],
                        d_up[:, ck * T:(ck + 1) * T],
                        deps=p3deps + [prev3.get(("b", ck - 2))])
            tn2 = kb.v(lambda e, sl=sl, q0=q0: e.tensor_scalar(
                chi[:, q0:q0 + T], gch[:, sl * T:(sl + 1) * T], scbc[:, 7:8],
                None, op0=ALU.mult),
                deps=[lt, prev3.get(("q", ck - 2))], inc=True)
            tq2 = kb.s(lambda e, q0=q0: e.activation(
                sga[:, q0:q0 + T], chi[:, q0:q0 + T], AF.Identity,
                scale=scbc[:, 6:7]),
                deps=[tn2, prev3.get(("sp", ck - 2))], inc=True)
            tnu = kb.v(lambda e, sl=sl, q0=q0: e.tensor_scalar(
                chi[:, q0:q0 + T], t2b[:, sl * T:(sl + 1) * T], scbc[:, 3:4],
                None, op0=ALU.mult), deps=[lu, tq2], inc=True)
            tqu = kb.s(lambda e, q0=q0: e.activation(
                gq[:, q0:q0 + T], chi[:, q0:q0 + T], AF.Identity,
                scale=scbc[:, 2:3]),
                deps=[tnu, prev3.get(("sp", ck - 2))], inc=True)
            to = kb.g(lambda e, sl=sl, q0=q0: e.tensor_tensor(
                osp[:, sl * T:(sl + 1) * T], sga[:, q0:q0 + T],
                gq[:, q0:q0 + T], op=ALU.mult),
                deps=[tqu, tq2, prev3.get(("sp", ck - 2))],
                inc=True)
            trd = kb.v(lambda e, sl=sl, mc=176 + (ck % 64): e.tensor_reduce(
                maxc[:, mc:mc + 1], osp[:, sl * T:(sl + 1) * T], axis=AX,
                op=ALU.max, apply_absolute_value=True),
                deps=[to, prev3.get(("rd", ck - 64))], inc=True)
            sp = kb.dma("g", d_oraw[:, ck * T:(ck + 1) * T],
                        osp[:, sl * T:(sl + 1) * T], deps=[to])
            prev3[("a", ck)] = tn2
            prev3[("b", ck)] = tnu
            prev3[("q", ck)] = tqu
            prev3[("rd", ck)] = trd
            prev3[("sp", ck)] = (sp if False else trd) and sp
            p3_toks += [sp, trd]
        t_m3 = colmax(176, 176 + min(64, NCK), nrow, RX + 4, p3_toks)
        t_x3 = cc_exchange(3, nrow, RX + 4, [t_m3])
        t_s3 = scale_pair(nrow, RX + 4, RX + 24, RX + 25, [t_x3])
        sb3 = bc_row32(nrow, RX + 24, 2, scbc, 8, [t_s3], bki=3)
        for nm in ("gch", "chiF", "sga", "gq", "t2b", "osp"):
            ar.free(nm, p3_toks[-1])

        foutb, c3 = ar.alloc("fout", NCH * 2 * T, F16)
        wdn, c4 = ar.alloc("wdn", NI * 128 * NCH, F16)
        oqst, c5a = ar.alloc("oqst", NI * 512, F16)
        chi2, c5b = ar.alloc("chi2", T, I32)
        p4deps = c3 + c4 + c5a + c5b + [sb3] + p3_toks
        lwall = kb.dma("y", wdn[:, :].rearrange("p (ii j) -> p ii j", j=C),
                       d_downT.rearrange("(ii p) j -> p ii j", p=128),
                       deps=p4deps)
        dn_ev = []
        oq_free = None
        for tg in range(8):
            lo = kb.dma("s", oqst[:, :].rearrange("p (ii t) -> p ii t", t=512),
                        d_oraw[:, :].rearrange("p (ii t) -> p ii t", t=S * T)
                        [:, :, tg * 512:(tg + 1) * 512],
                        deps=p4deps + [oq_free])
            qd_toks = []
            for sub in range(6):
                ti2 = kb.v(lambda e, sub=sub: e.tensor_scalar(
                    chi2[:, 0:T], oqst[:, sub * T:(sub + 1) * T],
                    scbc[:, 9:10], None, op0=ALU.mult),
                    deps=[lo] + ([qd_toks[-1]] if qd_toks else []), inc=True)
                tq2 = kb.s(lambda e, sub=sub: e.activation(
                    oqst[:, sub * T:(sub + 1) * T], chi2[:, 0:T],
                    AF.Identity, scale=scbc[:, 8:9]), deps=[ti2], inc=True)
                qd_toks.append(tq2)
            for J in range(NCH):
                bk = (tg * NCH + J) % 4
                tkmm = None
                for ii in range(NI):
                    tkmm = kb.t(lambda e, J=J, ii=ii, tg=tg, bk=bk: e.matmul(
                        bank(bk),
                        wdn[:, ii * C + J * 128: ii * C + (J + 1) * 128],
                        oqst[:, ii * 512:(ii + 1) * 512],
                        start=(ii == 0), stop=(ii == NI - 1)),
                        deps=(kb.bdep(bk) + [lwall] + qd_toks)
                        if ii == 0 else [], inc=(ii == NI - 1))
                ev = kb.s(lambda e, J=J, tg=tg, bk=bk: e.activation(
                    foutb[:, J * 2 * T + tg * 512: J * 2 * T + (tg + 1) * 512],
                    bank(bk), AF.Identity), deps=[tkmm], inc=True)
                kb.bset(ev, bk)
                dn_ev.append(ev)
            oq_free = dn_ev[-1]
        ar.free("oqst", dn_ev[-1])
        ar.free("chi2", dn_ev[-1])
        ar.free("wdn", dn_ev[-1])
        tap("fout", foutb[:, :], [dn_ev[-1]])

        lnr5, c5 = ar.alloc("lnrG", 3 * T, F32)
        sq2, c6 = ar.alloc("sq2G", 2 * T, F16)
        brw2, c7 = ar.alloc("brwG", 2 * T, F16)
        xfc, c8 = ar.alloc("xfc", 2 * T, F32)
        out_toks = []
        for s in range(S):
            sqf = [None, None]
            tmm = None
            for ci in range(NCH):
                sl = (ci % 2) * T
                col = ci * 2 * T + s * T
                tsq = kb.s(lambda e, col=col, sl=sl: e.activation(
                    sq2[:, sl:sl + T], foutb[:, col:col + T], AF.Square),
                    deps=[dn_ev[-1], sqf[ci % 2],
                          out_toks[-1] if out_toks else None]
                    + c5 + c6 + c7 + c8, inc=True)
                for g in range(4):
                    dep0 = (kb.bdep(6, 7) + [tok_ones]) if ci == 0 else []
                    kb.t(lambda e, col=col, g=g, ci=ci: e.matmul(
                        bank(6, r0=32 * g, r1=32 * g + 1), ones1[:, 0:1],
                        foutb[:, col + 512 * g: col + 512 * (g + 1)],
                        start=(ci == 0), stop=(ci == NCH - 1),
                        tile_position=(0, 32 * g), skip_group_check=True),
                        deps=dep0 + [dn_ev[-1]])
                    tmm = kb.t(lambda e, g=g, sl=sl, ci=ci: e.matmul(
                        bank(7, r0=32 * g, r1=32 * g + 1), ones1[:, 0:1],
                        sq2[:, sl + 512 * g: sl + 512 * (g + 1)],
                        start=(ci == 0), stop=(ci == NCH - 1),
                        tile_position=(0, 32 * g), skip_group_check=True),
                        deps=[tsq] + dep0, inc=(g == 3))
                sqf[ci % 2] = tmm
            for g in range(4):
                ev1 = kb.v(lambda e, g=g: e.tensor_copy(
                    stat_ev[32 * g:32 * g + 1, 0:512],
                    bank(6, r0=32 * g, r1=32 * g + 1)),
                    deps=([tmm] + kb.gd("stat_ev")) if g == 0 else [], inc=True)
            for g in range(4):
                ev2 = kb.v(lambda e, g=g: e.tensor_copy(
                    stat_ev[32 * g:32 * g + 1, 512:1024],
                    bank(7, r0=32 * g, r1=32 * g + 1)),
                    deps=[tmm] if g == 0 else [], inc=True)
            kb.bset(ev2, 6, 7)
            g1 = kb.dma("y", lnr5[0:1, 0:T], stat_ev[0:97:32, 0:512],
                        deps=[ev1])
            g2 = kb.dma("y", lnr5[0:1, T:2 * T], stat_ev[0:97:32, 512:1024],
                        deps=[ev2])
            kb.gset("stat_ev", g2)
            inv = 1.0 / C
            kb.v(lambda e: e.tensor_scalar(lnr5[0:1, 2 * T:3 * T],
                                           lnr5[0:1, 0:T], inv, None,
                                           op0=ALU.mult), deps=[g1])
            kb.v(lambda e: e.tensor_scalar(lnr5[0:1, T:2 * T],
                                           lnr5[0:1, T:2 * T], inv, None,
                                           op0=ALU.mult), deps=[g2])
            kb.v(lambda e: e.tensor_tensor(lnr5[0:1, 0:T],
                                           lnr5[0:1, 2 * T:3 * T],
                                           lnr5[0:1, 2 * T:3 * T],
                                           op=ALU.mult))
            tvar = kb.v(lambda e: e.tensor_tensor(lnr5[0:1, T:2 * T],
                                                  lnr5[0:1, T:2 * T],
                                                  lnr5[0:1, 0:T],
                                                  op=ALU.subtract), inc=True)
            tln = kb.s(lambda e: e.activation(lnr5[0:1, 0:T],
                                              lnr5[0:1, T:2 * T],
                                              AF.Ln, bias=epcol[0:1, 1:2]),
                       deps=[tvar, tok_ep], inc=True)
            tr = kb.s(lambda e: e.activation(lnr5[0:1, 0:T], lnr5[0:1, 0:T],
                                             AF.Exp, scale=-0.5),
                      deps=[tln], inc=True)
            tmr = kb.v(lambda e: e.tensor_tensor(lnr5[0:1, T:2 * T],
                                                 lnr5[0:1, 2 * T:3 * T],
                                                 lnr5[0:1, 0:T], op=ALU.mult),
                       deps=[tr], inc=True)
            bts = ln_bcast(lnr5, brw2, tr, tmr)
            prev_x = None
            for ci in range(NCH):
                col = ci * 2 * T + s * T
                u = ln_apply(sq2, 0, foutb, col, brw2, bts + [prev_x])
                xr = kb.dma("y", xfc[:, 0:T], d_x2[s, ci * 128:(ci + 1) * 128, :],
                            deps=[prev_x])
                ad = kb.v(lambda e: e.tensor_tensor(
                    xfc[:, T:2 * T], sq2[:, 0:T], xfc[:, 0:T], op=ALU.add),
                    deps=[u, xr], inc=True)
                so = kb.dma("y", dyo[s, ci * 128:(ci + 1) * 128, :],
                            xfc[:, T:2 * T], deps=[ad])
                prev_x = so
                out_toks.append(so)
        kb.y(lambda e: e.nop(), deps=[out_toks[-1]])
        return out_toks

    ph = set(_PHASES)
    done = []
    dq_tok, dq_names = None, ()
    if "dequant" in ph:
        dq_tok, dq_names = emit_dequant()
        done.append(dq_tok)
    if "attn0" in ph:
        a_toks0 = emit_attention(
            0, [], free_after_qkv=((dq_names, dq_tok) if dq_tok else None))
        done += a_toks0
    if "attn1" in ph:
        a_toks1 = emit_attention(1, [])
        done += a_toks1
    if "ffn" in ph:
        emit_ffn(done)
    else:
        prev = None
        for s in range(S):
            for ci in range(NCH):
                prev = kb.dma("y", dyo[s, ci * 128:(ci + 1) * 128, :],
                              dx[s, ci * 128:(ci + 1) * 128, :],
                              deps=done + [prev])
        kb.y(lambda e: e.nop(), deps=[prev])

    kb.build()
    return nc, d_taps


def _prep_inputs(inputs):
    x = np.asarray(inputs["L0_attn_x"], np.float32)
    conv_w = np.asarray(inputs["L0_attn_conv_w"], np.float32)
    cw = np.ascontiguousarray(conv_w.transpose(1, 2, 0).reshape(64, KTAP * HS))
    cw2 = np.concatenate([cw, cw], axis=0)
    base = {
        "w_attn": np.asarray(inputs["L0_attn_c_attn_w"], np.float32),
        "b_attn": np.ascontiguousarray(
            np.asarray(inputs["L0_attn_c_attn_b"], np.float32)
            .reshape(NI, 128).T),
        "w_proj": np.asarray(inputs["L0_attn_c_proj_w"], np.float32),
        "b_proj": np.ascontiguousarray(
            np.asarray(inputs["L0_attn_c_proj_b"], np.float32)
            .reshape(NCH, 128).T),
        "w_conv": cw2,
        "b_conv": np.ascontiguousarray(
            np.tile(np.asarray(inputs["L0_attn_conv_b"], np.float32)
                    .reshape(64, 1), (2, 1))),
        "comp_w": np.asarray(inputs["L0_attn_comp_ln_w"], np.float32)
        .reshape(1, HS),
        "comp_b": np.asarray(inputs["L0_attn_comp_ln_b"], np.float32)
        .reshape(1, HS),
        "lnl_w": np.ascontiguousarray(
            np.asarray(inputs["L0_attn_ln_local_w"], np.float32)
            .reshape(NCH, 128).T),
        "lnl_b": np.ascontiguousarray(
            np.asarray(inputs["L0_attn_ln_local_b"], np.float32)
            .reshape(NCH, 128).T),
        "g_w1": np.asarray(inputs["L0_attn_gate_w1"], np.float32),
        "g_b1": np.asarray(inputs["L0_attn_gate_b1"], np.float32)
        .reshape(1, C),
        "g_lnw": np.asarray(inputs["L0_attn_gate_ln_w"], np.float32)
        .reshape(1, C),
        "g_lnb": np.asarray(inputs["L0_attn_gate_ln_b"], np.float32)
        .reshape(1, C),
        "g_w2": np.asarray(inputs["L0_attn_gate_w2"], np.float32),
        "g_b2": np.asarray(inputs["L0_attn_gate_b2"], np.float32)
        .reshape(1, 2),
        "w_up": np.asarray(inputs["L0_ffn_w_up"], np.float32),
        "w_gate": np.asarray(inputs["L0_ffn_w_gate"], np.float32),
        "w_down": np.asarray(inputs["L0_ffn_w_down"], np.float32),
    }
    in_maps = []
    for c in range(NCORES):
        m = dict(base)
        m["x"] = np.ascontiguousarray(
            x[2 * c:2 * c + 2].transpose(0, 2, 1))
        in_maps.append(m)
    return in_maps


def kernel(**inputs):
    in_maps = _prep_inputs(inputs)
    nc, d_taps = build_program(taps=tuple(_DEBUG_TAPS))
    res = run_bass_kernel_spmd(nc, in_maps, core_ids=list(range(NCORES)))
    out = np.empty((16, 2048, 1024), np.float32)
    for c in range(NCORES):
        yv = np.asarray(res.results[c]["y"], np.float32)
        out[2 * c:2 * c + 2] = yv.transpose(0, 2, 1)
    if _DEBUG_TAPS:
        kernel._tap_results = [
            {k: np.asarray(res.results[c].get("tap_" + k))
             for k in _DEBUG_TAPS if ("tap_" + k) in res.results[c]}
            for c in range(NCORES)]
        kernel._spill_results = {
            k: np.asarray(res.results[0][k]) for k in
            ("x2_sp", "ln2_sp", "up_sp", "gate_sp", "t2_sp", "oraw_sp",
             "upT_sp", "gateT_sp", "downT_sp") if k in res.results[0]}
    return out


# version 08
# speedup vs baseline: 1.0420x; 1.0420x over previous
import numpy as np
import ml_dtypes
import concourse.bass as bass
import concourse.mybir as mybir
from concourse.bass_utils import run_bass_kernel_spmd
from contextlib import ExitStack

F32 = mybir.dt.float32
F16 = mybir.dt.float16
BF16 = mybir.dt.bfloat16
I32 = mybir.dt.int32
AF = mybir.ActivationFunctionType
ALU = mybir.AluOpType
AX = mybir.AxisListType.X

NCORES = 8
S, T, C, NCH, NH, HS, NB = 2, 2048, 1024, 8, 16, 64, 127
KTAP, CSTR = 32, 16
I3, NI = 3072, 24
EPS = 1e-5
QMAX = 32767.0
DEN_EPS = 1e-37
ARENA_BYTES = 172800

_DEBUG_TAPS = []
_PHASES = ("dequant", "attn0", "attn1", "ffn")
_ATTN_CUT = None


def _dtsz(dt):
    return {F32: 4, F16: 2, BF16: 2, I32: 4}[dt]


class KB:
    def __init__(self, nc):
        self.nc = nc
        self.ops = {e: [] for e in ("t", "v", "s", "g", "y")}
        self.sem = {}
        self.cnt = {}
        self.waited = {e: {} for e in ("t", "v", "s", "g", "y")}
        self.stack = ExitStack()
        names = ["st", "sv", "ss", "sg", "sy", "cc"]
        for q in ("dy", "dg", "ds"):
            names += [f"{q}{i}" for i in range(4)]
        for name in names:
            self.sem[name] = self.stack.enter_context(nc.semaphore(name))
            self.cnt[name] = 0
        self.qrr = {"y": 0, "g": 0, "s": 0}
        self.bank_tok = {}
        self.guard = {}

    def sb(self, name, shape, dt=F16):
        return self.stack.enter_context(self.nc.sbuf_tensor(name, shape, dt))

    AUTO_SER = {"v": "sv", "s": "ss", "g": "sg"}

    def emit(self, eng, fn, deps=(), inc=None, inc_by=1):
        own = self.AUTO_SER.get(eng)
        is_dma = isinstance(inc, str) and inc.startswith(("d", "c"))
        if own is not None and not is_dma:
            deps = list(deps) + [(own, self.cnt[own])]
            if not isinstance(inc, str):
                inc = own
        waits = []
        for d in deps:
            if d is None:
                continue
            sem, val = d
            if val <= 0 or self.waited[eng].get(sem, 0) >= val:
                continue
            self.waited[eng][sem] = val
            waits.append((self.sem[sem], val))
        tok, sem_h = None, None
        if inc:
            inc = {"t": "st", "v": "sv", "s": "ss", "g": "sg", "y": "sy"}[eng] \
                if inc is True else inc
            self.cnt[inc] += inc_by
            tok = (inc, self.cnt[inc])
            sem_h = self.sem[inc]

        def run(e, waits=waits, fn=fn, sem_h=sem_h, inc_by=inc_by):
            for (h, v) in waits:
                e.wait_ge(h, v)
            ins = fn(e)
            if sem_h is not None:
                ins.then_inc(sem_h, inc_by)
        self.ops[eng].append(run)
        return tok

    def t(self, fn, deps=(), inc=False):
        return self.emit("t", fn, deps, inc)

    def v(self, fn, deps=(), inc=False):
        return self.emit("v", fn, deps, inc)

    def s(self, fn, deps=(), inc=False):
        return self.emit("s", fn, deps, inc)

    def g(self, fn, deps=(), inc=False):
        return self.emit("g", fn, deps, inc)

    def y(self, fn, deps=(), inc=False):
        return self.emit("y", fn, deps, inc)

    def dma(self, queue, out, in_, deps=()):
        base = {"y": "dy", "g": "dg", "s": "ds"}[queue]
        sem = f"{base}{self.qrr[queue]}"
        self.qrr[queue] = (self.qrr[queue] + 1) % 4
        deps = list(deps) + [(sem, self.cnt[sem])]
        return self.emit(queue, lambda e, o=out, i=in_: e.dma_start(out=o, in_=i),
                         deps, sem, 16)

    def bdep(self, *banks):
        return [self.bank_tok.get(b) for b in banks]

    def bset(self, tok, *banks):
        for b in banks:
            self.bank_tok[b] = tok

    def gd(self, name):
        return [self.guard.get(name)]

    def gset(self, name, tok):
        self.guard[name] = tok

    def build(self):
        with self.nc.Block() as block:
            @block.tensor
            def _(e):
                for f in self.ops["t"]:
                    f(e)

            @block.vector
            def _(e):
                for f in self.ops["v"]:
                    f(e)

            @block.scalar
            def _(e):
                for f in self.ops["s"]:
                    f(e)

            @block.gpsimd
            def _(e):
                for f in self.ops["g"]:
                    f(e)

            @block.sync
            def _(e):
                for f in self.ops["y"]:
                    f(e)


class Arena:

    def __init__(self, kb, nbytes):
        self.kb = kb
        self.u8 = kb.sb("sb_ARENA", [128, nbytes], mybir.dt.uint8)
        self.free_list = [(0, nbytes, [])]
        self.live = {}

    def _check(self):
        spans = sorted([(o, o + s, "free") for (o, s, _) in self.free_list] +
                       [(o, o + s, n) for n, (o, s) in self.live.items()])
        for a, b in zip(spans, spans[1:]):
            assert a[1] <= b[0], f"arena overlap: {a} vs {b}"

    def alloc(self, name, nelem, dt=F16):
        self._check()
        size = (nelem * _dtsz(dt) + 31) & ~31
        for i, (off, sz, toks) in enumerate(self.free_list):
            if sz >= size:
                if sz == size:
                    self.free_list.pop(i)
                else:
                    self.free_list[i] = (off + size, sz - size, toks)
                view = self.u8[:, off:off + nelem * _dtsz(dt)].bitcast(dt)
                self.live[name] = (off, size)
                return view, list(toks)
        raise MemoryError(f"arena OOM for {name} ({size}B); free={self.free_list}")

    def free(self, name, tok):
        off, size = self.live.pop(name)
        self.free_list.append((off, size, [tok] if tok else []))
        self.free_list.sort()
        merged = []
        for blk in self.free_list:
            if merged and merged[-1][0] + merged[-1][1] == blk[0]:
                o, s, t1 = merged[-1]
                comb = {}
                for t in list(t1) + list(blk[2]):
                    if t is None:
                        continue
                    if t[0] not in comb or comb[t[0]][1] < t[1]:
                        comb[t[0]] = t
                merged[-1] = (o, s + blk[1], list(comb.values()))
            else:
                merged.append(blk)
        self.free_list = merged


def build_program(taps=()):
    nc = bass.Bass("TRN2", target_bir_lowering=False, debug=False)
    kb = KB(nc)
    sb = kb.sb
    ar = Arena(kb, ARENA_BYTES)

    dx = nc.dram_tensor("x", [S, C, T], F32, kind="ExternalInput").ap()
    dyo = nc.dram_tensor("y", [S, C, T], F32, kind="ExternalOutput").ap()
    d_wattn = nc.dram_tensor("w_attn", [C, 3 * C], F32, kind="ExternalInput").ap()
    d_battn = nc.dram_tensor("b_attn", [128, NI], F32, kind="ExternalInput").ap()
    d_wproj = nc.dram_tensor("w_proj", [C, C], F32, kind="ExternalInput").ap()
    d_bproj = nc.dram_tensor("b_proj", [128, NCH], F32, kind="ExternalInput").ap()
    d_wconv = nc.dram_tensor("w_conv", [128, KTAP * HS], F32, kind="ExternalInput").ap()
    d_bconv = nc.dram_tensor("b_conv", [128, 1], F32, kind="ExternalInput").ap()
    d_compw = nc.dram_tensor("comp_w", [1, HS], F32, kind="ExternalInput").ap()
    d_compb = nc.dram_tensor("comp_b", [1, HS], F32, kind="ExternalInput").ap()
    d_lnlw = nc.dram_tensor("lnl_w", [128, NCH], F32, kind="ExternalInput").ap()
    d_lnlb = nc.dram_tensor("lnl_b", [128, NCH], F32, kind="ExternalInput").ap()
    d_gw1 = nc.dram_tensor("g_w1", [2 * C, C], F32, kind="ExternalInput").ap()
    d_gb1 = nc.dram_tensor("g_b1", [1, C], F32, kind="ExternalInput").ap()
    d_glnw = nc.dram_tensor("g_lnw", [1, C], F32, kind="ExternalInput").ap()
    d_glnb = nc.dram_tensor("g_lnb", [1, C], F32, kind="ExternalInput").ap()
    d_gw2 = nc.dram_tensor("g_w2", [C, 2], F32, kind="ExternalInput").ap()
    d_gb2 = nc.dram_tensor("g_b2", [1, 2], F32, kind="ExternalInput").ap()
    d_wup = nc.dram_tensor("w_up", [I3, C], F32, kind="ExternalInput").ap()
    d_wgate = nc.dram_tensor("w_gate", [I3, C], F32, kind="ExternalInput").ap()
    d_wdown = nc.dram_tensor("w_down", [C, I3], F32, kind="ExternalInput").ap()

    d_x2 = nc.dram_tensor("x2_sp", [S, C, T], F32).ap()
    d_ln2 = nc.dram_tensor("ln2_sp", [1, S * 2 * T], F32).ap()
    d_upT = nc.dram_tensor("upT_sp", [C, I3], F16).ap()
    d_gateT = nc.dram_tensor("gateT_sp", [C, I3], F16).ap()
    d_downT = nc.dram_tensor("downT_sp", [I3, C], F16).ap()
    d_up = nc.dram_tensor("up_sp", [128, NI * S * T], F16).ap()
    d_gate = nc.dram_tensor("gate_sp", [128, NI * S * T], F16).ap()
    d_t2 = nc.dram_tensor("t2_sp", [128, NI * S * T], F16).ap()
    d_oraw = nc.dram_tensor("oraw_sp", [128, NI * S * T], F16).ap()
    d_hrow = nc.dram_tensor("hrow_sp", [1, C], F32).ap()
    d_bmov = nc.dram_tensor("bmov_sp", [64, 1024], F16).ap()
    d_ccin = [nc.dram_tensor(f"ccin{i}", [1, 8], F32).ap() for i in range(4)]
    d_ccout = [nc.dram_tensor(f"ccout{i}", [1, 8], F32).ap() for i in range(4)]
    d_taps = {}

    ones1 = sb("sb_ones1", [128, 1], F16)
    idf = sb("sb_idf", [128, 128], F16)
    bc_w = sb("sb_bc_w", [128, 128], F16)
    bc_w32 = sb("sb_bc_w32", [32, 128], F32)
    bc_stage = sb("sb_bc_stage", [32, 512], F16)
    bc_st32 = sb("sb_bc_st32", [32, 128], F32)
    div_st = sb("sb_div_st", [128, 1024], BF16)
    dvb = sb("sb_dvb", [64, 1024], BF16)
    bstg = sb("sb_bstg", [64, 1024], F16)
    b_attn = sb("sb_b_attn", [128, NI], F32)
    b_proj = sb("sb_b_proj", [128, NCH], F32)
    b_conv = sb("sb_b_conv", [128, 1], F32)
    lnl_w = sb("sb_lnl_w", [128, NCH], F32)
    lnl_b = sb("sb_lnl_b", [128, NCH], F32)
    w_conv = sb("sb_w_conv", [128, KTAP * HS], F16)
    w_g2 = sb("sb_w_g2", [128, NCH * 2], F16)
    crep_w = sb("sb_crep_w", [128, NH * HS], F16)
    crep_b = sb("sb_crep_b", [128, NH * HS], F16)
    mn_bc = sb("sb_mn_bc", [128, 32], F32)
    hcol = sb("sb_hcol", [128, NCH], F16)
    gbc = sb("sb_gbc", [128, 2], F32)
    scbc = sb("sb_scbc", [128, 16], F32)
    maxc = sb("sb_maxc", [128, 256], F32)
    stat_ev = sb("sb_stat_ev", [128, 1024], F32)
    nrow = sb("sb_nrow", [1, 128], F32)
    vcompT = sb("sb_vcompT", [128, NH * 65], F16)
    red16 = sb("sb_red16", [128, 4 * NH], F32)
    pt = sb("sb_pt", [128, 4 * 512], BF16)
    vt = sb("sb_vt", [128, 2 * 16 * 65], F16)
    kcomp = sb("sb_kcomp", [128, NCH * NB], F16)
    epcol = sb("sb_epcol", [128, 2], F32)

    RQ, RK, RKC, RM = 0, NH, 2 * NH, 3 * NH
    RG = 5 * NH
    RX = RG + 16

    PS = kb.stack.enter_context(nc.psum_tensor("PS", [128, 4096], F32))
    PS16 = PS[:, :].bitcast(F16)

    def bank(i, r0=0, r1=128, c0=0, c1=512):
        return PS[r0:r1, 512 * i + c0: 512 * i + c1]

    def tap(name, ap, deps):
        if name not in taps:
            return
        d = nc.dram_tensor("tap_" + name, list(ap.shape), ap.dtype,
                           kind="ExternalOutput").ap()
        d_taps[name] = d
        kb.dma("y", d, ap, deps=deps)

    tok_ones = kb.v(lambda e: e.memset(ones1[:, :], 1.0), inc=True)
    kb.v(lambda e: e.memset(nrow[:, :], 0.0))
    kb.v(lambda e: e.memset(maxc[:, :], 0.0))
    kb.v(lambda e: e.memset(stat_ev[:, :], 0.0))
    kb.v(lambda e: e.memset(bc_w[:, :], 0.0))
    kb.v(lambda e: e.memset(bc_w[0:1, :], 1.0))
    tok_bcw = kb.v(lambda e: e.memset(bc_w[64:65, :], 1.0), inc=True)
    kb.v(lambda e: e.memset(bc_w32[:, :], 0.0))
    tok_bcw32 = kb.v(lambda e: e.memset(bc_w32[0:1, :], 1.0), inc=True)
    tok_stage = kb.v(lambda e: e.memset(bc_stage[:, :], 0.0), inc=True)
    tok_st32 = kb.v(lambda e: e.memset(bc_st32[:, :], 0.0), inc=True)
    tok_dst = kb.v(lambda e: e.memset(div_st[:, :], 0.0), inc=True)
    kb.v(lambda e: e.memset(epcol[:, 0:1], DEN_EPS))
    tok_ep = kb.v(lambda e: e.memset(epcol[:, 1:2], EPS), inc=True)
    kb.v(lambda e: e.memset(vt[:, :], 0.0))
    tok_vt0 = kb.v(lambda e: e.memset(
        vt[:, :].rearrange("p (k c) -> p k c", c=65)[:, :, 64:65], 1.0),
        inc=True)
    kb.v(lambda e: e.memset(vcompT[:, :], 0.0))
    tok_vc0 = kb.v(lambda e: e.memset(
        vcompT[:, :].rearrange("p (h c) -> p h c", c=65)[:, :, 64:65], 1.0),
        inc=True)
    tok_idt = kb.g(lambda e: e.memset(idf[:, :], 1.0), inc=True)
    tok_id = kb.g(lambda e: e.affine_select(
        idf[:, :], idf[:, :], pattern=[[1, 128]], compare_op=ALU.is_equal,
        fill=0.0, base=0, channel_multiplier=-1), deps=[tok_idt], inc=True)

    tok_wconv = kb.dma("g", w_conv[:, :], d_wconv)
    tok_wg2 = kb.dma("g", w_g2[:, :].rearrange("p (ci j) -> p ci j", j=2),
                     d_gw2.rearrange("(ci p) j -> p ci j", p=128))
    tok_battn = kb.dma("y", b_attn[:, :], d_battn)
    tok_bproj = kb.dma("y", b_proj[:, :], d_bproj)
    tok_bconv = kb.dma("y", b_conv[:, :], d_bconv)
    tok_lnlw = kb.dma("y", lnl_w[:, :], d_lnlw)
    tok_lnlb = kb.dma("y", lnl_b[:, :], d_lnlb)
    tok_cw = kb.dma("y", bc_st32[0:1, 0:HS], d_compw, deps=[tok_st32])
    tok_cb = kb.dma("y", bc_st32[0:1, HS:2 * HS], d_compb, deps=[tok_st32])

    tok_m1 = kb.t(lambda e: e.matmul(bank(0, c1=64), bc_w32[0:32, :],
                                     bc_st32[0:32, 0:64], start=True, stop=True),
                  deps=[tok_cw, tok_st32, tok_bcw32], inc=True)
    tok_m2 = kb.t(lambda e: e.matmul(bank(1, c1=64), bc_w32[0:32, :],
                                     bc_st32[0:32, HS:2 * HS],
                                     start=True, stop=True),
                  deps=[tok_cb, tok_st32, tok_bcw32], inc=True)
    crep_toks = []
    for h in range(NH):
        crep_toks.append(kb.v(lambda e, h=h: e.tensor_copy(
            crep_w[:, h * 64:(h + 1) * 64], bank(0, c1=64)),
            deps=[tok_m1], inc=(h == NH - 1)))
    for h in range(NH):
        crep_toks.append(kb.v(lambda e, h=h: e.tensor_copy(
            crep_b[:, h * 64:(h + 1) * 64], bank(1, c1=64)),
            deps=[tok_m2], inc=(h == NH - 1)))
    tok_crep = crep_toks[-1]
    kb.bset(tok_crep, 0, 1)

    def bc_row16(src_row, src_off, n, dst16, dst_off, deps, banks=(0, 1)):
        toks = []
        off, bi = 0, 0
        prev_cp = None
        while off < n:
            w = min(512, n - off)
            bki = banks[bi % len(banks)]
            cp = kb.v(lambda e, o=off, w=w: e.tensor_copy(
                bc_stage[0:1, 0:w], src_row[0:1, src_off + o:src_off + o + w]),
                deps=list(deps) + [tok_stage, tok_bcw, prev_cp], inc=True)
            mm = kb.t(lambda e, bk=bki, w=w: e.matmul(
                bank(bk, c1=w), bc_w[0:32, :], bc_stage[0:32, 0:w],
                start=True, stop=True), deps=[cp] + kb.bdep(bki), inc=True)
            prev_cp = mm
            ev = kb.v(lambda e, bk=bki, o=off, w=w: e.tensor_copy(
                dst16[:, dst_off + o:dst_off + o + w], bank(bk, c1=w)),
                deps=[mm], inc=True)
            kb.bset(ev, bki)
            toks.append(ev)
            off += w
            bi += 1
        return toks

    def bc_row32(src_row, src_off, n, dst32, dst_off, deps, bki=0):
        cp = kb.v(lambda e, n=n: e.tensor_copy(
            bc_st32[0:1, 0:n], src_row[0:1, src_off:src_off + n]),
            deps=list(deps) + [tok_st32, tok_bcw32], inc=True)
        mm = kb.t(lambda e, n=n, bk=bki: e.matmul(
            bank(bk, c1=n), bc_w32[0:32, :], bc_st32[0:32, 0:n],
            start=True, stop=True), deps=[cp] + kb.bdep(bki), inc=True)
        ev = kb.v(lambda e, n=n, bk=bki: e.tensor_copy(
            dst32[:, dst_off:dst_off + n], bank(bk, c1=n)), deps=[mm], inc=True)
        kb.bset(ev, bki)
        return ev

    def ln_stats(src16, nch, lnr, sq2, src_deps, lnr_deps=()):
        sq_free = [None, None]
        tok_mm = None
        for ci in range(nch):
            sl = (ci % 2) * T
            tok_sq = kb.s(lambda e, ci=ci, sl=sl: e.activation(
                sq2[:, sl:sl + T], src16[:, ci * T:(ci + 1) * T], AF.Square),
                deps=list(src_deps) + [sq_free[ci % 2]], inc=True)
            for g in range(4):
                dep0 = (kb.bdep(6, 7) + list(src_deps) + [tok_ones]) \
                    if ci == 0 else []
                kb.t(lambda e, ci=ci, g=g: e.matmul(
                    bank(6, r0=32 * g, r1=32 * g + 1), ones1[:, 0:1],
                    src16[:, ci * T + 512 * g: ci * T + 512 * (g + 1)],
                    start=(ci == 0), stop=(ci == nch - 1),
                    tile_position=(0, 32 * g), skip_group_check=True),
                    deps=dep0)
                tok_mm = kb.t(lambda e, g=g, sl=sl, ci=ci: e.matmul(
                    bank(7, r0=32 * g, r1=32 * g + 1), ones1[:, 0:1],
                    sq2[:, sl + 512 * g: sl + 512 * (g + 1)],
                    start=(ci == 0), stop=(ci == nch - 1),
                    tile_position=(0, 32 * g), skip_group_check=True),
                    deps=[tok_sq] + (dep0 if ci == 0 else []),
                    inc=(g == 3))
            sq_free[ci % 2] = tok_mm
        for g in range(4):
            ev1 = kb.v(lambda e, g=g: e.tensor_copy(
                stat_ev[32 * g:32 * g + 1, 0:512],
                bank(6, r0=32 * g, r1=32 * g + 1)),
                deps=([tok_mm] + kb.gd("stat_ev")) if g == 0 else [], inc=True)
        for g in range(4):
            ev2 = kb.v(lambda e, g=g: e.tensor_copy(
                stat_ev[32 * g:32 * g + 1, 512:1024],
                bank(7, r0=32 * g, r1=32 * g + 1)),
                deps=[tok_mm] if g == 0 else [], inc=True)
        kb.bset(ev2, 6, 7)
        g1 = kb.dma("y", lnr[0:1, 0:T], stat_ev[0:97:32, 0:512],
                    deps=[ev1] + list(lnr_deps))
        g2 = kb.dma("y", lnr[0:1, T:2 * T], stat_ev[0:97:32, 512:1024],
                    deps=[ev2] + list(lnr_deps))
        kb.gset("stat_ev", g2)
        inv = 1.0 / (nch * 128)
        kb.v(lambda e: e.tensor_scalar(lnr[0:1, 2 * T:3 * T], lnr[0:1, 0:T],
                                       inv, None, op0=ALU.mult), deps=[g1])
        kb.v(lambda e: e.tensor_scalar(lnr[0:1, T:2 * T], lnr[0:1, T:2 * T],
                                       inv, None, op0=ALU.mult), deps=[g2])
        kb.v(lambda e: e.tensor_tensor(lnr[0:1, 0:T], lnr[0:1, 2 * T:3 * T],
                                       lnr[0:1, 2 * T:3 * T], op=ALU.mult))
        tvar = kb.v(lambda e: e.tensor_tensor(lnr[0:1, T:2 * T],
                                              lnr[0:1, T:2 * T],
                                              lnr[0:1, 0:T],
                                              op=ALU.subtract), inc=True)
        tln = kb.s(lambda e: e.activation(lnr[0:1, 0:T], lnr[0:1, T:2 * T],
                                          AF.Ln, bias=epcol[0:1, 1:2]),
                   deps=[tvar, tok_ep], inc=True)
        tr = kb.s(lambda e: e.activation(lnr[0:1, 0:T], lnr[0:1, 0:T],
                                         AF.Exp, scale=-0.5),
                  deps=[tln], inc=True)
        tmr = kb.v(lambda e: e.tensor_tensor(lnr[0:1, T:2 * T],
                                             lnr[0:1, 2 * T:3 * T],
                                             lnr[0:1, 0:T], op=ALU.mult),
                   deps=[tr], inc=True)
        return tr, tmr

    def ln_bcast(lnr, brw, tok_r, tok_mr):
        tR = bc_row16(lnr, 0, T, brw, 0, [tok_r], banks=(0, 1))
        tMR = bc_row16(lnr, T, T, brw, T, [tok_mr], banks=(2, 3))
        return tR + tMR

    def ln_apply(dst, dst_off, src, src_off, brw, deps):
        t1 = kb.v(lambda e: e.tensor_tensor(
            dst[:, dst_off:dst_off + T], src[:, src_off:src_off + T],
            brw[:, 0:T], op=ALU.mult), deps=deps, inc=True)
        return kb.v(lambda e: e.tensor_tensor(
            dst[:, dst_off:dst_off + T], dst[:, dst_off:dst_off + T],
            brw[:, T:2 * T], op=ALU.subtract), deps=[t1], inc=True)

    def emit_dequant():
        dqw, dw_dep = ar.alloc("dqw", I3, F32)
        dqs, ds_dep = ar.alloc("dqs", I3 // 32, F32)
        dqr, dr_dep = ar.alloc("dqr", I3 // 32, F32)
        dqi, di_dep = ar.alloc("dqi", I3, I32)
        dqq, dq_dep = ar.alloc("dqq", I3, F16)
        dqt, dt_dep = ar.alloc("dqt", I3, F16)
        base_dep = dw_dep + ds_dep + dr_dep + di_dep + dq_dep + dt_dep
        specs = [(d_wup, d_upT, I3, C), (d_wgate, d_gateT, I3, C),
                 (d_wdown, d_downT, C, I3)]
        prev_q = None
        prev_sp = None
        last_sp = None
        for (dsrc, ddst, O, I) in specs:
            nb = I // 32
            nic = I // 128
            for oc in range(O // 128):
                ld = kb.dma("y", dqw[:, 0:I], dsrc[oc * 128:(oc + 1) * 128, :],
                            deps=[prev_q] + base_dep)
                bm = kb.v(lambda e, I=I, nb=nb: e.tensor_reduce(
                    dqs[:, 0:nb],
                    dqw[:, 0:I].rearrange("p (nb b) -> p nb b", b=32),
                    axis=AX, op=ALU.max, apply_absolute_value=True),
                    deps=[ld], inc=True)
                sc = kb.v(lambda e, nb=nb: e.tensor_scalar(
                    dqs[:, 0:nb], dqs[:, 0:nb], 1.0 / 7.0, 1e-12,
                    op0=ALU.mult, op1=ALU.max), deps=[bm], inc=True)
                rc = kb.v(lambda e, nb=nb: e.reciprocal(dqr[:, 0:nb],
                                                        dqs[:, 0:nb]),
                          deps=[sc], inc=True)
                nn = kb.v(lambda e, I=I, nb=nb: e.tensor_tensor(
                    dqi[:, 0:I], dqw[:, 0:I],
                    dqr[:, 0:nb].unsqueeze(2).broadcast_to((128, nb, 32)), op=ALU.mult),
                    deps=[rc], inc=True)
                qq = kb.v(lambda e, I=I, nb=nb: e.tensor_tensor(
                    dqq[:, 0:I], dqi[:, 0:I],
                    dqs[:, 0:nb].unsqueeze(2).broadcast_to((128, nb, 32)), op=ALU.mult),
                    deps=[nn], inc=True)
                prev_q = qq
                ev_toks = []
                for grp in range((nic + 7) // 8):
                    tb = 4 + grp % 2
                    ntr = min(8, nic - grp * 8)
                    tk = None
                    for k in range(ntr):
                        ci = grp * 8 + k
                        tk = kb.t(lambda e, tb=tb, k=k, ci=ci: e.matmul(
                            PS16[:, tb * 1024 + 128 * k:
                                 tb * 1024 + 128 * (k + 1)],
                            dqq[:, ci * 128:(ci + 1) * 128], idf[:, :],
                            start=True, stop=True, is_transpose=True,
                            skip_group_check=True),
                            deps=([qq, tok_id] + kb.bdep(tb)) if k == 0 else [],
                            inc=(k == ntr - 1))
                    ev = kb.v(lambda e, tb=tb, grp=grp, ntr=ntr: e.tensor_copy(
                        dqt[:, grp * 1024: grp * 1024 + ntr * 128],
                        PS16[:, tb * 1024: tb * 1024 + ntr * 128]),
                        deps=[tk, prev_sp], inc=True)
                    kb.bset(ev, tb)
                    ev_toks.append(ev)
                sp = kb.dma("s", ddst.rearrange("(ci p) j -> p ci j", p=128)
                            [:, :, oc * 128:(oc + 1) * 128],
                            dqt[:, 0:I].rearrange("p (ci j) -> p ci j", j=128),
                            deps=ev_toks)
                prev_sp = sp
                last_sp = sp
        return last_sp, ("dqw", "dqs", "dqr", "dqi", "dqq", "dqt")

    def emit_attention(s, first_deps, free_after_qkv=None):
        xln, d5 = ar.alloc(f"xln{s}", NCH * T, F16)
        lnr, d1 = ar.alloc(f"lnr{s}", 3 * T, F32)
        sq2, d2 = ar.alloc(f"sq2{s}", 2 * T, F16)
        xst, d3 = ar.alloc(f"xst{s}", 2 * T, F16)
        brw, d4 = ar.alloc(f"brw{s}", 2 * T, F16)
        adeps = d1 + d2 + d3 + d4 + d5 + list(first_deps)
        xst_free = [None, None]
        st_deps = []
        for ci in range(NCH):
            sl = (ci % 2) * T
            lx = kb.dma("g", xst[:, sl:sl + T],
                        dx[s, ci * 128:(ci + 1) * 128, :],
                        deps=adeps + [xst_free[ci % 2]])
            tok_sq = kb.s(lambda e, sl=sl: e.activation(
                sq2[:, sl:sl + T], xst[:, sl:sl + T], AF.Square),
                deps=[lx, xst_free[ci % 2]], inc=True)
            for g in range(4):
                dep0 = (kb.bdep(6, 7) + [tok_ones]) if ci == 0 else []
                t1 = kb.t(lambda e, g=g, sl=sl, ci=ci: e.matmul(
                    bank(6, r0=32 * g, r1=32 * g + 1), ones1[:, 0:1],
                    xst[:, sl + 512 * g: sl + 512 * (g + 1)],
                    start=(ci == 0), stop=(ci == NCH - 1),
                    tile_position=(0, 32 * g), skip_group_check=True),
                    deps=dep0 + [lx], inc=(g == 3))
                t2 = kb.t(lambda e, g=g, sl=sl, ci=ci: e.matmul(
                    bank(7, r0=32 * g, r1=32 * g + 1), ones1[:, 0:1],
                    sq2[:, sl + 512 * g: sl + 512 * (g + 1)],
                    start=(ci == 0), stop=(ci == NCH - 1),
                    tile_position=(0, 32 * g), skip_group_check=True),
                    deps=[tok_sq] + dep0, inc=(g == 3))
            xst_free[ci % 2] = t2
            st_deps.append(t2)
        tok_mm = st_deps[-1]
        for g in range(4):
            ev1 = kb.v(lambda e, g=g: e.tensor_copy(
                stat_ev[32 * g:32 * g + 1, 0:512],
                bank(6, r0=32 * g, r1=32 * g + 1)),
                deps=([tok_mm] + kb.gd("stat_ev")) if g == 0 else [], inc=True)
        for g in range(4):
            ev2 = kb.v(lambda e, g=g: e.tensor_copy(
                stat_ev[32 * g:32 * g + 1, 512:1024],
                bank(7, r0=32 * g, r1=32 * g + 1)),
                deps=[tok_mm] if g == 0 else [], inc=True)
        kb.bset(ev2, 6, 7)
        g1 = kb.dma("y", lnr[0:1, 0:T], stat_ev[0:97:32, 0:512], deps=[ev1])
        g2 = kb.dma("y", lnr[0:1, T:2 * T], stat_ev[0:97:32, 512:1024],
                    deps=[ev2])
        kb.gset("stat_ev", g2)
        inv = 1.0 / C
        kb.v(lambda e: e.tensor_scalar(lnr[0:1, 2 * T:3 * T], lnr[0:1, 0:T],
                                       inv, None, op0=ALU.mult), deps=[g1])
        kb.v(lambda e: e.tensor_scalar(lnr[0:1, T:2 * T], lnr[0:1, T:2 * T],
                                       inv, None, op0=ALU.mult), deps=[g2])
        kb.v(lambda e: e.tensor_tensor(lnr[0:1, 0:T], lnr[0:1, 2 * T:3 * T],
                                       lnr[0:1, 2 * T:3 * T], op=ALU.mult))
        tvar = kb.v(lambda e: e.tensor_tensor(lnr[0:1, T:2 * T],
                                              lnr[0:1, T:2 * T],
                                              lnr[0:1, 0:T],
                                              op=ALU.subtract), inc=True)
        tln = kb.s(lambda e: e.activation(lnr[0:1, 0:T], lnr[0:1, T:2 * T],
                                          AF.Ln, bias=epcol[0:1, 1:2]),
                   deps=[tvar, tok_ep], inc=True)
        tr = kb.s(lambda e: e.activation(lnr[0:1, 0:T], lnr[0:1, 0:T],
                                         AF.Exp, scale=-0.5),
                  deps=[tln], inc=True)
        tmr = kb.v(lambda e: e.tensor_tensor(lnr[0:1, T:2 * T],
                                             lnr[0:1, 2 * T:3 * T],
                                             lnr[0:1, 0:T], op=ALU.mult),
                   deps=[tr], inc=True)
        bts = ln_bcast(lnr, brw, tr, tmr)
        xln_toks = []
        xst_free2 = [st_deps[-1], st_deps[-1]]
        for ci in range(NCH):
            sl = (ci % 2) * T
            lx = kb.dma("g", xst[:, sl:sl + T],
                        dx[s, ci * 128:(ci + 1) * 128, :],
                        deps=[xst_free2[ci % 2]] + bts)
            tk = ln_apply(xln, ci * T, xst, sl, brw, [lx] + bts)
            xst_free2[ci % 2] = tk
            xln_toks.append(tk)
        tap(f"xln{s}", xln[:, :], xln_toks)
        ar.free(f"lnr{s}", xln_toks[-1])
        ar.free(f"sq2{s}", xln_toks[-1])
        ar.free(f"xst{s}", xln_toks[-1])
        ar.free(f"brw{s}", xln_toks[-1])
        if _ATTN_CUT == 1:
            return xln_toks

        wst, d7 = ar.alloc(f"wst{s}", 2 * NCH * 128, F16)
        qkvb, d6 = ar.alloc(f"qkv{s}", 3 * NCH * T, F16)
        qb = qkvb[:, 0:NCH * T]
        kbu = qkvb[:, NCH * T:2 * NCH * T]
        vbu = qkvb[:, 2 * NCH * T:3 * NCH * T]
        qkv_ev = []
        wld = {}
        for J in range(NI):
            ws = J % 2
        wfree = {}
        for J in range(NI):
            ws = J % 2
            wld[J] = kb.dma("g",
                            wst[:, ws * 1024:(ws + 1) * 1024]
                            .rearrange("p (ci j) -> p ci j", j=128),
                            d_wattn.rearrange("(ci p) j -> p ci j", p=128)
                            [:, :, J * 128:(J + 1) * 128],
                            deps=d7 + d6 + [wfree.get(J - 2)])
            dci = J % NCH
            dstoff = (J // NCH) * NCH * T + dci * T
            for tg in range(4):
                bk = (J * 4 + tg) % 4
                tkmm = None
                for ci in range(NCH):
                    tkmm = kb.t(lambda e, J=J, ws=ws, tg=tg, ci=ci, bk=bk:
                                e.matmul(
                        bank(bk),
                        wst[:, ws * 1024 + ci * 128: ws * 1024 + (ci + 1) * 128],
                        xln[:, ci * T + tg * 512: ci * T + (tg + 1) * 512],
                        start=(ci == 0), stop=(ci == NCH - 1)),
                        deps=(kb.bdep(bk) + [wld[J], wfree.get(J - 2)]
                              + xln_toks) if ci == 0 else [],
                        inc=(ci == NCH - 1))
                ev = kb.s(lambda e, off=dstoff, tg=tg, bk=bk, J=J: e.activation(
                    qkvb[:, off + tg * 512: off + (tg + 1) * 512],
                    bank(bk), AF.Identity, bias=b_attn[:, J:J + 1]),
                    deps=[tkmm, tok_battn], inc=True)
                kb.bset(ev, bk)
                qkv_ev.append(ev)
            wfree[J] = qkv_ev[-1]
        ar.free(f"wst{s}", qkv_ev[-1])
        ar.free(f"xln{s}", qkv_ev[-1])
        if free_after_qkv is not None:
            for nm in free_after_qkv[0]:
                ar.free(nm, free_after_qkv[1])
        tap(f"qb{s}", qb, [qkv_ev[-1]])
        tap(f"kbu{s}", kbu, [qkv_ev[-1]])
        tap(f"vbu{s}", vbu, [qkv_ev[-1]])
        if _ATTN_CUT == 2:
            return [qkv_ev[-1]]

        ccst, d8 = ar.alloc(f"ccst{s}", NH * HS, F16)
        conv_ev, d9 = ar.alloc(f"cev{s}", 2 * 4 * NB, F16)
        sqn, d10 = ar.alloc(f"sqn{s}", 2 * T, F16)
        cdeps = d8 + d9 + d10
        kc_toks, vc_toks, kcn_tok = emit_compress(
            s, qb, kbu, vbu, ccst, conv_ev, sqn, qkv_ev, cdeps)
        if _ATTN_CUT == 3 or (_ATTN_CUT or 0) > 300:
            return [t for t in kc_toks + vc_toks + [kcn_tok]
                    if t is not None]

        tok_mn = emit_norms(s, qb, kbu, sqn, qkv_ev, kcn_tok)
        ar.free(f"ccst{s}", kc_toks[-1])
        ar.free(f"cev{s}", kc_toks[-1])
        ar.free(f"sqn{s}", tok_mn)
        if _ATTN_CUT == 4:
            return [tok_mn]

        locb, d11 = ar.alloc(f"loc{s}", NCH * T, F16)
        cmpb, d12 = ar.alloc(f"cmp{s}", NCH * T, F16)
        loc_toks, cmp_toks = emit_attn_core(
            s, qb, kbu, vbu, locb, cmpb, kc_toks, vc_toks, tok_mn, qkv_ev,
            d11 + d12)
        ar.free(f"qkv{s}", cmp_toks[-1])
        if _ATTN_CUT == 5:
            return loc_toks + cmp_toks

        lnmb, d13 = ar.alloc(f"lnm{s}", NCH * T, F16)
        lnrb, d14 = ar.alloc(f"lnr2{s}", 3 * T, F32)
        sq2b_, d15 = ar.alloc(f"sq2b{s}", 2 * T, F16)
        brwb, d16 = ar.alloc(f"brw2{s}", 2 * T, F16)
        ldeps = d13 + d14 + d15 + d16 + loc_toks + cmp_toks
        tr, tmr = ln_stats(locb, NCH, lnrb, sq2b_, ldeps)
        bts = ln_bcast(lnrb, brwb, tr, tmr)
        lnm_toks = []
        for ci in range(NCH):
            tt = ln_apply(lnmb, ci * T, locb, ci * T, brwb, ldeps + bts)
            tt2 = kb.v(lambda e, ci=ci: e.tensor_scalar(
                lnmb[:, ci * T:(ci + 1) * T], lnmb[:, ci * T:(ci + 1) * T],
                lnl_w[:, ci:ci + 1], lnl_b[:, ci:ci + 1],
                op0=ALU.mult, op1=ALU.add), deps=[tt, tok_lnlw, tok_lnlb],
                inc=True)
            lnm_toks.append(tt2)
        tap(f"lnm{s}", lnmb[:, :], lnm_toks)
        if _ATTN_CUT == 6:
            return lnm_toks

        wg1, dwg1 = ar.alloc(f"wg1{s}", 8 * C, F16)
        tok_wg1 = kb.dma("g",
                         wg1[:, :].rearrange("p (f j) -> p f j", j=C),
                         d_gw1.rearrange("(f p) j -> p f j", p=128)[:, 0:8, :],
                         deps=dwg1)
        tok_g = emit_gate(s, lnmb, cmpb, lnrb, lnm_toks, cmp_toks, wg1,
                          tok_wg1, d_gw1)
        ar.free(f"wg1{s}", tok_g)
        if _ATTN_CUT == 7:
            return [tok_g]

        comb_toks = []
        for ci in range(NCH):
            t1 = kb.v(lambda e, ci=ci: e.tensor_scalar(
                cmpb[:, ci * T:(ci + 1) * T], cmpb[:, ci * T:(ci + 1) * T],
                gbc[:, 1:2], None, op0=ALU.mult),
                deps=[tok_g], inc=True)
            t2 = kb.v(lambda e, ci=ci: e.scalar_tensor_tensor(
                locb[:, ci * T:(ci + 1) * T], lnmb[:, ci * T:(ci + 1) * T],
                gbc[:, 0:1], cmpb[:, ci * T:(ci + 1) * T],
                op0=ALU.mult, op1=ALU.add), deps=[t1], inc=True)
            comb_toks.append(t2)
        ar.free(f"cmp{s}", comb_toks[-1])
        tap(f"comb{s}", locb[:, :], comb_toks)
        if _ATTN_CUT == 8:
            return comb_toks

        wstp, d17 = ar.alloc(f"wstp{s}", 2 * NCH * 128, F16)
        proj_ev = []
        pwld = {}
        pwfree = {}
        for J in range(NCH):
            ws = J % 2
            pwld[J] = kb.dma("g",
                             wstp[:, ws * 1024:(ws + 1) * 1024]
                             .rearrange("p (ci j) -> p ci j", j=128),
                             d_wproj.rearrange("(ci p) j -> p ci j", p=128)
                             [:, :, J * 128:(J + 1) * 128],
                             deps=d17 + [pwfree.get(J - 2)])
            for tg in range(4):
                bk = (J * 4 + tg) % 4
                tkmm = None
                for ci in range(NCH):
                    tkmm = kb.t(lambda e, ws=ws, J=J, tg=tg, ci=ci, bk=bk:
                                e.matmul(
                        bank(bk),
                        wstp[:, ws * 1024 + ci * 128: ws * 1024 + (ci + 1) * 128],
                        locb[:, ci * T + tg * 512: ci * T + (tg + 1) * 512],
                        start=(ci == 0), stop=(ci == NCH - 1)),
                        deps=(kb.bdep(bk) + [pwld[J]] + comb_toks)
                        if ci == 0 else [], inc=(ci == NCH - 1))
                ev = kb.s(lambda e, J=J, tg=tg, bk=bk: e.activation(
                    lnmb[:, J * T + tg * 512: J * T + (tg + 1) * 512],
                    bank(bk), AF.Identity, bias=b_proj[:, J:J + 1]),
                    deps=[tkmm, tok_bproj] + lnm_toks, inc=True)
                kb.bset(ev, bk)
                proj_ev.append(ev)
            pwfree[J] = proj_ev[-1]
        ar.free(f"wstp{s}", proj_ev[-1])
        tap(f"attn{s}", lnmb[:, :], [proj_ev[-1]])
        if _ATTN_CUT == 9:
            return [proj_ev[-1]]

        x2c, d18 = ar.alloc(f"x2c{s}", 2 * T, F32)
        tr, tmr = ln_stats(lnmb, NCH, lnrb, sq2b_, proj_ev + d18,
                           lnr_deps=[tok_g])
        bts = ln_bcast(lnrb, brwb, tr, tmr)
        x2_toks = []
        ln2_mm = None
        ln2_mm6 = None
        prev_ld = None
        prev_cst = [None]
        for ci in range(NCH):
            u = ln_apply(locb, ci * T, lnmb, ci * T, brwb,
                         proj_ev + bts + comb_toks)
            xr = kb.dma("y", x2c[:, 0:T], dx[s, ci * 128:(ci + 1) * 128, :],
                        deps=[prev_ld])
            ad = kb.v(lambda e, ci=ci: e.tensor_tensor(
                x2c[:, T:2 * T], locb[:, ci * T:(ci + 1) * T],
                x2c[:, 0:T], op=ALU.add),
                deps=[u, xr, prev_cst[0]], inc=True)
            st = kb.dma("y", d_x2[s, ci * 128:(ci + 1) * 128, :],
                        x2c[:, T:2 * T], deps=[ad])
            prev_ld = st
            cst = kb.s(lambda e: e.activation(
                sq2b_[:, 0:T], x2c[:, T:2 * T], AF.Identity),
                deps=[ad, ln2_mm6, ln2_mm], inc=True)
            prev_cst[0] = cst
            sqq = kb.s(lambda e: e.activation(
                sq2b_[:, T:2 * T], sq2b_[:, 0:T], AF.Square),
                deps=[cst, ln2_mm], inc=True)
            for g in range(4):
                dep0 = (kb.bdep(6, 7)) if ci == 0 else []
                ln2_mm6 = kb.t(lambda e, g=g, ci=ci: e.matmul(
                    bank(6, r0=32 * g, r1=32 * g + 1), ones1[:, 0:1],
                    sq2b_[:, 512 * g: 512 * (g + 1)],
                    start=(ci == 0), stop=(ci == NCH - 1),
                    tile_position=(0, 32 * g), skip_group_check=True),
                    deps=dep0 + [cst], inc=(g == 3))
                ln2_mm = kb.t(lambda e, g=g, ci=ci: e.matmul(
                    bank(7, r0=32 * g, r1=32 * g + 1), ones1[:, 0:1],
                    sq2b_[:, T + 512 * g: T + 512 * (g + 1)],
                    start=(ci == 0), stop=(ci == NCH - 1),
                    tile_position=(0, 32 * g), skip_group_check=True),
                    deps=dep0 + [sqq], inc=(g == 3))
            x2_toks.append(st)
        for g in range(4):
            ev1 = kb.v(lambda e, g=g: e.tensor_copy(
                stat_ev[32 * g:32 * g + 1, 0:512],
                bank(6, r0=32 * g, r1=32 * g + 1)),
                deps=([ln2_mm] + kb.gd("stat_ev")) if g == 0 else [], inc=True)
        for g in range(4):
            ev2 = kb.v(lambda e, g=g: e.tensor_copy(
                stat_ev[32 * g:32 * g + 1, 512:1024],
                bank(7, r0=32 * g, r1=32 * g + 1)),
                deps=[ln2_mm] if g == 0 else [], inc=True)
        kb.bset(ev2, 6, 7)
        g1 = kb.dma("y", lnrb[0:1, 0:T], stat_ev[0:97:32, 0:512], deps=[ev1])
        g2 = kb.dma("y", lnrb[0:1, T:2 * T], stat_ev[0:97:32, 512:1024],
                    deps=[ev2])
        kb.gset("stat_ev", g2)
        inv = 1.0 / C
        kb.v(lambda e: e.tensor_scalar(lnrb[0:1, 2 * T:3 * T], lnrb[0:1, 0:T],
                                       inv, None, op0=ALU.mult), deps=[g1])
        kb.v(lambda e: e.tensor_scalar(lnrb[0:1, T:2 * T], lnrb[0:1, T:2 * T],
                                       inv, None, op0=ALU.mult), deps=[g2])
        kb.v(lambda e: e.tensor_tensor(lnrb[0:1, 0:T], lnrb[0:1, 2 * T:3 * T],
                                       lnrb[0:1, 2 * T:3 * T], op=ALU.mult))
        tvar = kb.v(lambda e: e.tensor_tensor(lnrb[0:1, T:2 * T],
                                              lnrb[0:1, T:2 * T],
                                              lnrb[0:1, 0:T],
                                              op=ALU.subtract), inc=True)
        tln = kb.s(lambda e: e.activation(lnrb[0:1, 0:T], lnrb[0:1, T:2 * T],
                                          AF.Ln, bias=epcol[0:1, 1:2]),
                   deps=[tvar, tok_ep], inc=True)
        trr = kb.s(lambda e: e.activation(lnrb[0:1, 0:T], lnrb[0:1, 0:T],
                                          AF.Exp, scale=-0.5),
                   deps=[tln], inc=True)
        sv1 = kb.dma("y", d_ln2[0:1, s * 2 * T: s * 2 * T + T],
                     lnrb[0:1, 2 * T:3 * T], deps=[trr])
        sv2 = kb.dma("y", d_ln2[0:1, s * 2 * T + T: s * 2 * T + 2 * T],
                     lnrb[0:1, 0:T], deps=[trr, sv1])
        end = sv2
        for nm in (f"lnm{s}", f"lnr2{s}", f"sq2b{s}", f"brw2{s}", f"x2c{s}",
                   f"loc{s}"):
            ar.free(nm, end)
        return x2_toks + [sv1, sv2]

    def emit_compress(s, qb, kbu, vbu, ccst, conv_ev, sqn, qkv_ev, cdeps):
        kc_toks, vc_toks, kcn = [], [], [None]
        for (zbuf, is_k) in ((kbu, True), (vbu, False)):
            mm_done = []
            for tap_ in range(KTAP):
                for halfc in range(2):
                    for par in range(2):
                        bk = halfc + 2 * par
                        deps = (qkv_ev + [tok_wconv] + kb.bdep(bk) + cdeps) \
                            if tap_ == 0 else []
                        tk = kb.t(lambda e, tap_=tap_, halfc=halfc, par=par,
                                  bk=bk, z=zbuf: e.matmul(
                            bank(bk, r0=0, r1=64, c1=508),
                            w_conv[par * 64:(par + 1) * 64,
                                   tap_ * 64:(tap_ + 1) * 64],
                            z[par * 64:(par + 1) * 64, :]
                            .rearrange("p (ci t) -> p ci t", t=T)
                            [:, 4 * halfc:4 * halfc + 4,
                             tap_:tap_ + CSTR * 126 + 1:CSTR],
                            start=(tap_ == 0), stop=(tap_ == KTAP - 1),
                            tile_position=(par * 64, 0)),
                            deps=deps, inc=(tap_ == KTAP - 1))
                        if tap_ == KTAP - 1:
                            mm_done.append(tk)
            ev_toks = []
            for halfc in range(2):
                for par in range(2):
                    bk = halfc + 2 * par
                    ev = kb.s(lambda e, halfc=halfc, par=par, bk=bk:
                              e.activation(
                        conv_ev[par * 64:(par + 1) * 64,
                                halfc * 508:(halfc + 1) * 508],
                        bank(bk, r0=0, r1=64, c1=508), AF.Identity,
                        bias=b_conv[par * 64:(par + 1) * 64, :]),
                        deps=mm_done + [tok_bconv], inc=True)
                    kb.bset(ev, bk)
                    ev_toks.append(ev)
            if _ATTN_CUT == 301:
                return ev_toks, [], None
            if _ATTN_CUT == 305 and not is_k:
                return kc_toks, [ev_toks[-1]], kcn[0]
            tk = None
            for par in range(2):
                hs = [h for h in range(NH) if h % 2 == par]
                for j, h in enumerate(hs):
                    halfc, pos = divmod(h // 2, 4)
                    first = (par == 0 and j == 0)
                    tk = kb.t(lambda e, h=h, par=par, halfc=halfc, pos=pos:
                              e.matmul(
                        PS16[0:NB, 4 * 1024 + 64 * h: 4 * 1024 + 64 * (h + 1)],
                        conv_ev[par * 64:(par + 1) * 64,
                                halfc * 508 + pos * 127:
                                halfc * 508 + (pos + 1) * 127],
                        idf[par * 64:(par + 1) * 64, par * 64:par * 64 + 64],
                        start=True, stop=True, is_transpose=True,
                        skip_group_check=True),
                        deps=(ev_toks + [tok_id] + kb.bdep(4)) if first
                        else ([tk] if j == 0 else []),
                        inc=(j == len(hs) - 1))
            if _ATTN_CUT == 302:
                ev302 = kb.v(lambda e: e.tensor_copy(
                    ccst[0:NB, :], PS16[0:NB, 4 * 1024: 4 * 1024 + 1024]),
                    deps=[tk], inc=True)
                kb.bset(ev302, 4)
                return [ev302], [], None
            t_sz = kb.v(lambda e: e.tensor_reduce(
                red16[0:NB, 0:NH],
                PS16[0:NB, 4 * 1024: 4 * 1024 + 1024]
                .rearrange("p (h c) -> p h c", c=64), axis=AX, op=ALU.add),
                deps=[tk], inc=True)
            if _ATTN_CUT == 3031:
                kb.bset(t_sz, 4)
                return [t_sz], [], None
            t_sq = kb.s(lambda e: e.activation(
                ccst[0:NB, :], PS16[0:NB, 4 * 1024: 4 * 1024 + 1024],
                AF.Square), deps=[tk, t_sz], inc=True)
            if _ATTN_CUT == 3032:
                kb.bset(t_sq, 4)
                return [t_sz, t_sq], [], None
            t_s2 = kb.v(lambda e: e.tensor_reduce(
                red16[0:NB, NH:2 * NH],
                ccst[0:NB, :].rearrange("p (h c) -> p h c", c=64),
                axis=AX, op=ALU.add), deps=[t_sq], inc=True)
            kb.v(lambda e: e.tensor_scalar(red16[0:NB, 0:NH],
                                           red16[0:NB, 0:NH], 1.0 / 64, None,
                                           op0=ALU.mult), deps=[t_sz])
            kb.v(lambda e: e.tensor_scalar(red16[0:NB, NH:2 * NH],
                                           red16[0:NB, NH:2 * NH], 1.0 / 64,
                                           None, op0=ALU.mult), deps=[t_s2])
            kb.v(lambda e: e.tensor_tensor(red16[0:NB, 2 * NH:3 * NH],
                                           red16[0:NB, 0:NH],
                                           red16[0:NB, 0:NH], op=ALU.mult))
            kb.v(lambda e: e.tensor_tensor(red16[0:NB, NH:2 * NH],
                                           red16[0:NB, NH:2 * NH],
                                           red16[0:NB, 2 * NH:3 * NH],
                                           op=ALU.subtract))
            kb.v(lambda e: e.tensor_scalar(red16[0:NB, NH:2 * NH],
                                           red16[0:NB, NH:2 * NH], EPS, None,
                                           op0=ALU.add))
            t_iv = kb.v(lambda e: e.reciprocal(red16[0:NB, NH:2 * NH],
                                               red16[0:NB, NH:2 * NH]),
                        inc=True)
            t_r = kb.s(lambda e: e.activation(red16[0:NB, NH:2 * NH],
                                              red16[0:NB, NH:2 * NH],
                                              AF.Sqrt), deps=[t_iv], inc=True)
            t_nmr = kb.v(lambda e: e.tensor_tensor(
                red16[0:NB, 2 * NH:3 * NH], red16[0:NB, 0:NH],
                red16[0:NB, NH:2 * NH], op=ALU.mult), deps=[t_r], inc=True)
            t_nm2 = kb.v(lambda e: e.tensor_scalar(
                red16[0:NB, 2 * NH:3 * NH], red16[0:NB, 2 * NH:3 * NH],
                -1.0, None, op0=ALU.mult), deps=[t_nmr], inc=True)
            if _ATTN_CUT == 3033:
                kb.bset(t_nm2, 4)
                return [t_nm2, t_sz], [], None
            nrm = []
            for h in range(NH):
                nrm.append(kb.v(lambda e, h=h: e.tensor_scalar(
                    ccst[0:NB, h * 64:(h + 1) * 64],
                    PS16[0:NB, 4 * 1024 + 64 * h: 4 * 1024 + 64 * (h + 1)],
                    red16[0:NB, NH + h:NH + h + 1],
                    red16[0:NB, 2 * NH + h:2 * NH + h + 1],
                    op0=ALU.mult, op1=ALU.add), deps=[t_nm2], inc=True))
            kb.bset(nrm[-1], 4)
            if _ATTN_CUT == 3034:
                return [nrm[-1], t_sz], [], None
            t_gm = kb.v(lambda e: e.tensor_tensor(
                ccst[0:NB, :], ccst[0:NB, :], crep_w[0:NB, :], op=ALU.mult),
                deps=nrm + [tok_crep], inc=True)
            t_gb = kb.v(lambda e: e.tensor_tensor(
                ccst[0:NB, :], ccst[0:NB, :], crep_b[0:NB, :], op=ALU.add),
                deps=[t_gm], inc=True)
            if _ATTN_CUT == 303:
                kb.bset(t_gb, 4)
                return [t_gb], [], None
            if _ATTN_CUT == 306 and not is_k:
                kb.bset(t_gb, 4)
                return kc_toks, [t_gb], kcn[0]
            if is_k:
                t_act = kb.s(lambda e: e.activation(
                    ccst[0:NB, :], ccst[0:NB, :], AF.Gelu), deps=[t_gb],
                    inc=True)
                t_k2 = kb.s(lambda e: e.activation(
                    sqn[0:NB, 0:NH * HS], ccst[0:NB, :], AF.Square),
                    deps=[t_act], inc=True)
                kcn[0] = kb.v(lambda e: e.tensor_reduce(
                    red16[0:NB, 3 * NH:4 * NH],
                    sqn[0:NB, 0:NH * HS].rearrange("p (h c) -> p h c", c=64),
                    axis=AX, op=ALU.add), deps=[t_k2], inc=True)
                for h in range(NH):
                    par, ci = h % 2, h // 2
                    tk2 = kb.t(lambda e, h=h: e.matmul(
                        PS16[0:64, 5 * 1024: 5 * 1024 + NB],
                        ccst[0:NB, h * 64:(h + 1) * 64], idf[0:NB, 0:NB],
                        start=True, stop=True, is_transpose=True,
                        skip_group_check=True),
                        deps=[t_act, tok_id] + kb.bdep(5), inc=True)
                    ev3 = kb.v(lambda e, par=par, ci=ci: e.tensor_copy(
                        kcomp[par * 64:(par + 1) * 64,
                              ci * NB:(ci + 1) * NB],
                        PS16[0:64, 5 * 1024: 5 * 1024 + NB]), deps=[tk2],
                        inc=True)
                    kb.bset(ev3, 5)
                    kc_toks.append(ev3)
                tap(f"kcomp{s}", kcomp[:, :], [kc_toks[-1]])
                if _ATTN_CUT == 304:
                    return kc_toks, [], kcn[0]
            else:
                t_act = kb.s(lambda e: e.activation(
                    vcompT[0:NB, :].rearrange("p (h c) -> p h c", c=65)
                    [:, :, 0:64],
                    ccst[0:NB, :].rearrange("p (h c) -> p h c", c=64),
                    AF.Gelu), deps=[t_gb, tok_vc0], inc=True)
                vc_toks = [t_act]
                kb.bset(t_act, 4)
                tap(f"vcompT{s}", vcompT[:, :], vc_toks)
        return kc_toks, vc_toks, kcn[0]

    def emit_norms(s, qb, kbu, sqn, qkv_ev, kcn_tok):
        sq_free = [None, None]
        last_rd = None
        for (zbuf, base_off) in ((qb, RQ), (kbu, RK)):
            for ci in range(NCH):
                sl = (ci % 2) * T
                tsq = kb.s(lambda e, z=zbuf, ci=ci, sl=sl: e.activation(
                    sqn[:, sl:sl + T], z[:, ci * T:(ci + 1) * T], AF.Square),
                    deps=qkv_ev + [sq_free[ci % 2]], inc=True)
                for par in range(2):
                    h = ci * 2 + par
                    mm_last = None
                    for tg in range(4):
                        mm_last = kb.t(lambda e, par=par, sl=sl, tg=tg:
                                       e.matmul(
                            bank(4 + tg, r0=0, r1=1),
                            ones1[par * 64:(par + 1) * 64, 0:1],
                            sqn[par * 64:(par + 1) * 64,
                                sl + tg * 512: sl + (tg + 1) * 512],
                            start=True, stop=True,
                            tile_position=(par * 64, 0)),
                            deps=[tsq] + kb.bdep(4 + tg), inc=(tg == 3))
                    rd = kb.v(lambda e, off=base_off, h=h: e.tensor_reduce(
                        nrow[0:1, off + h: off + h + 1],
                        PS[0:1, 4 * 512: 8 * 512], axis=AX, op=ALU.max),
                        deps=[mm_last], inc=True)
                    kb.bset(rd, 4, 5, 6, 7)
                    last_rd = rd
                sq_free[ci % 2] = last_rd
        kc_rd = None
        for h in range(NH):
            gdt = kb.dma("y", stat_ev[0:1, 0:NB],
                         red16[0:NB, 3 * NH + h:3 * NH + h + 1],
                         deps=[kcn_tok, kc_rd] + kb.gd("stat_ev"))
            kc_rd = kb.v(lambda e, h=h: e.tensor_reduce(
                nrow[0:1, RKC + h:RKC + h + 1], stat_ev[0:1, 0:NB],
                axis=AX, op=ALU.max), deps=[gdt], inc=True)
        kb.gset("stat_ev", kc_rd)
        t_sqr = kb.s(lambda e: e.activation(nrow[0:1, RQ:RQ + 3 * NH],
                                            nrow[0:1, RQ:RQ + 3 * NH],
                                            AF.Sqrt),
                     deps=[kc_rd, last_rd], inc=True)
        kb.v(lambda e: e.tensor_tensor(nrow[0:1, RM:RM + NH],
                                       nrow[0:1, RQ:RQ + NH],
                                       nrow[0:1, RK:RK + NH], op=ALU.mult),
             deps=[t_sqr])
        kb.v(lambda e: e.tensor_tensor(nrow[0:1, RM + NH:RM + 2 * NH],
                                       nrow[0:1, RQ:RQ + NH],
                                       nrow[0:1, RKC:RKC + NH], op=ALU.mult),
             deps=[t_sqr])
        t_M = kb.v(lambda e: e.tensor_scalar(nrow[0:1, RM:RM + 2 * NH],
                                             nrow[0:1, RM:RM + 2 * NH],
                                             -0.125, None, op0=ALU.mult),
                   inc=True)
        tok_mn = bc_row32(nrow, RM, 32, mn_bc, 0, [t_M], bki=0)
        tap(f"mrow{s}", nrow[0:1, 0:5 * NH], [tok_mn])
        return tok_mn

    def emit_attn_core(s, qb, kbu, vbu, locb, cmpb, kc_toks, vc_toks,
                       tok_mn, qkv_ev, adeps):
        loc_toks, cmp_toks = [], []
        for branch in ("local", "comp"):
            dst = locb if branch == "local" else cmpb
            klen = 128 if branch == "local" else NB
            for ch in range(NCH):
                pair_toks = []
                vt_ev = []
                if branch == "local":
                    for par in range(2):
                        tk = None
                        for kt in range(16):
                            tk = kb.t(lambda e, par=par, kt=kt, ch=ch:
                                      e.matmul(
                                PS16[0:128, 4 * 1024 + kt * 64:
                                     4 * 1024 + (kt + 1) * 64],
                                vbu[par * 64:(par + 1) * 64,
                                    ch * T + kt * 128:
                                    ch * T + (kt + 1) * 128],
                                idf[par * 64:(par + 1) * 64,
                                    par * 64:par * 64 + 64],
                                start=True, stop=True, is_transpose=True,
                                skip_group_check=True),
                                deps=(qkv_ev + [tok_id] + kb.bdep(4))
                                if kt == 0 else [], inc=(kt == 15))
                        ev = kb.v(lambda e, par=par: e.tensor_copy(
                            vt[:, par * 1040:(par + 1) * 1040]
                            .rearrange("p (k c) -> p k c", c=65)[:, :, 0:64],
                            PS16[0:128, 4 * 1024: 4 * 1024 + 1024]
                            .rearrange("p (k c) -> p k c", c=64)),
                            deps=[tk, tok_vt0], inc=True)
                        kb.bset(ev, 4)
                        vt_ev.append(ev)
                for half in range(2):
                    q0 = half * 1024
                    ktmax = (8 * half + 8) if branch == "local" else 1
                    units = []
                    for kt in range(ktmax):
                        for qc in range(2):
                            cs = q0 + qc * 512
                            lo = max(kt * 128, cs) if branch == "local" else cs
                            if lo >= cs + 512:
                                continue
                            units.append((kt, qc, lo, cs + 512 - lo))
                    firstu, lastu = {}, {}
                    for i, (kt, qc, lo, w) in enumerate(units):
                        lastu[qc] = i
                        if qc not in firstu:
                            firstu[qc] = i
                    exp_toks, pv_toks = {}, {}
                    for i, (kt, qc, lo, w) in enumerate(units):
                        slot = i % 2
                        for par in range(2):
                            h = ch * 2 + par
                            sbk = slot + 2 * par
                            deps_s = kb.bdep(sbk) + [tok_mn] + qkv_ev + adeps
                            if branch == "comp":
                                deps_s = deps_s + kc_toks
                            if branch == "local":
                                stat = kbu[par * 64:(par + 1) * 64,
                                           ch * T + kt * 128:
                                           ch * T + kt * 128 + 128]
                            else:
                                stat = kcomp[par * 64:(par + 1) * 64,
                                             ch * NB:(ch + 1) * NB]
                            mm = kb.t(lambda e, stat=stat, par=par, ch=ch,
                                      lo=lo, w=w, sbk=sbk, klen=klen:
                                      e.matmul(
                                bank(sbk, r0=0, r1=klen, c1=w), stat,
                                qb[par * 64:(par + 1) * 64,
                                   ch * T + lo: ch * T + lo + w],
                                start=True, stop=True,
                                tile_position=(par * 64, 0)),
                                deps=deps_s + [exp_toks.get((i - 2, par))],
                                inc=True)
                            mcol = h if branch == "local" else NH + h
                            pslot = slot + 2 * par
                            ex = kb.s(lambda e, pslot=pslot, w=w, sbk=sbk,
                                      mcol=mcol, klen=klen: e.activation(
                                pt[0:klen, pslot * 512: pslot * 512 + w],
                                bank(sbk, r0=0, r1=klen, c1=w), AF.Exp,
                                bias=mn_bc[0:klen, mcol:mcol + 1],
                                scale=0.125),
                                deps=[mm, pv_toks.get((i - 2, par))],
                                inc=True)
                            msk = None
                            if branch == "local" and lo == kt * 128:
                                msk = kb.g(lambda e, pslot=pslot:
                                           e.affine_select(
                                    pt[0:128, pslot * 512: pslot * 512 + 128],
                                    pt[0:128, pslot * 512: pslot * 512 + 128],
                                    pattern=[[1, 128]], compare_op=ALU.is_ge,
                                    fill=0.0, base=0, channel_multiplier=-1),
                                    deps=[ex], inc=True)
                            if branch == "comp" and half == 0 and qc == 0:
                                msk = kb.g(lambda e, pslot=pslot:
                                           e.affine_select(
                                    pt[0:NB, pslot * 512: pslot * 512 + NB],
                                    pt[0:NB, pslot * 512: pslot * 512 + NB],
                                    pattern=[[1, NB]], compare_op=ALU.is_ge,
                                    fill=0.0, base=0, channel_multiplier=-1),
                                    deps=[ex], inc=True)
                            exp_toks[(i, par)] = msk or ex
                            ob = 4 + 2 * par
                            col = lo - q0
                            if branch == "local":
                                pvl = vt[:, par * 1040 + kt * 65:
                                         par * 1040 + (kt + 1) * 65]
                            else:
                                pvl = vcompT[0:NB, h * 65:(h + 1) * 65]
                            dep_pv = [exp_toks[(i, par)]]
                            if i == firstu[qc]:
                                dep_pv += kb.bdep(ob + qc)
                                dep_pv += vt_ev if branch == "local" \
                                    else vc_toks
                            pv = kb.t(lambda e, pvl=pvl, pslot=pslot, col=col,
                                      w=w, ob=ob, klen=klen,
                                      first=(i == firstu[qc]),
                                      last=(i == lastu[qc]): e.matmul(
                                PS[0:65, ob * 512 + col: ob * 512 + col + w],
                                pvl,
                                pt[0:klen, pslot * 512: pslot * 512 + w],
                                start=first, stop=last), deps=dep_pv,
                                inc=True)
                            pv_toks[(i, par)] = pv
                            if i == lastu[qc]:
                                pv_toks[("last", par, qc)] = pv
                    nun = len(units)
                    for par in range(2):
                        for back in (1, 2):
                            if nun - back >= 0 and (nun - back, par) in exp_toks:
                                kb.bset(exp_toks[(nun - back, par)],
                                        (nun - back) % 2 + 2 * par)
                    for par in range(2):
                        ob = 4 + 2 * par
                        dps = [pv_toks[("last", par, 0)],
                               pv_toks[("last", par, 1)]]
                        t1 = kb.s(lambda e, ob=ob: e.activation(
                            stat_ev[64:65, 0:1024],
                            PS[64:65, ob * 512: ob * 512 + 1024],
                            AF.Ln, bias=epcol[64:65, 0:1]),
                            deps=dps + [tok_ep] + kb.gd("div")
                            + kb.gd("stat_ev"), inc=True)
                        t1b = kb.s(lambda e: e.activation(
                            stat_ev[64:65, 0:1024], stat_ev[64:65, 0:1024],
                            AF.Exp, scale=-0.5), deps=[t1], inc=True)
                        t2 = kb.s(lambda e: e.activation(
                            div_st[64:65, 0:1024], stat_ev[64:65, 0:1024],
                            AF.Square), deps=[t1b, tok_dst], inc=True)
                        kb.gset("stat_ev", t2)
                        wr_toks = []
                        for qc in range(2):
                            bb = qc
                            mmb = kb.t(lambda e, qc=qc, bb=bb: e.matmul(
                                bank(bb), bc_w[64:96, :],
                                div_st[64:96, qc * 512:(qc + 1) * 512],
                                start=True, stop=True),
                                deps=[t2] + kb.bdep(bb), inc=True)
                            evb = kb.v(lambda e, qc=qc, bb=bb: e.tensor_copy(
                                dvb[0:64, qc * 512:(qc + 1) * 512],
                                bank(bb, r0=0, r1=64)),
                                deps=[mmb] + kb.gd("dvb"), inc=True)
                            kb.bset(evb, bb)
                            wdst = dst[0:64, ch * T + q0 + qc * 512:
                                       ch * T + q0 + (qc + 1) * 512] \
                                if par == 0 else \
                                bstg[0:64, qc * 512:(qc + 1) * 512]
                            wr = kb.v(lambda e, wdst=wdst, qc=qc, ob=ob:
                                      e.tensor_tensor(
                                wdst,
                                PS[0:64, ob * 512 + qc * 512:
                                   ob * 512 + (qc + 1) * 512],
                                dvb[0:64, qc * 512:(qc + 1) * 512],
                                op=ALU.mult),
                                deps=[evb] + (kb.gd("bstg")
                                              if par == 1 else []), inc=True)
                            wr_toks.append(wr)
                        kb.gset("div", wr_toks[-1])
                        kb.gset("dvb", wr_toks[-1])
                        if par == 1:
                            mv1 = kb.dma("g", d_bmov, bstg[0:64, :],
                                         deps=wr_toks)
                            mv2 = kb.dma("g",
                                         dst[64:128,
                                             ch * T + q0: ch * T + q0 + 1024],
                                         d_bmov, deps=[mv1])
                            kb.gset("bstg", mv2)
                            kb.bset(mv2, ob, ob + 1)
                            pair_toks.append(mv2)
                        else:
                            kb.bset(wr_toks[-1], ob, ob + 1)
                            pair_toks.append(wr_toks[-1])
                if branch == "local":
                    loc_toks.extend(pair_toks)
                else:
                    cmp_toks.extend(pair_toks)
        tap(f"loc{s}", locb[:, :], loc_toks)
        tap(f"cmp{s}", cmpb[:, :], cmp_toks)
        return loc_toks, cmp_toks

    def emit_gate(s, lnmb, cmpb, lnr, lnm_toks, cmp_toks, wg1, tok_wg1,
                  d_gw1):
        lb1 = kb.dma("y", lnr[0:1, 2 * T:2 * T + C], d_gb1,
                     deps=[lnm_toks[-1]])
        lw = kb.dma("y", lnr[0:1, T:T + C], d_glnw, deps=[lnm_toks[-1]])
        lb = kb.dma("y", lnr[0:1, T + C:T + 2 * C], d_glnb,
                     deps=[lnm_toks[-1]])
        mm_h = None
        tok_wg1b = None
        for f in range(16):
            srcb = lnmb if f < NCH else cmpb
            ci = f % NCH
            if f == 8:
                tok_wg1b = kb.dma("g",
                                  wg1[:, :].rearrange("p (f j) -> p f j", j=C),
                                  d_gw1.rearrange("(f p) j -> p f j", p=128)
                                  [:, 8:16, :], deps=[mm_h])
            for hf in range(2):
                mm_h = kb.t(lambda e, f=f, ci=ci, srcb=srcb, hf=hf: e.matmul(
                    bank(2 + hf, r0=0, r1=1),
                    srcb[:, ci * T + T - 1: ci * T + T],
                    wg1[:, (f % 8) * C + hf * 512:
                        (f % 8) * C + (hf + 1) * 512],
                    start=(f == 0), stop=(f == 15)),
                    deps=(lnm_toks + cmp_toks + [tok_wg1] + kb.bdep(2, 3))
                    if f == 0 else ([tok_wg1b] if f == 8 else []),
                    inc=((f == 15 or f == 7) and hf == 1))
        t_h = kb.v(lambda e: e.tensor_tensor(
            lnr[0:1, 0:512], bank(2, r0=0, r1=1), lnr[0:1, 2 * T:2 * T + 512],
            op=ALU.add), deps=[mm_h, lb1], inc=True)
        t_h2 = kb.v(lambda e: e.tensor_tensor(
            lnr[0:1, 512:1024], bank(3, r0=0, r1=1),
            lnr[0:1, 2 * T + 512:2 * T + 1024], op=ALU.add),
            deps=[mm_h, lb1], inc=True)
        kb.bset(t_h2, 2, 3)
        t_s = kb.v(lambda e: e.tensor_reduce(
            nrow[0:1, RG:RG + 1], lnr[0:1, 0:C], axis=AX, op=ALU.add),
            deps=[t_h, t_h2], inc=True)
        t_q = kb.s(lambda e: e.activation(
            lnr[0:1, 2 * T:2 * T + C], lnr[0:1, 0:C], AF.Square,
            accum_out=nrow[0:1, RG + 1:RG + 2]), deps=[t_h, t_h2, lb1],
            inc=True)
        kb.v(lambda e: e.tensor_scalar(nrow[0:1, RG:RG + 2],
                                       nrow[0:1, RG:RG + 2], 1.0 / C, None,
                                       op0=ALU.mult), deps=[t_s, t_q])
        kb.v(lambda e: e.tensor_tensor(nrow[0:1, RG + 2:RG + 3],
                                       nrow[0:1, RG:RG + 1],
                                       nrow[0:1, RG:RG + 1], op=ALU.mult))
        kb.v(lambda e: e.tensor_tensor(nrow[0:1, RG + 1:RG + 2],
                                       nrow[0:1, RG + 1:RG + 2],
                                       nrow[0:1, RG + 2:RG + 3],
                                       op=ALU.subtract))
        kb.v(lambda e: e.tensor_scalar(nrow[0:1, RG + 1:RG + 2],
                                       nrow[0:1, RG + 1:RG + 2], EPS, None,
                                       op0=ALU.add))
        ti = kb.v(lambda e: e.reciprocal(nrow[0:1, RG + 1:RG + 2],
                                         nrow[0:1, RG + 1:RG + 2]), inc=True)
        trq = kb.s(lambda e: e.activation(nrow[0:1, RG + 1:RG + 2],
                                          nrow[0:1, RG + 1:RG + 2], AF.Sqrt),
                   deps=[ti], inc=True)
        tmr = kb.v(lambda e: e.tensor_tensor(nrow[0:1, RG + 2:RG + 3],
                                             nrow[0:1, RG:RG + 1],
                                             nrow[0:1, RG + 1:RG + 2],
                                             op=ALU.mult), deps=[trq],
                   inc=True)
        tnm = kb.v(lambda e: e.tensor_scalar(nrow[0:1, RG + 2:RG + 3],
                                             nrow[0:1, RG + 2:RG + 3], -1.0,
                                             None, op0=ALU.mult), deps=[tmr],
                   inc=True)
        t_n = kb.v(lambda e: e.tensor_scalar(
            lnr[0:1, 0:C], lnr[0:1, 0:C],
            nrow[0:1, RG + 1:RG + 2], nrow[0:1, RG + 2:RG + 3],
            op0=ALU.mult, op1=ALU.add), deps=[tnm], inc=True)
        t_g1 = kb.v(lambda e: e.tensor_tensor(
            lnr[0:1, 0:C], lnr[0:1, 0:C], lnr[0:1, T:T + C], op=ALU.mult),
            deps=[t_n, lw], inc=True)
        t_g2 = kb.v(lambda e: e.tensor_tensor(
            lnr[0:1, 0:C], lnr[0:1, 0:C], lnr[0:1, T + C:T + 2 * C],
            op=ALU.add), deps=[t_g1, lb], inc=True)
        t_rl = kb.v(lambda e: e.tensor_scalar(
            lnr[0:1, 0:C], lnr[0:1, 0:C], 0.0, None, op0=ALU.max),
            deps=[t_g2], inc=True)
        dd1 = kb.dma("y", d_hrow, lnr[0:1, 0:C], deps=[t_rl])
        sem_h = f"dg{kb.qrr['g']}"
        kb.qrr['g'] = (kb.qrr['g'] + 1) % 4
        dd2 = kb.emit("g", lambda e: e.dma_start(
            out=hcol[:, :],
            in_=d_hrow.rearrange("o (ci p) -> (o p) ci", p=128),
            allow_slow_non_contiguous=True),
            deps=[dd1, (sem_h, kb.cnt[sem_h])], inc=sem_h, inc_by=16)
        mm2 = None
        for ci in range(NCH):
            mm2 = kb.t(lambda e, ci=ci: e.matmul(
                bank(2, r0=0, r1=1, c1=2), hcol[:, ci:ci + 1],
                w_g2[:, ci * 2:(ci + 1) * 2], start=(ci == 0),
                stop=(ci == NCH - 1)),
                deps=([dd2, tok_wg2] + kb.bdep(2)) if ci == 0 else [],
                inc=(ci == NCH - 1))
        lg2 = kb.dma("y", nrow[0:1, RG + 10:RG + 12], d_gb2, deps=[])
        t_l = kb.v(lambda e: e.tensor_tensor(
            nrow[0:1, RG + 4:RG + 6], bank(2, r0=0, r1=1, c1=2),
            nrow[0:1, RG + 10:RG + 12], op=ALU.add), deps=[mm2, lg2],
            inc=True)
        kb.bset(t_l, 2)
        t_d = kb.v(lambda e: e.tensor_tensor(
            nrow[0:1, RG + 6:RG + 7], nrow[0:1, RG + 4:RG + 5],
            nrow[0:1, RG + 5:RG + 6], op=ALU.subtract), deps=[t_l], inc=True)
        t_sg = kb.s(lambda e: e.activation(nrow[0:1, RG + 8:RG + 9],
                                           nrow[0:1, RG + 6:RG + 7],
                                           AF.Sigmoid), deps=[t_d], inc=True)
        t_g1r = kb.v(lambda e: e.tensor_scalar(nrow[0:1, RG + 9:RG + 10],
                                               nrow[0:1, RG + 8:RG + 9], -1.0,
                                               1.0, op0=ALU.mult, op1=ALU.add),
                     deps=[t_sg], inc=True)
        tok_g = bc_row32(nrow, RG + 8, 2, gbc, 0, [t_g1r], bki=2)
        tap(f"gates{s}", nrow[0:1, RG + 8:RG + 10], [tok_g])
        return tok_g

    def cc_exchange(idx, row, off, deps):
        cp = kb.v(lambda e, o=off: e.tensor_copy(
            stat_ev[0:1, 0:8], row[0:1, o:o + 8]),
            deps=list(deps) + kb.gd("stat_ev"), inc=True)
        d1 = kb.dma("g", d_ccin[idx], stat_ev[0:1, 0:8], deps=[cp])
        t_cc = kb.emit("g", lambda e, i=idx: e.collective_compute(
            "AllReduce", ALU.max, replica_groups=[list(range(NCORES))],
            ins=[d_ccin[i].opt()], outs=[d_ccout[i].opt()]),
            deps=[d1], inc="cc")
        d2 = kb.dma("g", row[0:1, off:off + 8], d_ccout[idx], deps=[t_cc])
        kb.gset("stat_ev", d2)
        return d2

    def colmax(c0, c1, row, dst_off, deps):
        rd = kb.v(lambda e, c0=c0, c1=c1: e.tensor_reduce(
            maxc[:, 255:256], maxc[:, c0:c1], axis=AX, op=ALU.max),
            deps=deps, inc=True)
        gdd = kb.dma("y", stat_ev[0:1, 0:128], maxc[:, 255:256],
                     deps=[rd] + kb.gd("stat_ev"))
        rd2 = kb.v(lambda e, d=dst_off: e.tensor_reduce(
            row[0:1, d:d + 1], stat_ev[0:1, 0:128], axis=AX, op=ALU.max),
            deps=[gdd], inc=True)
        kb.gset("stat_ev", rd2)
        return rd2

    def scale_pair(row, src, dst_s, dst_rs, deps):
        t1 = kb.v(lambda e: e.tensor_scalar(
            row[0:1, dst_s:dst_s + 1], row[0:1, src:src + 1], 1.0 / QMAX,
            1e-12, op0=ALU.mult, op1=ALU.max), deps=deps, inc=True)
        t2 = kb.v(lambda e: e.reciprocal(
            row[0:1, dst_rs:dst_rs + 1], row[0:1, dst_s:dst_s + 1]),
            deps=[t1], inc=True)
        return t2

    def emit_ffn(attn_done):
        xqb, a1 = ar.alloc("xq", 16 * T, F16)
        lnr, a2 = ar.alloc("lnrF", 3 * T, F32)
        brw, a3 = ar.alloc("brwF", 2 * T, F16)
        x2c, a4 = ar.alloc("x2cF", 2 * T, F16)
        fdeps = a1 + a2 + a3 + a4 + list(attn_done)
        xq_toks = []
        red_toks = []
        mx = 0
        prev_lx = [None, None]
        prev_bts = []
        for s in range(S):
            lm = kb.dma("y", lnr[0:1, 2 * T:3 * T],
                        d_ln2[0:1, s * 2 * T: s * 2 * T + T],
                        deps=fdeps + prev_bts)
            lr = kb.dma("y", lnr[0:1, 0:T],
                        d_ln2[0:1, s * 2 * T + T: s * 2 * T + 2 * T],
                        deps=fdeps + prev_bts + [lm])
            tmr = kb.v(lambda e: e.tensor_tensor(
                lnr[0:1, T:2 * T], lnr[0:1, 2 * T:3 * T], lnr[0:1, 0:T],
                op=ALU.mult), deps=[lm, lr], inc=True)
            bts = ln_bcast(lnr, brw, lr, tmr)
            prev_bts = bts + [tmr]
            for ci in range(NCH):
                sl = (ci % 2) * T
                lx = kb.dma("g", x2c[:, sl:sl + T],
                            d_x2[s, ci * 128:(ci + 1) * 128, :],
                            deps=fdeps + [prev_lx[ci % 2]])
                xcol = (ci * 2 + s) * T
                tk = ln_apply(xqb, xcol, x2c, sl, brw, [lx] + bts)
                prev_lx[ci % 2] = tk
                rd = kb.v(lambda e, xcol=xcol, mx=mx: e.tensor_reduce(
                    maxc[:, mx:mx + 1], xqb[:, xcol:xcol + T], axis=AX,
                    op=ALU.max, apply_absolute_value=True), deps=[tk],
                    inc=True)
                red_toks.append(rd)
                xq_toks.append(tk)
                mx += 1
        tok_mx = colmax(0, 16, nrow, RX, red_toks)
        t_x0 = cc_exchange(0, nrow, RX, [tok_mx])
        t_s0 = scale_pair(nrow, RX, RX + 16, RX + 17, [t_x0])
        sb0 = bc_row32(nrow, RX + 16, 2, scbc, 0, [t_s0], bki=0)
        q_toks = []
        chiq, a5 = ar.alloc("chiq", T, I32)
        for col in range(16):
            xcol = col * T
            tin = kb.v(lambda e, xcol=xcol: e.tensor_scalar(
                chiq[:, 0:T], xqb[:, xcol:xcol + T], scbc[:, 1:2], None,
                op0=ALU.mult), deps=[sb0] + xq_toks + a5 +
                [q_toks[-1] if q_toks else None], inc=True)
            tq = kb.s(lambda e, xcol=xcol: e.activation(
                xqb[:, xcol:xcol + T], chiq[:, 0:T], AF.Identity,
                scale=scbc[:, 0:1]), deps=[tin], inc=True)
            q_toks.append(tq)
        tap("xq", xqb[:, :], q_toks)
        for nm in ("lnrF", "brwF", "x2cF"):
            ar.free(nm, q_toks[-1])
        ar.free("chiq", q_toks[-1])

        wst, a6 = ar.alloc("wstF", 2 * NCH * 128, F16)
        sps, a7 = ar.alloc("spsF", 2 * S * T, F16)
        p1deps = a6 + a7 + q_toks
        mxu0, mxg0 = 16, 64
        spill_toks = []
        wfree = {}
        spfree = {}
        for mi, (dW, dSp) in enumerate(((d_upT, d_up), (d_gateT, d_gate))):
            for J in range(NI):
                ws = (mi * NI + J) % 2
                lw = kb.dma("y",
                            wst[:, ws * 1024:(ws + 1) * 1024]
                            .rearrange("p (ci j) -> p ci j", j=128),
                            dW.rearrange("(ci p) j -> p ci j", p=128)
                            [:, :, J * 128:(J + 1) * 128],
                            deps=p1deps + [wfree.get((mi * NI + J) - 2)])
                ev_list = []
                for tg in range(8):
                    bk = tg % 4
                    tkmm = None
                    for ci in range(NCH):
                        tkmm = kb.t(lambda e, ws=ws, ci=ci, tg=tg, bk=bk:
                                    e.matmul(
                            bank(bk),
                            wst[:, ws * 1024 + ci * 128:
                                ws * 1024 + (ci + 1) * 128],
                            xqb[:, ci * 2 * T + tg * 512:
                                ci * 2 * T + (tg + 1) * 512],
                            start=(ci == 0), stop=(ci == NCH - 1)),
                            deps=(kb.bdep(bk) + [lw] + q_toks)
                            if ci == 0 else [], inc=(ci == NCH - 1))
                    ev = kb.s(lambda e, ws=ws, tg=tg, bk=bk: e.activation(
                        sps[:, ws * S * T + tg * 512:
                            ws * S * T + (tg + 1) * 512],
                        bank(bk), AF.Identity),
                        deps=[tkmm] + list(spfree.get((mi * NI + J) - 2, ())),
                        inc=True)
                    kb.bset(ev, bk)
                    ev_list.append(ev)
                rd = kb.v(lambda e, ws=ws, mc=(mxu0 if mi == 0 else mxg0) + J:
                          e.tensor_reduce(
                    maxc[:, mc:mc + 1], sps[:, ws * S * T:(ws + 1) * S * T],
                    axis=AX, op=ALU.max, apply_absolute_value=True),
                    deps=ev_list, inc=True)
                sp = kb.dma("s", dSp[:, J * S * T:(J + 1) * S * T],
                            sps[:, ws * S * T:(ws + 1) * S * T],
                            deps=ev_list)
                wfree[mi * NI + J] = tkmm
                spfree[mi * NI + J] = (sp, rd)
                spill_toks.append(sp)
                spill_toks.append(rd)
        t_mu = colmax(mxu0, mxu0 + NI, nrow, RX + 1, spill_toks)
        t_mg = colmax(mxg0, mxg0 + NI, nrow, RX + 2, [t_mu] + spill_toks)
        t_x1 = cc_exchange(1, nrow, RX + 1, [t_mg])
        t_su = scale_pair(nrow, RX + 1, RX + 18, RX + 19, [t_x1])
        t_sg = scale_pair(nrow, RX + 2, RX + 20, RX + 21, [t_su])
        sb1 = bc_row32(nrow, RX + 18, 4, scbc, 2, [t_sg], bki=1)
        ar.free("xq", spill_toks[-1])
        ar.free("wstF", spill_toks[-1])

        gch, b1 = ar.alloc("gch", 2 * T, F16)
        chi, b2 = ar.alloc("chiF", T, I32)
        sga, b3 = ar.alloc("sga", T, F16)
        gq, b4 = ar.alloc("gq", T, F16)
        t2b, b5 = ar.alloc("t2b", 2 * T, F16)
        p2deps = b1 + b2 + b3 + b4 + b5 + [sb1]
        NCK = NI * S * T // T
        prev = {}
        p2_toks = []
        for ck in range(NCK):
            sl = (ck % 2)
            lg = kb.dma("y", gch[:, sl * T:(sl + 1) * T],
                        d_gate[:, ck * T:(ck + 1) * T],
                        deps=p2deps + [prev.get(("ng", ck - 2))])
            tng = kb.v(lambda e, sl=sl: e.tensor_scalar(
                chi[:, 0:T], gch[:, sl * T:(sl + 1) * T], scbc[:, 5:6], None,
                op0=ALU.mult), deps=[lg, prev.get(("sig", ck - 1)),
                                     prev.get(("gq", ck - 1))], inc=True)
            tsg = kb.s(lambda e: e.activation(
                sga[:, 0:T], chi[:, 0:T], AF.Sigmoid, scale=scbc[:, 4:5]),
                deps=[tng, prev.get(("ns", ck - 1)),
                      prev.get(("t2", ck - 1))], inc=True)
            tgq = kb.s(lambda e: e.activation(
                gq[:, 0:T], chi[:, 0:T], AF.Identity, scale=scbc[:, 4:5]),
                deps=[tng, prev.get(("t2", ck - 1))], inc=True)
            tns = kb.v(lambda e: e.tensor_scalar(
                chi[:, 0:T], sga[:, 0:T], 65536.0, None, op0=ALU.mult),
                deps=[tsg, tgq], inc=True)
            tsq = kb.v(lambda e: e.tensor_scalar(
                sga[:, 0:T], chi[:, 0:T], 65535.0, 1.0 / 65536.0,
                op0=ALU.min, op1=ALU.mult), deps=[tns], inc=True)
            tt2 = kb.g(lambda e, sl=sl: e.tensor_tensor(
                t2b[:, sl * T:(sl + 1) * T], gq[:, 0:T], sga[:, 0:T],
                op=ALU.mult), deps=[tsq, tgq, prev.get(("t2", ck - 2))],
                inc=True)
            trd = kb.v(lambda e, sl=sl, mc=112 + (ck % 64): e.tensor_reduce(
                maxc[:, mc:mc + 1], t2b[:, sl * T:(sl + 1) * T], axis=AX,
                op=ALU.max, apply_absolute_value=True),
                deps=[tt2, prev.get(("rd", ck - 64))], inc=True)
            sp = kb.dma("g", d_t2[:, ck * T:(ck + 1) * T],
                        t2b[:, sl * T:(sl + 1) * T], deps=[tt2])
            prev[("sig", ck)] = tsg
            prev[("gq", ck)] = tgq
            prev[("ns", ck)] = tns
            prev[("ng", ck)] = tng
            prev[("t2", ck)] = sp
            prev[("rd", ck)] = trd
            p2_toks += [sp, trd]
        t_m2 = colmax(112, 112 + 64 if NCK > 64 else 112 + NCK, nrow, RX + 3,
                      p2_toks)
        t_x2 = cc_exchange(2, nrow, RX + 3, [t_m2])
        t_s2 = scale_pair(nrow, RX + 3, RX + 22, RX + 23, [t_x2])
        sb2 = bc_row32(nrow, RX + 22, 2, scbc, 6, [t_s2], bki=2)

        osp, c1 = ar.alloc("osp", 2 * T, F16)
        p3deps = c1 + [sb2] + p2_toks
        p3_toks = []
        prev3 = {}
        for ck in range(NCK):
            sl = ck % 2
            lt = kb.dma("y", gch[:, sl * T:(sl + 1) * T],
                        d_t2[:, ck * T:(ck + 1) * T],
                        deps=p3deps + [prev3.get(("a", ck - 2))])
            lu = kb.dma("s", t2b[:, sl * T:(sl + 1) * T],
                        d_up[:, ck * T:(ck + 1) * T],
                        deps=p3deps + [prev3.get(("b", ck - 2))])
            tn2 = kb.v(lambda e, sl=sl: e.tensor_scalar(
                chi[:, 0:T], gch[:, sl * T:(sl + 1) * T], scbc[:, 7:8], None,
                op0=ALU.mult), deps=[lt, prev3.get(("q", ck - 1))], inc=True)
            tq2 = kb.s(lambda e: e.activation(
                sga[:, 0:T], chi[:, 0:T], AF.Identity, scale=scbc[:, 6:7]),
                deps=[tn2, prev3.get(("sp", ck - 1))], inc=True)
            tnu = kb.v(lambda e, sl=sl: e.tensor_scalar(
                chi[:, 0:T], t2b[:, sl * T:(sl + 1) * T], scbc[:, 3:4], None,
                op0=ALU.mult), deps=[lu, tq2], inc=True)
            tqu = kb.s(lambda e: e.activation(
                gq[:, 0:T], chi[:, 0:T], AF.Identity, scale=scbc[:, 2:3]),
                deps=[tnu, prev3.get(("sp", ck - 1))], inc=True)
            to = kb.g(lambda e, sl=sl: e.tensor_tensor(
                osp[:, sl * T:(sl + 1) * T], sga[:, 0:T], gq[:, 0:T],
                op=ALU.mult), deps=[tqu, tq2, prev3.get(("sp", ck - 2))],
                inc=True)
            trd = kb.v(lambda e, sl=sl, mc=176 + (ck % 64): e.tensor_reduce(
                maxc[:, mc:mc + 1], osp[:, sl * T:(sl + 1) * T], axis=AX,
                op=ALU.max, apply_absolute_value=True),
                deps=[to, prev3.get(("rd", ck - 64))], inc=True)
            sp = kb.dma("g", d_oraw[:, ck * T:(ck + 1) * T],
                        osp[:, sl * T:(sl + 1) * T], deps=[to])
            prev3[("a", ck)] = tn2
            prev3[("b", ck)] = tnu
            prev3[("q", ck)] = tqu
            prev3[("rd", ck)] = trd
            prev3[("sp", ck)] = (sp if False else trd) and sp
            p3_toks += [sp, trd]
        t_m3 = colmax(176, 176 + min(64, NCK), nrow, RX + 4, p3_toks)
        t_x3 = cc_exchange(3, nrow, RX + 4, [t_m3])
        t_s3 = scale_pair(nrow, RX + 4, RX + 24, RX + 25, [t_x3])
        sb3 = bc_row32(nrow, RX + 24, 2, scbc, 8, [t_s3], bki=3)
        for nm in ("gch", "chiF", "sga", "gq", "t2b", "osp"):
            ar.free(nm, p3_toks[-1])

        foutb, c3 = ar.alloc("fout", NCH * 2 * T, F16)
        wdn, c4 = ar.alloc("wdn", NI * 128 * NCH, F16)
        oqst, c5a = ar.alloc("oqst", NI * 512, F16)
        chi2, c5b = ar.alloc("chi2", T, I32)
        p4deps = c3 + c4 + c5a + c5b + [sb3] + p3_toks
        lwall = kb.dma("y", wdn[:, :].rearrange("p (ii j) -> p ii j", j=C),
                       d_downT.rearrange("(ii p) j -> p ii j", p=128),
                       deps=p4deps)
        dn_ev = []
        oq_free = None
        for tg in range(8):
            lo = kb.dma("s", oqst[:, :].rearrange("p (ii t) -> p ii t", t=512),
                        d_oraw[:, :].rearrange("p (ii t) -> p ii t", t=S * T)
                        [:, :, tg * 512:(tg + 1) * 512],
                        deps=p4deps + [oq_free])
            qd_toks = []
            for sub in range(6):
                w = 2048
                ti2 = kb.v(lambda e, sub=sub: e.tensor_scalar(
                    chi2[:, 0:T], oqst[:, sub * T:(sub + 1) * T],
                    scbc[:, 9:10], None, op0=ALU.mult),
                    deps=[lo] + ([qd_toks[-1]] if qd_toks else []), inc=True)
                tq2 = kb.s(lambda e, sub=sub: e.activation(
                    oqst[:, sub * T:(sub + 1) * T], chi2[:, 0:T],
                    AF.Identity, scale=scbc[:, 8:9]), deps=[ti2], inc=True)
                qd_toks.append(tq2)
            for J in range(NCH):
                bk = (tg * NCH + J) % 4
                tkmm = None
                for ii in range(NI):
                    tkmm = kb.t(lambda e, J=J, ii=ii, tg=tg, bk=bk: e.matmul(
                        bank(bk),
                        wdn[:, ii * C + J * 128: ii * C + (J + 1) * 128],
                        oqst[:, ii * 512:(ii + 1) * 512],
                        start=(ii == 0), stop=(ii == NI - 1)),
                        deps=(kb.bdep(bk) + [lwall] + qd_toks)
                        if ii == 0 else [], inc=(ii == NI - 1))
                ev = kb.s(lambda e, J=J, tg=tg, bk=bk: e.activation(
                    foutb[:, J * 2 * T + tg * 512: J * 2 * T + (tg + 1) * 512],
                    bank(bk), AF.Identity), deps=[tkmm], inc=True)
                kb.bset(ev, bk)
                dn_ev.append(ev)
            oq_free = dn_ev[-1]
        ar.free("oqst", dn_ev[-1])
        ar.free("chi2", dn_ev[-1])
        ar.free("wdn", dn_ev[-1])
        tap("fout", foutb[:, :], [dn_ev[-1]])

        lnr5, c5 = ar.alloc("lnrG", 3 * T, F32)
        sq2, c6 = ar.alloc("sq2G", 2 * T, F16)
        brw2, c7 = ar.alloc("brwG", 2 * T, F16)
        xfc, c8 = ar.alloc("xfc", 2 * T, F32)
        out_toks = []
        for s in range(S):
            sqf = [None, None]
            tmm = None
            for ci in range(NCH):
                sl = (ci % 2) * T
                col = ci * 2 * T + s * T
                tsq = kb.s(lambda e, col=col, sl=sl: e.activation(
                    sq2[:, sl:sl + T], foutb[:, col:col + T], AF.Square),
                    deps=[dn_ev[-1], sqf[ci % 2],
                          out_toks[-1] if out_toks else None]
                    + c5 + c6 + c7 + c8, inc=True)
                for g in range(4):
                    dep0 = (kb.bdep(6, 7) + [tok_ones]) if ci == 0 else []
                    kb.t(lambda e, col=col, g=g, ci=ci: e.matmul(
                        bank(6, r0=32 * g, r1=32 * g + 1), ones1[:, 0:1],
                        foutb[:, col + 512 * g: col + 512 * (g + 1)],
                        start=(ci == 0), stop=(ci == NCH - 1),
                        tile_position=(0, 32 * g), skip_group_check=True),
                        deps=dep0 + [dn_ev[-1]])
                    tmm = kb.t(lambda e, g=g, sl=sl, ci=ci: e.matmul(
                        bank(7, r0=32 * g, r1=32 * g + 1), ones1[:, 0:1],
                        sq2[:, sl + 512 * g: sl + 512 * (g + 1)],
                        start=(ci == 0), stop=(ci == NCH - 1),
                        tile_position=(0, 32 * g), skip_group_check=True),
                        deps=[tsq] + dep0, inc=(g == 3))
                sqf[ci % 2] = tmm
            for g in range(4):
                ev1 = kb.v(lambda e, g=g: e.tensor_copy(
                    stat_ev[32 * g:32 * g + 1, 0:512],
                    bank(6, r0=32 * g, r1=32 * g + 1)),
                    deps=([tmm] + kb.gd("stat_ev")) if g == 0 else [], inc=True)
            for g in range(4):
                ev2 = kb.v(lambda e, g=g: e.tensor_copy(
                    stat_ev[32 * g:32 * g + 1, 512:1024],
                    bank(7, r0=32 * g, r1=32 * g + 1)),
                    deps=[tmm] if g == 0 else [], inc=True)
            kb.bset(ev2, 6, 7)
            g1 = kb.dma("y", lnr5[0:1, 0:T], stat_ev[0:97:32, 0:512],
                        deps=[ev1])
            g2 = kb.dma("y", lnr5[0:1, T:2 * T], stat_ev[0:97:32, 512:1024],
                        deps=[ev2])
            kb.gset("stat_ev", g2)
            inv = 1.0 / C
            kb.v(lambda e: e.tensor_scalar(lnr5[0:1, 2 * T:3 * T],
                                           lnr5[0:1, 0:T], inv, None,
                                           op0=ALU.mult), deps=[g1])
            kb.v(lambda e: e.tensor_scalar(lnr5[0:1, T:2 * T],
                                           lnr5[0:1, T:2 * T], inv, None,
                                           op0=ALU.mult), deps=[g2])
            kb.v(lambda e: e.tensor_tensor(lnr5[0:1, 0:T],
                                           lnr5[0:1, 2 * T:3 * T],
                                           lnr5[0:1, 2 * T:3 * T],
                                           op=ALU.mult))
            tvar = kb.v(lambda e: e.tensor_tensor(lnr5[0:1, T:2 * T],
                                                  lnr5[0:1, T:2 * T],
                                                  lnr5[0:1, 0:T],
                                                  op=ALU.subtract), inc=True)
            tln = kb.s(lambda e: e.activation(lnr5[0:1, 0:T],
                                              lnr5[0:1, T:2 * T],
                                              AF.Ln, bias=epcol[0:1, 1:2]),
                       deps=[tvar, tok_ep], inc=True)
            tr = kb.s(lambda e: e.activation(lnr5[0:1, 0:T], lnr5[0:1, 0:T],
                                             AF.Exp, scale=-0.5),
                      deps=[tln], inc=True)
            tmr = kb.v(lambda e: e.tensor_tensor(lnr5[0:1, T:2 * T],
                                                 lnr5[0:1, 2 * T:3 * T],
                                                 lnr5[0:1, 0:T], op=ALU.mult),
                       deps=[tr], inc=True)
            bts = ln_bcast(lnr5, brw2, tr, tmr)
            prev_x = None
            for ci in range(NCH):
                col = ci * 2 * T + s * T
                u = ln_apply(sq2, 0, foutb, col, brw2, bts + [prev_x])
                xr = kb.dma("y", xfc[:, 0:T], d_x2[s, ci * 128:(ci + 1) * 128, :],
                            deps=[prev_x])
                ad = kb.v(lambda e: e.tensor_tensor(
                    xfc[:, T:2 * T], sq2[:, 0:T], xfc[:, 0:T], op=ALU.add),
                    deps=[u, xr], inc=True)
                so = kb.dma("y", dyo[s, ci * 128:(ci + 1) * 128, :],
                            xfc[:, T:2 * T], deps=[ad])
                prev_x = so
                out_toks.append(so)
        kb.y(lambda e: e.nop(), deps=[out_toks[-1]])
        return out_toks

    ph = set(_PHASES)
    done = []
    dq_tok, dq_names = None, ()
    if "attn0" in ph:
        a_toks0 = emit_attention(0, [], free_after_qkv=None)
        done += a_toks0
    if "dequant" in ph:
        dq_tok, dq_names = emit_dequant()
        for nm in dq_names:
            ar.free(nm, dq_tok)
        done.append(dq_tok)
    if "attn1" in ph:
        a_toks1 = emit_attention(1, [])
        done += a_toks1
    if "ffn" in ph:
        emit_ffn(done)
    else:
        prev = None
        for s in range(S):
            for ci in range(NCH):
                prev = kb.dma("y", dyo[s, ci * 128:(ci + 1) * 128, :],
                              dx[s, ci * 128:(ci + 1) * 128, :],
                              deps=done + [prev])
        kb.y(lambda e: e.nop(), deps=[prev])

    kb.build()
    return nc, d_taps


def _prep_inputs(inputs):
    x = np.asarray(inputs["L0_attn_x"], np.float32)
    conv_w = np.asarray(inputs["L0_attn_conv_w"], np.float32)
    cw = np.ascontiguousarray(conv_w.transpose(1, 2, 0).reshape(64, KTAP * HS))
    cw2 = np.concatenate([cw, cw], axis=0)
    base = {
        "w_attn": np.asarray(inputs["L0_attn_c_attn_w"], np.float32),
        "b_attn": np.ascontiguousarray(
            np.asarray(inputs["L0_attn_c_attn_b"], np.float32)
            .reshape(NI, 128).T),
        "w_proj": np.asarray(inputs["L0_attn_c_proj_w"], np.float32),
        "b_proj": np.ascontiguousarray(
            np.asarray(inputs["L0_attn_c_proj_b"], np.float32)
            .reshape(NCH, 128).T),
        "w_conv": cw2,
        "b_conv": np.ascontiguousarray(
            np.tile(np.asarray(inputs["L0_attn_conv_b"], np.float32)
                    .reshape(64, 1), (2, 1))),
        "comp_w": np.asarray(inputs["L0_attn_comp_ln_w"], np.float32)
        .reshape(1, HS),
        "comp_b": np.asarray(inputs["L0_attn_comp_ln_b"], np.float32)
        .reshape(1, HS),
        "lnl_w": np.ascontiguousarray(
            np.asarray(inputs["L0_attn_ln_local_w"], np.float32)
            .reshape(NCH, 128).T),
        "lnl_b": np.ascontiguousarray(
            np.asarray(inputs["L0_attn_ln_local_b"], np.float32)
            .reshape(NCH, 128).T),
        "g_w1": np.asarray(inputs["L0_attn_gate_w1"], np.float32),
        "g_b1": np.asarray(inputs["L0_attn_gate_b1"], np.float32)
        .reshape(1, C),
        "g_lnw": np.asarray(inputs["L0_attn_gate_ln_w"], np.float32)
        .reshape(1, C),
        "g_lnb": np.asarray(inputs["L0_attn_gate_ln_b"], np.float32)
        .reshape(1, C),
        "g_w2": np.asarray(inputs["L0_attn_gate_w2"], np.float32),
        "g_b2": np.asarray(inputs["L0_attn_gate_b2"], np.float32)
        .reshape(1, 2),
        "w_up": np.asarray(inputs["L0_ffn_w_up"], np.float32),
        "w_gate": np.asarray(inputs["L0_ffn_w_gate"], np.float32),
        "w_down": np.asarray(inputs["L0_ffn_w_down"], np.float32),
    }
    in_maps = []
    for c in range(NCORES):
        m = dict(base)
        m["x"] = np.ascontiguousarray(
            x[2 * c:2 * c + 2].transpose(0, 2, 1))
        in_maps.append(m)
    return in_maps


def kernel(**inputs):
    in_maps = _prep_inputs(inputs)
    nc, d_taps = build_program(taps=tuple(_DEBUG_TAPS))
    res = run_bass_kernel_spmd(nc, in_maps, core_ids=list(range(NCORES)))
    out = np.empty((16, 2048, 1024), np.float32)
    for c in range(NCORES):
        yv = np.asarray(res.results[c]["y"], np.float32)
        out[2 * c:2 * c + 2] = yv.transpose(0, 2, 1)
    if _DEBUG_TAPS:
        kernel._tap_results = [
            {k: np.asarray(res.results[c].get("tap_" + k))
             for k in _DEBUG_TAPS if ("tap_" + k) in res.results[c]}
            for c in range(NCORES)]
        kernel._spill_results = {
            k: np.asarray(res.results[0][k]) for k in
            ("x2_sp", "ln2_sp", "up_sp", "gate_sp", "t2_sp", "oraw_sp",
             "upT_sp", "gateT_sp", "downT_sp") if k in res.results[0]}
    return out


# version 09 (final)
# speedup vs baseline: 1.0538x; 1.0114x over previous
import numpy as np
import ml_dtypes
import concourse.bass as bass
import concourse.mybir as mybir
from concourse.bass_utils import run_bass_kernel_spmd
from contextlib import ExitStack

F32 = mybir.dt.float32
F16 = mybir.dt.float16
BF16 = mybir.dt.bfloat16
I32 = mybir.dt.int32
AF = mybir.ActivationFunctionType
ALU = mybir.AluOpType
AX = mybir.AxisListType.X

NCORES = 8
S, T, C, NCH, NH, HS, NB = 2, 2048, 1024, 8, 16, 64, 127
KTAP, CSTR = 32, 16
I3, NI = 3072, 24
EPS = 1e-5
QMAX = 32767.0
DEN_EPS = 1e-37
ARENA_BYTES = 172800

_DEBUG_TAPS = []
_PHASES = ("dequant", "attn0", "attn1", "ffn")
_ATTN_CUT = None


def _dtsz(dt):
    return {F32: 4, F16: 2, BF16: 2, I32: 4}[dt]


class KB:
    def __init__(self, nc):
        self.nc = nc
        self.ops = {e: [] for e in ("t", "v", "s", "g", "y")}
        self.sem = {}
        self.cnt = {}
        self.waited = {e: {} for e in ("t", "v", "s", "g", "y")}
        self.stack = ExitStack()
        names = ["st", "sv", "ss", "sg", "sy", "cc"]
        for q in ("dy", "dg", "ds"):
            names += [f"{q}{i}" for i in range(4)]
        for name in names:
            self.sem[name] = self.stack.enter_context(nc.semaphore(name))
            self.cnt[name] = 0
        self.qrr = {"y": 0, "g": 0, "s": 0}
        self.bank_tok = {}
        self.guard = {}

    def sb(self, name, shape, dt=F16):
        return self.stack.enter_context(self.nc.sbuf_tensor(name, shape, dt))

    AUTO_SER = {"v": "sv", "s": "ss", "g": "sg"}

    def emit(self, eng, fn, deps=(), inc=None, inc_by=1):
        own = self.AUTO_SER.get(eng)
        is_dma = isinstance(inc, str) and inc.startswith(("d", "c"))
        if own is not None and not is_dma:
            deps = list(deps) + [(own, self.cnt[own])]
            if not isinstance(inc, str):
                inc = own
        waits = []
        for d in deps:
            if d is None:
                continue
            sem, val = d
            if val <= 0 or self.waited[eng].get(sem, 0) >= val:
                continue
            self.waited[eng][sem] = val
            waits.append((self.sem[sem], val))
        tok, sem_h = None, None
        if inc:
            inc = {"t": "st", "v": "sv", "s": "ss", "g": "sg", "y": "sy"}[eng] \
                if inc is True else inc
            self.cnt[inc] += inc_by
            tok = (inc, self.cnt[inc])
            sem_h = self.sem[inc]

        def run(e, waits=waits, fn=fn, sem_h=sem_h, inc_by=inc_by):
            for (h, v) in waits:
                e.wait_ge(h, v)
            ins = fn(e)
            if sem_h is not None:
                ins.then_inc(sem_h, inc_by)
        self.ops[eng].append(run)
        return tok

    def t(self, fn, deps=(), inc=False):
        return self.emit("t", fn, deps, inc)

    def v(self, fn, deps=(), inc=False):
        return self.emit("v", fn, deps, inc)

    def s(self, fn, deps=(), inc=False):
        return self.emit("s", fn, deps, inc)

    def g(self, fn, deps=(), inc=False):
        return self.emit("g", fn, deps, inc)

    def y(self, fn, deps=(), inc=False):
        return self.emit("y", fn, deps, inc)

    def dma(self, queue, out, in_, deps=()):
        base = {"y": "dy", "g": "dg", "s": "ds"}[queue]
        sem = f"{base}{self.qrr[queue]}"
        self.qrr[queue] = (self.qrr[queue] + 1) % 4
        deps = list(deps) + [(sem, self.cnt[sem])]
        return self.emit(queue, lambda e, o=out, i=in_: e.dma_start(out=o, in_=i),
                         deps, sem, 16)

    def bdep(self, *banks):
        return [self.bank_tok.get(b) for b in banks]

    def bset(self, tok, *banks):
        for b in banks:
            self.bank_tok[b] = tok

    def gd(self, name):
        return [self.guard.get(name)]

    def gset(self, name, tok):
        self.guard[name] = tok

    def build(self):
        with self.nc.Block() as block:
            @block.tensor
            def _(e):
                for f in self.ops["t"]:
                    f(e)

            @block.vector
            def _(e):
                for f in self.ops["v"]:
                    f(e)

            @block.scalar
            def _(e):
                for f in self.ops["s"]:
                    f(e)

            @block.gpsimd
            def _(e):
                for f in self.ops["g"]:
                    f(e)

            @block.sync
            def _(e):
                for f in self.ops["y"]:
                    f(e)


class Arena:

    def __init__(self, kb, nbytes):
        self.kb = kb
        self.u8 = kb.sb("sb_ARENA", [128, nbytes], mybir.dt.uint8)
        self.free_list = [(0, nbytes, [])]
        self.live = {}

    def _check(self):
        spans = sorted([(o, o + s, "free") for (o, s, _) in self.free_list] +
                       [(o, o + s, n) for n, (o, s) in self.live.items()])
        for a, b in zip(spans, spans[1:]):
            assert a[1] <= b[0], f"arena overlap: {a} vs {b}"

    def alloc(self, name, nelem, dt=F16):
        self._check()
        size = (nelem * _dtsz(dt) + 31) & ~31
        for i, (off, sz, toks) in enumerate(self.free_list):
            if sz >= size:
                if sz == size:
                    self.free_list.pop(i)
                else:
                    self.free_list[i] = (off + size, sz - size, toks)
                view = self.u8[:, off:off + nelem * _dtsz(dt)].bitcast(dt)
                self.live[name] = (off, size)
                return view, list(toks)
        raise MemoryError(f"arena OOM for {name} ({size}B); free={self.free_list}")

    def free(self, name, tok):
        off, size = self.live.pop(name)
        self.free_list.append((off, size, [tok] if tok else []))
        self.free_list.sort()
        merged = []
        for blk in self.free_list:
            if merged and merged[-1][0] + merged[-1][1] == blk[0]:
                o, s, t1 = merged[-1]
                comb = {}
                for t in list(t1) + list(blk[2]):
                    if t is None:
                        continue
                    if t[0] not in comb or comb[t[0]][1] < t[1]:
                        comb[t[0]] = t
                merged[-1] = (o, s + blk[1], list(comb.values()))
            else:
                merged.append(blk)
        self.free_list = merged


def build_program(taps=()):
    nc = bass.Bass("TRN2", target_bir_lowering=False, debug=False)
    kb = KB(nc)
    sb = kb.sb
    ar = Arena(kb, ARENA_BYTES)

    dx = nc.dram_tensor("x", [S, C, T], F32, kind="ExternalInput").ap()
    dyo = nc.dram_tensor("y", [S, C, T], F32, kind="ExternalOutput").ap()
    d_wattn = nc.dram_tensor("w_attn", [C, 3 * C], F32, kind="ExternalInput").ap()
    d_battn = nc.dram_tensor("b_attn", [128, NI], F32, kind="ExternalInput").ap()
    d_wproj = nc.dram_tensor("w_proj", [C, C], F32, kind="ExternalInput").ap()
    d_bproj = nc.dram_tensor("b_proj", [128, NCH], F32, kind="ExternalInput").ap()
    d_wconv = nc.dram_tensor("w_conv", [128, KTAP * HS], F32, kind="ExternalInput").ap()
    d_bconv = nc.dram_tensor("b_conv", [128, 1], F32, kind="ExternalInput").ap()
    d_compw = nc.dram_tensor("comp_w", [1, HS], F32, kind="ExternalInput").ap()
    d_compb = nc.dram_tensor("comp_b", [1, HS], F32, kind="ExternalInput").ap()
    d_lnlw = nc.dram_tensor("lnl_w", [128, NCH], F32, kind="ExternalInput").ap()
    d_lnlb = nc.dram_tensor("lnl_b", [128, NCH], F32, kind="ExternalInput").ap()
    d_gw1 = nc.dram_tensor("g_w1", [2 * C, C], F32, kind="ExternalInput").ap()
    d_gb1 = nc.dram_tensor("g_b1", [1, C], F32, kind="ExternalInput").ap()
    d_glnw = nc.dram_tensor("g_lnw", [1, C], F32, kind="ExternalInput").ap()
    d_glnb = nc.dram_tensor("g_lnb", [1, C], F32, kind="ExternalInput").ap()
    d_gw2 = nc.dram_tensor("g_w2", [C, 2], F32, kind="ExternalInput").ap()
    d_gb2 = nc.dram_tensor("g_b2", [1, 2], F32, kind="ExternalInput").ap()
    d_wup = nc.dram_tensor("w_up", [I3, C], F32, kind="ExternalInput").ap()
    d_wgate = nc.dram_tensor("w_gate", [I3, C], F32, kind="ExternalInput").ap()
    d_wdown = nc.dram_tensor("w_down", [C, I3], F32, kind="ExternalInput").ap()

    d_x2 = nc.dram_tensor("x2_sp", [S, C, T], F32).ap()
    d_ln2 = nc.dram_tensor("ln2_sp", [1, S * 2 * T], F32).ap()
    d_upT = nc.dram_tensor("upT_sp", [C, I3], F16).ap()
    d_gateT = nc.dram_tensor("gateT_sp", [C, I3], F16).ap()
    d_downT = nc.dram_tensor("downT_sp", [I3, C], F16).ap()
    d_up = nc.dram_tensor("up_sp", [128, NI * S * T], F16).ap()
    d_gate = nc.dram_tensor("gate_sp", [128, NI * S * T], F16).ap()
    d_t2 = nc.dram_tensor("t2_sp", [128, NI * S * T], F16).ap()
    d_oraw = nc.dram_tensor("oraw_sp", [128, NI * S * T], F16).ap()
    d_hrow = nc.dram_tensor("hrow_sp", [1, C], F32).ap()
    d_bmov = nc.dram_tensor("bmov_sp", [64, 1024], F16).ap()
    d_ccin = [nc.dram_tensor(f"ccin{i}", [1, 8], F32).ap() for i in range(4)]
    d_ccout = [nc.dram_tensor(f"ccout{i}", [1, 8], F32).ap() for i in range(4)]
    d_taps = {}

    ones1 = sb("sb_ones1", [128, 1], F16)
    idf = sb("sb_idf", [128, 128], F16)
    bc_w = sb("sb_bc_w", [128, 128], F16)
    bc_w32 = sb("sb_bc_w32", [32, 128], F32)
    bc_stage = sb("sb_bc_stage", [32, 512], F16)
    bc_st32 = sb("sb_bc_st32", [32, 128], F32)
    div_st = sb("sb_div_st", [128, 1024], BF16)
    dvb = sb("sb_dvb", [64, 1024], BF16)
    bstg = sb("sb_bstg", [64, 1024], F16)
    b_attn = sb("sb_b_attn", [128, NI], F32)
    b_proj = sb("sb_b_proj", [128, NCH], F32)
    b_conv = sb("sb_b_conv", [128, 1], F32)
    lnl_w = sb("sb_lnl_w", [128, NCH], F32)
    lnl_b = sb("sb_lnl_b", [128, NCH], F32)
    w_conv = sb("sb_w_conv", [128, KTAP * HS], F16)
    w_g2 = sb("sb_w_g2", [128, NCH * 2], F16)
    crep_w = sb("sb_crep_w", [128, NH * HS], F16)
    crep_b = sb("sb_crep_b", [128, NH * HS], F16)
    mn_bc = sb("sb_mn_bc", [128, 32], F32)
    hcol = sb("sb_hcol", [128, NCH], F16)
    gbc = sb("sb_gbc", [128, 2], F32)
    scbc = sb("sb_scbc", [128, 16], F32)
    maxc = sb("sb_maxc", [128, 256], F32)
    stat_ev = sb("sb_stat_ev", [128, 1024], F32)
    nrow = sb("sb_nrow", [1, 128], F32)
    vcompT = sb("sb_vcompT", [128, NH * 65], F16)
    red16 = sb("sb_red16", [128, 4 * NH], F32)
    pt = sb("sb_pt", [128, 4 * 512], BF16)
    vt = sb("sb_vt", [128, 2 * 16 * 65], F16)
    kcomp = sb("sb_kcomp", [128, NCH * NB], F16)
    epcol = sb("sb_epcol", [128, 3], F32)

    RQ, RK, RKC, RM = 0, NH, 2 * NH, 3 * NH
    RG = 5 * NH
    RX = RG + 16

    PS = kb.stack.enter_context(nc.psum_tensor("PS", [128, 4096], F32))
    PS16 = PS[:, :].bitcast(F16)

    def bank(i, r0=0, r1=128, c0=0, c1=512):
        return PS[r0:r1, 512 * i + c0: 512 * i + c1]

    def tap(name, ap, deps):
        if name not in taps:
            return
        d = nc.dram_tensor("tap_" + name, list(ap.shape), ap.dtype,
                           kind="ExternalOutput").ap()
        d_taps[name] = d
        kb.dma("y", d, ap, deps=deps)

    tok_ones = kb.v(lambda e: e.memset(ones1[:, :], 1.0), inc=True)
    kb.v(lambda e: e.memset(nrow[:, :], 0.0))
    kb.v(lambda e: e.memset(maxc[:, :], 0.0))
    kb.v(lambda e: e.memset(stat_ev[:, :], 0.0))
    kb.v(lambda e: e.memset(bc_w[:, :], 0.0))
    kb.v(lambda e: e.memset(bc_w[0:1, :], 1.0))
    tok_bcw = kb.v(lambda e: e.memset(bc_w[64:65, :], 1.0), inc=True)
    kb.v(lambda e: e.memset(bc_w32[:, :], 0.0))
    tok_bcw32 = kb.v(lambda e: e.memset(bc_w32[0:1, :], 1.0), inc=True)
    tok_stage = kb.v(lambda e: e.memset(bc_stage[:, :], 0.0), inc=True)
    tok_st32 = kb.v(lambda e: e.memset(bc_st32[:, :], 0.0), inc=True)
    tok_dst = kb.v(lambda e: e.memset(div_st[:, :], 0.0), inc=True)
    kb.v(lambda e: e.memset(epcol[:, 0:1], DEN_EPS))
    kb.v(lambda e: e.memset(epcol[:, 2:3], 1.0397207708399179))
    tok_ep = kb.v(lambda e: e.memset(epcol[:, 1:2], EPS), inc=True)
    kb.v(lambda e: e.memset(vt[:, :], 0.0))
    tok_vt0 = kb.v(lambda e: e.memset(
        vt[:, :].rearrange("p (k c) -> p k c", c=65)[:, :, 64:65], 1.0),
        inc=True)
    kb.v(lambda e: e.memset(vcompT[:, :], 0.0))
    tok_vc0 = kb.v(lambda e: e.memset(
        vcompT[:, :].rearrange("p (h c) -> p h c", c=65)[:, :, 64:65], 1.0),
        inc=True)
    tok_idt = kb.g(lambda e: e.memset(idf[:, :], 1.0), inc=True)
    tok_id = kb.g(lambda e: e.affine_select(
        idf[:, :], idf[:, :], pattern=[[1, 128]], compare_op=ALU.is_equal,
        fill=0.0, base=0, channel_multiplier=-1), deps=[tok_idt], inc=True)

    tok_wconv = kb.dma("g", w_conv[:, :], d_wconv)
    tok_wg2 = kb.dma("g", w_g2[:, :].rearrange("p (ci j) -> p ci j", j=2),
                     d_gw2.rearrange("(ci p) j -> p ci j", p=128))
    tok_battn = kb.dma("y", b_attn[:, :], d_battn)
    tok_bproj = kb.dma("y", b_proj[:, :], d_bproj)
    tok_bconv = kb.dma("y", b_conv[:, :], d_bconv)
    tok_lnlw = kb.dma("y", lnl_w[:, :], d_lnlw)
    tok_lnlb = kb.dma("y", lnl_b[:, :], d_lnlb)
    tok_cw = kb.dma("y", bc_st32[0:1, 0:HS], d_compw, deps=[tok_st32])
    tok_cb = kb.dma("y", bc_st32[0:1, HS:2 * HS], d_compb, deps=[tok_st32])

    tok_m1 = kb.t(lambda e: e.matmul(bank(0, c1=64), bc_w32[0:32, :],
                                     bc_st32[0:32, 0:64], start=True, stop=True),
                  deps=[tok_cw, tok_st32, tok_bcw32], inc=True)
    tok_m2 = kb.t(lambda e: e.matmul(bank(1, c1=64), bc_w32[0:32, :],
                                     bc_st32[0:32, HS:2 * HS],
                                     start=True, stop=True),
                  deps=[tok_cb, tok_st32, tok_bcw32], inc=True)
    crep_toks = []
    for h in range(NH):
        crep_toks.append(kb.v(lambda e, h=h: e.tensor_copy(
            crep_w[:, h * 64:(h + 1) * 64], bank(0, c1=64)),
            deps=[tok_m1], inc=(h == NH - 1)))
    for h in range(NH):
        crep_toks.append(kb.v(lambda e, h=h: e.tensor_copy(
            crep_b[:, h * 64:(h + 1) * 64], bank(1, c1=64)),
            deps=[tok_m2], inc=(h == NH - 1)))
    tok_crep = crep_toks[-1]
    kb.bset(tok_crep, 0, 1)

    def bc_row16(src_row, src_off, n, dst16, dst_off, deps, banks=(0, 1)):
        toks = []
        off, bi = 0, 0
        prev_cp = None
        while off < n:
            w = min(512, n - off)
            bki = banks[bi % len(banks)]
            cp = kb.v(lambda e, o=off, w=w: e.tensor_copy(
                bc_stage[0:1, 0:w], src_row[0:1, src_off + o:src_off + o + w]),
                deps=list(deps) + [tok_stage, tok_bcw, prev_cp], inc=True)
            mm = kb.t(lambda e, bk=bki, w=w: e.matmul(
                bank(bk, c1=w), bc_w[0:32, :], bc_stage[0:32, 0:w],
                start=True, stop=True), deps=[cp] + kb.bdep(bki), inc=True)
            prev_cp = mm
            ev = kb.v(lambda e, bk=bki, o=off, w=w: e.tensor_copy(
                dst16[:, dst_off + o:dst_off + o + w], bank(bk, c1=w)),
                deps=[mm], inc=True)
            kb.bset(ev, bki)
            toks.append(ev)
            off += w
            bi += 1
        return toks

    def bc_row32(src_row, src_off, n, dst32, dst_off, deps, bki=0):
        cp = kb.v(lambda e, n=n: e.tensor_copy(
            bc_st32[0:1, 0:n], src_row[0:1, src_off:src_off + n]),
            deps=list(deps) + [tok_st32, tok_bcw32], inc=True)
        mm = kb.t(lambda e, n=n, bk=bki: e.matmul(
            bank(bk, c1=n), bc_w32[0:32, :], bc_st32[0:32, 0:n],
            start=True, stop=True), deps=[cp] + kb.bdep(bki), inc=True)
        ev = kb.v(lambda e, n=n, bk=bki: e.tensor_copy(
            dst32[:, dst_off:dst_off + n], bank(bk, c1=n)), deps=[mm], inc=True)
        kb.bset(ev, bki)
        return ev

    def ln_stats(src16, nch, lnr, sq2, src_deps, lnr_deps=()):
        sq_free = [None, None]
        tok_mm = None
        for ci in range(nch):
            sl = (ci % 2) * T
            tok_sq = kb.s(lambda e, ci=ci, sl=sl: e.activation(
                sq2[:, sl:sl + T], src16[:, ci * T:(ci + 1) * T], AF.Square),
                deps=list(src_deps) + [sq_free[ci % 2]], inc=True)
            for g in range(4):
                dep0 = (kb.bdep(6, 7) + list(src_deps) + [tok_ones]) \
                    if ci == 0 else []
                kb.t(lambda e, ci=ci, g=g: e.matmul(
                    bank(6, r0=32 * g, r1=32 * g + 1), ones1[:, 0:1],
                    src16[:, ci * T + 512 * g: ci * T + 512 * (g + 1)],
                    start=(ci == 0), stop=(ci == nch - 1),
                    tile_position=(0, 32 * g), skip_group_check=True),
                    deps=dep0)
                tok_mm = kb.t(lambda e, g=g, sl=sl, ci=ci: e.matmul(
                    bank(7, r0=32 * g, r1=32 * g + 1), ones1[:, 0:1],
                    sq2[:, sl + 512 * g: sl + 512 * (g + 1)],
                    start=(ci == 0), stop=(ci == nch - 1),
                    tile_position=(0, 32 * g), skip_group_check=True),
                    deps=[tok_sq] + (dep0 if ci == 0 else []),
                    inc=(g == 3))
            sq_free[ci % 2] = tok_mm
        for g in range(4):
            ev1 = kb.v(lambda e, g=g: e.tensor_copy(
                stat_ev[32 * g:32 * g + 1, 0:512],
                bank(6, r0=32 * g, r1=32 * g + 1)),
                deps=([tok_mm] + kb.gd("stat_ev")) if g == 0 else [], inc=True)
        for g in range(4):
            ev2 = kb.v(lambda e, g=g: e.tensor_copy(
                stat_ev[32 * g:32 * g + 1, 512:1024],
                bank(7, r0=32 * g, r1=32 * g + 1)),
                deps=[tok_mm] if g == 0 else [], inc=True)
        kb.bset(ev2, 6, 7)
        g1 = kb.dma("y", lnr[0:1, 0:T], stat_ev[0:97:32, 0:512],
                    deps=[ev1] + list(lnr_deps))
        g2 = kb.dma("y", lnr[0:1, T:2 * T], stat_ev[0:97:32, 512:1024],
                    deps=[ev2] + list(lnr_deps))
        kb.gset("stat_ev", g2)
        inv = 1.0 / (nch * 128)
        kb.v(lambda e: e.tensor_scalar(lnr[0:1, 2 * T:3 * T], lnr[0:1, 0:T],
                                       inv, None, op0=ALU.mult), deps=[g1])
        kb.v(lambda e: e.tensor_scalar(lnr[0:1, T:2 * T], lnr[0:1, T:2 * T],
                                       inv, None, op0=ALU.mult), deps=[g2])
        kb.v(lambda e: e.tensor_tensor(lnr[0:1, 0:T], lnr[0:1, 2 * T:3 * T],
                                       lnr[0:1, 2 * T:3 * T], op=ALU.mult))
        tvar = kb.v(lambda e: e.tensor_tensor(lnr[0:1, T:2 * T],
                                              lnr[0:1, T:2 * T],
                                              lnr[0:1, 0:T],
                                              op=ALU.subtract), inc=True)
        tln = kb.s(lambda e: e.activation(lnr[0:1, 0:T], lnr[0:1, T:2 * T],
                                          AF.Ln, bias=epcol[0:1, 1:2]),
                   deps=[tvar, tok_ep], inc=True)
        tr = kb.s(lambda e: e.activation(lnr[0:1, 0:T], lnr[0:1, 0:T],
                                         AF.Exp, scale=-0.5),
                  deps=[tln], inc=True)
        tmr = kb.v(lambda e: e.tensor_tensor(lnr[0:1, T:2 * T],
                                             lnr[0:1, 2 * T:3 * T],
                                             lnr[0:1, 0:T], op=ALU.mult),
                   deps=[tr], inc=True)
        return tr, tmr

    def ln_bcast(lnr, brw, tok_r, tok_mr):
        tR = bc_row16(lnr, 0, T, brw, 0, [tok_r], banks=(0, 1))
        tMR = bc_row16(lnr, T, T, brw, T, [tok_mr], banks=(2, 3))
        return tR + tMR

    def ln_apply(dst, dst_off, src, src_off, brw, deps):
        t1 = kb.v(lambda e: e.tensor_tensor(
            dst[:, dst_off:dst_off + T], src[:, src_off:src_off + T],
            brw[:, 0:T], op=ALU.mult), deps=deps, inc=True)
        return kb.v(lambda e: e.tensor_tensor(
            dst[:, dst_off:dst_off + T], dst[:, dst_off:dst_off + T],
            brw[:, T:2 * T], op=ALU.subtract), deps=[t1], inc=True)

    def emit_dequant():
        dqw, dw_dep = ar.alloc("dqw", I3, F32)
        dqs, ds_dep = ar.alloc("dqs", I3 // 32, F32)
        dqr, dr_dep = ar.alloc("dqr", I3 // 32, F32)
        dqi, di_dep = ar.alloc("dqi", I3, I32)
        dqq, dq_dep = ar.alloc("dqq", I3, F16)
        dqt, dt_dep = ar.alloc("dqt", I3, F16)
        base_dep = dw_dep + ds_dep + dr_dep + di_dep + dq_dep + dt_dep
        specs = [(d_wup, d_upT, I3, C), (d_wgate, d_gateT, I3, C),
                 (d_wdown, d_downT, C, I3)]
        prev_q = None
        prev_sp = None
        last_sp = None
        for (dsrc, ddst, O, I) in specs:
            nb = I // 32
            nic = I // 128
            for oc in range(O // 128):
                ld = kb.dma("y", dqw[:, 0:I], dsrc[oc * 128:(oc + 1) * 128, :],
                            deps=[prev_q] + base_dep)
                bm = kb.v(lambda e, I=I, nb=nb: e.tensor_reduce(
                    dqs[:, 0:nb],
                    dqw[:, 0:I].rearrange("p (nb b) -> p nb b", b=32),
                    axis=AX, op=ALU.max, apply_absolute_value=True),
                    deps=[ld], inc=True)
                sc = kb.v(lambda e, nb=nb: e.tensor_scalar(
                    dqs[:, 0:nb], dqs[:, 0:nb], 1.0 / 7.0, 1e-12,
                    op0=ALU.mult, op1=ALU.max), deps=[bm], inc=True)
                rc = kb.v(lambda e, nb=nb: e.reciprocal(dqr[:, 0:nb],
                                                        dqs[:, 0:nb]),
                          deps=[sc], inc=True)
                nn = kb.v(lambda e, I=I, nb=nb: e.tensor_tensor(
                    dqi[:, 0:I], dqw[:, 0:I],
                    dqr[:, 0:nb].unsqueeze(2).broadcast_to((128, nb, 32)), op=ALU.mult),
                    deps=[rc], inc=True)
                qq = kb.v(lambda e, I=I, nb=nb: e.tensor_tensor(
                    dqq[:, 0:I], dqi[:, 0:I],
                    dqs[:, 0:nb].unsqueeze(2).broadcast_to((128, nb, 32)), op=ALU.mult),
                    deps=[nn], inc=True)
                prev_q = qq
                ev_toks = []
                for grp in range((nic + 7) // 8):
                    tb = 4 + grp % 2
                    ntr = min(8, nic - grp * 8)
                    tk = None
                    for k in range(ntr):
                        ci = grp * 8 + k
                        tk = kb.t(lambda e, tb=tb, k=k, ci=ci: e.matmul(
                            PS16[:, tb * 1024 + 128 * k:
                                 tb * 1024 + 128 * (k + 1)],
                            dqq[:, ci * 128:(ci + 1) * 128], idf[:, :],
                            start=True, stop=True, is_transpose=True,
                            skip_group_check=True),
                            deps=([qq, tok_id] + kb.bdep(tb)) if k == 0 else [],
                            inc=(k == ntr - 1))
                    ev = kb.v(lambda e, tb=tb, grp=grp, ntr=ntr: e.tensor_copy(
                        dqt[:, grp * 1024: grp * 1024 + ntr * 128],
                        PS16[:, tb * 1024: tb * 1024 + ntr * 128]),
                        deps=[tk, prev_sp], inc=True)
                    kb.bset(ev, tb)
                    ev_toks.append(ev)
                sp = kb.dma("s", ddst.rearrange("(ci p) j -> p ci j", p=128)
                            [:, :, oc * 128:(oc + 1) * 128],
                            dqt[:, 0:I].rearrange("p (ci j) -> p ci j", j=128),
                            deps=ev_toks)
                prev_sp = sp
                last_sp = sp
        return last_sp, ("dqw", "dqs", "dqr", "dqi", "dqq", "dqt")

    def emit_attention(s, first_deps, free_after_qkv=None):
        xln, d5 = ar.alloc(f"xln{s}", NCH * T, F16)
        lnr, d1 = ar.alloc(f"lnr{s}", 3 * T, F32)
        sq2, d2 = ar.alloc(f"sq2{s}", 2 * T, F16)
        xst, d3 = ar.alloc(f"xst{s}", 2 * T, F16)
        brw, d4 = ar.alloc(f"brw{s}", 2 * T, F16)
        adeps = d1 + d2 + d3 + d4 + d5 + list(first_deps)
        xst_free = [None, None]
        st_deps = []
        for ci in range(NCH):
            sl = (ci % 2) * T
            lx = kb.dma("g", xst[:, sl:sl + T],
                        dx[s, ci * 128:(ci + 1) * 128, :],
                        deps=adeps + [xst_free[ci % 2]])
            tok_sq = kb.s(lambda e, sl=sl: e.activation(
                sq2[:, sl:sl + T], xst[:, sl:sl + T], AF.Square),
                deps=[lx, xst_free[ci % 2]], inc=True)
            for g in range(4):
                dep0 = (kb.bdep(6, 7) + [tok_ones]) if ci == 0 else []
                t1 = kb.t(lambda e, g=g, sl=sl, ci=ci: e.matmul(
                    bank(6, r0=32 * g, r1=32 * g + 1), ones1[:, 0:1],
                    xst[:, sl + 512 * g: sl + 512 * (g + 1)],
                    start=(ci == 0), stop=(ci == NCH - 1),
                    tile_position=(0, 32 * g), skip_group_check=True),
                    deps=dep0 + [lx], inc=(g == 3))
                t2 = kb.t(lambda e, g=g, sl=sl, ci=ci: e.matmul(
                    bank(7, r0=32 * g, r1=32 * g + 1), ones1[:, 0:1],
                    sq2[:, sl + 512 * g: sl + 512 * (g + 1)],
                    start=(ci == 0), stop=(ci == NCH - 1),
                    tile_position=(0, 32 * g), skip_group_check=True),
                    deps=[tok_sq] + dep0, inc=(g == 3))
            xst_free[ci % 2] = t2
            st_deps.append(t2)
        tok_mm = st_deps[-1]
        for g in range(4):
            ev1 = kb.v(lambda e, g=g: e.tensor_copy(
                stat_ev[32 * g:32 * g + 1, 0:512],
                bank(6, r0=32 * g, r1=32 * g + 1)),
                deps=([tok_mm] + kb.gd("stat_ev")) if g == 0 else [], inc=True)
        for g in range(4):
            ev2 = kb.v(lambda e, g=g: e.tensor_copy(
                stat_ev[32 * g:32 * g + 1, 512:1024],
                bank(7, r0=32 * g, r1=32 * g + 1)),
                deps=[tok_mm] if g == 0 else [], inc=True)
        kb.bset(ev2, 6, 7)
        g1 = kb.dma("y", lnr[0:1, 0:T], stat_ev[0:97:32, 0:512], deps=[ev1])
        g2 = kb.dma("y", lnr[0:1, T:2 * T], stat_ev[0:97:32, 512:1024],
                    deps=[ev2])
        kb.gset("stat_ev", g2)
        inv = 1.0 / C
        kb.v(lambda e: e.tensor_scalar(lnr[0:1, 2 * T:3 * T], lnr[0:1, 0:T],
                                       inv, None, op0=ALU.mult), deps=[g1])
        kb.v(lambda e: e.tensor_scalar(lnr[0:1, T:2 * T], lnr[0:1, T:2 * T],
                                       inv, None, op0=ALU.mult), deps=[g2])
        kb.v(lambda e: e.tensor_tensor(lnr[0:1, 0:T], lnr[0:1, 2 * T:3 * T],
                                       lnr[0:1, 2 * T:3 * T], op=ALU.mult))
        tvar = kb.v(lambda e: e.tensor_tensor(lnr[0:1, T:2 * T],
                                              lnr[0:1, T:2 * T],
                                              lnr[0:1, 0:T],
                                              op=ALU.subtract), inc=True)
        tln = kb.s(lambda e: e.activation(lnr[0:1, 0:T], lnr[0:1, T:2 * T],
                                          AF.Ln, bias=epcol[0:1, 1:2]),
                   deps=[tvar, tok_ep], inc=True)
        tr = kb.s(lambda e: e.activation(lnr[0:1, 0:T], lnr[0:1, 0:T],
                                         AF.Exp, scale=-0.5),
                  deps=[tln], inc=True)
        tmr = kb.v(lambda e: e.tensor_tensor(lnr[0:1, T:2 * T],
                                             lnr[0:1, 2 * T:3 * T],
                                             lnr[0:1, 0:T], op=ALU.mult),
                   deps=[tr], inc=True)
        bts = ln_bcast(lnr, brw, tr, tmr)
        xln_toks = []
        xst_free2 = [st_deps[-1], st_deps[-1]]
        for ci in range(NCH):
            sl = (ci % 2) * T
            lx = kb.dma("g", xst[:, sl:sl + T],
                        dx[s, ci * 128:(ci + 1) * 128, :],
                        deps=[xst_free2[ci % 2]] + bts)
            tk = ln_apply(xln, ci * T, xst, sl, brw, [lx] + bts)
            xst_free2[ci % 2] = tk
            xln_toks.append(tk)
        tap(f"xln{s}", xln[:, :], xln_toks)
        ar.free(f"lnr{s}", xln_toks[-1])
        ar.free(f"sq2{s}", xln_toks[-1])
        ar.free(f"xst{s}", xln_toks[-1])
        ar.free(f"brw{s}", xln_toks[-1])
        if _ATTN_CUT == 1:
            return xln_toks

        wst, d7 = ar.alloc(f"wst{s}", 2 * NCH * 128, F16)
        qkvb, d6 = ar.alloc(f"qkv{s}", 3 * NCH * T, F16)
        qb = qkvb[:, 0:NCH * T]
        kbu = qkvb[:, NCH * T:2 * NCH * T]
        vbu = qkvb[:, 2 * NCH * T:3 * NCH * T]
        qkv_ev = []
        wld = {}
        for J in range(NI):
            ws = J % 2
        wfree = {}
        for J in range(NI):
            ws = J % 2
            wld[J] = kb.dma("g",
                            wst[:, ws * 1024:(ws + 1) * 1024]
                            .rearrange("p (ci j) -> p ci j", j=128),
                            d_wattn.rearrange("(ci p) j -> p ci j", p=128)
                            [:, :, J * 128:(J + 1) * 128],
                            deps=d7 + d6 + [wfree.get(J - 2)])
            dci = J % NCH
            dstoff = (J // NCH) * NCH * T + dci * T
            for tg in range(4):
                bk = (J * 4 + tg) % 4
                tkmm = None
                for ci in range(NCH):
                    tkmm = kb.t(lambda e, J=J, ws=ws, tg=tg, ci=ci, bk=bk:
                                e.matmul(
                        bank(bk),
                        wst[:, ws * 1024 + ci * 128: ws * 1024 + (ci + 1) * 128],
                        xln[:, ci * T + tg * 512: ci * T + (tg + 1) * 512],
                        start=(ci == 0), stop=(ci == NCH - 1)),
                        deps=(kb.bdep(bk) + [wld[J], wfree.get(J - 2)]
                              + xln_toks) if ci == 0 else [],
                        inc=(ci == NCH - 1))
                ev = kb.s(lambda e, off=dstoff, tg=tg, bk=bk, J=J: e.activation(
                    qkvb[:, off + tg * 512: off + (tg + 1) * 512],
                    bank(bk), AF.Identity, bias=b_attn[:, J:J + 1]),
                    deps=[tkmm, tok_battn], inc=True)
                kb.bset(ev, bk)
                qkv_ev.append(ev)
            wfree[J] = qkv_ev[-1]
        ar.free(f"wst{s}", qkv_ev[-1])
        ar.free(f"xln{s}", qkv_ev[-1])
        if free_after_qkv is not None:
            for nm in free_after_qkv[0]:
                ar.free(nm, free_after_qkv[1])
        tap(f"qb{s}", qb, [qkv_ev[-1]])
        tap(f"kbu{s}", kbu, [qkv_ev[-1]])
        tap(f"vbu{s}", vbu, [qkv_ev[-1]])
        if _ATTN_CUT == 2:
            return [qkv_ev[-1]]

        ccst, d8 = ar.alloc(f"ccst{s}", NH * HS, F16)
        conv_ev, d9 = ar.alloc(f"cev{s}", 2 * 4 * NB, F16)
        sqn, d10 = ar.alloc(f"sqn{s}", 2 * T, F16)
        cdeps = d8 + d9 + d10
        kc_toks, vc_toks, kcn_tok = emit_compress(
            s, qb, kbu, vbu, ccst, conv_ev, sqn, qkv_ev, cdeps)
        if _ATTN_CUT == 3 or (_ATTN_CUT or 0) > 300:
            return [t for t in kc_toks + vc_toks + [kcn_tok]
                    if t is not None]

        tok_mn = emit_norms(s, qb, kbu, sqn, qkv_ev, kcn_tok)
        ar.free(f"ccst{s}", kc_toks[-1])
        ar.free(f"cev{s}", kc_toks[-1])
        ar.free(f"sqn{s}", tok_mn)
        if _ATTN_CUT == 4:
            return [tok_mn]

        locb, d11 = ar.alloc(f"loc{s}", NCH * T, F16)
        cmpb, d12 = ar.alloc(f"cmp{s}", NCH * T, F16)
        loc_toks, cmp_toks = emit_attn_core(
            s, qb, kbu, vbu, locb, cmpb, kc_toks, vc_toks, tok_mn, qkv_ev,
            d11 + d12)
        ar.free(f"qkv{s}", cmp_toks[-1])
        if _ATTN_CUT == 5:
            return loc_toks + cmp_toks

        lnmb, d13 = ar.alloc(f"lnm{s}", NCH * T, F16)
        lnrb, d14 = ar.alloc(f"lnr2{s}", 3 * T, F32)
        sq2b_, d15 = ar.alloc(f"sq2b{s}", 2 * T, F16)
        brwb, d16 = ar.alloc(f"brw2{s}", 2 * T, F16)
        ldeps = d13 + d14 + d15 + d16 + loc_toks + cmp_toks
        tr, tmr = ln_stats(locb, NCH, lnrb, sq2b_, ldeps)
        bts = ln_bcast(lnrb, brwb, tr, tmr)
        lnm_toks = []
        for ci in range(NCH):
            tt = ln_apply(lnmb, ci * T, locb, ci * T, brwb, ldeps + bts)
            tt2 = kb.v(lambda e, ci=ci: e.tensor_scalar(
                lnmb[:, ci * T:(ci + 1) * T], lnmb[:, ci * T:(ci + 1) * T],
                lnl_w[:, ci:ci + 1], lnl_b[:, ci:ci + 1],
                op0=ALU.mult, op1=ALU.add), deps=[tt, tok_lnlw, tok_lnlb],
                inc=True)
            lnm_toks.append(tt2)
        tap(f"lnm{s}", lnmb[:, :], lnm_toks)
        if _ATTN_CUT == 6:
            return lnm_toks

        wg1, dwg1 = ar.alloc(f"wg1{s}", 8 * C, F16)
        tok_wg1 = kb.dma("g",
                         wg1[:, :].rearrange("p (f j) -> p f j", j=C),
                         d_gw1.rearrange("(f p) j -> p f j", p=128)[:, 0:8, :],
                         deps=dwg1)
        tok_g = emit_gate(s, lnmb, cmpb, lnrb, lnm_toks, cmp_toks, wg1,
                          tok_wg1, d_gw1)
        ar.free(f"wg1{s}", tok_g)
        if _ATTN_CUT == 7:
            return [tok_g]

        comb_toks = []
        for ci in range(NCH):
            t1 = kb.v(lambda e, ci=ci: e.tensor_scalar(
                cmpb[:, ci * T:(ci + 1) * T], cmpb[:, ci * T:(ci + 1) * T],
                gbc[:, 1:2], None, op0=ALU.mult),
                deps=[tok_g], inc=True)
            t2 = kb.v(lambda e, ci=ci: e.scalar_tensor_tensor(
                locb[:, ci * T:(ci + 1) * T], lnmb[:, ci * T:(ci + 1) * T],
                gbc[:, 0:1], cmpb[:, ci * T:(ci + 1) * T],
                op0=ALU.mult, op1=ALU.add), deps=[t1], inc=True)
            comb_toks.append(t2)
        ar.free(f"cmp{s}", comb_toks[-1])
        tap(f"comb{s}", locb[:, :], comb_toks)
        if _ATTN_CUT == 8:
            return comb_toks

        wstp, d17 = ar.alloc(f"wstp{s}", 2 * NCH * 128, F16)
        proj_ev = []
        pwld = {}
        pwfree = {}
        for J in range(NCH):
            ws = J % 2
            pwld[J] = kb.dma("g",
                             wstp[:, ws * 1024:(ws + 1) * 1024]
                             .rearrange("p (ci j) -> p ci j", j=128),
                             d_wproj.rearrange("(ci p) j -> p ci j", p=128)
                             [:, :, J * 128:(J + 1) * 128],
                             deps=d17 + [pwfree.get(J - 2)])
            for tg in range(4):
                bk = (J * 4 + tg) % 4
                tkmm = None
                for ci in range(NCH):
                    tkmm = kb.t(lambda e, ws=ws, J=J, tg=tg, ci=ci, bk=bk:
                                e.matmul(
                        bank(bk),
                        wstp[:, ws * 1024 + ci * 128: ws * 1024 + (ci + 1) * 128],
                        locb[:, ci * T + tg * 512: ci * T + (tg + 1) * 512],
                        start=(ci == 0), stop=(ci == NCH - 1)),
                        deps=(kb.bdep(bk) + [pwld[J]] + comb_toks)
                        if ci == 0 else [], inc=(ci == NCH - 1))
                ev = kb.s(lambda e, J=J, tg=tg, bk=bk: e.activation(
                    lnmb[:, J * T + tg * 512: J * T + (tg + 1) * 512],
                    bank(bk), AF.Identity, bias=b_proj[:, J:J + 1]),
                    deps=[tkmm, tok_bproj] + lnm_toks, inc=True)
                kb.bset(ev, bk)
                proj_ev.append(ev)
            pwfree[J] = proj_ev[-1]
        ar.free(f"wstp{s}", proj_ev[-1])
        tap(f"attn{s}", lnmb[:, :], [proj_ev[-1]])
        if _ATTN_CUT == 9:
            return [proj_ev[-1]]

        x2c, d18 = ar.alloc(f"x2c{s}", 2 * T, F32)
        tr, tmr = ln_stats(lnmb, NCH, lnrb, sq2b_, proj_ev + d18,
                           lnr_deps=[tok_g])
        bts = ln_bcast(lnrb, brwb, tr, tmr)
        x2_toks = []
        ln2_mm = None
        ln2_mm6 = None
        prev_ld = None
        prev_cst = [None]
        for ci in range(NCH):
            u = ln_apply(locb, ci * T, lnmb, ci * T, brwb,
                         proj_ev + bts + comb_toks)
            xr = kb.dma("y", x2c[:, 0:T], dx[s, ci * 128:(ci + 1) * 128, :],
                        deps=[prev_ld])
            ad = kb.v(lambda e, ci=ci: e.tensor_tensor(
                x2c[:, T:2 * T], locb[:, ci * T:(ci + 1) * T],
                x2c[:, 0:T], op=ALU.add),
                deps=[u, xr, prev_cst[0]], inc=True)
            st = kb.dma("y", d_x2[s, ci * 128:(ci + 1) * 128, :],
                        x2c[:, T:2 * T], deps=[ad])
            prev_ld = st
            cst = kb.s(lambda e: e.activation(
                sq2b_[:, 0:T], x2c[:, T:2 * T], AF.Identity),
                deps=[ad, ln2_mm6, ln2_mm], inc=True)
            prev_cst[0] = cst
            sqq = kb.s(lambda e: e.activation(
                sq2b_[:, T:2 * T], sq2b_[:, 0:T], AF.Square),
                deps=[cst, ln2_mm], inc=True)
            for g in range(4):
                dep0 = (kb.bdep(6, 7)) if ci == 0 else []
                ln2_mm6 = kb.t(lambda e, g=g, ci=ci: e.matmul(
                    bank(6, r0=32 * g, r1=32 * g + 1), ones1[:, 0:1],
                    sq2b_[:, 512 * g: 512 * (g + 1)],
                    start=(ci == 0), stop=(ci == NCH - 1),
                    tile_position=(0, 32 * g), skip_group_check=True),
                    deps=dep0 + [cst], inc=(g == 3))
                ln2_mm = kb.t(lambda e, g=g, ci=ci: e.matmul(
                    bank(7, r0=32 * g, r1=32 * g + 1), ones1[:, 0:1],
                    sq2b_[:, T + 512 * g: T + 512 * (g + 1)],
                    start=(ci == 0), stop=(ci == NCH - 1),
                    tile_position=(0, 32 * g), skip_group_check=True),
                    deps=dep0 + [sqq], inc=(g == 3))
            x2_toks.append(st)
        for g in range(4):
            ev1 = kb.v(lambda e, g=g: e.tensor_copy(
                stat_ev[32 * g:32 * g + 1, 0:512],
                bank(6, r0=32 * g, r1=32 * g + 1)),
                deps=([ln2_mm] + kb.gd("stat_ev")) if g == 0 else [], inc=True)
        for g in range(4):
            ev2 = kb.v(lambda e, g=g: e.tensor_copy(
                stat_ev[32 * g:32 * g + 1, 512:1024],
                bank(7, r0=32 * g, r1=32 * g + 1)),
                deps=[ln2_mm] if g == 0 else [], inc=True)
        kb.bset(ev2, 6, 7)
        g1 = kb.dma("y", lnrb[0:1, 0:T], stat_ev[0:97:32, 0:512], deps=[ev1])
        g2 = kb.dma("y", lnrb[0:1, T:2 * T], stat_ev[0:97:32, 512:1024],
                    deps=[ev2])
        kb.gset("stat_ev", g2)
        inv = 1.0 / C
        kb.v(lambda e: e.tensor_scalar(lnrb[0:1, 2 * T:3 * T], lnrb[0:1, 0:T],
                                       inv, None, op0=ALU.mult), deps=[g1])
        kb.v(lambda e: e.tensor_scalar(lnrb[0:1, T:2 * T], lnrb[0:1, T:2 * T],
                                       inv, None, op0=ALU.mult), deps=[g2])
        kb.v(lambda e: e.tensor_tensor(lnrb[0:1, 0:T], lnrb[0:1, 2 * T:3 * T],
                                       lnrb[0:1, 2 * T:3 * T], op=ALU.mult))
        tvar = kb.v(lambda e: e.tensor_tensor(lnrb[0:1, T:2 * T],
                                              lnrb[0:1, T:2 * T],
                                              lnrb[0:1, 0:T],
                                              op=ALU.subtract), inc=True)
        tln = kb.s(lambda e: e.activation(lnrb[0:1, 0:T], lnrb[0:1, T:2 * T],
                                          AF.Ln, bias=epcol[0:1, 1:2]),
                   deps=[tvar, tok_ep], inc=True)
        trr = kb.s(lambda e: e.activation(lnrb[0:1, 0:T], lnrb[0:1, 0:T],
                                          AF.Exp, scale=-0.5),
                   deps=[tln], inc=True)
        sv1 = kb.dma("y", d_ln2[0:1, s * 2 * T: s * 2 * T + T],
                     lnrb[0:1, 2 * T:3 * T], deps=[trr])
        sv2 = kb.dma("y", d_ln2[0:1, s * 2 * T + T: s * 2 * T + 2 * T],
                     lnrb[0:1, 0:T], deps=[trr, sv1])
        end = sv2
        for nm in (f"lnm{s}", f"lnr2{s}", f"sq2b{s}", f"brw2{s}", f"x2c{s}",
                   f"loc{s}"):
            ar.free(nm, end)
        return x2_toks + [sv1, sv2]

    def emit_compress(s, qb, kbu, vbu, ccst, conv_ev, sqn, qkv_ev, cdeps):
        kc_toks, vc_toks, kcn = [], [], [None]
        for (zbuf, is_k) in ((kbu, True), (vbu, False)):
            mm_done = []
            for tap_ in range(KTAP):
                for halfc in range(2):
                    for par in range(2):
                        bk = halfc + 2 * par
                        deps = (qkv_ev + [tok_wconv] + kb.bdep(bk) + cdeps) \
                            if tap_ == 0 else []
                        tk = kb.t(lambda e, tap_=tap_, halfc=halfc, par=par,
                                  bk=bk, z=zbuf: e.matmul(
                            bank(bk, r0=0, r1=64, c1=508),
                            w_conv[par * 64:(par + 1) * 64,
                                   tap_ * 64:(tap_ + 1) * 64],
                            z[par * 64:(par + 1) * 64, :]
                            .rearrange("p (ci t) -> p ci t", t=T)
                            [:, 4 * halfc:4 * halfc + 4,
                             tap_:tap_ + CSTR * 126 + 1:CSTR],
                            start=(tap_ == 0), stop=(tap_ == KTAP - 1),
                            tile_position=(par * 64, 0)),
                            deps=deps, inc=(tap_ == KTAP - 1))
                        if tap_ == KTAP - 1:
                            mm_done.append(tk)
            ev_toks = []
            for halfc in range(2):
                for par in range(2):
                    bk = halfc + 2 * par
                    ev = kb.s(lambda e, halfc=halfc, par=par, bk=bk:
                              e.activation(
                        conv_ev[par * 64:(par + 1) * 64,
                                halfc * 508:(halfc + 1) * 508],
                        bank(bk, r0=0, r1=64, c1=508), AF.Identity,
                        bias=b_conv[par * 64:(par + 1) * 64, :]),
                        deps=mm_done + [tok_bconv], inc=True)
                    kb.bset(ev, bk)
                    ev_toks.append(ev)
            if _ATTN_CUT == 301:
                return ev_toks, [], None
            if _ATTN_CUT == 305 and not is_k:
                return kc_toks, [ev_toks[-1]], kcn[0]
            tk = None
            for par in range(2):
                hs = [h for h in range(NH) if h % 2 == par]
                for j, h in enumerate(hs):
                    halfc, pos = divmod(h // 2, 4)
                    first = (par == 0 and j == 0)
                    tk = kb.t(lambda e, h=h, par=par, halfc=halfc, pos=pos:
                              e.matmul(
                        PS16[0:NB, 4 * 1024 + 64 * h: 4 * 1024 + 64 * (h + 1)],
                        conv_ev[par * 64:(par + 1) * 64,
                                halfc * 508 + pos * 127:
                                halfc * 508 + (pos + 1) * 127],
                        idf[par * 64:(par + 1) * 64, par * 64:par * 64 + 64],
                        start=True, stop=True, is_transpose=True,
                        skip_group_check=True),
                        deps=(ev_toks + [tok_id] + kb.bdep(4)) if first
                        else ([tk] if j == 0 else []),
                        inc=(j == len(hs) - 1))
            if _ATTN_CUT == 302:
                ev302 = kb.v(lambda e: e.tensor_copy(
                    ccst[0:NB, :], PS16[0:NB, 4 * 1024: 4 * 1024 + 1024]),
                    deps=[tk], inc=True)
                kb.bset(ev302, 4)
                return [ev302], [], None
            t_sz = kb.v(lambda e: e.tensor_reduce(
                red16[0:NB, 0:NH],
                PS16[0:NB, 4 * 1024: 4 * 1024 + 1024]
                .rearrange("p (h c) -> p h c", c=64), axis=AX, op=ALU.add),
                deps=[tk], inc=True)
            if _ATTN_CUT == 3031:
                kb.bset(t_sz, 4)
                return [t_sz], [], None
            t_sq = kb.s(lambda e: e.activation(
                ccst[0:NB, :], PS16[0:NB, 4 * 1024: 4 * 1024 + 1024],
                AF.Square), deps=[tk, t_sz], inc=True)
            if _ATTN_CUT == 3032:
                kb.bset(t_sq, 4)
                return [t_sz, t_sq], [], None
            t_s2 = kb.v(lambda e: e.tensor_reduce(
                red16[0:NB, NH:2 * NH],
                ccst[0:NB, :].rearrange("p (h c) -> p h c", c=64),
                axis=AX, op=ALU.add), deps=[t_sq], inc=True)
            kb.v(lambda e: e.tensor_scalar(red16[0:NB, 0:NH],
                                           red16[0:NB, 0:NH], 1.0 / 64, None,
                                           op0=ALU.mult), deps=[t_sz])
            kb.v(lambda e: e.tensor_scalar(red16[0:NB, NH:2 * NH],
                                           red16[0:NB, NH:2 * NH], 1.0 / 64,
                                           None, op0=ALU.mult), deps=[t_s2])
            kb.v(lambda e: e.tensor_tensor(red16[0:NB, 2 * NH:3 * NH],
                                           red16[0:NB, 0:NH],
                                           red16[0:NB, 0:NH], op=ALU.mult))
            kb.v(lambda e: e.tensor_tensor(red16[0:NB, NH:2 * NH],
                                           red16[0:NB, NH:2 * NH],
                                           red16[0:NB, 2 * NH:3 * NH],
                                           op=ALU.subtract))
            kb.v(lambda e: e.tensor_scalar(red16[0:NB, NH:2 * NH],
                                           red16[0:NB, NH:2 * NH], EPS, None,
                                           op0=ALU.add))
            t_iv = kb.v(lambda e: e.reciprocal(red16[0:NB, NH:2 * NH],
                                               red16[0:NB, NH:2 * NH]),
                        inc=True)
            t_r = kb.s(lambda e: e.activation(red16[0:NB, NH:2 * NH],
                                              red16[0:NB, NH:2 * NH],
                                              AF.Sqrt), deps=[t_iv], inc=True)
            t_nmr = kb.v(lambda e: e.tensor_tensor(
                red16[0:NB, 2 * NH:3 * NH], red16[0:NB, 0:NH],
                red16[0:NB, NH:2 * NH], op=ALU.mult), deps=[t_r], inc=True)
            t_nm2 = kb.v(lambda e: e.tensor_scalar(
                red16[0:NB, 2 * NH:3 * NH], red16[0:NB, 2 * NH:3 * NH],
                -1.0, None, op0=ALU.mult), deps=[t_nmr], inc=True)
            if _ATTN_CUT == 3033:
                kb.bset(t_nm2, 4)
                return [t_nm2, t_sz], [], None
            nrm = []
            for h in range(NH):
                nrm.append(kb.v(lambda e, h=h: e.tensor_scalar(
                    ccst[0:NB, h * 64:(h + 1) * 64],
                    PS16[0:NB, 4 * 1024 + 64 * h: 4 * 1024 + 64 * (h + 1)],
                    red16[0:NB, NH + h:NH + h + 1],
                    red16[0:NB, 2 * NH + h:2 * NH + h + 1],
                    op0=ALU.mult, op1=ALU.add), deps=[t_nm2], inc=True))
            kb.bset(nrm[-1], 4)
            if _ATTN_CUT == 3034:
                return [nrm[-1], t_sz], [], None
            t_gm = kb.v(lambda e: e.tensor_tensor(
                ccst[0:NB, :], ccst[0:NB, :], crep_w[0:NB, :], op=ALU.mult),
                deps=nrm + [tok_crep], inc=True)
            t_gb = kb.v(lambda e: e.tensor_tensor(
                ccst[0:NB, :], ccst[0:NB, :], crep_b[0:NB, :], op=ALU.add),
                deps=[t_gm], inc=True)
            if _ATTN_CUT == 303:
                kb.bset(t_gb, 4)
                return [t_gb], [], None
            if _ATTN_CUT == 306 and not is_k:
                kb.bset(t_gb, 4)
                return kc_toks, [t_gb], kcn[0]
            if is_k:
                t_act = kb.s(lambda e: e.activation(
                    ccst[0:NB, :], ccst[0:NB, :], AF.Gelu), deps=[t_gb],
                    inc=True)
                t_k2 = kb.s(lambda e: e.activation(
                    sqn[0:NB, 0:NH * HS], ccst[0:NB, :], AF.Square),
                    deps=[t_act], inc=True)
                kcn[0] = kb.v(lambda e: e.tensor_reduce(
                    red16[0:NB, 3 * NH:4 * NH],
                    sqn[0:NB, 0:NH * HS].rearrange("p (h c) -> p h c", c=64),
                    axis=AX, op=ALU.add), deps=[t_k2], inc=True)
                for h in range(NH):
                    par, ci = h % 2, h // 2
                    tk2 = kb.t(lambda e, h=h: e.matmul(
                        PS16[0:64, 5 * 1024: 5 * 1024 + NB],
                        ccst[0:NB, h * 64:(h + 1) * 64], idf[0:NB, 0:NB],
                        start=True, stop=True, is_transpose=True,
                        skip_group_check=True),
                        deps=[t_act, tok_id] + kb.bdep(5), inc=True)
                    ev3 = kb.v(lambda e, par=par, ci=ci: e.tensor_copy(
                        kcomp[par * 64:(par + 1) * 64,
                              ci * NB:(ci + 1) * NB],
                        PS16[0:64, 5 * 1024: 5 * 1024 + NB]), deps=[tk2],
                        inc=True)
                    kb.bset(ev3, 5)
                    kc_toks.append(ev3)
                tap(f"kcomp{s}", kcomp[:, :], [kc_toks[-1]])
                if _ATTN_CUT == 304:
                    return kc_toks, [], kcn[0]
            else:
                t_act = kb.s(lambda e: e.activation(
                    vcompT[0:NB, :].rearrange("p (h c) -> p h c", c=65)
                    [:, :, 0:64],
                    ccst[0:NB, :].rearrange("p (h c) -> p h c", c=64),
                    AF.Gelu), deps=[t_gb, tok_vc0], inc=True)
                vc_toks = [t_act]
                kb.bset(t_act, 4)
                tap(f"vcompT{s}", vcompT[:, :], vc_toks)
        return kc_toks, vc_toks, kcn[0]

    def emit_norms(s, qb, kbu, sqn, qkv_ev, kcn_tok):
        sq_free = [None, None]
        last_rd = None
        for (zbuf, base_off) in ((qb, RQ), (kbu, RK)):
            for ci in range(NCH):
                sl = (ci % 2) * T
                tsq = kb.s(lambda e, z=zbuf, ci=ci, sl=sl: e.activation(
                    sqn[:, sl:sl + T], z[:, ci * T:(ci + 1) * T], AF.Square),
                    deps=qkv_ev + [sq_free[ci % 2]], inc=True)
                for par in range(2):
                    h = ci * 2 + par
                    mm_last = None
                    for tg in range(4):
                        mm_last = kb.t(lambda e, par=par, sl=sl, tg=tg:
                                       e.matmul(
                            bank(4 + tg, r0=0, r1=1),
                            ones1[par * 64:(par + 1) * 64, 0:1],
                            sqn[par * 64:(par + 1) * 64,
                                sl + tg * 512: sl + (tg + 1) * 512],
                            start=True, stop=True,
                            tile_position=(par * 64, 0)),
                            deps=[tsq] + kb.bdep(4 + tg), inc=(tg == 3))
                    rd = kb.v(lambda e, off=base_off, h=h: e.tensor_reduce(
                        nrow[0:1, off + h: off + h + 1],
                        PS[0:1, 4 * 512: 8 * 512], axis=AX, op=ALU.max),
                        deps=[mm_last], inc=True)
                    kb.bset(rd, 4, 5, 6, 7)
                    last_rd = rd
                sq_free[ci % 2] = last_rd
        kc_rd = None
        for h in range(NH):
            gdt = kb.dma("y", stat_ev[0:1, 0:NB],
                         red16[0:NB, 3 * NH + h:3 * NH + h + 1],
                         deps=[kcn_tok, kc_rd] + kb.gd("stat_ev"))
            kc_rd = kb.v(lambda e, h=h: e.tensor_reduce(
                nrow[0:1, RKC + h:RKC + h + 1], stat_ev[0:1, 0:NB],
                axis=AX, op=ALU.max), deps=[gdt], inc=True)
        kb.gset("stat_ev", kc_rd)
        t_sqr = kb.s(lambda e: e.activation(nrow[0:1, RQ:RQ + 3 * NH],
                                            nrow[0:1, RQ:RQ + 3 * NH],
                                            AF.Sqrt),
                     deps=[kc_rd, last_rd], inc=True)
        kb.v(lambda e: e.tensor_tensor(nrow[0:1, RM:RM + NH],
                                       nrow[0:1, RQ:RQ + NH],
                                       nrow[0:1, RK:RK + NH], op=ALU.mult),
             deps=[t_sqr])
        kb.v(lambda e: e.tensor_tensor(nrow[0:1, RM + NH:RM + 2 * NH],
                                       nrow[0:1, RQ:RQ + NH],
                                       nrow[0:1, RKC:RKC + NH], op=ALU.mult),
             deps=[t_sqr])
        t_M = kb.v(lambda e: e.tensor_scalar(nrow[0:1, RM:RM + 2 * NH],
                                             nrow[0:1, RM:RM + 2 * NH],
                                             -0.125, None, op0=ALU.mult),
                   inc=True)
        tok_mn = bc_row32(nrow, RM, 32, mn_bc, 0, [t_M], bki=0)
        tap(f"mrow{s}", nrow[0:1, 0:5 * NH], [tok_mn])
        return tok_mn

    def emit_attn_core(s, qb, kbu, vbu, locb, cmpb, kc_toks, vc_toks,
                       tok_mn, qkv_ev, adeps):
        loc_toks, cmp_toks = [], []
        for branch in ("local", "comp"):
            dst = locb if branch == "local" else cmpb
            klen = 128 if branch == "local" else NB
            for ch in range(NCH):
                pair_toks = []
                vt_ev = []
                if branch == "local":
                    for par in range(2):
                        tk = None
                        for kt in range(16):
                            tk = kb.t(lambda e, par=par, kt=kt, ch=ch:
                                      e.matmul(
                                PS16[0:128, 4 * 1024 + kt * 64:
                                     4 * 1024 + (kt + 1) * 64],
                                vbu[par * 64:(par + 1) * 64,
                                    ch * T + kt * 128:
                                    ch * T + (kt + 1) * 128],
                                idf[par * 64:(par + 1) * 64,
                                    par * 64:par * 64 + 64],
                                start=True, stop=True, is_transpose=True,
                                skip_group_check=True),
                                deps=(qkv_ev + [tok_id] + kb.bdep(4))
                                if kt == 0 else [], inc=(kt == 15))
                        ev = kb.v(lambda e, par=par: e.tensor_copy(
                            vt[:, par * 1040:(par + 1) * 1040]
                            .rearrange("p (k c) -> p k c", c=65)[:, :, 0:64],
                            PS16[0:128, 4 * 1024: 4 * 1024 + 1024]
                            .rearrange("p (k c) -> p k c", c=64)),
                            deps=[tk, tok_vt0], inc=True)
                        kb.bset(ev, 4)
                        vt_ev.append(ev)
                for half in range(2):
                    q0 = half * 1024
                    ktmax = (8 * half + 8) if branch == "local" else 1
                    units = []
                    for kt in range(ktmax):
                        for qc in range(2):
                            cs = q0 + qc * 512
                            lo = max(kt * 128, cs) if branch == "local" else cs
                            if lo >= cs + 512:
                                continue
                            units.append((kt, qc, lo, cs + 512 - lo))
                    firstu, lastu = {}, {}
                    for i, (kt, qc, lo, w) in enumerate(units):
                        lastu[qc] = i
                        if qc not in firstu:
                            firstu[qc] = i
                    exp_toks, pv_toks = {}, {}
                    for i, (kt, qc, lo, w) in enumerate(units):
                        slot = i % 2
                        for par in range(2):
                            h = ch * 2 + par
                            sbk = slot + 2 * par
                            deps_s = kb.bdep(sbk) + [tok_mn] + qkv_ev + adeps
                            if branch == "comp":
                                deps_s = deps_s + kc_toks
                            if branch == "local":
                                stat = kbu[par * 64:(par + 1) * 64,
                                           ch * T + kt * 128:
                                           ch * T + kt * 128 + 128]
                            else:
                                stat = kcomp[par * 64:(par + 1) * 64,
                                             ch * NB:(ch + 1) * NB]
                            mm = kb.t(lambda e, stat=stat, par=par, ch=ch,
                                      lo=lo, w=w, sbk=sbk, klen=klen:
                                      e.matmul(
                                bank(sbk, r0=0, r1=klen, c1=w), stat,
                                qb[par * 64:(par + 1) * 64,
                                   ch * T + lo: ch * T + lo + w],
                                start=True, stop=True,
                                tile_position=(par * 64, 0)),
                                deps=deps_s + [exp_toks.get((i - 2, par))],
                                inc=True)
                            mcol = h if branch == "local" else NH + h
                            pslot = slot + 2 * par
                            ex = kb.s(lambda e, pslot=pslot, w=w, sbk=sbk,
                                      mcol=mcol, klen=klen: e.activation(
                                pt[0:klen, pslot * 512: pslot * 512 + w],
                                bank(sbk, r0=0, r1=klen, c1=w), AF.Exp,
                                bias=mn_bc[0:klen, mcol:mcol + 1],
                                scale=0.125),
                                deps=[mm, pv_toks.get((i - 2, par))],
                                inc=True)
                            msk = None
                            if branch == "local" and lo == kt * 128:
                                msk = kb.g(lambda e, pslot=pslot:
                                           e.affine_select(
                                    pt[0:128, pslot * 512: pslot * 512 + 128],
                                    pt[0:128, pslot * 512: pslot * 512 + 128],
                                    pattern=[[1, 128]], compare_op=ALU.is_ge,
                                    fill=0.0, base=0, channel_multiplier=-1),
                                    deps=[ex], inc=True)
                            if branch == "comp" and half == 0 and qc == 0:
                                msk = kb.g(lambda e, pslot=pslot:
                                           e.affine_select(
                                    pt[0:NB, pslot * 512: pslot * 512 + NB],
                                    pt[0:NB, pslot * 512: pslot * 512 + NB],
                                    pattern=[[1, NB]], compare_op=ALU.is_ge,
                                    fill=0.0, base=0, channel_multiplier=-1),
                                    deps=[ex], inc=True)
                            exp_toks[(i, par)] = msk or ex
                            ob = 4 + 2 * par
                            col = lo - q0
                            if branch == "local":
                                pvl = vt[:, par * 1040 + kt * 65:
                                         par * 1040 + (kt + 1) * 65]
                            else:
                                pvl = vcompT[0:NB, h * 65:(h + 1) * 65]
                            dep_pv = [exp_toks[(i, par)]]
                            if i == firstu[qc]:
                                dep_pv += kb.bdep(ob + qc)
                                dep_pv += vt_ev if branch == "local" \
                                    else vc_toks
                            pv = kb.t(lambda e, pvl=pvl, pslot=pslot, col=col,
                                      w=w, ob=ob, klen=klen,
                                      first=(i == firstu[qc]),
                                      last=(i == lastu[qc]): e.matmul(
                                PS[0:65, ob * 512 + col: ob * 512 + col + w],
                                pvl,
                                pt[0:klen, pslot * 512: pslot * 512 + w],
                                start=first, stop=last), deps=dep_pv,
                                inc=True)
                            pv_toks[(i, par)] = pv
                            if i == lastu[qc]:
                                pv_toks[("last", par, qc)] = pv
                    nun = len(units)
                    for par in range(2):
                        for back in (1, 2):
                            if nun - back >= 0 and (nun - back, par) in exp_toks:
                                kb.bset(exp_toks[(nun - back, par)],
                                        (nun - back) % 2 + 2 * par)
                    for par in range(2):
                        ob = 4 + 2 * par
                        dps = [pv_toks[("last", par, 0)],
                               pv_toks[("last", par, 1)]]
                        ev_raw = []
                        for qc in range(2):
                            wdst = dst[0:64, ch * T + q0 + qc * 512:
                                       ch * T + q0 + (qc + 1) * 512] \
                                if par == 0 else \
                                bstg[0:64, qc * 512:(qc + 1) * 512]
                            w1 = kb.s(lambda e, wdst=wdst, qc=qc, ob=ob:
                                      e.activation(
                                wdst,
                                PS[0:64, ob * 512 + qc * 512:
                                   ob * 512 + (qc + 1) * 512], AF.Identity,
                                scale=0.125),
                                deps=[pv_toks[("last", par, qc)]]
                                + (kb.gd("bstg") if par == 1 else []),
                                inc=True)
                            ev_raw.append(w1)
                        t1 = kb.s(lambda e, ob=ob: e.activation(
                            stat_ev[64:65, 0:1024],
                            PS[64:65, ob * 512: ob * 512 + 1024],
                            AF.Ln, bias=epcol[64:65, 0:1]),
                            deps=dps + [tok_ep] + kb.gd("div")
                            + kb.gd("stat_ev"), inc=True)
                        kb.bset(t1, ob, ob + 1)
                        t1b = kb.s(lambda e: e.activation(
                            stat_ev[64:65, 0:1024], stat_ev[64:65, 0:1024],
                            AF.Exp, scale=-0.5, bias=epcol[64:65, 2:3]),
                            deps=[t1], inc=True)
                        t2 = kb.s(lambda e: e.activation(
                            div_st[64:65, 0:1024], stat_ev[64:65, 0:1024],
                            AF.Square), deps=[t1b, tok_dst], inc=True)
                        kb.gset("stat_ev", t2)
                        wr_toks = []
                        for qc in range(2):
                            bb = qc
                            mmb = kb.t(lambda e, qc=qc, bb=bb: e.matmul(
                                bank(bb), bc_w[64:96, :],
                                div_st[64:96, qc * 512:(qc + 1) * 512],
                                start=True, stop=True),
                                deps=[t2] + kb.bdep(bb), inc=True)
                            evb = kb.v(lambda e, qc=qc, bb=bb: e.tensor_copy(
                                dvb[0:64, qc * 512:(qc + 1) * 512],
                                bank(bb, r0=0, r1=64)),
                                deps=[mmb] + kb.gd("dvb"), inc=True)
                            kb.bset(evb, bb)
                            wdst = dst[0:64, ch * T + q0 + qc * 512:
                                       ch * T + q0 + (qc + 1) * 512] \
                                if par == 0 else \
                                bstg[0:64, qc * 512:(qc + 1) * 512]
                            wr = kb.v(lambda e, wdst=wdst, qc=qc:
                                      e.tensor_tensor(
                                wdst, wdst,
                                dvb[0:64, qc * 512:(qc + 1) * 512],
                                op=ALU.mult),
                                deps=[evb, ev_raw[qc]], inc=True)
                            wr_toks.append(wr)
                        kb.gset("div", wr_toks[-1])
                        kb.gset("dvb", wr_toks[-1])
                        if par == 1:
                            mv1 = kb.dma("g", d_bmov, bstg[0:64, :],
                                         deps=wr_toks)
                            mv2 = kb.dma("g",
                                         dst[64:128,
                                             ch * T + q0: ch * T + q0 + 1024],
                                         d_bmov, deps=[mv1])
                            kb.gset("bstg", mv2)
                            pair_toks.append(mv2)
                        else:
                            pair_toks.append(wr_toks[-1])
                if branch == "local":
                    loc_toks.extend(pair_toks)
                else:
                    cmp_toks.extend(pair_toks)
        tap(f"loc{s}", locb[:, :], loc_toks)
        tap(f"cmp{s}", cmpb[:, :], cmp_toks)
        return loc_toks, cmp_toks

    def emit_gate(s, lnmb, cmpb, lnr, lnm_toks, cmp_toks, wg1, tok_wg1,
                  d_gw1):
        lb1 = kb.dma("y", lnr[0:1, 2 * T:2 * T + C], d_gb1,
                     deps=[lnm_toks[-1]])
        lw = kb.dma("y", lnr[0:1, T:T + C], d_glnw, deps=[lnm_toks[-1]])
        lb = kb.dma("y", lnr[0:1, T + C:T + 2 * C], d_glnb,
                     deps=[lnm_toks[-1]])
        mm_h = None
        tok_wg1b = None
        for f in range(16):
            srcb = lnmb if f < NCH else cmpb
            ci = f % NCH
            if f == 8:
                tok_wg1b = kb.dma("g",
                                  wg1[:, :].rearrange("p (f j) -> p f j", j=C),
                                  d_gw1.rearrange("(f p) j -> p f j", p=128)
                                  [:, 8:16, :], deps=[mm_h])
            for hf in range(2):
                mm_h = kb.t(lambda e, f=f, ci=ci, srcb=srcb, hf=hf: e.matmul(
                    bank(2 + hf, r0=0, r1=1),
                    srcb[:, ci * T + T - 1: ci * T + T],
                    wg1[:, (f % 8) * C + hf * 512:
                        (f % 8) * C + (hf + 1) * 512],
                    start=(f == 0), stop=(f == 15)),
                    deps=(lnm_toks + cmp_toks + [tok_wg1] + kb.bdep(2, 3))
                    if f == 0 else ([tok_wg1b] if f == 8 else []),
                    inc=((f == 15 or f == 7) and hf == 1))
        t_h = kb.v(lambda e: e.tensor_tensor(
            lnr[0:1, 0:512], bank(2, r0=0, r1=1), lnr[0:1, 2 * T:2 * T + 512],
            op=ALU.add), deps=[mm_h, lb1], inc=True)
        t_h2 = kb.v(lambda e: e.tensor_tensor(
            lnr[0:1, 512:1024], bank(3, r0=0, r1=1),
            lnr[0:1, 2 * T + 512:2 * T + 1024], op=ALU.add),
            deps=[mm_h, lb1], inc=True)
        kb.bset(t_h2, 2, 3)
        t_s = kb.v(lambda e: e.tensor_reduce(
            nrow[0:1, RG:RG + 1], lnr[0:1, 0:C], axis=AX, op=ALU.add),
            deps=[t_h, t_h2], inc=True)
        t_q = kb.s(lambda e: e.activation(
            lnr[0:1, 2 * T:2 * T + C], lnr[0:1, 0:C], AF.Square,
            accum_out=nrow[0:1, RG + 1:RG + 2]), deps=[t_h, t_h2, lb1],
            inc=True)
        kb.v(lambda e: e.tensor_scalar(nrow[0:1, RG:RG + 2],
                                       nrow[0:1, RG:RG + 2], 1.0 / C, None,
                                       op0=ALU.mult), deps=[t_s, t_q])
        kb.v(lambda e: e.tensor_tensor(nrow[0:1, RG + 2:RG + 3],
                                       nrow[0:1, RG:RG + 1],
                                       nrow[0:1, RG:RG + 1], op=ALU.mult))
        kb.v(lambda e: e.tensor_tensor(nrow[0:1, RG + 1:RG + 2],
                                       nrow[0:1, RG + 1:RG + 2],
                                       nrow[0:1, RG + 2:RG + 3],
                                       op=ALU.subtract))
        kb.v(lambda e: e.tensor_scalar(nrow[0:1, RG + 1:RG + 2],
                                       nrow[0:1, RG + 1:RG + 2], EPS, None,
                                       op0=ALU.add))
        ti = kb.v(lambda e: e.reciprocal(nrow[0:1, RG + 1:RG + 2],
                                         nrow[0:1, RG + 1:RG + 2]), inc=True)
        trq = kb.s(lambda e: e.activation(nrow[0:1, RG + 1:RG + 2],
                                          nrow[0:1, RG + 1:RG + 2], AF.Sqrt),
                   deps=[ti], inc=True)
        tmr = kb.v(lambda e: e.tensor_tensor(nrow[0:1, RG + 2:RG + 3],
                                             nrow[0:1, RG:RG + 1],
                                             nrow[0:1, RG + 1:RG + 2],
                                             op=ALU.mult), deps=[trq],
                   inc=True)
        tnm = kb.v(lambda e: e.tensor_scalar(nrow[0:1, RG + 2:RG + 3],
                                             nrow[0:1, RG + 2:RG + 3], -1.0,
                                             None, op0=ALU.mult), deps=[tmr],
                   inc=True)
        t_n = kb.v(lambda e: e.tensor_scalar(
            lnr[0:1, 0:C], lnr[0:1, 0:C],
            nrow[0:1, RG + 1:RG + 2], nrow[0:1, RG + 2:RG + 3],
            op0=ALU.mult, op1=ALU.add), deps=[tnm], inc=True)
        t_g1 = kb.v(lambda e: e.tensor_tensor(
            lnr[0:1, 0:C], lnr[0:1, 0:C], lnr[0:1, T:T + C], op=ALU.mult),
            deps=[t_n, lw], inc=True)
        t_g2 = kb.v(lambda e: e.tensor_tensor(
            lnr[0:1, 0:C], lnr[0:1, 0:C], lnr[0:1, T + C:T + 2 * C],
            op=ALU.add), deps=[t_g1, lb], inc=True)
        t_rl = kb.v(lambda e: e.tensor_scalar(
            lnr[0:1, 0:C], lnr[0:1, 0:C], 0.0, None, op0=ALU.max),
            deps=[t_g2], inc=True)
        dd1 = kb.dma("y", d_hrow, lnr[0:1, 0:C], deps=[t_rl])
        sem_h = f"dg{kb.qrr['g']}"
        kb.qrr['g'] = (kb.qrr['g'] + 1) % 4
        dd2 = kb.emit("g", lambda e: e.dma_start(
            out=hcol[:, :],
            in_=d_hrow.rearrange("o (ci p) -> (o p) ci", p=128),
            allow_slow_non_contiguous=True),
            deps=[dd1, (sem_h, kb.cnt[sem_h])], inc=sem_h, inc_by=16)
        mm2 = None
        for ci in range(NCH):
            mm2 = kb.t(lambda e, ci=ci: e.matmul(
                bank(2, r0=0, r1=1, c1=2), hcol[:, ci:ci + 1],
                w_g2[:, ci * 2:(ci + 1) * 2], start=(ci == 0),
                stop=(ci == NCH - 1)),
                deps=([dd2, tok_wg2] + kb.bdep(2)) if ci == 0 else [],
                inc=(ci == NCH - 1))
        lg2 = kb.dma("y", nrow[0:1, RG + 10:RG + 12], d_gb2, deps=[])
        t_l = kb.v(lambda e: e.tensor_tensor(
            nrow[0:1, RG + 4:RG + 6], bank(2, r0=0, r1=1, c1=2),
            nrow[0:1, RG + 10:RG + 12], op=ALU.add), deps=[mm2, lg2],
            inc=True)
        kb.bset(t_l, 2)
        t_d = kb.v(lambda e: e.tensor_tensor(
            nrow[0:1, RG + 6:RG + 7], nrow[0:1, RG + 4:RG + 5],
            nrow[0:1, RG + 5:RG + 6], op=ALU.subtract), deps=[t_l], inc=True)
        t_sg = kb.s(lambda e: e.activation(nrow[0:1, RG + 8:RG + 9],
                                           nrow[0:1, RG + 6:RG + 7],
                                           AF.Sigmoid), deps=[t_d], inc=True)
        t_g1r = kb.v(lambda e: e.tensor_scalar(nrow[0:1, RG + 9:RG + 10],
                                               nrow[0:1, RG + 8:RG + 9], -1.0,
                                               1.0, op0=ALU.mult, op1=ALU.add),
                     deps=[t_sg], inc=True)
        tok_g = bc_row32(nrow, RG + 8, 2, gbc, 0, [t_g1r], bki=2)
        tap(f"gates{s}", nrow[0:1, RG + 8:RG + 10], [tok_g])
        return tok_g

    def cc_exchange(idx, row, off, deps):
        cp = kb.v(lambda e, o=off: e.tensor_copy(
            stat_ev[0:1, 0:8], row[0:1, o:o + 8]),
            deps=list(deps) + kb.gd("stat_ev"), inc=True)
        d1 = kb.dma("g", d_ccin[idx], stat_ev[0:1, 0:8], deps=[cp])
        t_cc = kb.emit("g", lambda e, i=idx: e.collective_compute(
            "AllReduce", ALU.max, replica_groups=[list(range(NCORES))],
            ins=[d_ccin[i].opt()], outs=[d_ccout[i].opt()]),
            deps=[d1], inc="cc")
        d2 = kb.dma("g", row[0:1, off:off + 8], d_ccout[idx], deps=[t_cc])
        kb.gset("stat_ev", d2)
        return d2

    def colmax(c0, c1, row, dst_off, deps):
        rd = kb.v(lambda e, c0=c0, c1=c1: e.tensor_reduce(
            maxc[:, 255:256], maxc[:, c0:c1], axis=AX, op=ALU.max),
            deps=deps, inc=True)
        gdd = kb.dma("y", stat_ev[0:1, 0:128], maxc[:, 255:256],
                     deps=[rd] + kb.gd("stat_ev"))
        rd2 = kb.v(lambda e, d=dst_off: e.tensor_reduce(
            row[0:1, d:d + 1], stat_ev[0:1, 0:128], axis=AX, op=ALU.max),
            deps=[gdd], inc=True)
        kb.gset("stat_ev", rd2)
        return rd2

    def scale_pair(row, src, dst_s, dst_rs, deps):
        t1 = kb.v(lambda e: e.tensor_scalar(
            row[0:1, dst_s:dst_s + 1], row[0:1, src:src + 1], 1.0 / QMAX,
            1e-12, op0=ALU.mult, op1=ALU.max), deps=deps, inc=True)
        t2 = kb.v(lambda e: e.reciprocal(
            row[0:1, dst_rs:dst_rs + 1], row[0:1, dst_s:dst_s + 1]),
            deps=[t1], inc=True)
        return t2

    def emit_ffn(attn_done):
        xqb, a1 = ar.alloc("xq", 16 * T, F16)
        lnr, a2 = ar.alloc("lnrF", 3 * T, F32)
        brw, a3 = ar.alloc("brwF", 2 * T, F16)
        x2c, a4 = ar.alloc("x2cF", 2 * T, F16)
        fdeps = a1 + a2 + a3 + a4 + list(attn_done)
        xq_toks = []
        red_toks = []
        mx = 0
        prev_lx = [None, None]
        prev_bts = []
        for s in range(S):
            lm = kb.dma("y", lnr[0:1, 2 * T:3 * T],
                        d_ln2[0:1, s * 2 * T: s * 2 * T + T],
                        deps=fdeps + prev_bts)
            lr = kb.dma("y", lnr[0:1, 0:T],
                        d_ln2[0:1, s * 2 * T + T: s * 2 * T + 2 * T],
                        deps=fdeps + prev_bts + [lm])
            tmr = kb.v(lambda e: e.tensor_tensor(
                lnr[0:1, T:2 * T], lnr[0:1, 2 * T:3 * T], lnr[0:1, 0:T],
                op=ALU.mult), deps=[lm, lr], inc=True)
            bts = ln_bcast(lnr, brw, lr, tmr)
            prev_bts = bts + [tmr]
            for ci in range(NCH):
                sl = (ci % 2) * T
                lx = kb.dma("g", x2c[:, sl:sl + T],
                            d_x2[s, ci * 128:(ci + 1) * 128, :],
                            deps=fdeps + [prev_lx[ci % 2]])
                xcol = (ci * 2 + s) * T
                tk = ln_apply(xqb, xcol, x2c, sl, brw, [lx] + bts)
                prev_lx[ci % 2] = tk
                rd = kb.v(lambda e, xcol=xcol, mx=mx: e.tensor_reduce(
                    maxc[:, mx:mx + 1], xqb[:, xcol:xcol + T], axis=AX,
                    op=ALU.max, apply_absolute_value=True), deps=[tk],
                    inc=True)
                red_toks.append(rd)
                xq_toks.append(tk)
                mx += 1
        tok_mx = colmax(0, 16, nrow, RX, red_toks)
        t_x0 = cc_exchange(0, nrow, RX, [tok_mx])
        t_s0 = scale_pair(nrow, RX, RX + 16, RX + 17, [t_x0])
        sb0 = bc_row32(nrow, RX + 16, 2, scbc, 0, [t_s0], bki=0)
        q_toks = []
        chiq, a5 = ar.alloc("chiq", T, I32)
        for col in range(16):
            xcol = col * T
            tin = kb.v(lambda e, xcol=xcol: e.tensor_scalar(
                chiq[:, 0:T], xqb[:, xcol:xcol + T], scbc[:, 1:2], None,
                op0=ALU.mult), deps=[sb0] + xq_toks + a5 +
                [q_toks[-1] if q_toks else None], inc=True)
            tq = kb.s(lambda e, xcol=xcol: e.activation(
                xqb[:, xcol:xcol + T], chiq[:, 0:T], AF.Identity,
                scale=scbc[:, 0:1]), deps=[tin], inc=True)
            q_toks.append(tq)
        tap("xq", xqb[:, :], q_toks)
        for nm in ("lnrF", "brwF", "x2cF"):
            ar.free(nm, q_toks[-1])
        ar.free("chiq", q_toks[-1])

        wst, a6 = ar.alloc("wstF", 2 * NCH * 128, F16)
        sps, a7 = ar.alloc("spsF", 2 * S * T, F16)
        p1deps = a6 + a7 + q_toks
        mxu0, mxg0 = 16, 64
        spill_toks = []
        wfree = {}
        spfree = {}
        for mi, (dW, dSp) in enumerate(((d_upT, d_up), (d_gateT, d_gate))):
            for J in range(NI):
                ws = (mi * NI + J) % 2
                lw = kb.dma("y",
                            wst[:, ws * 1024:(ws + 1) * 1024]
                            .rearrange("p (ci j) -> p ci j", j=128),
                            dW.rearrange("(ci p) j -> p ci j", p=128)
                            [:, :, J * 128:(J + 1) * 128],
                            deps=p1deps + [wfree.get((mi * NI + J) - 2)])
                ev_list = []
                for tg in range(8):
                    bk = tg % 4
                    tkmm = None
                    for ci in range(NCH):
                        tkmm = kb.t(lambda e, ws=ws, ci=ci, tg=tg, bk=bk:
                                    e.matmul(
                            bank(bk),
                            wst[:, ws * 1024 + ci * 128:
                                ws * 1024 + (ci + 1) * 128],
                            xqb[:, ci * 2 * T + tg * 512:
                                ci * 2 * T + (tg + 1) * 512],
                            start=(ci == 0), stop=(ci == NCH - 1)),
                            deps=(kb.bdep(bk) + [lw] + q_toks)
                            if ci == 0 else [], inc=(ci == NCH - 1))
                    ev = kb.s(lambda e, ws=ws, tg=tg, bk=bk: e.activation(
                        sps[:, ws * S * T + tg * 512:
                            ws * S * T + (tg + 1) * 512],
                        bank(bk), AF.Identity),
                        deps=[tkmm] + list(spfree.get((mi * NI + J) - 2, ())),
                        inc=True)
                    kb.bset(ev, bk)
                    ev_list.append(ev)
                rd = kb.v(lambda e, ws=ws, mc=(mxu0 if mi == 0 else mxg0) + J:
                          e.tensor_reduce(
                    maxc[:, mc:mc + 1], sps[:, ws * S * T:(ws + 1) * S * T],
                    axis=AX, op=ALU.max, apply_absolute_value=True),
                    deps=ev_list, inc=True)
                sp = kb.dma("s", dSp[:, J * S * T:(J + 1) * S * T],
                            sps[:, ws * S * T:(ws + 1) * S * T],
                            deps=ev_list)
                wfree[mi * NI + J] = tkmm
                spfree[mi * NI + J] = (sp, rd)
                spill_toks.append(sp)
                spill_toks.append(rd)
        t_mu = colmax(mxu0, mxu0 + NI, nrow, RX + 1, spill_toks)
        t_mg = colmax(mxg0, mxg0 + NI, nrow, RX + 2, [t_mu] + spill_toks)
        t_x1 = cc_exchange(1, nrow, RX + 1, [t_mg])
        t_su = scale_pair(nrow, RX + 1, RX + 18, RX + 19, [t_x1])
        t_sg = scale_pair(nrow, RX + 2, RX + 20, RX + 21, [t_su])
        sb1 = bc_row32(nrow, RX + 18, 4, scbc, 2, [t_sg], bki=1)
        ar.free("xq", spill_toks[-1])
        ar.free("wstF", spill_toks[-1])

        gch, b1 = ar.alloc("gch", 2 * T, F16)
        chi, b2 = ar.alloc("chiF", T, I32)
        sga, b3 = ar.alloc("sga", T, F16)
        gq, b4 = ar.alloc("gq", T, F16)
        t2b, b5 = ar.alloc("t2b", 2 * T, F16)
        p2deps = b1 + b2 + b3 + b4 + b5 + [sb1]
        NCK = NI * S * T // T
        prev = {}
        p2_toks = []
        for ck in range(NCK):
            sl = (ck % 2)
            lg = kb.dma("y", gch[:, sl * T:(sl + 1) * T],
                        d_gate[:, ck * T:(ck + 1) * T],
                        deps=p2deps + [prev.get(("ng", ck - 2))])
            tng = kb.v(lambda e, sl=sl: e.tensor_scalar(
                chi[:, 0:T], gch[:, sl * T:(sl + 1) * T], scbc[:, 5:6], None,
                op0=ALU.mult), deps=[lg, prev.get(("sig", ck - 1)),
                                     prev.get(("gq", ck - 1))], inc=True)
            tsg = kb.s(lambda e: e.activation(
                sga[:, 0:T], chi[:, 0:T], AF.Sigmoid, scale=scbc[:, 4:5]),
                deps=[tng, prev.get(("ns", ck - 1)),
                      prev.get(("t2", ck - 1))], inc=True)
            tgq = kb.s(lambda e: e.activation(
                gq[:, 0:T], chi[:, 0:T], AF.Identity, scale=scbc[:, 4:5]),
                deps=[tng, prev.get(("t2", ck - 1))], inc=True)
            tns = kb.v(lambda e: e.tensor_scalar(
                chi[:, 0:T], sga[:, 0:T], 65536.0, None, op0=ALU.mult),
                deps=[tsg, tgq], inc=True)
            tsq = kb.v(lambda e: e.tensor_scalar(
                sga[:, 0:T], chi[:, 0:T], 65535.0, 1.0 / 65536.0,
                op0=ALU.min, op1=ALU.mult), deps=[tns], inc=True)
            tt2 = kb.g(lambda e, sl=sl: e.tensor_tensor(
                t2b[:, sl * T:(sl + 1) * T], gq[:, 0:T], sga[:, 0:T],
                op=ALU.mult), deps=[tsq, tgq, prev.get(("t2", ck - 2))],
                inc=True)
            trd = kb.v(lambda e, sl=sl, mc=112 + (ck % 64): e.tensor_reduce(
                maxc[:, mc:mc + 1], t2b[:, sl * T:(sl + 1) * T], axis=AX,
                op=ALU.max, apply_absolute_value=True),
                deps=[tt2, prev.get(("rd", ck - 64))], inc=True)
            sp = kb.dma("g", d_t2[:, ck * T:(ck + 1) * T],
                        t2b[:, sl * T:(sl + 1) * T], deps=[tt2])
            prev[("sig", ck)] = tsg
            prev[("gq", ck)] = tgq
            prev[("ns", ck)] = tns
            prev[("ng", ck)] = tng
            prev[("t2", ck)] = sp
            prev[("rd", ck)] = trd
            p2_toks += [sp, trd]
        t_m2 = colmax(112, 112 + 64 if NCK > 64 else 112 + NCK, nrow, RX + 3,
                      p2_toks)
        t_x2 = cc_exchange(2, nrow, RX + 3, [t_m2])
        t_s2 = scale_pair(nrow, RX + 3, RX + 22, RX + 23, [t_x2])
        sb2 = bc_row32(nrow, RX + 22, 2, scbc, 6, [t_s2], bki=2)

        osp, c1 = ar.alloc("osp", 2 * T, F16)
        p3deps = c1 + [sb2] + p2_toks
        p3_toks = []
        prev3 = {}
        for ck in range(NCK):
            sl = ck % 2
            lt = kb.dma("y", gch[:, sl * T:(sl + 1) * T],
                        d_t2[:, ck * T:(ck + 1) * T],
                        deps=p3deps + [prev3.get(("a", ck - 2))])
            lu = kb.dma("s", t2b[:, sl * T:(sl + 1) * T],
                        d_up[:, ck * T:(ck + 1) * T],
                        deps=p3deps + [prev3.get(("b", ck - 2))])
            tn2 = kb.v(lambda e, sl=sl: e.tensor_scalar(
                chi[:, 0:T], gch[:, sl * T:(sl + 1) * T], scbc[:, 7:8], None,
                op0=ALU.mult), deps=[lt, prev3.get(("q", ck - 1))], inc=True)
            tq2 = kb.s(lambda e: e.activation(
                sga[:, 0:T], chi[:, 0:T], AF.Identity, scale=scbc[:, 6:7]),
                deps=[tn2, prev3.get(("sp", ck - 1))], inc=True)
            tnu = kb.v(lambda e, sl=sl: e.tensor_scalar(
                chi[:, 0:T], t2b[:, sl * T:(sl + 1) * T], scbc[:, 3:4], None,
                op0=ALU.mult), deps=[lu, tq2], inc=True)
            tqu = kb.s(lambda e: e.activation(
                gq[:, 0:T], chi[:, 0:T], AF.Identity, scale=scbc[:, 2:3]),
                deps=[tnu, prev3.get(("sp", ck - 1))], inc=True)
            to = kb.g(lambda e, sl=sl: e.tensor_tensor(
                osp[:, sl * T:(sl + 1) * T], sga[:, 0:T], gq[:, 0:T],
                op=ALU.mult), deps=[tqu, tq2, prev3.get(("sp", ck - 2))],
                inc=True)
            trd = kb.v(lambda e, sl=sl, mc=176 + (ck % 64): e.tensor_reduce(
                maxc[:, mc:mc + 1], osp[:, sl * T:(sl + 1) * T], axis=AX,
                op=ALU.max, apply_absolute_value=True),
                deps=[to, prev3.get(("rd", ck - 64))], inc=True)
            sp = kb.dma("g", d_oraw[:, ck * T:(ck + 1) * T],
                        osp[:, sl * T:(sl + 1) * T], deps=[to])
            prev3[("a", ck)] = tn2
            prev3[("b", ck)] = tnu
            prev3[("q", ck)] = tqu
            prev3[("rd", ck)] = trd
            prev3[("sp", ck)] = (sp if False else trd) and sp
            p3_toks += [sp, trd]
        t_m3 = colmax(176, 176 + min(64, NCK), nrow, RX + 4, p3_toks)
        t_x3 = cc_exchange(3, nrow, RX + 4, [t_m3])
        t_s3 = scale_pair(nrow, RX + 4, RX + 24, RX + 25, [t_x3])
        sb3 = bc_row32(nrow, RX + 24, 2, scbc, 8, [t_s3], bki=3)
        for nm in ("gch", "chiF", "sga", "gq", "t2b", "osp"):
            ar.free(nm, p3_toks[-1])

        foutb, c3 = ar.alloc("fout", NCH * 2 * T, F16)
        wdn, c4 = ar.alloc("wdn", NI * 128 * NCH, F16)
        oqst, c5a = ar.alloc("oqst", NI * 512, F16)
        chi2, c5b = ar.alloc("chi2", T, I32)
        p4deps = c3 + c4 + c5a + c5b + [sb3] + p3_toks
        lwall = kb.dma("y", wdn[:, :].rearrange("p (ii j) -> p ii j", j=C),
                       d_downT.rearrange("(ii p) j -> p ii j", p=128),
                       deps=p4deps)
        dn_ev = []
        oq_free = None
        for tg in range(8):
            lo = kb.dma("s", oqst[:, :].rearrange("p (ii t) -> p ii t", t=512),
                        d_oraw[:, :].rearrange("p (ii t) -> p ii t", t=S * T)
                        [:, :, tg * 512:(tg + 1) * 512],
                        deps=p4deps + [oq_free])
            qd_toks = []
            for sub in range(6):
                w = 2048
                ti2 = kb.v(lambda e, sub=sub: e.tensor_scalar(
                    chi2[:, 0:T], oqst[:, sub * T:(sub + 1) * T],
                    scbc[:, 9:10], None, op0=ALU.mult),
                    deps=[lo] + ([qd_toks[-1]] if qd_toks else []), inc=True)
                tq2 = kb.s(lambda e, sub=sub: e.activation(
                    oqst[:, sub * T:(sub + 1) * T], chi2[:, 0:T],
                    AF.Identity, scale=scbc[:, 8:9]), deps=[ti2], inc=True)
                qd_toks.append(tq2)
            for J in range(NCH):
                bk = (tg * NCH + J) % 4
                tkmm = None
                for ii in range(NI):
                    tkmm = kb.t(lambda e, J=J, ii=ii, tg=tg, bk=bk: e.matmul(
                        bank(bk),
                        wdn[:, ii * C + J * 128: ii * C + (J + 1) * 128],
                        oqst[:, ii * 512:(ii + 1) * 512],
                        start=(ii == 0), stop=(ii == NI - 1)),
                        deps=(kb.bdep(bk) + [lwall] + qd_toks)
                        if ii == 0 else [], inc=(ii == NI - 1))
                ev = kb.s(lambda e, J=J, tg=tg, bk=bk: e.activation(
                    foutb[:, J * 2 * T + tg * 512: J * 2 * T + (tg + 1) * 512],
                    bank(bk), AF.Identity), deps=[tkmm], inc=True)
                kb.bset(ev, bk)
                dn_ev.append(ev)
            oq_free = dn_ev[-1]
        ar.free("oqst", dn_ev[-1])
        ar.free("chi2", dn_ev[-1])
        ar.free("wdn", dn_ev[-1])
        tap("fout", foutb[:, :], [dn_ev[-1]])

        lnr5, c5 = ar.alloc("lnrG", 3 * T, F32)
        sq2, c6 = ar.alloc("sq2G", 2 * T, F16)
        brw2, c7 = ar.alloc("brwG", 2 * T, F16)
        xfc, c8 = ar.alloc("xfc", 2 * T, F32)
        out_toks = []
        for s in range(S):
            sqf = [None, None]
            tmm = None
            for ci in range(NCH):
                sl = (ci % 2) * T
                col = ci * 2 * T + s * T
                tsq = kb.s(lambda e, col=col, sl=sl: e.activation(
                    sq2[:, sl:sl + T], foutb[:, col:col + T], AF.Square),
                    deps=[dn_ev[-1], sqf[ci % 2],
                          out_toks[-1] if out_toks else None]
                    + c5 + c6 + c7 + c8, inc=True)
                for g in range(4):
                    dep0 = (kb.bdep(6, 7) + [tok_ones]) if ci == 0 else []
                    kb.t(lambda e, col=col, g=g, ci=ci: e.matmul(
                        bank(6, r0=32 * g, r1=32 * g + 1), ones1[:, 0:1],
                        foutb[:, col + 512 * g: col + 512 * (g + 1)],
                        start=(ci == 0), stop=(ci == NCH - 1),
                        tile_position=(0, 32 * g), skip_group_check=True),
                        deps=dep0 + [dn_ev[-1]])
                    tmm = kb.t(lambda e, g=g, sl=sl, ci=ci: e.matmul(
                        bank(7, r0=32 * g, r1=32 * g + 1), ones1[:, 0:1],
                        sq2[:, sl + 512 * g: sl + 512 * (g + 1)],
                        start=(ci == 0), stop=(ci == NCH - 1),
                        tile_position=(0, 32 * g), skip_group_check=True),
                        deps=[tsq] + dep0, inc=(g == 3))
                sqf[ci % 2] = tmm
            for g in range(4):
                ev1 = kb.v(lambda e, g=g: e.tensor_copy(
                    stat_ev[32 * g:32 * g + 1, 0:512],
                    bank(6, r0=32 * g, r1=32 * g + 1)),
                    deps=([tmm] + kb.gd("stat_ev")) if g == 0 else [], inc=True)
            for g in range(4):
                ev2 = kb.v(lambda e, g=g: e.tensor_copy(
                    stat_ev[32 * g:32 * g + 1, 512:1024],
                    bank(7, r0=32 * g, r1=32 * g + 1)),
                    deps=[tmm] if g == 0 else [], inc=True)
            kb.bset(ev2, 6, 7)
            g1 = kb.dma("y", lnr5[0:1, 0:T], stat_ev[0:97:32, 0:512],
                        deps=[ev1])
            g2 = kb.dma("y", lnr5[0:1, T:2 * T], stat_ev[0:97:32, 512:1024],
                        deps=[ev2])
            kb.gset("stat_ev", g2)
            inv = 1.0 / C
            kb.v(lambda e: e.tensor_scalar(lnr5[0:1, 2 * T:3 * T],
                                           lnr5[0:1, 0:T], inv, None,
                                           op0=ALU.mult), deps=[g1])
            kb.v(lambda e: e.tensor_scalar(lnr5[0:1, T:2 * T],
                                           lnr5[0:1, T:2 * T], inv, None,
                                           op0=ALU.mult), deps=[g2])
            kb.v(lambda e: e.tensor_tensor(lnr5[0:1, 0:T],
                                           lnr5[0:1, 2 * T:3 * T],
                                           lnr5[0:1, 2 * T:3 * T],
                                           op=ALU.mult))
            tvar = kb.v(lambda e: e.tensor_tensor(lnr5[0:1, T:2 * T],
                                                  lnr5[0:1, T:2 * T],
                                                  lnr5[0:1, 0:T],
                                                  op=ALU.subtract), inc=True)
            tln = kb.s(lambda e: e.activation(lnr5[0:1, 0:T],
                                              lnr5[0:1, T:2 * T],
                                              AF.Ln, bias=epcol[0:1, 1:2]),
                       deps=[tvar, tok_ep], inc=True)
            tr = kb.s(lambda e: e.activation(lnr5[0:1, 0:T], lnr5[0:1, 0:T],
                                             AF.Exp, scale=-0.5),
                      deps=[tln], inc=True)
            tmr = kb.v(lambda e: e.tensor_tensor(lnr5[0:1, T:2 * T],
                                                 lnr5[0:1, 2 * T:3 * T],
                                                 lnr5[0:1, 0:T], op=ALU.mult),
                       deps=[tr], inc=True)
            bts = ln_bcast(lnr5, brw2, tr, tmr)
            prev_x = None
            for ci in range(NCH):
                col = ci * 2 * T + s * T
                u = ln_apply(sq2, 0, foutb, col, brw2, bts + [prev_x])
                xr = kb.dma("y", xfc[:, 0:T], d_x2[s, ci * 128:(ci + 1) * 128, :],
                            deps=[prev_x])
                ad = kb.v(lambda e: e.tensor_tensor(
                    xfc[:, T:2 * T], sq2[:, 0:T], xfc[:, 0:T], op=ALU.add),
                    deps=[u, xr], inc=True)
                so = kb.dma("y", dyo[s, ci * 128:(ci + 1) * 128, :],
                            xfc[:, T:2 * T], deps=[ad])
                prev_x = so
                out_toks.append(so)
        kb.y(lambda e: e.nop(), deps=[out_toks[-1]])
        return out_toks

    ph = set(_PHASES)
    done = []
    dq_tok, dq_names = None, ()
    if "dequant" in ph:
        dq_tok, dq_names = emit_dequant()
        done.append(dq_tok)
    if "attn0" in ph:
        a_toks0 = emit_attention(
            0, [], free_after_qkv=((dq_names, dq_tok) if dq_tok else None))
        done += a_toks0
    if "attn1" in ph:
        a_toks1 = emit_attention(1, [])
        done += a_toks1
    if "ffn" in ph:
        emit_ffn(done)
    else:
        prev = None
        for s in range(S):
            for ci in range(NCH):
                prev = kb.dma("y", dyo[s, ci * 128:(ci + 1) * 128, :],
                              dx[s, ci * 128:(ci + 1) * 128, :],
                              deps=done + [prev])
        kb.y(lambda e: e.nop(), deps=[prev])

    kb.build()
    return nc, d_taps


def _prep_inputs(inputs):
    x = np.asarray(inputs["L0_attn_x"], np.float32)
    conv_w = np.asarray(inputs["L0_attn_conv_w"], np.float32)
    cw = np.ascontiguousarray(conv_w.transpose(1, 2, 0).reshape(64, KTAP * HS))
    cw2 = np.concatenate([cw, cw], axis=0)
    base = {
        "w_attn": np.asarray(inputs["L0_attn_c_attn_w"], np.float32),
        "b_attn": np.ascontiguousarray(
            np.asarray(inputs["L0_attn_c_attn_b"], np.float32)
            .reshape(NI, 128).T),
        "w_proj": np.asarray(inputs["L0_attn_c_proj_w"], np.float32),
        "b_proj": np.ascontiguousarray(
            np.asarray(inputs["L0_attn_c_proj_b"], np.float32)
            .reshape(NCH, 128).T),
        "w_conv": cw2,
        "b_conv": np.ascontiguousarray(
            np.tile(np.asarray(inputs["L0_attn_conv_b"], np.float32)
                    .reshape(64, 1), (2, 1))),
        "comp_w": np.asarray(inputs["L0_attn_comp_ln_w"], np.float32)
        .reshape(1, HS),
        "comp_b": np.asarray(inputs["L0_attn_comp_ln_b"], np.float32)
        .reshape(1, HS),
        "lnl_w": np.ascontiguousarray(
            np.asarray(inputs["L0_attn_ln_local_w"], np.float32)
            .reshape(NCH, 128).T),
        "lnl_b": np.ascontiguousarray(
            np.asarray(inputs["L0_attn_ln_local_b"], np.float32)
            .reshape(NCH, 128).T),
        "g_w1": np.asarray(inputs["L0_attn_gate_w1"], np.float32),
        "g_b1": np.asarray(inputs["L0_attn_gate_b1"], np.float32)
        .reshape(1, C),
        "g_lnw": np.asarray(inputs["L0_attn_gate_ln_w"], np.float32)
        .reshape(1, C),
        "g_lnb": np.asarray(inputs["L0_attn_gate_ln_b"], np.float32)
        .reshape(1, C),
        "g_w2": np.asarray(inputs["L0_attn_gate_w2"], np.float32),
        "g_b2": np.asarray(inputs["L0_attn_gate_b2"], np.float32)
        .reshape(1, 2),
        "w_up": np.asarray(inputs["L0_ffn_w_up"], np.float32),
        "w_gate": np.asarray(inputs["L0_ffn_w_gate"], np.float32),
        "w_down": np.asarray(inputs["L0_ffn_w_down"], np.float32),
    }
    in_maps = []
    for c in range(NCORES):
        m = dict(base)
        m["x"] = np.ascontiguousarray(
            x[2 * c:2 * c + 2].transpose(0, 2, 1))
        in_maps.append(m)
    return in_maps


def kernel(**inputs):
    in_maps = _prep_inputs(inputs)
    nc, d_taps = build_program(taps=tuple(_DEBUG_TAPS))
    res = run_bass_kernel_spmd(nc, in_maps, core_ids=list(range(NCORES)))
    out = np.empty((16, 2048, 1024), np.float32)
    for c in range(NCORES):
        yv = np.asarray(res.results[c]["y"], np.float32)
        out[2 * c:2 * c + 2] = yv.transpose(0, 2, 1)
    if _DEBUG_TAPS:
        kernel._tap_results = [
            {k: np.asarray(res.results[c].get("tap_" + k))
             for k in _DEBUG_TAPS if ("tap_" + k) in res.results[c]}
            for c in range(NCORES)]
        kernel._spill_results = {
            k: np.asarray(res.results[0][k]) for k in
            ("x2_sp", "ln2_sp", "up_sp", "gate_sp", "t2_sp", "oraw_sp",
             "upT_sp", "gateT_sp", "downT_sp") if k in res.results[0]}
    return out


# version 10
# speedup vs baseline: 1.0542x; 1.0003x over previous
import numpy as np
import ml_dtypes
import concourse.bass as bass
import concourse.mybir as mybir
from concourse.bass_utils import run_bass_kernel_spmd
from contextlib import ExitStack

F32 = mybir.dt.float32
F16 = mybir.dt.float16
BF16 = mybir.dt.bfloat16
I32 = mybir.dt.int32
AF = mybir.ActivationFunctionType
ALU = mybir.AluOpType
AX = mybir.AxisListType.X

NCORES = 8
S, T, C, NCH, NH, HS, NB = 2, 2048, 1024, 8, 16, 64, 127
KTAP, CSTR = 32, 16
I3, NI = 3072, 24
EPS = 1e-5
QMAX = 32767.0
DEN_EPS = 1e-37
ARENA_BYTES = 172800

_DEBUG_TAPS = []
_PHASES = ("dequant", "attn0", "attn1", "ffn")
_ATTN_CUT = None


def _dtsz(dt):
    return {F32: 4, F16: 2, BF16: 2, I32: 4}[dt]


class KB:
    def __init__(self, nc):
        self.nc = nc
        self.ops = {e: [] for e in ("t", "v", "s", "g", "y")}
        self.sem = {}
        self.cnt = {}
        self.waited = {e: {} for e in ("t", "v", "s", "g", "y")}
        self.stack = ExitStack()
        names = ["st", "sv", "ss", "sg", "sy", "cc"]
        for q in ("dy", "dg", "ds"):
            names += [f"{q}{i}" for i in range(4)]
        for name in names:
            self.sem[name] = self.stack.enter_context(nc.semaphore(name))
            self.cnt[name] = 0
        self.qrr = {"y": 0, "g": 0, "s": 0}
        self.bank_tok = {}
        self.guard = {}

    def sb(self, name, shape, dt=F16):
        return self.stack.enter_context(self.nc.sbuf_tensor(name, shape, dt))

    AUTO_SER = {"v": "sv", "s": "ss", "g": "sg"}

    def emit(self, eng, fn, deps=(), inc=None, inc_by=1):
        own = self.AUTO_SER.get(eng)
        is_dma = isinstance(inc, str) and inc.startswith(("d", "c"))
        if own is not None and not is_dma:
            deps = list(deps) + [(own, self.cnt[own])]
            if not isinstance(inc, str):
                inc = own
        waits = []
        for d in deps:
            if d is None:
                continue
            sem, val = d
            if val <= 0 or self.waited[eng].get(sem, 0) >= val:
                continue
            self.waited[eng][sem] = val
            waits.append((self.sem[sem], val))
        tok, sem_h = None, None
        if inc:
            inc = {"t": "st", "v": "sv", "s": "ss", "g": "sg", "y": "sy"}[eng] \
                if inc is True else inc
            self.cnt[inc] += inc_by
            tok = (inc, self.cnt[inc])
            sem_h = self.sem[inc]

        def run(e, waits=waits, fn=fn, sem_h=sem_h, inc_by=inc_by):
            for (h, v) in waits:
                e.wait_ge(h, v)
            ins = fn(e)
            if sem_h is not None:
                ins.then_inc(sem_h, inc_by)
        self.ops[eng].append(run)
        return tok

    def t(self, fn, deps=(), inc=False):
        return self.emit("t", fn, deps, inc)

    def v(self, fn, deps=(), inc=False):
        return self.emit("v", fn, deps, inc)

    def s(self, fn, deps=(), inc=False):
        return self.emit("s", fn, deps, inc)

    def g(self, fn, deps=(), inc=False):
        return self.emit("g", fn, deps, inc)

    def y(self, fn, deps=(), inc=False):
        return self.emit("y", fn, deps, inc)

    def dma(self, queue, out, in_, deps=()):
        base = {"y": "dy", "g": "dg", "s": "ds"}[queue]
        sem = f"{base}{self.qrr[queue]}"
        self.qrr[queue] = (self.qrr[queue] + 1) % 4
        deps = list(deps) + [(sem, self.cnt[sem])]
        return self.emit(queue, lambda e, o=out, i=in_: e.dma_start(out=o, in_=i),
                         deps, sem, 16)

    def bdep(self, *banks):
        return [self.bank_tok.get(b) for b in banks]

    def bset(self, tok, *banks):
        for b in banks:
            self.bank_tok[b] = tok

    def gd(self, name):
        return [self.guard.get(name)]

    def gset(self, name, tok):
        self.guard[name] = tok

    def build(self):
        with self.nc.Block() as block:
            @block.tensor
            def _(e):
                for f in self.ops["t"]:
                    f(e)

            @block.vector
            def _(e):
                for f in self.ops["v"]:
                    f(e)

            @block.scalar
            def _(e):
                for f in self.ops["s"]:
                    f(e)

            @block.gpsimd
            def _(e):
                for f in self.ops["g"]:
                    f(e)

            @block.sync
            def _(e):
                for f in self.ops["y"]:
                    f(e)


class Arena:

    def __init__(self, kb, nbytes):
        self.kb = kb
        self.u8 = kb.sb("sb_ARENA", [128, nbytes], mybir.dt.uint8)
        self.free_list = [(0, nbytes, [])]
        self.live = {}

    def _check(self):
        spans = sorted([(o, o + s, "free") for (o, s, _) in self.free_list] +
                       [(o, o + s, n) for n, (o, s) in self.live.items()])
        for a, b in zip(spans, spans[1:]):
            assert a[1] <= b[0], f"arena overlap: {a} vs {b}"

    def alloc(self, name, nelem, dt=F16):
        self._check()
        size = (nelem * _dtsz(dt) + 31) & ~31
        for i, (off, sz, toks) in enumerate(self.free_list):
            if sz >= size:
                if sz == size:
                    self.free_list.pop(i)
                else:
                    self.free_list[i] = (off + size, sz - size, toks)
                view = self.u8[:, off:off + nelem * _dtsz(dt)].bitcast(dt)
                self.live[name] = (off, size)
                return view, list(toks)
        raise MemoryError(f"arena OOM for {name} ({size}B); free={self.free_list}")

    def free(self, name, tok):
        off, size = self.live.pop(name)
        self.free_list.append((off, size, [tok] if tok else []))
        self.free_list.sort()
        merged = []
        for blk in self.free_list:
            if merged and merged[-1][0] + merged[-1][1] == blk[0]:
                o, s, t1 = merged[-1]
                comb = {}
                for t in list(t1) + list(blk[2]):
                    if t is None:
                        continue
                    if t[0] not in comb or comb[t[0]][1] < t[1]:
                        comb[t[0]] = t
                merged[-1] = (o, s + blk[1], list(comb.values()))
            else:
                merged.append(blk)
        self.free_list = merged


def build_program(taps=()):
    nc = bass.Bass("TRN2", target_bir_lowering=False, debug=False)
    kb = KB(nc)
    sb = kb.sb
    ar = Arena(kb, ARENA_BYTES)

    dx = nc.dram_tensor("x", [S, C, T], F32, kind="ExternalInput").ap()
    dyo = nc.dram_tensor("y", [S, C, T], F32, kind="ExternalOutput").ap()
    d_wattn = nc.dram_tensor("w_attn", [C, 3 * C], F32, kind="ExternalInput").ap()
    d_battn = nc.dram_tensor("b_attn", [128, NI], F32, kind="ExternalInput").ap()
    d_wproj = nc.dram_tensor("w_proj", [C, C], F32, kind="ExternalInput").ap()
    d_bproj = nc.dram_tensor("b_proj", [128, NCH], F32, kind="ExternalInput").ap()
    d_wconv = nc.dram_tensor("w_conv", [128, KTAP * HS], F32, kind="ExternalInput").ap()
    d_bconv = nc.dram_tensor("b_conv", [128, 1], F32, kind="ExternalInput").ap()
    d_compw = nc.dram_tensor("comp_w", [1, HS], F32, kind="ExternalInput").ap()
    d_compb = nc.dram_tensor("comp_b", [1, HS], F32, kind="ExternalInput").ap()
    d_lnlw = nc.dram_tensor("lnl_w", [128, NCH], F32, kind="ExternalInput").ap()
    d_lnlb = nc.dram_tensor("lnl_b", [128, NCH], F32, kind="ExternalInput").ap()
    d_gw1 = nc.dram_tensor("g_w1", [2 * C, C], F32, kind="ExternalInput").ap()
    d_gb1 = nc.dram_tensor("g_b1", [1, C], F32, kind="ExternalInput").ap()
    d_glnw = nc.dram_tensor("g_lnw", [1, C], F32, kind="ExternalInput").ap()
    d_glnb = nc.dram_tensor("g_lnb", [1, C], F32, kind="ExternalInput").ap()
    d_gw2 = nc.dram_tensor("g_w2", [C, 2], F32, kind="ExternalInput").ap()
    d_gb2 = nc.dram_tensor("g_b2", [1, 2], F32, kind="ExternalInput").ap()
    d_wup = nc.dram_tensor("w_up", [I3, C], F32, kind="ExternalInput").ap()
    d_wgate = nc.dram_tensor("w_gate", [I3, C], F32, kind="ExternalInput").ap()
    d_wdown = nc.dram_tensor("w_down", [C, I3], F32, kind="ExternalInput").ap()

    d_x2 = nc.dram_tensor("x2_sp", [S, C, T], F32).ap()
    d_ln2 = nc.dram_tensor("ln2_sp", [1, S * 2 * T], F32).ap()
    d_upT = nc.dram_tensor("upT_sp", [C, I3], F16).ap()
    d_gateT = nc.dram_tensor("gateT_sp", [C, I3], F16).ap()
    d_downT = nc.dram_tensor("downT_sp", [I3, C], F16).ap()
    d_up = nc.dram_tensor("up_sp", [128, NI * S * T], F16).ap()
    d_gate = nc.dram_tensor("gate_sp", [128, NI * S * T], F16).ap()
    d_t2 = nc.dram_tensor("t2_sp", [128, NI * S * T], F16).ap()
    d_oraw = nc.dram_tensor("oraw_sp", [128, NI * S * T], F16).ap()
    d_hrow = nc.dram_tensor("hrow_sp", [1, C], F32).ap()
    d_bmov = nc.dram_tensor("bmov_sp", [64, 1024], F16).ap()
    d_ccin = [nc.dram_tensor(f"ccin{i}", [1, 8], F32).ap() for i in range(4)]
    d_ccout = [nc.dram_tensor(f"ccout{i}", [1, 8], F32).ap() for i in range(4)]
    d_taps = {}

    ones1 = sb("sb_ones1", [128, 1], F16)
    idf = sb("sb_idf", [128, 128], F16)
    bc_w = sb("sb_bc_w", [128, 128], F16)
    bc_w32 = sb("sb_bc_w32", [32, 128], F32)
    bc_stage = sb("sb_bc_stage", [32, 512], F16)
    bc_st32 = sb("sb_bc_st32", [32, 128], F32)
    div_st = sb("sb_div_st", [128, 1024], BF16)
    dvb = sb("sb_dvb", [64, 1024], BF16)
    bstg = sb("sb_bstg", [64, 1024], F16)
    b_attn = sb("sb_b_attn", [128, NI], F32)
    b_proj = sb("sb_b_proj", [128, NCH], F32)
    b_conv = sb("sb_b_conv", [128, 1], F32)
    lnl_w = sb("sb_lnl_w", [128, NCH], F32)
    lnl_b = sb("sb_lnl_b", [128, NCH], F32)
    w_conv = sb("sb_w_conv", [128, KTAP * HS], F16)
    w_g2 = sb("sb_w_g2", [128, NCH * 2], F16)
    crep_w = sb("sb_crep_w", [128, NH * HS], F16)
    crep_b = sb("sb_crep_b", [128, NH * HS], F16)
    mn_bc = sb("sb_mn_bc", [128, 32], F32)
    hcol = sb("sb_hcol", [128, NCH], F16)
    gbc = sb("sb_gbc", [128, 2], F32)
    scbc = sb("sb_scbc", [128, 16], F32)
    maxc = sb("sb_maxc", [128, 256], F32)
    stat_ev = sb("sb_stat_ev", [128, 1024], F32)
    nrow = sb("sb_nrow", [1, 128], F32)
    vcompT = sb("sb_vcompT", [128, NH * 65], F16)
    red16 = sb("sb_red16", [128, 4 * NH], F32)
    pt = sb("sb_pt", [128, 4 * 512], BF16)
    vt = sb("sb_vt", [128, 2 * 16 * 65], F16)
    kcomp = sb("sb_kcomp", [128, NCH * NB], F16)
    epcol = sb("sb_epcol", [128, 3], F32)

    RQ, RK, RKC, RM = 0, NH, 2 * NH, 3 * NH
    RG = 5 * NH
    RX = RG + 16

    PS = kb.stack.enter_context(nc.psum_tensor("PS", [128, 4096], F32))
    PS16 = PS[:, :].bitcast(F16)

    def bank(i, r0=0, r1=128, c0=0, c1=512):
        return PS[r0:r1, 512 * i + c0: 512 * i + c1]

    def tap(name, ap, deps):
        if name not in taps:
            return
        d = nc.dram_tensor("tap_" + name, list(ap.shape), ap.dtype,
                           kind="ExternalOutput").ap()
        d_taps[name] = d
        kb.dma("y", d, ap, deps=deps)

    tok_ones = kb.v(lambda e: e.memset(ones1[:, :], 1.0), inc=True)
    kb.v(lambda e: e.memset(nrow[:, :], 0.0))
    kb.v(lambda e: e.memset(maxc[:, :], 0.0))
    kb.v(lambda e: e.memset(stat_ev[:, :], 0.0))
    kb.v(lambda e: e.memset(bc_w[:, :], 0.0))
    kb.v(lambda e: e.memset(bc_w[0:1, :], 1.0))
    tok_bcw = kb.v(lambda e: e.memset(bc_w[64:65, :], 1.0), inc=True)
    kb.v(lambda e: e.memset(bc_w32[:, :], 0.0))
    tok_bcw32 = kb.v(lambda e: e.memset(bc_w32[0:1, :], 1.0), inc=True)
    tok_stage = kb.v(lambda e: e.memset(bc_stage[:, :], 0.0), inc=True)
    tok_st32 = kb.v(lambda e: e.memset(bc_st32[:, :], 0.0), inc=True)
    tok_dst = kb.v(lambda e: e.memset(div_st[:, :], 0.0), inc=True)
    kb.v(lambda e: e.memset(epcol[:, 0:1], DEN_EPS))
    kb.v(lambda e: e.memset(epcol[:, 2:3], 1.0397207708399179))
    tok_ep = kb.v(lambda e: e.memset(epcol[:, 1:2], EPS), inc=True)
    kb.v(lambda e: e.memset(vt[:, :], 0.0))
    tok_vt0 = kb.v(lambda e: e.memset(
        vt[:, :].rearrange("p (k c) -> p k c", c=65)[:, :, 64:65], 1.0),
        inc=True)
    kb.v(lambda e: e.memset(vcompT[:, :], 0.0))
    tok_vc0 = kb.v(lambda e: e.memset(
        vcompT[:, :].rearrange("p (h c) -> p h c", c=65)[:, :, 64:65], 1.0),
        inc=True)
    tok_idt = kb.g(lambda e: e.memset(idf[:, :], 1.0), inc=True)
    tok_id = kb.g(lambda e: e.affine_select(
        idf[:, :], idf[:, :], pattern=[[1, 128]], compare_op=ALU.is_equal,
        fill=0.0, base=0, channel_multiplier=-1), deps=[tok_idt], inc=True)

    tok_wconv = kb.dma("g", w_conv[:, :], d_wconv)
    tok_wg2 = kb.dma("g", w_g2[:, :].rearrange("p (ci j) -> p ci j", j=2),
                     d_gw2.rearrange("(ci p) j -> p ci j", p=128))
    tok_battn = kb.dma("y", b_attn[:, :], d_battn)
    tok_bproj = kb.dma("y", b_proj[:, :], d_bproj)
    tok_bconv = kb.dma("y", b_conv[:, :], d_bconv)
    tok_lnlw = kb.dma("y", lnl_w[:, :], d_lnlw)
    tok_lnlb = kb.dma("y", lnl_b[:, :], d_lnlb)
    tok_cw = kb.dma("y", bc_st32[0:1, 0:HS], d_compw, deps=[tok_st32])
    tok_cb = kb.dma("y", bc_st32[0:1, HS:2 * HS], d_compb, deps=[tok_st32])

    tok_m1 = kb.t(lambda e: e.matmul(bank(0, c1=64), bc_w32[0:32, :],
                                     bc_st32[0:32, 0:64], start=True, stop=True),
                  deps=[tok_cw, tok_st32, tok_bcw32], inc=True)
    tok_m2 = kb.t(lambda e: e.matmul(bank(1, c1=64), bc_w32[0:32, :],
                                     bc_st32[0:32, HS:2 * HS],
                                     start=True, stop=True),
                  deps=[tok_cb, tok_st32, tok_bcw32], inc=True)
    crep_toks = []
    for h in range(NH):
        crep_toks.append(kb.v(lambda e, h=h: e.tensor_copy(
            crep_w[:, h * 64:(h + 1) * 64], bank(0, c1=64)),
            deps=[tok_m1], inc=(h == NH - 1)))
    for h in range(NH):
        crep_toks.append(kb.v(lambda e, h=h: e.tensor_copy(
            crep_b[:, h * 64:(h + 1) * 64], bank(1, c1=64)),
            deps=[tok_m2], inc=(h == NH - 1)))
    tok_crep = crep_toks[-1]
    kb.bset(tok_crep, 0, 1)

    def bc_row16(src_row, src_off, n, dst16, dst_off, deps, banks=(0, 1)):
        toks = []
        off, bi = 0, 0
        prev_cp = None
        while off < n:
            w = min(512, n - off)
            bki = banks[bi % len(banks)]
            cp = kb.v(lambda e, o=off, w=w: e.tensor_copy(
                bc_stage[0:1, 0:w], src_row[0:1, src_off + o:src_off + o + w]),
                deps=list(deps) + [tok_stage, tok_bcw, prev_cp], inc=True)
            mm = kb.t(lambda e, bk=bki, w=w: e.matmul(
                bank(bk, c1=w), bc_w[0:32, :], bc_stage[0:32, 0:w],
                start=True, stop=True), deps=[cp] + kb.bdep(bki), inc=True)
            prev_cp = mm
            ev = kb.v(lambda e, bk=bki, o=off, w=w: e.tensor_copy(
                dst16[:, dst_off + o:dst_off + o + w], bank(bk, c1=w)),
                deps=[mm], inc=True)
            kb.bset(ev, bki)
            toks.append(ev)
            off += w
            bi += 1
        return toks

    def bc_row32(src_row, src_off, n, dst32, dst_off, deps, bki=0):
        cp = kb.v(lambda e, n=n: e.tensor_copy(
            bc_st32[0:1, 0:n], src_row[0:1, src_off:src_off + n]),
            deps=list(deps) + [tok_st32, tok_bcw32], inc=True)
        mm = kb.t(lambda e, n=n, bk=bki: e.matmul(
            bank(bk, c1=n), bc_w32[0:32, :], bc_st32[0:32, 0:n],
            start=True, stop=True), deps=[cp] + kb.bdep(bki), inc=True)
        ev = kb.v(lambda e, n=n, bk=bki: e.tensor_copy(
            dst32[:, dst_off:dst_off + n], bank(bk, c1=n)), deps=[mm], inc=True)
        kb.bset(ev, bki)
        return ev

    def ln_stats(src16, nch, lnr, sq2, src_deps, lnr_deps=()):
        sq_free = [None, None]
        tok_mm = None
        for ci in range(nch):
            sl = (ci % 2) * T
            tok_sq = kb.s(lambda e, ci=ci, sl=sl: e.activation(
                sq2[:, sl:sl + T], src16[:, ci * T:(ci + 1) * T], AF.Square),
                deps=list(src_deps) + [sq_free[ci % 2]], inc=True)
            for g in range(4):
                dep0 = (kb.bdep(6, 7) + list(src_deps) + [tok_ones]) \
                    if ci == 0 else []
                kb.t(lambda e, ci=ci, g=g: e.matmul(
                    bank(6, r0=32 * g, r1=32 * g + 1), ones1[:, 0:1],
                    src16[:, ci * T + 512 * g: ci * T + 512 * (g + 1)],
                    start=(ci == 0), stop=(ci == nch - 1),
                    tile_position=(0, 32 * g), skip_group_check=True),
                    deps=dep0)
                tok_mm = kb.t(lambda e, g=g, sl=sl, ci=ci: e.matmul(
                    bank(7, r0=32 * g, r1=32 * g + 1), ones1[:, 0:1],
                    sq2[:, sl + 512 * g: sl + 512 * (g + 1)],
                    start=(ci == 0), stop=(ci == nch - 1),
                    tile_position=(0, 32 * g), skip_group_check=True),
                    deps=[tok_sq] + (dep0 if ci == 0 else []),
                    inc=(g == 3))
            sq_free[ci % 2] = tok_mm
        for g in range(4):
            ev1 = kb.v(lambda e, g=g: e.tensor_copy(
                stat_ev[32 * g:32 * g + 1, 0:512],
                bank(6, r0=32 * g, r1=32 * g + 1)),
                deps=([tok_mm] + kb.gd("stat_ev")) if g == 0 else [], inc=True)
        for g in range(4):
            ev2 = kb.v(lambda e, g=g: e.tensor_copy(
                stat_ev[32 * g:32 * g + 1, 512:1024],
                bank(7, r0=32 * g, r1=32 * g + 1)),
                deps=[tok_mm] if g == 0 else [], inc=True)
        kb.bset(ev2, 6, 7)
        g1 = kb.dma("y", lnr[0:1, 0:T], stat_ev[0:97:32, 0:512],
                    deps=[ev1] + list(lnr_deps))
        g2 = kb.dma("y", lnr[0:1, T:2 * T], stat_ev[0:97:32, 512:1024],
                    deps=[ev2] + list(lnr_deps))
        kb.gset("stat_ev", g2)
        inv = 1.0 / (nch * 128)
        kb.v(lambda e: e.tensor_scalar(lnr[0:1, 2 * T:3 * T], lnr[0:1, 0:T],
                                       inv, None, op0=ALU.mult), deps=[g1])
        kb.v(lambda e: e.tensor_scalar(lnr[0:1, T:2 * T], lnr[0:1, T:2 * T],
                                       inv, None, op0=ALU.mult), deps=[g2])
        kb.v(lambda e: e.tensor_tensor(lnr[0:1, 0:T], lnr[0:1, 2 * T:3 * T],
                                       lnr[0:1, 2 * T:3 * T], op=ALU.mult))
        tvar = kb.v(lambda e: e.tensor_tensor(lnr[0:1, T:2 * T],
                                              lnr[0:1, T:2 * T],
                                              lnr[0:1, 0:T],
                                              op=ALU.subtract), inc=True)
        tln = kb.s(lambda e: e.activation(lnr[0:1, 0:T], lnr[0:1, T:2 * T],
                                          AF.Ln, bias=epcol[0:1, 1:2]),
                   deps=[tvar, tok_ep], inc=True)
        tr = kb.s(lambda e: e.activation(lnr[0:1, 0:T], lnr[0:1, 0:T],
                                         AF.Exp, scale=-0.5),
                  deps=[tln], inc=True)
        tmr = kb.v(lambda e: e.tensor_tensor(lnr[0:1, T:2 * T],
                                             lnr[0:1, 2 * T:3 * T],
                                             lnr[0:1, 0:T], op=ALU.mult),
                   deps=[tr], inc=True)
        return tr, tmr

    def ln_bcast(lnr, brw, tok_r, tok_mr):
        tR = bc_row16(lnr, 0, T, brw, 0, [tok_r], banks=(0, 1))
        tMR = bc_row16(lnr, T, T, brw, T, [tok_mr], banks=(2, 3))
        return tR + tMR

    def ln_apply(dst, dst_off, src, src_off, brw, deps):
        t1 = kb.v(lambda e: e.tensor_tensor(
            dst[:, dst_off:dst_off + T], src[:, src_off:src_off + T],
            brw[:, 0:T], op=ALU.mult), deps=deps, inc=True)
        return kb.v(lambda e: e.tensor_tensor(
            dst[:, dst_off:dst_off + T], dst[:, dst_off:dst_off + T],
            brw[:, T:2 * T], op=ALU.subtract), deps=[t1], inc=True)

    def emit_dequant():
        dqw, dw_dep = ar.alloc("dqw", I3, F32)
        dqs, ds_dep = ar.alloc("dqs", I3 // 32, F32)
        dqr, dr_dep = ar.alloc("dqr", I3 // 32, F32)
        dqi, di_dep = ar.alloc("dqi", I3, I32)
        dqq, dq_dep = ar.alloc("dqq", I3, F16)
        dqt, dt_dep = ar.alloc("dqt", I3, F16)
        base_dep = dw_dep + ds_dep + dr_dep + di_dep + dq_dep + dt_dep
        specs = [(d_wup, d_upT, I3, C), (d_wgate, d_gateT, I3, C),
                 (d_wdown, d_downT, C, I3)]
        prev_q = None
        prev_sp = None
        last_sp = None
        for (dsrc, ddst, O, I) in specs:
            nb = I // 32
            nic = I // 128
            for oc in range(O // 128):
                ld = kb.dma("y", dqw[:, 0:I], dsrc[oc * 128:(oc + 1) * 128, :],
                            deps=[prev_q] + base_dep)
                bm = kb.v(lambda e, I=I, nb=nb: e.tensor_reduce(
                    dqs[:, 0:nb],
                    dqw[:, 0:I].rearrange("p (nb b) -> p nb b", b=32),
                    axis=AX, op=ALU.max, apply_absolute_value=True),
                    deps=[ld], inc=True)
                sc = kb.v(lambda e, nb=nb: e.tensor_scalar(
                    dqs[:, 0:nb], dqs[:, 0:nb], 1.0 / 7.0, 1e-12,
                    op0=ALU.mult, op1=ALU.max), deps=[bm], inc=True)
                rc = kb.v(lambda e, nb=nb: e.reciprocal(dqr[:, 0:nb],
                                                        dqs[:, 0:nb]),
                          deps=[sc], inc=True)
                nn = kb.v(lambda e, I=I, nb=nb: e.tensor_tensor(
                    dqi[:, 0:I], dqw[:, 0:I],
                    dqr[:, 0:nb].unsqueeze(2).broadcast_to((128, nb, 32)), op=ALU.mult),
                    deps=[rc], inc=True)
                qq = kb.v(lambda e, I=I, nb=nb: e.tensor_tensor(
                    dqq[:, 0:I], dqi[:, 0:I],
                    dqs[:, 0:nb].unsqueeze(2).broadcast_to((128, nb, 32)), op=ALU.mult),
                    deps=[nn], inc=True)
                prev_q = qq
                ev_toks = []
                for grp in range((nic + 7) // 8):
                    tb = 4 + grp % 2
                    ntr = min(8, nic - grp * 8)
                    tk = None
                    for k in range(ntr):
                        ci = grp * 8 + k
                        tk = kb.t(lambda e, tb=tb, k=k, ci=ci: e.matmul(
                            PS16[:, tb * 1024 + 128 * k:
                                 tb * 1024 + 128 * (k + 1)],
                            dqq[:, ci * 128:(ci + 1) * 128], idf[:, :],
                            start=True, stop=True, is_transpose=True,
                            skip_group_check=True),
                            deps=([qq, tok_id] + kb.bdep(tb)) if k == 0 else [],
                            inc=(k == ntr - 1))
                    ev = kb.v(lambda e, tb=tb, grp=grp, ntr=ntr: e.tensor_copy(
                        dqt[:, grp * 1024: grp * 1024 + ntr * 128],
                        PS16[:, tb * 1024: tb * 1024 + ntr * 128]),
                        deps=[tk, prev_sp], inc=True)
                    kb.bset(ev, tb)
                    ev_toks.append(ev)
                sp = kb.dma("s", ddst.rearrange("(ci p) j -> p ci j", p=128)
                            [:, :, oc * 128:(oc + 1) * 128],
                            dqt[:, 0:I].rearrange("p (ci j) -> p ci j", j=128),
                            deps=ev_toks)
                prev_sp = sp
                last_sp = sp
        return last_sp, ("dqw", "dqs", "dqr", "dqi", "dqq", "dqt")

    def emit_attention(s, first_deps, free_after_qkv=None):
        xln, d5 = ar.alloc(f"xln{s}", NCH * T, F16)
        lnr, d1 = ar.alloc(f"lnr{s}", 3 * T, F32)
        sq2, d2 = ar.alloc(f"sq2{s}", 2 * T, F16)
        xst, d3 = ar.alloc(f"xst{s}", 2 * T, F16)
        brw, d4 = ar.alloc(f"brw{s}", 2 * T, F16)
        adeps = d1 + d2 + d3 + d4 + d5 + list(first_deps)
        xst_free = [None, None]
        st_deps = []
        for ci in range(NCH):
            sl = (ci % 2) * T
            lx = kb.dma("g", xst[:, sl:sl + T],
                        dx[s, ci * 128:(ci + 1) * 128, :],
                        deps=adeps + [xst_free[ci % 2]])
            tok_sq = kb.s(lambda e, sl=sl: e.activation(
                sq2[:, sl:sl + T], xst[:, sl:sl + T], AF.Square),
                deps=[lx, xst_free[ci % 2]], inc=True)
            for g in range(4):
                dep0 = (kb.bdep(6, 7) + [tok_ones]) if ci == 0 else []
                t1 = kb.t(lambda e, g=g, sl=sl, ci=ci: e.matmul(
                    bank(6, r0=32 * g, r1=32 * g + 1), ones1[:, 0:1],
                    xst[:, sl + 512 * g: sl + 512 * (g + 1)],
                    start=(ci == 0), stop=(ci == NCH - 1),
                    tile_position=(0, 32 * g), skip_group_check=True),
                    deps=dep0 + [lx], inc=(g == 3))
                t2 = kb.t(lambda e, g=g, sl=sl, ci=ci: e.matmul(
                    bank(7, r0=32 * g, r1=32 * g + 1), ones1[:, 0:1],
                    sq2[:, sl + 512 * g: sl + 512 * (g + 1)],
                    start=(ci == 0), stop=(ci == NCH - 1),
                    tile_position=(0, 32 * g), skip_group_check=True),
                    deps=[tok_sq] + dep0, inc=(g == 3))
            xst_free[ci % 2] = t2
            st_deps.append(t2)
        tok_mm = st_deps[-1]
        for g in range(4):
            ev1 = kb.v(lambda e, g=g: e.tensor_copy(
                stat_ev[32 * g:32 * g + 1, 0:512],
                bank(6, r0=32 * g, r1=32 * g + 1)),
                deps=([tok_mm] + kb.gd("stat_ev")) if g == 0 else [], inc=True)
        for g in range(4):
            ev2 = kb.v(lambda e, g=g: e.tensor_copy(
                stat_ev[32 * g:32 * g + 1, 512:1024],
                bank(7, r0=32 * g, r1=32 * g + 1)),
                deps=[tok_mm] if g == 0 else [], inc=True)
        kb.bset(ev2, 6, 7)
        g1 = kb.dma("y", lnr[0:1, 0:T], stat_ev[0:97:32, 0:512], deps=[ev1])
        g2 = kb.dma("y", lnr[0:1, T:2 * T], stat_ev[0:97:32, 512:1024],
                    deps=[ev2])
        kb.gset("stat_ev", g2)
        inv = 1.0 / C
        kb.v(lambda e: e.tensor_scalar(lnr[0:1, 2 * T:3 * T], lnr[0:1, 0:T],
                                       inv, None, op0=ALU.mult), deps=[g1])
        kb.v(lambda e: e.tensor_scalar(lnr[0:1, T:2 * T], lnr[0:1, T:2 * T],
                                       inv, None, op0=ALU.mult), deps=[g2])
        kb.v(lambda e: e.tensor_tensor(lnr[0:1, 0:T], lnr[0:1, 2 * T:3 * T],
                                       lnr[0:1, 2 * T:3 * T], op=ALU.mult))
        tvar = kb.v(lambda e: e.tensor_tensor(lnr[0:1, T:2 * T],
                                              lnr[0:1, T:2 * T],
                                              lnr[0:1, 0:T],
                                              op=ALU.subtract), inc=True)
        tln = kb.s(lambda e: e.activation(lnr[0:1, 0:T], lnr[0:1, T:2 * T],
                                          AF.Ln, bias=epcol[0:1, 1:2]),
                   deps=[tvar, tok_ep], inc=True)
        tr = kb.s(lambda e: e.activation(lnr[0:1, 0:T], lnr[0:1, 0:T],
                                         AF.Exp, scale=-0.5),
                  deps=[tln], inc=True)
        tmr = kb.v(lambda e: e.tensor_tensor(lnr[0:1, T:2 * T],
                                             lnr[0:1, 2 * T:3 * T],
                                             lnr[0:1, 0:T], op=ALU.mult),
                   deps=[tr], inc=True)
        bts = ln_bcast(lnr, brw, tr, tmr)
        xln_toks = []
        xst_free2 = [st_deps[-1], st_deps[-1]]
        for ci in range(NCH):
            sl = (ci % 2) * T
            lx = kb.dma("g", xst[:, sl:sl + T],
                        dx[s, ci * 128:(ci + 1) * 128, :],
                        deps=[xst_free2[ci % 2]] + bts)
            tk = ln_apply(xln, ci * T, xst, sl, brw, [lx] + bts)
            xst_free2[ci % 2] = tk
            xln_toks.append(tk)
        tap(f"xln{s}", xln[:, :], xln_toks)
        ar.free(f"lnr{s}", xln_toks[-1])
        ar.free(f"sq2{s}", xln_toks[-1])
        ar.free(f"xst{s}", xln_toks[-1])
        ar.free(f"brw{s}", xln_toks[-1])
        if _ATTN_CUT == 1:
            return xln_toks

        wst, d7 = ar.alloc(f"wst{s}", 2 * NCH * 128, F16)
        qkvb, d6 = ar.alloc(f"qkv{s}", 3 * NCH * T, F16)
        qb = qkvb[:, 0:NCH * T]
        kbu = qkvb[:, NCH * T:2 * NCH * T]
        vbu = qkvb[:, 2 * NCH * T:3 * NCH * T]
        qkv_ev = []
        wld = {}
        for J in range(NI):
            ws = J % 2
        wfree = {}
        for J in range(NI):
            ws = J % 2
            wld[J] = kb.dma("g",
                            wst[:, ws * 1024:(ws + 1) * 1024]
                            .rearrange("p (ci j) -> p ci j", j=128),
                            d_wattn.rearrange("(ci p) j -> p ci j", p=128)
                            [:, :, J * 128:(J + 1) * 128],
                            deps=d7 + d6 + [wfree.get(J - 2)])
            dci = J % NCH
            dstoff = (J // NCH) * NCH * T + dci * T
            for tg in range(4):
                bk = (J * 4 + tg) % 4
                tkmm = None
                for ci in range(NCH):
                    tkmm = kb.t(lambda e, J=J, ws=ws, tg=tg, ci=ci, bk=bk:
                                e.matmul(
                        bank(bk),
                        wst[:, ws * 1024 + ci * 128: ws * 1024 + (ci + 1) * 128],
                        xln[:, ci * T + tg * 512: ci * T + (tg + 1) * 512],
                        start=(ci == 0), stop=(ci == NCH - 1)),
                        deps=(kb.bdep(bk) + [wld[J], wfree.get(J - 2)]
                              + xln_toks) if ci == 0 else [],
                        inc=(ci == NCH - 1))
                ev = kb.s(lambda e, off=dstoff, tg=tg, bk=bk, J=J: e.activation(
                    qkvb[:, off + tg * 512: off + (tg + 1) * 512],
                    bank(bk), AF.Identity, bias=b_attn[:, J:J + 1]),
                    deps=[tkmm, tok_battn], inc=True)
                kb.bset(ev, bk)
                qkv_ev.append(ev)
            wfree[J] = qkv_ev[-1]
        ar.free(f"wst{s}", qkv_ev[-1])
        ar.free(f"xln{s}", qkv_ev[-1])
        if free_after_qkv is not None:
            for nm in free_after_qkv[0]:
                ar.free(nm, free_after_qkv[1])
        tap(f"qb{s}", qb, [qkv_ev[-1]])
        tap(f"kbu{s}", kbu, [qkv_ev[-1]])
        tap(f"vbu{s}", vbu, [qkv_ev[-1]])
        if _ATTN_CUT == 2:
            return [qkv_ev[-1]]

        ccst, d8 = ar.alloc(f"ccst{s}", NH * HS, F16)
        conv_ev, d9 = ar.alloc(f"cev{s}", 2 * 4 * NB, F16)
        sqn, d10 = ar.alloc(f"sqn{s}", 2 * T, F16)
        cdeps = d8 + d9 + d10
        kc_toks, vc_toks, kcn_tok = emit_compress(
            s, qb, kbu, vbu, ccst, conv_ev, sqn, qkv_ev, cdeps)
        if _ATTN_CUT == 3 or (_ATTN_CUT or 0) > 300:
            return [t for t in kc_toks + vc_toks + [kcn_tok]
                    if t is not None]

        tok_mn = emit_norms(s, qb, kbu, sqn, qkv_ev, kcn_tok)
        ar.free(f"ccst{s}", kc_toks[-1])
        ar.free(f"cev{s}", kc_toks[-1])
        ar.free(f"sqn{s}", tok_mn)
        if _ATTN_CUT == 4:
            return [tok_mn]

        locb, d11 = ar.alloc(f"loc{s}", NCH * T, F16)
        cmpb, d12 = ar.alloc(f"cmp{s}", NCH * T, F16)
        loc_toks, cmp_toks = emit_attn_core(
            s, qb, kbu, vbu, locb, cmpb, kc_toks, vc_toks, tok_mn, qkv_ev,
            d11 + d12)
        ar.free(f"qkv{s}", cmp_toks[-1])
        if _ATTN_CUT == 5:
            return loc_toks + cmp_toks

        lnmb, d13 = ar.alloc(f"lnm{s}", NCH * T, F16)
        lnrb, d14 = ar.alloc(f"lnr2{s}", 3 * T, F32)
        sq2b_, d15 = ar.alloc(f"sq2b{s}", 2 * T, F16)
        brwb, d16 = ar.alloc(f"brw2{s}", 2 * T, F16)
        ldeps = d13 + d14 + d15 + d16 + loc_toks + cmp_toks
        tr, tmr = ln_stats(locb, NCH, lnrb, sq2b_, ldeps)
        bts = ln_bcast(lnrb, brwb, tr, tmr)
        lnm_toks = []
        for ci in range(NCH):
            tt = ln_apply(lnmb, ci * T, locb, ci * T, brwb, ldeps + bts)
            tt2 = kb.v(lambda e, ci=ci: e.tensor_scalar(
                lnmb[:, ci * T:(ci + 1) * T], lnmb[:, ci * T:(ci + 1) * T],
                lnl_w[:, ci:ci + 1], lnl_b[:, ci:ci + 1],
                op0=ALU.mult, op1=ALU.add), deps=[tt, tok_lnlw, tok_lnlb],
                inc=True)
            lnm_toks.append(tt2)
        tap(f"lnm{s}", lnmb[:, :], lnm_toks)
        if _ATTN_CUT == 6:
            return lnm_toks

        wg1, dwg1 = ar.alloc(f"wg1{s}", 8 * C, F16)
        tok_wg1 = kb.dma("g",
                         wg1[:, :].rearrange("p (f j) -> p f j", j=C),
                         d_gw1.rearrange("(f p) j -> p f j", p=128)[:, 0:8, :],
                         deps=dwg1)
        tok_g = emit_gate(s, lnmb, cmpb, lnrb, lnm_toks, cmp_toks, wg1,
                          tok_wg1, d_gw1)
        ar.free(f"wg1{s}", tok_g)
        if _ATTN_CUT == 7:
            return [tok_g]

        comb_toks = []
        for ci in range(NCH):
            t1 = kb.v(lambda e, ci=ci: e.tensor_scalar(
                cmpb[:, ci * T:(ci + 1) * T], cmpb[:, ci * T:(ci + 1) * T],
                gbc[:, 1:2], None, op0=ALU.mult),
                deps=[tok_g], inc=True)
            t2 = kb.v(lambda e, ci=ci: e.scalar_tensor_tensor(
                locb[:, ci * T:(ci + 1) * T], lnmb[:, ci * T:(ci + 1) * T],
                gbc[:, 0:1], cmpb[:, ci * T:(ci + 1) * T],
                op0=ALU.mult, op1=ALU.add), deps=[t1], inc=True)
            comb_toks.append(t2)
        ar.free(f"cmp{s}", comb_toks[-1])
        tap(f"comb{s}", locb[:, :], comb_toks)
        if _ATTN_CUT == 8:
            return comb_toks

        wstp, d17 = ar.alloc(f"wstp{s}", 2 * NCH * 128, F16)
        proj_ev = []
        pwld = {}
        pwfree = {}
        for J in range(NCH):
            ws = J % 2
            pwld[J] = kb.dma("g",
                             wstp[:, ws * 1024:(ws + 1) * 1024]
                             .rearrange("p (ci j) -> p ci j", j=128),
                             d_wproj.rearrange("(ci p) j -> p ci j", p=128)
                             [:, :, J * 128:(J + 1) * 128],
                             deps=d17 + [pwfree.get(J - 2)])
            for tg in range(4):
                bk = (J * 4 + tg) % 4
                tkmm = None
                for ci in range(NCH):
                    tkmm = kb.t(lambda e, ws=ws, J=J, tg=tg, ci=ci, bk=bk:
                                e.matmul(
                        bank(bk),
                        wstp[:, ws * 1024 + ci * 128: ws * 1024 + (ci + 1) * 128],
                        locb[:, ci * T + tg * 512: ci * T + (tg + 1) * 512],
                        start=(ci == 0), stop=(ci == NCH - 1)),
                        deps=(kb.bdep(bk) + [pwld[J]] + comb_toks)
                        if ci == 0 else [], inc=(ci == NCH - 1))
                ev = kb.s(lambda e, J=J, tg=tg, bk=bk: e.activation(
                    lnmb[:, J * T + tg * 512: J * T + (tg + 1) * 512],
                    bank(bk), AF.Identity, bias=b_proj[:, J:J + 1]),
                    deps=[tkmm, tok_bproj] + lnm_toks, inc=True)
                kb.bset(ev, bk)
                proj_ev.append(ev)
            pwfree[J] = proj_ev[-1]
        ar.free(f"wstp{s}", proj_ev[-1])
        tap(f"attn{s}", lnmb[:, :], [proj_ev[-1]])
        if _ATTN_CUT == 9:
            return [proj_ev[-1]]

        x2c, d18 = ar.alloc(f"x2c{s}", 2 * T, F32)
        tr, tmr = ln_stats(lnmb, NCH, lnrb, sq2b_, proj_ev + d18,
                           lnr_deps=[tok_g])
        bts = ln_bcast(lnrb, brwb, tr, tmr)
        x2_toks = []
        ln2_mm = None
        ln2_mm6 = None
        prev_ld = None
        prev_cst = [None]
        for ci in range(NCH):
            u = ln_apply(locb, ci * T, lnmb, ci * T, brwb,
                         proj_ev + bts + comb_toks)
            xr = kb.dma("y", x2c[:, 0:T], dx[s, ci * 128:(ci + 1) * 128, :],
                        deps=[prev_ld])
            ad = kb.v(lambda e, ci=ci: e.tensor_tensor(
                x2c[:, T:2 * T], locb[:, ci * T:(ci + 1) * T],
                x2c[:, 0:T], op=ALU.add),
                deps=[u, xr, prev_cst[0]], inc=True)
            st = kb.dma("y", d_x2[s, ci * 128:(ci + 1) * 128, :],
                        x2c[:, T:2 * T], deps=[ad])
            prev_ld = st
            cst = kb.s(lambda e: e.activation(
                sq2b_[:, 0:T], x2c[:, T:2 * T], AF.Identity),
                deps=[ad, ln2_mm6, ln2_mm], inc=True)
            prev_cst[0] = cst
            sqq = kb.s(lambda e: e.activation(
                sq2b_[:, T:2 * T], sq2b_[:, 0:T], AF.Square),
                deps=[cst, ln2_mm], inc=True)
            for g in range(4):
                dep0 = (kb.bdep(6, 7)) if ci == 0 else []
                ln2_mm6 = kb.t(lambda e, g=g, ci=ci: e.matmul(
                    bank(6, r0=32 * g, r1=32 * g + 1), ones1[:, 0:1],
                    sq2b_[:, 512 * g: 512 * (g + 1)],
                    start=(ci == 0), stop=(ci == NCH - 1),
                    tile_position=(0, 32 * g), skip_group_check=True),
                    deps=dep0 + [cst], inc=(g == 3))
                ln2_mm = kb.t(lambda e, g=g, ci=ci: e.matmul(
                    bank(7, r0=32 * g, r1=32 * g + 1), ones1[:, 0:1],
                    sq2b_[:, T + 512 * g: T + 512 * (g + 1)],
                    start=(ci == 0), stop=(ci == NCH - 1),
                    tile_position=(0, 32 * g), skip_group_check=True),
                    deps=dep0 + [sqq], inc=(g == 3))
            x2_toks.append(st)
        for g in range(4):
            ev1 = kb.v(lambda e, g=g: e.tensor_copy(
                stat_ev[32 * g:32 * g + 1, 0:512],
                bank(6, r0=32 * g, r1=32 * g + 1)),
                deps=([ln2_mm] + kb.gd("stat_ev")) if g == 0 else [], inc=True)
        for g in range(4):
            ev2 = kb.v(lambda e, g=g: e.tensor_copy(
                stat_ev[32 * g:32 * g + 1, 512:1024],
                bank(7, r0=32 * g, r1=32 * g + 1)),
                deps=[ln2_mm] if g == 0 else [], inc=True)
        kb.bset(ev2, 6, 7)
        g1 = kb.dma("y", lnrb[0:1, 0:T], stat_ev[0:97:32, 0:512], deps=[ev1])
        g2 = kb.dma("y", lnrb[0:1, T:2 * T], stat_ev[0:97:32, 512:1024],
                    deps=[ev2])
        kb.gset("stat_ev", g2)
        inv = 1.0 / C
        kb.v(lambda e: e.tensor_scalar(lnrb[0:1, 2 * T:3 * T], lnrb[0:1, 0:T],
                                       inv, None, op0=ALU.mult), deps=[g1])
        kb.v(lambda e: e.tensor_scalar(lnrb[0:1, T:2 * T], lnrb[0:1, T:2 * T],
                                       inv, None, op0=ALU.mult), deps=[g2])
        kb.v(lambda e: e.tensor_tensor(lnrb[0:1, 0:T], lnrb[0:1, 2 * T:3 * T],
                                       lnrb[0:1, 2 * T:3 * T], op=ALU.mult))
        tvar = kb.v(lambda e: e.tensor_tensor(lnrb[0:1, T:2 * T],
                                              lnrb[0:1, T:2 * T],
                                              lnrb[0:1, 0:T],
                                              op=ALU.subtract), inc=True)
        tln = kb.s(lambda e: e.activation(lnrb[0:1, 0:T], lnrb[0:1, T:2 * T],
                                          AF.Ln, bias=epcol[0:1, 1:2]),
                   deps=[tvar, tok_ep], inc=True)
        trr = kb.s(lambda e: e.activation(lnrb[0:1, 0:T], lnrb[0:1, 0:T],
                                          AF.Exp, scale=-0.5),
                   deps=[tln], inc=True)
        sv1 = kb.dma("y", d_ln2[0:1, s * 2 * T: s * 2 * T + T],
                     lnrb[0:1, 2 * T:3 * T], deps=[trr])
        sv2 = kb.dma("y", d_ln2[0:1, s * 2 * T + T: s * 2 * T + 2 * T],
                     lnrb[0:1, 0:T], deps=[trr, sv1])
        end = sv2
        for nm in (f"lnm{s}", f"lnr2{s}", f"sq2b{s}", f"brw2{s}", f"x2c{s}",
                   f"loc{s}"):
            ar.free(nm, end)
        return x2_toks + [sv1, sv2]

    def emit_compress(s, qb, kbu, vbu, ccst, conv_ev, sqn, qkv_ev, cdeps):
        kc_toks, vc_toks, kcn = [], [], [None]
        for (zbuf, is_k) in ((kbu, True), (vbu, False)):
            mm_done = []
            for tap_ in range(KTAP):
                for halfc in range(2):
                    for par in range(2):
                        bk = halfc + 2 * par
                        deps = (qkv_ev + [tok_wconv] + kb.bdep(bk) + cdeps) \
                            if tap_ == 0 else []
                        tk = kb.t(lambda e, tap_=tap_, halfc=halfc, par=par,
                                  bk=bk, z=zbuf: e.matmul(
                            bank(bk, r0=0, r1=64, c1=508),
                            w_conv[par * 64:(par + 1) * 64,
                                   tap_ * 64:(tap_ + 1) * 64],
                            z[par * 64:(par + 1) * 64, :]
                            .rearrange("p (ci t) -> p ci t", t=T)
                            [:, 4 * halfc:4 * halfc + 4,
                             tap_:tap_ + CSTR * 126 + 1:CSTR],
                            start=(tap_ == 0), stop=(tap_ == KTAP - 1),
                            tile_position=(par * 64, 0)),
                            deps=deps, inc=(tap_ == KTAP - 1))
                        if tap_ == KTAP - 1:
                            mm_done.append(tk)
            ev_toks = []
            for halfc in range(2):
                for par in range(2):
                    bk = halfc + 2 * par
                    ev = kb.s(lambda e, halfc=halfc, par=par, bk=bk:
                              e.activation(
                        conv_ev[par * 64:(par + 1) * 64,
                                halfc * 508:(halfc + 1) * 508],
                        bank(bk, r0=0, r1=64, c1=508), AF.Identity,
                        bias=b_conv[par * 64:(par + 1) * 64, :]),
                        deps=mm_done + [tok_bconv], inc=True)
                    kb.bset(ev, bk)
                    ev_toks.append(ev)
            if _ATTN_CUT == 301:
                return ev_toks, [], None
            if _ATTN_CUT == 305 and not is_k:
                return kc_toks, [ev_toks[-1]], kcn[0]
            tk = None
            for par in range(2):
                hs = [h for h in range(NH) if h % 2 == par]
                for j, h in enumerate(hs):
                    halfc, pos = divmod(h // 2, 4)
                    first = (par == 0 and j == 0)
                    tk = kb.t(lambda e, h=h, par=par, halfc=halfc, pos=pos:
                              e.matmul(
                        PS16[0:NB, 4 * 1024 + 64 * h: 4 * 1024 + 64 * (h + 1)],
                        conv_ev[par * 64:(par + 1) * 64,
                                halfc * 508 + pos * 127:
                                halfc * 508 + (pos + 1) * 127],
                        idf[par * 64:(par + 1) * 64, par * 64:par * 64 + 64],
                        start=True, stop=True, is_transpose=True,
                        skip_group_check=True),
                        deps=(ev_toks + [tok_id] + kb.bdep(4)) if first
                        else ([tk] if j == 0 else []),
                        inc=(j == len(hs) - 1))
            if _ATTN_CUT == 302:
                ev302 = kb.v(lambda e: e.tensor_copy(
                    ccst[0:NB, :], PS16[0:NB, 4 * 1024: 4 * 1024 + 1024]),
                    deps=[tk], inc=True)
                kb.bset(ev302, 4)
                return [ev302], [], None
            t_sz = kb.v(lambda e: e.tensor_reduce(
                red16[0:NB, 0:NH],
                PS16[0:NB, 4 * 1024: 4 * 1024 + 1024]
                .rearrange("p (h c) -> p h c", c=64), axis=AX, op=ALU.add),
                deps=[tk], inc=True)
            if _ATTN_CUT == 3031:
                kb.bset(t_sz, 4)
                return [t_sz], [], None
            t_sq = kb.s(lambda e: e.activation(
                ccst[0:NB, :], PS16[0:NB, 4 * 1024: 4 * 1024 + 1024],
                AF.Square), deps=[tk, t_sz], inc=True)
            if _ATTN_CUT == 3032:
                kb.bset(t_sq, 4)
                return [t_sz, t_sq], [], None
            t_s2 = kb.v(lambda e: e.tensor_reduce(
                red16[0:NB, NH:2 * NH],
                ccst[0:NB, :].rearrange("p (h c) -> p h c", c=64),
                axis=AX, op=ALU.add), deps=[t_sq], inc=True)
            kb.v(lambda e: e.tensor_scalar(red16[0:NB, 0:NH],
                                           red16[0:NB, 0:NH], 1.0 / 64, None,
                                           op0=ALU.mult), deps=[t_sz])
            kb.v(lambda e: e.tensor_scalar(red16[0:NB, NH:2 * NH],
                                           red16[0:NB, NH:2 * NH], 1.0 / 64,
                                           None, op0=ALU.mult), deps=[t_s2])
            kb.v(lambda e: e.tensor_tensor(red16[0:NB, 2 * NH:3 * NH],
                                           red16[0:NB, 0:NH],
                                           red16[0:NB, 0:NH], op=ALU.mult))
            kb.v(lambda e: e.tensor_tensor(red16[0:NB, NH:2 * NH],
                                           red16[0:NB, NH:2 * NH],
                                           red16[0:NB, 2 * NH:3 * NH],
                                           op=ALU.subtract))
            kb.v(lambda e: e.tensor_scalar(red16[0:NB, NH:2 * NH],
                                           red16[0:NB, NH:2 * NH], EPS, None,
                                           op0=ALU.add))
            t_iv = kb.v(lambda e: e.reciprocal(red16[0:NB, NH:2 * NH],
                                               red16[0:NB, NH:2 * NH]),
                        inc=True)
            t_r = kb.s(lambda e: e.activation(red16[0:NB, NH:2 * NH],
                                              red16[0:NB, NH:2 * NH],
                                              AF.Sqrt), deps=[t_iv], inc=True)
            t_nmr = kb.v(lambda e: e.tensor_tensor(
                red16[0:NB, 2 * NH:3 * NH], red16[0:NB, 0:NH],
                red16[0:NB, NH:2 * NH], op=ALU.mult), deps=[t_r], inc=True)
            t_nm2 = kb.v(lambda e: e.tensor_scalar(
                red16[0:NB, 2 * NH:3 * NH], red16[0:NB, 2 * NH:3 * NH],
                -1.0, None, op0=ALU.mult), deps=[t_nmr], inc=True)
            if _ATTN_CUT == 3033:
                kb.bset(t_nm2, 4)
                return [t_nm2, t_sz], [], None
            nrm = []
            for h in range(NH):
                nrm.append(kb.v(lambda e, h=h: e.tensor_scalar(
                    ccst[0:NB, h * 64:(h + 1) * 64],
                    PS16[0:NB, 4 * 1024 + 64 * h: 4 * 1024 + 64 * (h + 1)],
                    red16[0:NB, NH + h:NH + h + 1],
                    red16[0:NB, 2 * NH + h:2 * NH + h + 1],
                    op0=ALU.mult, op1=ALU.add), deps=[t_nm2], inc=True))
            kb.bset(nrm[-1], 4)
            if _ATTN_CUT == 3034:
                return [nrm[-1], t_sz], [], None
            t_gm = kb.v(lambda e: e.tensor_tensor(
                ccst[0:NB, :], ccst[0:NB, :], crep_w[0:NB, :], op=ALU.mult),
                deps=nrm + [tok_crep], inc=True)
            t_gb = kb.v(lambda e: e.tensor_tensor(
                ccst[0:NB, :], ccst[0:NB, :], crep_b[0:NB, :], op=ALU.add),
                deps=[t_gm], inc=True)
            if _ATTN_CUT == 303:
                kb.bset(t_gb, 4)
                return [t_gb], [], None
            if _ATTN_CUT == 306 and not is_k:
                kb.bset(t_gb, 4)
                return kc_toks, [t_gb], kcn[0]
            if is_k:
                t_act = kb.s(lambda e: e.activation(
                    ccst[0:NB, :], ccst[0:NB, :], AF.Gelu), deps=[t_gb],
                    inc=True)
                t_k2 = kb.s(lambda e: e.activation(
                    sqn[0:NB, 0:NH * HS], ccst[0:NB, :], AF.Square),
                    deps=[t_act], inc=True)
                kcn[0] = kb.v(lambda e: e.tensor_reduce(
                    red16[0:NB, 3 * NH:4 * NH],
                    sqn[0:NB, 0:NH * HS].rearrange("p (h c) -> p h c", c=64),
                    axis=AX, op=ALU.add), deps=[t_k2], inc=True)
                for h in range(NH):
                    par, ci = h % 2, h // 2
                    tk2 = kb.t(lambda e, h=h: e.matmul(
                        PS16[0:64, 5 * 1024: 5 * 1024 + NB],
                        ccst[0:NB, h * 64:(h + 1) * 64], idf[0:NB, 0:NB],
                        start=True, stop=True, is_transpose=True,
                        skip_group_check=True),
                        deps=[t_act, tok_id] + kb.bdep(5), inc=True)
                    ev3 = kb.v(lambda e, par=par, ci=ci: e.tensor_copy(
                        kcomp[par * 64:(par + 1) * 64,
                              ci * NB:(ci + 1) * NB],
                        PS16[0:64, 5 * 1024: 5 * 1024 + NB]), deps=[tk2],
                        inc=True)
                    kb.bset(ev3, 5)
                    kc_toks.append(ev3)
                tap(f"kcomp{s}", kcomp[:, :], [kc_toks[-1]])
                if _ATTN_CUT == 304:
                    return kc_toks, [], kcn[0]
            else:
                t_act = kb.s(lambda e: e.activation(
                    vcompT[0:NB, :].rearrange("p (h c) -> p h c", c=65)
                    [:, :, 0:64],
                    ccst[0:NB, :].rearrange("p (h c) -> p h c", c=64),
                    AF.Gelu), deps=[t_gb, tok_vc0], inc=True)
                vc_toks = [t_act]
                kb.bset(t_act, 4)
                tap(f"vcompT{s}", vcompT[:, :], vc_toks)
        return kc_toks, vc_toks, kcn[0]

    def emit_norms(s, qb, kbu, sqn, qkv_ev, kcn_tok):
        sq_free = [None, None]
        last_rd = None
        for (zbuf, base_off) in ((qb, RQ), (kbu, RK)):
            for ci in range(NCH):
                sl = (ci % 2) * T
                tsq = kb.s(lambda e, z=zbuf, ci=ci, sl=sl: e.activation(
                    sqn[:, sl:sl + T], z[:, ci * T:(ci + 1) * T], AF.Square),
                    deps=qkv_ev + [sq_free[ci % 2]], inc=True)
                for par in range(2):
                    h = ci * 2 + par
                    mm_last = None
                    for tg in range(4):
                        mm_last = kb.t(lambda e, par=par, sl=sl, tg=tg:
                                       e.matmul(
                            bank(4 + tg, r0=0, r1=1),
                            ones1[par * 64:(par + 1) * 64, 0:1],
                            sqn[par * 64:(par + 1) * 64,
                                sl + tg * 512: sl + (tg + 1) * 512],
                            start=True, stop=True,
                            tile_position=(par * 64, 0)),
                            deps=[tsq] + kb.bdep(4 + tg), inc=(tg == 3))
                    rd = kb.v(lambda e, off=base_off, h=h: e.tensor_reduce(
                        nrow[0:1, off + h: off + h + 1],
                        PS[0:1, 4 * 512: 8 * 512], axis=AX, op=ALU.max),
                        deps=[mm_last], inc=True)
                    kb.bset(rd, 4, 5, 6, 7)
                    last_rd = rd
                sq_free[ci % 2] = last_rd
        kc_rd = None
        for h in range(NH):
            gdt = kb.dma("y", stat_ev[0:1, 0:NB],
                         red16[0:NB, 3 * NH + h:3 * NH + h + 1],
                         deps=[kcn_tok, kc_rd] + kb.gd("stat_ev"))
            kc_rd = kb.v(lambda e, h=h: e.tensor_reduce(
                nrow[0:1, RKC + h:RKC + h + 1], stat_ev[0:1, 0:NB],
                axis=AX, op=ALU.max), deps=[gdt], inc=True)
        kb.gset("stat_ev", kc_rd)
        t_sqr = kb.s(lambda e: e.activation(nrow[0:1, RQ:RQ + 3 * NH],
                                            nrow[0:1, RQ:RQ + 3 * NH],
                                            AF.Sqrt),
                     deps=[kc_rd, last_rd], inc=True)
        kb.v(lambda e: e.tensor_tensor(nrow[0:1, RM:RM + NH],
                                       nrow[0:1, RQ:RQ + NH],
                                       nrow[0:1, RK:RK + NH], op=ALU.mult),
             deps=[t_sqr])
        kb.v(lambda e: e.tensor_tensor(nrow[0:1, RM + NH:RM + 2 * NH],
                                       nrow[0:1, RQ:RQ + NH],
                                       nrow[0:1, RKC:RKC + NH], op=ALU.mult),
             deps=[t_sqr])
        t_M = kb.v(lambda e: e.tensor_scalar(nrow[0:1, RM:RM + 2 * NH],
                                             nrow[0:1, RM:RM + 2 * NH],
                                             -0.125, None, op0=ALU.mult),
                   inc=True)
        tok_mn = bc_row32(nrow, RM, 32, mn_bc, 0, [t_M], bki=0)
        tap(f"mrow{s}", nrow[0:1, 0:5 * NH], [tok_mn])
        return tok_mn

    def emit_attn_core(s, qb, kbu, vbu, locb, cmpb, kc_toks, vc_toks,
                       tok_mn, qkv_ev, adeps):
        loc_toks, cmp_toks = [], []
        for branch in ("local", "comp"):
            dst = locb if branch == "local" else cmpb
            klen = 128 if branch == "local" else NB
            for ch in range(NCH):
                pair_toks = []
                vt_ev = []
                if branch == "local":
                    for par in range(2):
                        tk = None
                        for kt in range(16):
                            tk = kb.t(lambda e, par=par, kt=kt, ch=ch:
                                      e.matmul(
                                PS16[0:128, 4 * 1024 + kt * 64:
                                     4 * 1024 + (kt + 1) * 64],
                                vbu[par * 64:(par + 1) * 64,
                                    ch * T + kt * 128:
                                    ch * T + (kt + 1) * 128],
                                idf[par * 64:(par + 1) * 64,
                                    par * 64:par * 64 + 64],
                                start=True, stop=True, is_transpose=True,
                                skip_group_check=True),
                                deps=(qkv_ev + [tok_id] + kb.bdep(4))
                                if kt == 0 else [], inc=(kt == 15))
                        ev = kb.v(lambda e, par=par: e.tensor_copy(
                            vt[:, par * 1040:(par + 1) * 1040]
                            .rearrange("p (k c) -> p k c", c=65)[:, :, 0:64],
                            PS16[0:128, 4 * 1024: 4 * 1024 + 1024]
                            .rearrange("p (k c) -> p k c", c=64)),
                            deps=[tk, tok_vt0], inc=True)
                        kb.bset(ev, 4)
                        vt_ev.append(ev)
                for half in range(2):
                    q0 = half * 1024
                    ktmax = (8 * half + 8) if branch == "local" else 1
                    units = []
                    for kt in range(ktmax):
                        for qc in range(2):
                            cs = q0 + qc * 512
                            lo = max(kt * 128, cs) if branch == "local" else cs
                            if lo >= cs + 512:
                                continue
                            units.append((kt, qc, lo, cs + 512 - lo))
                    firstu, lastu = {}, {}
                    for i, (kt, qc, lo, w) in enumerate(units):
                        lastu[qc] = i
                        if qc not in firstu:
                            firstu[qc] = i
                    exp_toks, pv_toks = {}, {}
                    for i, (kt, qc, lo, w) in enumerate(units):
                        slot = i % 2
                        for par in range(2):
                            h = ch * 2 + par
                            sbk = slot + 2 * par
                            deps_s = kb.bdep(sbk) + [tok_mn] + qkv_ev + adeps
                            if branch == "comp":
                                deps_s = deps_s + kc_toks
                            if branch == "local":
                                stat = kbu[par * 64:(par + 1) * 64,
                                           ch * T + kt * 128:
                                           ch * T + kt * 128 + 128]
                            else:
                                stat = kcomp[par * 64:(par + 1) * 64,
                                             ch * NB:(ch + 1) * NB]
                            mm = kb.t(lambda e, stat=stat, par=par, ch=ch,
                                      lo=lo, w=w, sbk=sbk, klen=klen:
                                      e.matmul(
                                bank(sbk, r0=0, r1=klen, c1=w), stat,
                                qb[par * 64:(par + 1) * 64,
                                   ch * T + lo: ch * T + lo + w],
                                start=True, stop=True,
                                tile_position=(par * 64, 0)),
                                deps=deps_s + [exp_toks.get((i - 2, par))],
                                inc=True)
                            mcol = h if branch == "local" else NH + h
                            pslot = slot + 2 * par
                            ex = kb.s(lambda e, pslot=pslot, w=w, sbk=sbk,
                                      mcol=mcol, klen=klen: e.activation(
                                pt[0:klen, pslot * 512: pslot * 512 + w],
                                bank(sbk, r0=0, r1=klen, c1=w), AF.Exp,
                                bias=mn_bc[0:klen, mcol:mcol + 1],
                                scale=0.125),
                                deps=[mm, pv_toks.get((i - 2, par))],
                                inc=True)
                            msk = None
                            if branch == "local" and lo == kt * 128:
                                msk = kb.g(lambda e, pslot=pslot:
                                           e.affine_select(
                                    pt[0:128, pslot * 512: pslot * 512 + 128],
                                    pt[0:128, pslot * 512: pslot * 512 + 128],
                                    pattern=[[1, 128]], compare_op=ALU.is_ge,
                                    fill=0.0, base=0, channel_multiplier=-1),
                                    deps=[ex], inc=True)
                            if branch == "comp" and half == 0 and qc == 0:
                                msk = kb.g(lambda e, pslot=pslot:
                                           e.affine_select(
                                    pt[0:NB, pslot * 512: pslot * 512 + NB],
                                    pt[0:NB, pslot * 512: pslot * 512 + NB],
                                    pattern=[[1, NB]], compare_op=ALU.is_ge,
                                    fill=0.0, base=0, channel_multiplier=-1),
                                    deps=[ex], inc=True)
                            exp_toks[(i, par)] = msk or ex
                            ob = 4 + 2 * par
                            col = lo - q0
                            if branch == "local":
                                pvl = vt[:, par * 1040 + kt * 65:
                                         par * 1040 + (kt + 1) * 65]
                            else:
                                pvl = vcompT[0:NB, h * 65:(h + 1) * 65]
                            dep_pv = [exp_toks[(i, par)]]
                            if i == firstu[qc]:
                                dep_pv += kb.bdep(ob + qc)
                                dep_pv += vt_ev if branch == "local" \
                                    else vc_toks
                            pv = kb.t(lambda e, pvl=pvl, pslot=pslot, col=col,
                                      w=w, ob=ob, klen=klen,
                                      first=(i == firstu[qc]),
                                      last=(i == lastu[qc]): e.matmul(
                                PS[0:65, ob * 512 + col: ob * 512 + col + w],
                                pvl,
                                pt[0:klen, pslot * 512: pslot * 512 + w],
                                start=first, stop=last), deps=dep_pv,
                                inc=True)
                            pv_toks[(i, par)] = pv
                            if i == lastu[qc]:
                                pv_toks[("last", par, qc)] = pv
                    nun = len(units)
                    for par in range(2):
                        for back in (1, 2):
                            if nun - back >= 0 and (nun - back, par) in exp_toks:
                                kb.bset(exp_toks[(nun - back, par)],
                                        (nun - back) % 2 + 2 * par)
                    for par in range(2):
                        ob = 4 + 2 * par
                        dps = [pv_toks[("last", par, 0)],
                               pv_toks[("last", par, 1)]]
                        ev_raw = []
                        for qc in range(2):
                            wdst = dst[0:64, ch * T + q0 + qc * 512:
                                       ch * T + q0 + (qc + 1) * 512] \
                                if par == 0 else \
                                bstg[0:64, qc * 512:(qc + 1) * 512]
                            w1 = kb.s(lambda e, wdst=wdst, qc=qc, ob=ob:
                                      e.activation(
                                wdst,
                                PS[0:64, ob * 512 + qc * 512:
                                   ob * 512 + (qc + 1) * 512], AF.Identity,
                                scale=0.125),
                                deps=[pv_toks[("last", par, qc)]]
                                + (kb.gd("bstg") if par == 1 else []),
                                inc=True)
                            ev_raw.append(w1)
                        t1 = kb.s(lambda e, ob=ob: e.activation(
                            stat_ev[64:65, 0:1024],
                            PS[64:65, ob * 512: ob * 512 + 1024],
                            AF.Ln, bias=epcol[64:65, 0:1]),
                            deps=dps + [tok_ep] + kb.gd("div")
                            + kb.gd("stat_ev"), inc=True)
                        kb.bset(t1, ob, ob + 1)
                        t1b = kb.s(lambda e: e.activation(
                            stat_ev[64:65, 0:1024], stat_ev[64:65, 0:1024],
                            AF.Exp, scale=-0.5, bias=epcol[64:65, 2:3]),
                            deps=[t1], inc=True)
                        t2 = kb.s(lambda e: e.activation(
                            div_st[64:65, 0:1024], stat_ev[64:65, 0:1024],
                            AF.Square), deps=[t1b, tok_dst], inc=True)
                        kb.gset("stat_ev", t2)
                        wr_toks = []
                        for qc in range(2):
                            bb = ob + qc
                            mmb = kb.t(lambda e, qc=qc, bb=bb: e.matmul(
                                bank(bb), bc_w[64:96, :],
                                div_st[64:96, qc * 512:(qc + 1) * 512],
                                start=True, stop=True),
                                deps=[t2] + kb.bdep(bb), inc=True)
                            evb = kb.v(lambda e, qc=qc, bb=bb: e.tensor_copy(
                                dvb[0:64, qc * 512:(qc + 1) * 512],
                                bank(bb, r0=0, r1=64)),
                                deps=[mmb] + kb.gd("dvb"), inc=True)
                            kb.bset(evb, bb)
                            wdst = dst[0:64, ch * T + q0 + qc * 512:
                                       ch * T + q0 + (qc + 1) * 512] \
                                if par == 0 else \
                                bstg[0:64, qc * 512:(qc + 1) * 512]
                            wr = kb.v(lambda e, wdst=wdst, qc=qc:
                                      e.tensor_tensor(
                                wdst, wdst,
                                dvb[0:64, qc * 512:(qc + 1) * 512],
                                op=ALU.mult),
                                deps=[evb, ev_raw[qc]], inc=True)
                            wr_toks.append(wr)
                        kb.gset("div", wr_toks[-1])
                        kb.gset("dvb", wr_toks[-1])
                        if par == 1:
                            mv1 = kb.dma("g", d_bmov, bstg[0:64, :],
                                         deps=wr_toks)
                            mv2 = kb.dma("g",
                                         dst[64:128,
                                             ch * T + q0: ch * T + q0 + 1024],
                                         d_bmov, deps=[mv1])
                            kb.gset("bstg", mv2)
                            pair_toks.append(mv2)
                        else:
                            pair_toks.append(wr_toks[-1])
                if branch == "local":
                    loc_toks.extend(pair_toks)
                else:
                    cmp_toks.extend(pair_toks)
        tap(f"loc{s}", locb[:, :], loc_toks)
        tap(f"cmp{s}", cmpb[:, :], cmp_toks)
        return loc_toks, cmp_toks

    def emit_gate(s, lnmb, cmpb, lnr, lnm_toks, cmp_toks, wg1, tok_wg1,
                  d_gw1):
        lb1 = kb.dma("y", lnr[0:1, 2 * T:2 * T + C], d_gb1,
                     deps=[lnm_toks[-1]])
        lw = kb.dma("y", lnr[0:1, T:T + C], d_glnw, deps=[lnm_toks[-1]])
        lb = kb.dma("y", lnr[0:1, T + C:T + 2 * C], d_glnb,
                     deps=[lnm_toks[-1]])
        mm_h = None
        tok_wg1b = None
        for f in range(16):
            srcb = lnmb if f < NCH else cmpb
            ci = f % NCH
            if f == 8:
                tok_wg1b = kb.dma("g",
                                  wg1[:, :].rearrange("p (f j) -> p f j", j=C),
                                  d_gw1.rearrange("(f p) j -> p f j", p=128)
                                  [:, 8:16, :], deps=[mm_h])
            for hf in range(2):
                mm_h = kb.t(lambda e, f=f, ci=ci, srcb=srcb, hf=hf: e.matmul(
                    bank(2 + hf, r0=0, r1=1),
                    srcb[:, ci * T + T - 1: ci * T + T],
                    wg1[:, (f % 8) * C + hf * 512:
                        (f % 8) * C + (hf + 1) * 512],
                    start=(f == 0), stop=(f == 15)),
                    deps=(lnm_toks + cmp_toks + [tok_wg1] + kb.bdep(2, 3))
                    if f == 0 else ([tok_wg1b] if f == 8 else []),
                    inc=((f == 15 or f == 7) and hf == 1))
        t_h = kb.v(lambda e: e.tensor_tensor(
            lnr[0:1, 0:512], bank(2, r0=0, r1=1), lnr[0:1, 2 * T:2 * T + 512],
            op=ALU.add), deps=[mm_h, lb1], inc=True)
        t_h2 = kb.v(lambda e: e.tensor_tensor(
            lnr[0:1, 512:1024], bank(3, r0=0, r1=1),
            lnr[0:1, 2 * T + 512:2 * T + 1024], op=ALU.add),
            deps=[mm_h, lb1], inc=True)
        kb.bset(t_h2, 2, 3)
        t_s = kb.v(lambda e: e.tensor_reduce(
            nrow[0:1, RG:RG + 1], lnr[0:1, 0:C], axis=AX, op=ALU.add),
            deps=[t_h, t_h2], inc=True)
        t_q = kb.s(lambda e: e.activation(
            lnr[0:1, 2 * T:2 * T + C], lnr[0:1, 0:C], AF.Square,
            accum_out=nrow[0:1, RG + 1:RG + 2]), deps=[t_h, t_h2, lb1],
            inc=True)
        kb.v(lambda e: e.tensor_scalar(nrow[0:1, RG:RG + 2],
                                       nrow[0:1, RG:RG + 2], 1.0 / C, None,
                                       op0=ALU.mult), deps=[t_s, t_q])
        kb.v(lambda e: e.tensor_tensor(nrow[0:1, RG + 2:RG + 3],
                                       nrow[0:1, RG:RG + 1],
                                       nrow[0:1, RG:RG + 1], op=ALU.mult))
        kb.v(lambda e: e.tensor_tensor(nrow[0:1, RG + 1:RG + 2],
                                       nrow[0:1, RG + 1:RG + 2],
                                       nrow[0:1, RG + 2:RG + 3],
                                       op=ALU.subtract))
        kb.v(lambda e: e.tensor_scalar(nrow[0:1, RG + 1:RG + 2],
                                       nrow[0:1, RG + 1:RG + 2], EPS, None,
                                       op0=ALU.add))
        ti = kb.v(lambda e: e.reciprocal(nrow[0:1, RG + 1:RG + 2],
                                         nrow[0:1, RG + 1:RG + 2]), inc=True)
        trq = kb.s(lambda e: e.activation(nrow[0:1, RG + 1:RG + 2],
                                          nrow[0:1, RG + 1:RG + 2], AF.Sqrt),
                   deps=[ti], inc=True)
        tmr = kb.v(lambda e: e.tensor_tensor(nrow[0:1, RG + 2:RG + 3],
                                             nrow[0:1, RG:RG + 1],
                                             nrow[0:1, RG + 1:RG + 2],
                                             op=ALU.mult), deps=[trq],
                   inc=True)
        tnm = kb.v(lambda e: e.tensor_scalar(nrow[0:1, RG + 2:RG + 3],
                                             nrow[0:1, RG + 2:RG + 3], -1.0,
                                             None, op0=ALU.mult), deps=[tmr],
                   inc=True)
        t_n = kb.v(lambda e: e.tensor_scalar(
            lnr[0:1, 0:C], lnr[0:1, 0:C],
            nrow[0:1, RG + 1:RG + 2], nrow[0:1, RG + 2:RG + 3],
            op0=ALU.mult, op1=ALU.add), deps=[tnm], inc=True)
        t_g1 = kb.v(lambda e: e.tensor_tensor(
            lnr[0:1, 0:C], lnr[0:1, 0:C], lnr[0:1, T:T + C], op=ALU.mult),
            deps=[t_n, lw], inc=True)
        t_g2 = kb.v(lambda e: e.tensor_tensor(
            lnr[0:1, 0:C], lnr[0:1, 0:C], lnr[0:1, T + C:T + 2 * C],
            op=ALU.add), deps=[t_g1, lb], inc=True)
        t_rl = kb.v(lambda e: e.tensor_scalar(
            lnr[0:1, 0:C], lnr[0:1, 0:C], 0.0, None, op0=ALU.max),
            deps=[t_g2], inc=True)
        dd1 = kb.dma("y", d_hrow, lnr[0:1, 0:C], deps=[t_rl])
        sem_h = f"dg{kb.qrr['g']}"
        kb.qrr['g'] = (kb.qrr['g'] + 1) % 4
        dd2 = kb.emit("g", lambda e: e.dma_start(
            out=hcol[:, :],
            in_=d_hrow.rearrange("o (ci p) -> (o p) ci", p=128),
            allow_slow_non_contiguous=True),
            deps=[dd1, (sem_h, kb.cnt[sem_h])], inc=sem_h, inc_by=16)
        mm2 = None
        for ci in range(NCH):
            mm2 = kb.t(lambda e, ci=ci: e.matmul(
                bank(2, r0=0, r1=1, c1=2), hcol[:, ci:ci + 1],
                w_g2[:, ci * 2:(ci + 1) * 2], start=(ci == 0),
                stop=(ci == NCH - 1)),
                deps=([dd2, tok_wg2] + kb.bdep(2)) if ci == 0 else [],
                inc=(ci == NCH - 1))
        lg2 = kb.dma("y", nrow[0:1, RG + 10:RG + 12], d_gb2, deps=[])
        t_l = kb.v(lambda e: e.tensor_tensor(
            nrow[0:1, RG + 4:RG + 6], bank(2, r0=0, r1=1, c1=2),
            nrow[0:1, RG + 10:RG + 12], op=ALU.add), deps=[mm2, lg2],
            inc=True)
        kb.bset(t_l, 2)
        t_d = kb.v(lambda e: e.tensor_tensor(
            nrow[0:1, RG + 6:RG + 7], nrow[0:1, RG + 4:RG + 5],
            nrow[0:1, RG + 5:RG + 6], op=ALU.subtract), deps=[t_l], inc=True)
        t_sg = kb.s(lambda e: e.activation(nrow[0:1, RG + 8:RG + 9],
                                           nrow[0:1, RG + 6:RG + 7],
                                           AF.Sigmoid), deps=[t_d], inc=True)
        t_g1r = kb.v(lambda e: e.tensor_scalar(nrow[0:1, RG + 9:RG + 10],
                                               nrow[0:1, RG + 8:RG + 9], -1.0,
                                               1.0, op0=ALU.mult, op1=ALU.add),
                     deps=[t_sg], inc=True)
        tok_g = bc_row32(nrow, RG + 8, 2, gbc, 0, [t_g1r], bki=2)
        tap(f"gates{s}", nrow[0:1, RG + 8:RG + 10], [tok_g])
        return tok_g

    def cc_exchange(idx, row, off, deps):
        cp = kb.v(lambda e, o=off: e.tensor_copy(
            stat_ev[0:1, 0:8], row[0:1, o:o + 8]),
            deps=list(deps) + kb.gd("stat_ev"), inc=True)
        d1 = kb.dma("g", d_ccin[idx], stat_ev[0:1, 0:8], deps=[cp])
        t_cc = kb.emit("g", lambda e, i=idx: e.collective_compute(
            "AllReduce", ALU.max, replica_groups=[list(range(NCORES))],
            ins=[d_ccin[i].opt()], outs=[d_ccout[i].opt()]),
            deps=[d1], inc="cc")
        d2 = kb.dma("g", row[0:1, off:off + 8], d_ccout[idx], deps=[t_cc])
        kb.gset("stat_ev", d2)
        return d2

    def colmax(c0, c1, row, dst_off, deps):
        rd = kb.v(lambda e, c0=c0, c1=c1: e.tensor_reduce(
            maxc[:, 255:256], maxc[:, c0:c1], axis=AX, op=ALU.max),
            deps=deps, inc=True)
        gdd = kb.dma("y", stat_ev[0:1, 0:128], maxc[:, 255:256],
                     deps=[rd] + kb.gd("stat_ev"))
        rd2 = kb.v(lambda e, d=dst_off: e.tensor_reduce(
            row[0:1, d:d + 1], stat_ev[0:1, 0:128], axis=AX, op=ALU.max),
            deps=[gdd], inc=True)
        kb.gset("stat_ev", rd2)
        return rd2

    def scale_pair(row, src, dst_s, dst_rs, deps):
        t1 = kb.v(lambda e: e.tensor_scalar(
            row[0:1, dst_s:dst_s + 1], row[0:1, src:src + 1], 1.0 / QMAX,
            1e-12, op0=ALU.mult, op1=ALU.max), deps=deps, inc=True)
        t2 = kb.v(lambda e: e.reciprocal(
            row[0:1, dst_rs:dst_rs + 1], row[0:1, dst_s:dst_s + 1]),
            deps=[t1], inc=True)
        return t2

    def emit_ffn(attn_done):
        xqb, a1 = ar.alloc("xq", 16 * T, F16)
        lnr, a2 = ar.alloc("lnrF", 3 * T, F32)
        brw, a3 = ar.alloc("brwF", 2 * T, F16)
        x2c, a4 = ar.alloc("x2cF", 2 * T, F16)
        fdeps = a1 + a2 + a3 + a4 + list(attn_done)
        xq_toks = []
        red_toks = []
        mx = 0
        prev_lx = [None, None]
        prev_bts = []
        for s in range(S):
            lm = kb.dma("y", lnr[0:1, 2 * T:3 * T],
                        d_ln2[0:1, s * 2 * T: s * 2 * T + T],
                        deps=fdeps + prev_bts)
            lr = kb.dma("y", lnr[0:1, 0:T],
                        d_ln2[0:1, s * 2 * T + T: s * 2 * T + 2 * T],
                        deps=fdeps + prev_bts + [lm])
            tmr = kb.v(lambda e: e.tensor_tensor(
                lnr[0:1, T:2 * T], lnr[0:1, 2 * T:3 * T], lnr[0:1, 0:T],
                op=ALU.mult), deps=[lm, lr], inc=True)
            bts = ln_bcast(lnr, brw, lr, tmr)
            prev_bts = bts + [tmr]
            for ci in range(NCH):
                sl = (ci % 2) * T
                lx = kb.dma("g", x2c[:, sl:sl + T],
                            d_x2[s, ci * 128:(ci + 1) * 128, :],
                            deps=fdeps + [prev_lx[ci % 2]])
                xcol = (ci * 2 + s) * T
                tk = ln_apply(xqb, xcol, x2c, sl, brw, [lx] + bts)
                prev_lx[ci % 2] = tk
                rd = kb.v(lambda e, xcol=xcol, mx=mx: e.tensor_reduce(
                    maxc[:, mx:mx + 1], xqb[:, xcol:xcol + T], axis=AX,
                    op=ALU.max, apply_absolute_value=True), deps=[tk],
                    inc=True)
                red_toks.append(rd)
                xq_toks.append(tk)
                mx += 1
        tok_mx = colmax(0, 16, nrow, RX, red_toks)
        t_x0 = cc_exchange(0, nrow, RX, [tok_mx])
        t_s0 = scale_pair(nrow, RX, RX + 16, RX + 17, [t_x0])
        sb0 = bc_row32(nrow, RX + 16, 2, scbc, 0, [t_s0], bki=0)
        q_toks = []
        chiq, a5 = ar.alloc("chiq", T, I32)
        for col in range(16):
            xcol = col * T
            tin = kb.v(lambda e, xcol=xcol: e.tensor_scalar(
                chiq[:, 0:T], xqb[:, xcol:xcol + T], scbc[:, 1:2], None,
                op0=ALU.mult), deps=[sb0] + xq_toks + a5 +
                [q_toks[-1] if q_toks else None], inc=True)
            tq = kb.s(lambda e, xcol=xcol: e.activation(
                xqb[:, xcol:xcol + T], chiq[:, 0:T], AF.Identity,
                scale=scbc[:, 0:1]), deps=[tin], inc=True)
            q_toks.append(tq)
        tap("xq", xqb[:, :], q_toks)
        for nm in ("lnrF", "brwF", "x2cF"):
            ar.free(nm, q_toks[-1])
        ar.free("chiq", q_toks[-1])

        wst, a6 = ar.alloc("wstF", 2 * NCH * 128, F16)
        sps, a7 = ar.alloc("spsF", 2 * S * T, F16)
        p1deps = a6 + a7 + q_toks
        mxu0, mxg0 = 16, 64
        spill_toks = []
        wfree = {}
        spfree = {}
        for mi, (dW, dSp) in enumerate(((d_upT, d_up), (d_gateT, d_gate))):
            for J in range(NI):
                ws = (mi * NI + J) % 2
                lw = kb.dma("y",
                            wst[:, ws * 1024:(ws + 1) * 1024]
                            .rearrange("p (ci j) -> p ci j", j=128),
                            dW.rearrange("(ci p) j -> p ci j", p=128)
                            [:, :, J * 128:(J + 1) * 128],
                            deps=p1deps + [wfree.get((mi * NI + J) - 2)])
                ev_list = []
                for tg in range(8):
                    bk = tg % 4
                    tkmm = None
                    for ci in range(NCH):
                        tkmm = kb.t(lambda e, ws=ws, ci=ci, tg=tg, bk=bk:
                                    e.matmul(
                            bank(bk),
                            wst[:, ws * 1024 + ci * 128:
                                ws * 1024 + (ci + 1) * 128],
                            xqb[:, ci * 2 * T + tg * 512:
                                ci * 2 * T + (tg + 1) * 512],
                            start=(ci == 0), stop=(ci == NCH - 1)),
                            deps=(kb.bdep(bk) + [lw] + q_toks)
                            if ci == 0 else [], inc=(ci == NCH - 1))
                    ev = kb.s(lambda e, ws=ws, tg=tg, bk=bk: e.activation(
                        sps[:, ws * S * T + tg * 512:
                            ws * S * T + (tg + 1) * 512],
                        bank(bk), AF.Identity),
                        deps=[tkmm] + list(spfree.get((mi * NI + J) - 2, ())),
                        inc=True)
                    kb.bset(ev, bk)
                    ev_list.append(ev)
                rd = kb.v(lambda e, ws=ws, mc=(mxu0 if mi == 0 else mxg0) + J:
                          e.tensor_reduce(
                    maxc[:, mc:mc + 1], sps[:, ws * S * T:(ws + 1) * S * T],
                    axis=AX, op=ALU.max, apply_absolute_value=True),
                    deps=ev_list, inc=True)
                sp = kb.dma("s", dSp[:, J * S * T:(J + 1) * S * T],
                            sps[:, ws * S * T:(ws + 1) * S * T],
                            deps=ev_list)
                wfree[mi * NI + J] = tkmm
                spfree[mi * NI + J] = (sp, rd)
                spill_toks.append(sp)
                spill_toks.append(rd)
        t_mu = colmax(mxu0, mxu0 + NI, nrow, RX + 1, spill_toks)
        t_mg = colmax(mxg0, mxg0 + NI, nrow, RX + 2, [t_mu] + spill_toks)
        t_x1 = cc_exchange(1, nrow, RX + 1, [t_mg])
        t_su = scale_pair(nrow, RX + 1, RX + 18, RX + 19, [t_x1])
        t_sg = scale_pair(nrow, RX + 2, RX + 20, RX + 21, [t_su])
        sb1 = bc_row32(nrow, RX + 18, 4, scbc, 2, [t_sg], bki=1)
        ar.free("xq", spill_toks[-1])
        ar.free("wstF", spill_toks[-1])

        gch, b1 = ar.alloc("gch", 2 * T, F16)
        chi, b2 = ar.alloc("chiF", T, I32)
        sga, b3 = ar.alloc("sga", T, F16)
        gq, b4 = ar.alloc("gq", T, F16)
        t2b, b5 = ar.alloc("t2b", 2 * T, F16)
        p2deps = b1 + b2 + b3 + b4 + b5 + [sb1]
        NCK = NI * S * T // T
        prev = {}
        p2_toks = []
        for ck in range(NCK):
            sl = (ck % 2)
            lg = kb.dma("y", gch[:, sl * T:(sl + 1) * T],
                        d_gate[:, ck * T:(ck + 1) * T],
                        deps=p2deps + [prev.get(("ng", ck - 2))])
            tng = kb.v(lambda e, sl=sl: e.tensor_scalar(
                chi[:, 0:T], gch[:, sl * T:(sl + 1) * T], scbc[:, 5:6], None,
                op0=ALU.mult), deps=[lg, prev.get(("sig", ck - 1)),
                                     prev.get(("gq", ck - 1))], inc=True)
            tsg = kb.s(lambda e: e.activation(
                sga[:, 0:T], chi[:, 0:T], AF.Sigmoid, scale=scbc[:, 4:5]),
                deps=[tng, prev.get(("ns", ck - 1)),
                      prev.get(("t2", ck - 1))], inc=True)
            tgq = kb.s(lambda e: e.activation(
                gq[:, 0:T], chi[:, 0:T], AF.Identity, scale=scbc[:, 4:5]),
                deps=[tng, prev.get(("t2", ck - 1))], inc=True)
            tns = kb.v(lambda e: e.tensor_scalar(
                chi[:, 0:T], sga[:, 0:T], 65536.0, None, op0=ALU.mult),
                deps=[tsg, tgq], inc=True)
            tsq = kb.v(lambda e: e.tensor_scalar(
                sga[:, 0:T], chi[:, 0:T], 65535.0, 1.0 / 65536.0,
                op0=ALU.min, op1=ALU.mult), deps=[tns], inc=True)
            tt2 = kb.g(lambda e, sl=sl: e.tensor_tensor(
                t2b[:, sl * T:(sl + 1) * T], gq[:, 0:T], sga[:, 0:T],
                op=ALU.mult), deps=[tsq, tgq, prev.get(("t2", ck - 2))],
                inc=True)
            trd = kb.v(lambda e, sl=sl, mc=112 + (ck % 64): e.tensor_reduce(
                maxc[:, mc:mc + 1], t2b[:, sl * T:(sl + 1) * T], axis=AX,
                op=ALU.max, apply_absolute_value=True),
                deps=[tt2, prev.get(("rd", ck - 64))], inc=True)
            sp = kb.dma("g", d_t2[:, ck * T:(ck + 1) * T],
                        t2b[:, sl * T:(sl + 1) * T], deps=[tt2])
            prev[("sig", ck)] = tsg
            prev[("gq", ck)] = tgq
            prev[("ns", ck)] = tns
            prev[("ng", ck)] = tng
            prev[("t2", ck)] = sp
            prev[("rd", ck)] = trd
            p2_toks += [sp, trd]
        t_m2 = colmax(112, 112 + 64 if NCK > 64 else 112 + NCK, nrow, RX + 3,
                      p2_toks)
        t_x2 = cc_exchange(2, nrow, RX + 3, [t_m2])
        t_s2 = scale_pair(nrow, RX + 3, RX + 22, RX + 23, [t_x2])
        sb2 = bc_row32(nrow, RX + 22, 2, scbc, 6, [t_s2], bki=2)

        osp, c1 = ar.alloc("osp", 2 * T, F16)
        p3deps = c1 + [sb2] + p2_toks
        p3_toks = []
        prev3 = {}
        for ck in range(NCK):
            sl = ck % 2
            lt = kb.dma("y", gch[:, sl * T:(sl + 1) * T],
                        d_t2[:, ck * T:(ck + 1) * T],
                        deps=p3deps + [prev3.get(("a", ck - 2))])
            lu = kb.dma("s", t2b[:, sl * T:(sl + 1) * T],
                        d_up[:, ck * T:(ck + 1) * T],
                        deps=p3deps + [prev3.get(("b", ck - 2))])
            tn2 = kb.v(lambda e, sl=sl: e.tensor_scalar(
                chi[:, 0:T], gch[:, sl * T:(sl + 1) * T], scbc[:, 7:8], None,
                op0=ALU.mult), deps=[lt, prev3.get(("q", ck - 1))], inc=True)
            tq2 = kb.s(lambda e: e.activation(
                sga[:, 0:T], chi[:, 0:T], AF.Identity, scale=scbc[:, 6:7]),
                deps=[tn2, prev3.get(("sp", ck - 1))], inc=True)
            tnu = kb.v(lambda e, sl=sl: e.tensor_scalar(
                chi[:, 0:T], t2b[:, sl * T:(sl + 1) * T], scbc[:, 3:4], None,
                op0=ALU.mult), deps=[lu, tq2], inc=True)
            tqu = kb.s(lambda e: e.activation(
                gq[:, 0:T], chi[:, 0:T], AF.Identity, scale=scbc[:, 2:3]),
                deps=[tnu, prev3.get(("sp", ck - 1))], inc=True)
            to = kb.g(lambda e, sl=sl: e.tensor_tensor(
                osp[:, sl * T:(sl + 1) * T], sga[:, 0:T], gq[:, 0:T],
                op=ALU.mult), deps=[tqu, tq2, prev3.get(("sp", ck - 2))],
                inc=True)
            trd = kb.v(lambda e, sl=sl, mc=176 + (ck % 64): e.tensor_reduce(
                maxc[:, mc:mc + 1], osp[:, sl * T:(sl + 1) * T], axis=AX,
                op=ALU.max, apply_absolute_value=True),
                deps=[to, prev3.get(("rd", ck - 64))], inc=True)
            sp = kb.dma("g", d_oraw[:, ck * T:(ck + 1) * T],
                        osp[:, sl * T:(sl + 1) * T], deps=[to])
            prev3[("a", ck)] = tn2
            prev3[("b", ck)] = tnu
            prev3[("q", ck)] = tqu
            prev3[("rd", ck)] = trd
            prev3[("sp", ck)] = (sp if False else trd) and sp
            p3_toks += [sp, trd]
        t_m3 = colmax(176, 176 + min(64, NCK), nrow, RX + 4, p3_toks)
        t_x3 = cc_exchange(3, nrow, RX + 4, [t_m3])
        t_s3 = scale_pair(nrow, RX + 4, RX + 24, RX + 25, [t_x3])
        sb3 = bc_row32(nrow, RX + 24, 2, scbc, 8, [t_s3], bki=3)
        for nm in ("gch", "chiF", "sga", "gq", "t2b", "osp"):
            ar.free(nm, p3_toks[-1])

        foutb, c3 = ar.alloc("fout", NCH * 2 * T, F16)
        wdn, c4 = ar.alloc("wdn", NI * 128 * NCH, F16)
        oqst, c5a = ar.alloc("oqst", NI * 512, F16)
        chi2, c5b = ar.alloc("chi2", T, I32)
        p4deps = c3 + c4 + c5a + c5b + [sb3] + p3_toks
        lwall = kb.dma("y", wdn[:, :].rearrange("p (ii j) -> p ii j", j=C),
                       d_downT.rearrange("(ii p) j -> p ii j", p=128),
                       deps=p4deps)
        dn_ev = []
        oq_free = None
        for tg in range(8):
            lo = kb.dma("s", oqst[:, :].rearrange("p (ii t) -> p ii t", t=512),
                        d_oraw[:, :].rearrange("p (ii t) -> p ii t", t=S * T)
                        [:, :, tg * 512:(tg + 1) * 512],
                        deps=p4deps + [oq_free])
            qd_toks = []
            for sub in range(6):
                w = 2048
                ti2 = kb.v(lambda e, sub=sub: e.tensor_scalar(
                    chi2[:, 0:T], oqst[:, sub * T:(sub + 1) * T],
                    scbc[:, 9:10], None, op0=ALU.mult),
                    deps=[lo] + ([qd_toks[-1]] if qd_toks else []), inc=True)
                tq2 = kb.s(lambda e, sub=sub: e.activation(
                    oqst[:, sub * T:(sub + 1) * T], chi2[:, 0:T],
                    AF.Identity, scale=scbc[:, 8:9]), deps=[ti2], inc=True)
                qd_toks.append(tq2)
            for J in range(NCH):
                bk = (tg * NCH + J) % 4
                tkmm = None
                for ii in range(NI):
                    tkmm = kb.t(lambda e, J=J, ii=ii, tg=tg, bk=bk: e.matmul(
                        bank(bk),
                        wdn[:, ii * C + J * 128: ii * C + (J + 1) * 128],
                        oqst[:, ii * 512:(ii + 1) * 512],
                        start=(ii == 0), stop=(ii == NI - 1)),
                        deps=(kb.bdep(bk) + [lwall] + qd_toks)
                        if ii == 0 else [], inc=(ii == NI - 1))
                ev = kb.s(lambda e, J=J, tg=tg, bk=bk: e.activation(
                    foutb[:, J * 2 * T + tg * 512: J * 2 * T + (tg + 1) * 512],
                    bank(bk), AF.Identity), deps=[tkmm], inc=True)
                kb.bset(ev, bk)
                dn_ev.append(ev)
            oq_free = dn_ev[-1]
        ar.free("oqst", dn_ev[-1])
        ar.free("chi2", dn_ev[-1])
        ar.free("wdn", dn_ev[-1])
        tap("fout", foutb[:, :], [dn_ev[-1]])

        lnr5, c5 = ar.alloc("lnrG", 3 * T, F32)
        sq2, c6 = ar.alloc("sq2G", 2 * T, F16)
        brw2, c7 = ar.alloc("brwG", 2 * T, F16)
        xfc, c8 = ar.alloc("xfc", 2 * T, F32)
        out_toks = []
        for s in range(S):
            sqf = [None, None]
            tmm = None
            for ci in range(NCH):
                sl = (ci % 2) * T
                col = ci * 2 * T + s * T
                tsq = kb.s(lambda e, col=col, sl=sl: e.activation(
                    sq2[:, sl:sl + T], foutb[:, col:col + T], AF.Square),
                    deps=[dn_ev[-1], sqf[ci % 2],
                          out_toks[-1] if out_toks else None]
                    + c5 + c6 + c7 + c8, inc=True)
                for g in range(4):
                    dep0 = (kb.bdep(6, 7) + [tok_ones]) if ci == 0 else []
                    kb.t(lambda e, col=col, g=g, ci=ci: e.matmul(
                        bank(6, r0=32 * g, r1=32 * g + 1), ones1[:, 0:1],
                        foutb[:, col + 512 * g: col + 512 * (g + 1)],
                        start=(ci == 0), stop=(ci == NCH - 1),
                        tile_position=(0, 32 * g), skip_group_check=True),
                        deps=dep0 + [dn_ev[-1]])
                    tmm = kb.t(lambda e, g=g, sl=sl, ci=ci: e.matmul(
                        bank(7, r0=32 * g, r1=32 * g + 1), ones1[:, 0:1],
                        sq2[:, sl + 512 * g: sl + 512 * (g + 1)],
                        start=(ci == 0), stop=(ci == NCH - 1),
                        tile_position=(0, 32 * g), skip_group_check=True),
                        deps=[tsq] + dep0, inc=(g == 3))
                sqf[ci % 2] = tmm
            for g in range(4):
                ev1 = kb.v(lambda e, g=g: e.tensor_copy(
                    stat_ev[32 * g:32 * g + 1, 0:512],
                    bank(6, r0=32 * g, r1=32 * g + 1)),
                    deps=([tmm] + kb.gd("stat_ev")) if g == 0 else [], inc=True)
            for g in range(4):
                ev2 = kb.v(lambda e, g=g: e.tensor_copy(
                    stat_ev[32 * g:32 * g + 1, 512:1024],
                    bank(7, r0=32 * g, r1=32 * g + 1)),
                    deps=[tmm] if g == 0 else [], inc=True)
            kb.bset(ev2, 6, 7)
            g1 = kb.dma("y", lnr5[0:1, 0:T], stat_ev[0:97:32, 0:512],
                        deps=[ev1])
            g2 = kb.dma("y", lnr5[0:1, T:2 * T], stat_ev[0:97:32, 512:1024],
                        deps=[ev2])
            kb.gset("stat_ev", g2)
            inv = 1.0 / C
            kb.v(lambda e: e.tensor_scalar(lnr5[0:1, 2 * T:3 * T],
                                           lnr5[0:1, 0:T], inv, None,
                                           op0=ALU.mult), deps=[g1])
            kb.v(lambda e: e.tensor_scalar(lnr5[0:1, T:2 * T],
                                           lnr5[0:1, T:2 * T], inv, None,
                                           op0=ALU.mult), deps=[g2])
            kb.v(lambda e: e.tensor_tensor(lnr5[0:1, 0:T],
                                           lnr5[0:1, 2 * T:3 * T],
                                           lnr5[0:1, 2 * T:3 * T],
                                           op=ALU.mult))
            tvar = kb.v(lambda e: e.tensor_tensor(lnr5[0:1, T:2 * T],
                                                  lnr5[0:1, T:2 * T],
                                                  lnr5[0:1, 0:T],
                                                  op=ALU.subtract), inc=True)
            tln = kb.s(lambda e: e.activation(lnr5[0:1, 0:T],
                                              lnr5[0:1, T:2 * T],
                                              AF.Ln, bias=epcol[0:1, 1:2]),
                       deps=[tvar, tok_ep], inc=True)
            tr = kb.s(lambda e: e.activation(lnr5[0:1, 0:T], lnr5[0:1, 0:T],
                                             AF.Exp, scale=-0.5),
                      deps=[tln], inc=True)
            tmr = kb.v(lambda e: e.tensor_tensor(lnr5[0:1, T:2 * T],
                                                 lnr5[0:1, 2 * T:3 * T],
                                                 lnr5[0:1, 0:T], op=ALU.mult),
                       deps=[tr], inc=True)
            bts = ln_bcast(lnr5, brw2, tr, tmr)
            prev_x = None
            for ci in range(NCH):
                col = ci * 2 * T + s * T
                u = ln_apply(sq2, 0, foutb, col, brw2, bts + [prev_x])
                xr = kb.dma("y", xfc[:, 0:T], d_x2[s, ci * 128:(ci + 1) * 128, :],
                            deps=[prev_x])
                ad = kb.v(lambda e: e.tensor_tensor(
                    xfc[:, T:2 * T], sq2[:, 0:T], xfc[:, 0:T], op=ALU.add),
                    deps=[u, xr], inc=True)
                so = kb.dma("y", dyo[s, ci * 128:(ci + 1) * 128, :],
                            xfc[:, T:2 * T], deps=[ad])
                prev_x = so
                out_toks.append(so)
        kb.y(lambda e: e.nop(), deps=[out_toks[-1]])
        return out_toks

    ph = set(_PHASES)
    done = []
    dq_tok, dq_names = None, ()
    if "dequant" in ph:
        dq_tok, dq_names = emit_dequant()
        done.append(dq_tok)
    if "attn0" in ph:
        a_toks0 = emit_attention(
            0, [], free_after_qkv=((dq_names, dq_tok) if dq_tok else None))
        done += a_toks0
    if "attn1" in ph:
        a_toks1 = emit_attention(1, [])
        done += a_toks1
    if "ffn" in ph:
        emit_ffn(done)
    else:
        prev = None
        for s in range(S):
            for ci in range(NCH):
                prev = kb.dma("y", dyo[s, ci * 128:(ci + 1) * 128, :],
                              dx[s, ci * 128:(ci + 1) * 128, :],
                              deps=done + [prev])
        kb.y(lambda e: e.nop(), deps=[prev])

    kb.build()
    return nc, d_taps


def _prep_inputs(inputs):
    x = np.asarray(inputs["L0_attn_x"], np.float32)
    conv_w = np.asarray(inputs["L0_attn_conv_w"], np.float32)
    cw = np.ascontiguousarray(conv_w.transpose(1, 2, 0).reshape(64, KTAP * HS))
    cw2 = np.concatenate([cw, cw], axis=0)
    base = {
        "w_attn": np.asarray(inputs["L0_attn_c_attn_w"], np.float32),
        "b_attn": np.ascontiguousarray(
            np.asarray(inputs["L0_attn_c_attn_b"], np.float32)
            .reshape(NI, 128).T),
        "w_proj": np.asarray(inputs["L0_attn_c_proj_w"], np.float32),
        "b_proj": np.ascontiguousarray(
            np.asarray(inputs["L0_attn_c_proj_b"], np.float32)
            .reshape(NCH, 128).T),
        "w_conv": cw2,
        "b_conv": np.ascontiguousarray(
            np.tile(np.asarray(inputs["L0_attn_conv_b"], np.float32)
                    .reshape(64, 1), (2, 1))),
        "comp_w": np.asarray(inputs["L0_attn_comp_ln_w"], np.float32)
        .reshape(1, HS),
        "comp_b": np.asarray(inputs["L0_attn_comp_ln_b"], np.float32)
        .reshape(1, HS),
        "lnl_w": np.ascontiguousarray(
            np.asarray(inputs["L0_attn_ln_local_w"], np.float32)
            .reshape(NCH, 128).T),
        "lnl_b": np.ascontiguousarray(
            np.asarray(inputs["L0_attn_ln_local_b"], np.float32)
            .reshape(NCH, 128).T),
        "g_w1": np.asarray(inputs["L0_attn_gate_w1"], np.float32),
        "g_b1": np.asarray(inputs["L0_attn_gate_b1"], np.float32)
        .reshape(1, C),
        "g_lnw": np.asarray(inputs["L0_attn_gate_ln_w"], np.float32)
        .reshape(1, C),
        "g_lnb": np.asarray(inputs["L0_attn_gate_ln_b"], np.float32)
        .reshape(1, C),
        "g_w2": np.asarray(inputs["L0_attn_gate_w2"], np.float32),
        "g_b2": np.asarray(inputs["L0_attn_gate_b2"], np.float32)
        .reshape(1, 2),
        "w_up": np.asarray(inputs["L0_ffn_w_up"], np.float32),
        "w_gate": np.asarray(inputs["L0_ffn_w_gate"], np.float32),
        "w_down": np.asarray(inputs["L0_ffn_w_down"], np.float32),
    }
    in_maps = []
    for c in range(NCORES):
        m = dict(base)
        m["x"] = np.ascontiguousarray(
            x[2 * c:2 * c + 2].transpose(0, 2, 1))
        in_maps.append(m)
    return in_maps


def kernel(**inputs):
    in_maps = _prep_inputs(inputs)
    nc, d_taps = build_program(taps=tuple(_DEBUG_TAPS))
    res = run_bass_kernel_spmd(nc, in_maps, core_ids=list(range(NCORES)))
    out = np.empty((16, 2048, 1024), np.float32)
    for c in range(NCORES):
        yv = np.asarray(res.results[c]["y"], np.float32)
        out[2 * c:2 * c + 2] = yv.transpose(0, 2, 1)
    if _DEBUG_TAPS:
        kernel._tap_results = [
            {k: np.asarray(res.results[c].get("tap_" + k))
             for k in _DEBUG_TAPS if ("tap_" + k) in res.results[c]}
            for c in range(NCORES)]
        kernel._spill_results = {
            k: np.asarray(res.results[0][k]) for k in
            ("x2_sp", "ln2_sp", "up_sp", "gate_sp", "t2_sp", "oraw_sp",
             "upT_sp", "gateT_sp", "downT_sp") if k in res.results[0]}
    return out
